# version 10
# baseline (speedup 1.0000x reference)
import contextlib

import numpy as np
import ml_dtypes

import concourse.bass as bass
import concourse.mybir as mybir
from concourse.alu_op_type import AluOpType as OP
from concourse.bass_utils import run_bass_kernel_spmd

F32 = mybir.dt.float32
BF16 = mybir.dt.bfloat16
ACT = mybir.ActivationFunctionType
AX = mybir.AxisListType.X
BF = ml_dtypes.bfloat16

NCORES = 8
B, S, D = 32, 2048, 1024
NSEQ = B // NCORES
T = NSEQ * S
H, HD = 16, 64
IFF = 2816
ICH = IFF // 128
DCH = D // 128
MAGIC32 = 12582912.0
CBIG = float(np.float32(1.5 * 2**82))
QMAX = 127.0


def _host_consts():
    half = 32
    freqs = 1.0 / 10000.0 ** (np.arange(half, dtype=np.float64) / half)
    pos = np.arange(S, dtype=np.float64)
    ang = pos[None, :] * freqs[:, None]
    cos32 = np.cos(ang)
    sin32 = np.sin(ang)
    r = np.arange(128)
    i_of_r = r % 32
    sgn = np.where((r % 64) < 32, -1.0, 1.0)
    tcos = cos32[i_of_r, :]
    tsin = sin32[i_of_r, :] * sgn[:, None]
    perm = np.zeros((128, 128), dtype=np.float64)
    for blk in range(2):
        b0 = 64 * blk
        perm[b0:b0 + 32, b0 + 32:b0 + 64] = np.eye(32)
        perm[b0 + 32:b0 + 64, b0:b0 + 32] = np.eye(32)
    rr, cc = np.meshgrid(np.arange(128), np.arange(128), indexing="ij")
    tri01 = (rr <= cc).astype(np.float64)
    c = {
        "tcos": tcos, "tsin": tsin,
        "permc": perm, "tri01": tri01,
    }
    return {k: np.asarray(v, dtype=np.float32).astype(BF) for k, v in c.items()}


def build_program(dbg=(), stop_after=None):
    nc = bass.Bass("TRN2", target_bir_lowering=False, debug=False)
    es = contextlib.ExitStack()

    def dram(name, shape, dtype, kind=None):
        if name in dbg:
            kind = "ExternalOutput"
        if kind is None:
            return nc.dram_tensor(name, shape, dtype)
        return nc.dram_tensor(name, shape, dtype, kind=kind)

    x_d = dram("x", [T, D], F32, "ExternalInput")
    qkvw_d = dram("qkv_w", [3 * D, D], F32, "ExternalInput")
    qkvb_d = dram("qkv_b", [3 * D], F32, "ExternalInput")
    pjw_d = dram("proj_w", [D, D], F32, "ExternalInput")
    pjb_d = dram("proj_b", [D], F32, "ExternalInput")
    clx_d = dram("clip_x", [1], F32, "ExternalInput")
    cld_d = dram("clip_down", [1], F32, "ExternalInput")
    wg_d = dram("w_gate", [IFF, D], F32, "ExternalInput")
    wu_d = dram("w_up", [IFF, D], F32, "ExternalInput")
    wd_d = dram("w_down", [D, IFF], F32, "ExternalInput")
    tcos_d = dram("tcos", [128, S], BF16, "ExternalInput")
    tsin_d = dram("tsin", [128, S], BF16, "ExternalInput")
    perm_d = dram("permc", [128, 128], BF16, "ExternalInput")
    tri_d = dram("tri01", [128, 128], BF16, "ExternalInput")
    out_d = dram("out", [T, D], F32, "ExternalOutput")
    xn32_d = dram("xn32", [T, D], F32)
    xnbf_d = dram("xnbf", [T, D], BF16)
    xqbf_d = dram("xqbf", [T, D], BF16)
    wkvbf_d = dram("wkvbf", [3 * D, D], BF16)
    wpjbf_d = dram("wpjbf", [D, D], BF16)
    wgi_d = dram("wgi", [IFF, D], BF16)
    wui_d = dram("wui", [IFF, D], BF16)
    wdi_d = dram("wdi", [D, IFF], BF16)
    h32_d = dram("h32", [ICH, 128, T], F32)
    attnb_d = dram("attnb", [DCH, 128, T], BF16)
    cc1i_d = dram("cc1i", [1, 1], F32)
    cc1o_d = dram("cc1o", [1, 1], F32)
    cc2i_d = dram("cc2i", [1, 1], F32)
    cc2o_d = dram("cc2o", [1, 1], F32)
    red_d = dram("redsc", [128], F32)
    dxnf_d = dram("dxnf", [128, DCH, S], BF16)
    dqro_d = dram("dqro", [128, DCH, S], BF16)
    dkro_d = dram("dkro", [128, DCH, S], BF16)
    dvt_d = dram("dvt", [128, 16, 1040], BF16)
    dnegb_d = dram("dnegb", [128, 16], F32)
    dpt0_d = dram("dpt0", [128, 1024], BF16)
    dpt1_d = dram("dpt1", [128, 1024], BF16)
    dav_d = dram("dav", [128, 512], F32)
    dwdf_d = dram("dwdf", [128, 22, 1024], BF16)
    dwpj_d = dram("dwpj", [128, 8, 1024], BF16)
    dith_d = dram("dith", [128, 22, 512], BF16)
    datst_d = dram("datst", [128, 8, 512], BF16)
    dbias_d = dram("dbias", [2, 4096], BF16)
    dsc_d = dram("dsc", [128, 8], F32)
    drr_d = dram("drr", [65, 512], F32)
    drcf_d = dram("drcf", [64, 512], F32)
    datb_d = dram("datb", [128, 512], BF16)
    hlscr_d = dram("hlscr", [2, 2048], BF16)

    sems = {}
    for s in ("sy", "tr", "gp", "te", "ve", "ac", "cc", "bar"):
        sems[s] = es.enter_context(nc.semaphore("sem_" + s))
    CN = {k: 0 for k in sems}
    DMA_FAM = {"sy": ["sy"], "tr": ["tr"], "gp": ["gp"]}

    def dsem(name, fam):
        if name not in sems:
            sems[name] = es.enter_context(nc.semaphore("sem_" + name))
            CN[name] = 0
            DMA_FAM[fam].append(name)
        return name

    def sb(name, shape, dtype):
        return es.enter_context(nc.sbuf_tensor(name, shape, dtype))

    ones2 = sb("ones2", [2, 512], BF16)
    magic = sb("magict", [128, 1], F32)
    epsc = sb("epsc", [128, 1], F32)
    onetmp = sb("onetmp", [128, 1], F32)
    bhlqk = sb("bhlqk", [2, 2 * D], BF16)
    pbhl = sb("pbhl", [2, D], BF16)
    vbp = sb("vbp", [2, 1040], BF16)
    cl32 = sb("cl32", [1, 2], F32)
    wgsc = sb("wgsc", [128, ICH], F32)
    wusc = sb("wusc", [128, ICH], F32)
    wdsc = sb("wdsc", [128, DCH], F32)
    scg = sb("scg", [128, ICH], F32)
    scu = sb("scu", [128, ICH], F32)
    rsx_bc = sb("rsx_bc", [128, 1], F32)
    sx_bc = sb("sx_bc", [128, 1], F32)
    rsh_bc = sb("rsh_bc", [128, 1], F32)
    sh_bc = sb("sh_bc", [128, 1], F32)
    tiny = sb("tinyt", [1, 16], F32)
    tinyb = sb("tinyb", [1, 16], BF16)
    trow = sb("trowt", [1, 128], F32)
    xmg = sb("xmg", [128, 64], F32)
    hmg = sb("hmg", [128, 88], F32)
    onesPb = sb("onesPb", [128, 128], BF16)
    ind1e = sb("ind1e", [128, 1], BF16)
    ind1o = sb("ind1o", [128, 1], BF16)

    ps_sA = es.enter_context(nc.psum_tensor("ps_sA", [128, 1024], F32))
    ps_sB = es.enter_context(nc.psum_tensor("ps_sB", [128, 1024], F32))
    ps_avA = es.enter_context(nc.psum_tensor("ps_avA", [128, 512], F32))
    ps_avB = es.enter_context(nc.psum_tensor("ps_avB", [128, 512], F32))
    ps_xA = es.enter_context(nc.psum_tensor("ps_xA", [128, 512], F32))
    ps_xB = es.enter_context(nc.psum_tensor("ps_xB", [128, 512], F32))

    block = es.enter_context(nc.Block())
    REC = {k: [] for k in ("sy", "sc", "ve", "te", "gp")}

    class _Pend:
        __slots__ = ("incs",)

        def __init__(self):
            self.incs = []

        def then_inc(self, sem, n=1):
            self.incs.append((sem, n))
            return self

    class _Eng:
        def __init__(self, key):
            self._k = key

        def __getattr__(self, name):
            q = REC[self._k]

            def call(*a, **kw):
                p = _Pend()
                q.append((name, a, kw, p))
                return p
            return call

    SY, SC, VE, TE, GP = _Eng("sy"), _Eng("sc"), _Eng("ve"), _Eng("te"), _Eng("gp")

    def inc(instr, s, n=1):
        instr.then_inc(sems[s], n)
        CN[s] += n
        return CN[s]

    def mark(key, s):
        name = REC[key][-1][0]
        assert name not in ("wait_ge",), name

        REC[key][-1][3].incs.append((sems[s], 1))
        CN[s] += 1
        return CN[s]

    def vemark():
        return mark("ve", "ve")

    def vsync():
        VE.drain()
        return mark("ve", "ve")

    def temark():
        return mark("te", "te")

    def w(eng, s, val=None):
        eng.wait_ge(sems[s], CN[s] if val is None else val)

    def barrier():
        for s in DMA_FAM["sy"]:
            SY.wait_ge(sems[s], CN[s])
        inc(SY.nop(), "bar")
        for s in DMA_FAM["tr"]:
            SC.wait_ge(sems[s], CN[s])
        inc(SC.nop(), "bar")
        for s in DMA_FAM["gp"]:
            GP.wait_ge(sems[s], CN[s])
        inc(GP.nop(), "bar")
        inc(TE.nop(), "bar")
        inc(VE.nop(), "bar")
        tgt = CN["bar"]
        for eng in (TE, VE, SC, GP, SY):
            eng.wait_ge(sems["bar"], tgt)

    def _finish():
        def _replay(key):
            def f(eng):
                for (name, a, kw, p) in REC[key]:
                    if key in ("ve", "te", "sc") and name == "nop" and p.incs:
                        eng.drain()
                    r = getattr(eng, name)(*a, **kw)
                    for (sem, n) in p.incs:
                        r.then_inc(sem, n)
            return f
        block.sync(_replay("sy"))
        block.scalar(_replay("sc"))
        block.vector(_replay("ve"))
        block.tensor(_replay("te"))
        block.gpsimd(_replay("gp"))
        return nc, es

    dsem("gpc", "gp")
    inc(GP.dma_start(out=wkvbf_d[:, :], in_=qkvw_d[:, :]), "gpc", 16)
    inc(GP.dma_start(out=wpjbf_d[:, :], in_=pjw_d[:, :]), "gpc", 16)
    gp_casts = dict(CN)

    VE.memset(ones2[:, :], 1.0)
    VE.memset(epsc[:, :], 1e-5)
    VE.memset(onesPb[:, :], 1.0)
    VE.memset(ind1e[0:64, 0:1], 1.0)
    VE.memset(ind1e[64:128, 0:1], 0.0)
    VE.memset(ind1o[0:64, 0:1], 0.0)
    VE.memset(ind1o[64:128, 0:1], 1.0)
    VE.memset(onetmp[:, :], 1.0)
    VE.drain()
    VE.tensor_scalar(magic[:, :], onetmp[:, :], float(2.0 ** -30), 1.0,
                     OP.add, OP.subtract)
    VE.drain()
    VE.tensor_scalar(magic[:, :], magic[:, :], CBIG, MAGIC32,
                     OP.mult, OP.add)
    VE.drain()

    p0 = contextlib.ExitStack()
    qb32 = p0.enter_context(nc.sbuf_tensor("qb32", [1, 3 * D], F32))
    hl16 = p0.enter_context(nc.sbuf_tensor("hl16", [1, 3 * D], BF16))
    hl32 = p0.enter_context(nc.sbuf_tensor("hl32", [1, 3 * D], F32))
    lo32 = p0.enter_context(nc.sbuf_tensor("lo32", [1, 3 * D], F32))
    lo16 = p0.enter_context(nc.sbuf_tensor("lo16", [1, 3 * D], BF16))
    vb32 = p0.enter_context(nc.sbuf_tensor("vb32", [1, 1040], F32))

    dsem("bia", "sy")
    inc(SY.dma_start(out=qb32[:, :], in_=qkvb_d.ap().rearrange("(a d) -> a d", a=1)), "bia", 16)
    inc(SY.dma_start(out=cl32[:, 0:1], in_=clx_d.ap().rearrange("(a d) -> a d", a=1)), "bia", 16)
    inc(SY.dma_start(out=cl32[:, 1:2], in_=cld_d.ap().rearrange("(a d) -> a d", a=1)), "bia", 16)
    ld_bias = CN["bia"]

    def hilo_row(src_ap, n, dst, dst_off):
        w(VE, "gpb") if "gpb" in sems else None
        VE.tensor_copy(hl16[:, 0:n], src_ap)
        VE.drain()
        VE.tensor_copy(hl32[:, 0:n], hl16[:, 0:n])
        VE.drain()
        VE.tensor_tensor(lo32[:, 0:n], src_ap, hl32[:, 0:n], OP.subtract)
        VE.drain()
        VE.tensor_copy(lo16[:, 0:n], lo32[:, 0:n])
        v = vsync()
        dsem("gpb", "gp")
        w(GP, "ve", v)
        inc(GP.dma_start(out=hlscr_d[0:1, 0:n], in_=hl16[:, 0:n]), "gpb", 16)
        inc(GP.dma_start(out=hlscr_d[1:2, 0:n], in_=lo16[:, 0:n]), "gpb", 16)
        w(GP, "gpb")
        inc(GP.dma_start(out=dst[0:2, dst_off:dst_off + n], in_=hlscr_d[0:2, 0:n]), "gpb", 16)

    w(VE, "bia", ld_bias)
    hilo_row(qb32[:, 0:2 * D], 2 * D, bhlqk, 0)
    VE.memset(vb32[:, :], 0.0)
    VE.tensor_copy(bass.AP(vb32, 0, [[1040, 1], [65, 16], [1, 64]]),
                   bass.AP(qb32, 2 * D, [[3 * D, 1], [64, 16], [1, 64]]))
    VE.memset(bass.AP(vb32, 64, [[1040, 1], [65, 16], [1, 1]]), 1.0)
    VE.drain()
    hilo_row(vb32[:, 0:1040], 1040, vbp, 0)
    pb32 = qb32
    pv = CN["ve"]
    w(SY, "ve", pv)
    w(SY, "gp")
    inc(SY.dma_start(out=pb32[:, 0:D], in_=pjb_d.ap().rearrange("(a d) -> a d", a=1)), "bia", 16)
    w(VE, "bia")
    hilo_row(pb32[:, 0:D], D, pbhl, 0)

    bias_gp = CN["gp"]
    barrier()
    p0.close()

    with contextlib.ExitStack() as p1:
        def sb1(name, shape, dtype):
            return p1.enter_context(nc.sbuf_tensor(name, shape, dtype))
        xb = sb1("xb", [128, 2, 4096], F32)
        sqb = sb1("sqb", [128, 2, 4096], F32)
        xn32b = sb1("xn32b", [128, 2, 4096], F32)
        xn16b = sb1("xn16b", [128, 2, 4096], BF16)
        st_sum = sb1("st_sum", [128, 2, 4], F32)
        st_sq = sb1("st_sq", [128, 2, 4], F32)
        st_m = sb1("st_m", [128, 2, 4], F32)
        st_e = sb1("st_e", [128, 2, 4], F32)
        st_v = sb1("st_v", [128, 2, 4], F32)
        st_sd = sb1("st_sd", [128, 2, 4], F32)
        st_r = sb1("st_r", [128, 2, 4], F32)
        st_n = sb1("st_n", [128, 2, 4], F32)

        slot_tok = [[], []]
        spill_tok = [[], []]
        x_rr = x_d.ap().rearrange("(b m p) d -> b p m d", p=128, m=4)
        xn32_rr = xn32_d.ap().rearrange("(b m p) d -> b p m d", p=128, m=4)
        xnbf_rr = xnbf_d.ap().rearrange("(b m p) d -> b p m d", p=128, m=4)

        for b in range(16):
            sl = b % 2
            for (s, v) in slot_tok[sl]:
                SY.wait_ge(sems[s], v)
            slot_tok[sl] = []
            xs_s = dsem(f"x{sl}", "sy")
            ldv = inc(SY.dma_start(out=bass.AP(xb, sl * 4096, [[8192, 128], [1024, 4], [1, 1024]]),
                                   in_=x_rr[b]), xs_s, 16)
            w(VE, xs_s, ldv)
            VE.tensor_reduce(st_sum[:, sl, :],
                             bass.AP(xb, sl * 4096, [[8192, 128], [1024, 4], [1, 1024]]),
                             AX, OP.add)
            w(SC, xs_s, ldv)
            sqv = inc(SC.activation(sqb[:, sl, :], xb[:, sl, :], ACT.Square), "ac")
            w(VE, "ac", sqv)
            VE.tensor_reduce(st_sq[:, sl, :],
                             bass.AP(sqb, sl * 4096, [[8192, 128], [1024, 4], [1, 1024]]),
                             AX, OP.add)
            VE.drain()
            VE.tensor_scalar(st_m[:, sl, :], st_sum[:, sl, :], 1.0 / D, None, OP.mult, OP.bypass)
            VE.tensor_scalar(st_e[:, sl, :], st_sq[:, sl, :], 1.0 / D, None, OP.mult, OP.bypass)
            VE.drain()
            VE.tensor_tensor(st_v[:, sl, :], st_m[:, sl, :], st_m[:, sl, :], OP.mult)
            VE.drain()
            VE.tensor_tensor(st_v[:, sl, :], st_e[:, sl, :], st_v[:, sl, :], OP.subtract)
            vv = vemark()
            w(SC, "ve", vv)
            sdv = inc(SC.activation(st_sd[:, sl, :], st_v[:, sl, :], ACT.Sqrt, bias=epsc[:, 0:1]), "ac")
            w(VE, "ac", sdv)
            VE.reciprocal(st_r[:, sl, :], st_sd[:, sl, :])
            VE.drain()
            VE.tensor_tensor(st_n[:, sl, :], st_m[:, sl, :], st_r[:, sl, :], OP.mult)
            VE.drain()
            VE.tensor_scalar(st_n[:, sl, :], st_n[:, sl, :], -1.0, None, OP.mult, OP.bypass)
            VE.drain()
            for (s, v) in spill_tok[sl]:
                VE.wait_ge(sems[s], v)
            for mm in range(4):
                VE.tensor_scalar(xn32b[:, sl, 1024 * mm:1024 * mm + 1024],
                                 xb[:, sl, 1024 * mm:1024 * mm + 1024],
                                 st_r[:, sl, mm:mm + 1], st_n[:, sl, mm:mm + 1],
                                 OP.mult, OP.add)
            xnv = vsync()
            VE.tensor_reduce(xmg[:, 4 * b:4 * b + 4],
                             bass.AP(xn32b, sl * 4096, [[8192, 128], [1024, 4], [1, 1024]]),
                             AX, OP.max, apply_absolute_value=True)
            xmv = vemark()
            w(SC, "ve", xnv)
            cav = inc(SC.activation(xn16b[:, sl, :], xn32b[:, sl, :], ACT.Copy), "ac")
            sp32 = dsem(f"sp32_{sl}", "sy")
            sp16 = dsem(f"sp16_{sl}", "sy")
            w(SY, "ve", xnv)
            s1 = inc(SY.dma_start(out=xn32_rr[b],
                                  in_=bass.AP(xn32b, sl * 4096, [[8192, 128], [1024, 4], [1, 1024]])),
                     sp32, 16)
            w(SY, "ac", cav)
            s2 = inc(SY.dma_start(out=xnbf_rr[b],
                                  in_=bass.AP(xn16b, sl * 4096, [[8192, 128], [1024, 4], [1, 1024]])),
                     sp16, 16)
            slot_tok[sl] = [(sp16, s2), ("ve", xmv), ("ac", cav)]
            spill_tok[sl] = [(sp32, s1), (sp16, s2)]
        barrier()
        p1_done = dict(CN)
        if stop_after == "p1":
            barrier()
            return _finish()

    def partition_max_to_scalar(src_col, out11):
        dsem("gpe", "gp")
        v0 = vemark()
        w(GP, "ve", v0)
        inc(GP.dma_start(out=red_d[:], in_=src_col), "gpe", 16)
        w(GP, "gpe")
        inc(GP.dma_start(out=trow[:, :], in_=red_d.ap().rearrange("(a d) -> a d", a=1)), "gpe", 16)
        w(VE, "gpe")
        VE.tensor_reduce(out11, trow[:, :], AX, OP.max)
        VE.drain()

    def bcast128(src11_f32, dst_col, tmp_off):
        VE.tensor_copy(tinyb[:, tmp_off:tmp_off + 1], src11_f32)
        VE.drain()
        VE.tensor_copy(tiny[:, tmp_off:tmp_off + 1], tinyb[:, tmp_off:tmp_off + 1])
        VE.drain()
        VE.tensor_tensor(tiny[:, tmp_off + 1:tmp_off + 2], src11_f32,
                         tiny[:, tmp_off:tmp_off + 1], OP.subtract)
        VE.drain()
        VE.tensor_copy(tinyb[:, tmp_off + 1:tmp_off + 2], tiny[:, tmp_off + 1:tmp_off + 2])
        v1 = vsync()
        w(TE, "ve", v1)
        TE.matmul(ps_xA[0:128, 0:1], ones2[0:1, 0:128], tinyb[:, tmp_off:tmp_off + 1],
                  start=True, stop=False)
        t1 = inc(TE.matmul(ps_xA[0:128, 0:1], ones2[0:1, 0:128],
                           tinyb[:, tmp_off + 1:tmp_off + 2], start=False, stop=True), "te")
        w(VE, "te", t1)
        VE.tensor_copy(dst_col, ps_xA[0:128, 0:1])
        vd = vemark()
        return vd

    VE.tensor_reduce(onetmp[:, :], xmg[:, :], AX, OP.max)
    VE.drain()
    partition_max_to_scalar(onetmp[:, 0:1], tiny[:, 8:9])
    v0 = vemark()
    w(GP, "ve", v0)
    inc(GP.dma_start(out=cc1i_d[:, :], in_=tiny[:, 8:9]), "gpe", 16)
    w(GP, "gpe")
    ccv = inc(GP.collective_compute("AllReduce", OP.max,
                                    replica_groups=[list(range(NCORES))],
                                    ins=[cc1i_d.ap()], outs=[cc1o_d.ap()]), "cc")
    w(GP, "cc", ccv)
    g1 = inc(GP.dma_start(out=tiny[:, 9:10], in_=cc1o_d[:, :]), "gpe", 16)
    w(SC, "bia", ld_bias)
    sgv = inc(SC.activation(tiny[:, 10:11], cl32[:, 0:1], ACT.Sigmoid), "ac")
    w(VE, "gpe", g1)
    w(VE, "ac", sgv)
    VE.tensor_tensor(tiny[:, 11:12], tiny[:, 9:10], tiny[:, 10:11], OP.mult)
    VE.drain()
    VE.tensor_scalar(tiny[:, 11:12], tiny[:, 11:12], 1.0 / QMAX, None, OP.mult, OP.bypass)
    VE.drain()
    VE.reciprocal(tiny[:, 12:13], tiny[:, 11:12])
    VE.drain()
    bcast128(tiny[:, 11:12], sx_bc[:, 0:1], 0)
    bcast128(tiny[:, 12:13], rsx_bc[:, 0:1], 2)
    ex1_done = dict(CN)
    if stop_after == "ex1":
        barrier()
        return _finish()

    with contextlib.ExitStack() as p2:
        def sb2(name, shape, dtype):
            return p2.enter_context(nc.sbuf_tensor(name, shape, dtype))
        tcos = sb2("tcos_s", [128, S], BF16)
        tsin = sb2("tsin_s", [128, S], BF16)
        permt = sb2("permt", [128, 128], BF16)
        trit = sb2("trit", [128, 128], BF16)
        wqkvst = sb2("wqkvst", [128, 2, DCH, 128], BF16)
        wvp = sb2("wvp", [128, DCH, 1040], BF16)
        vstage = sb2("vstage", [128, D], BF16)
        xnf = sb2("xnf", [128, DCH, S], BF16)
        qro = sb2("qro", [128, DCH, S], BF16)
        kro = sb2("kro", [128, DCH, S], BF16)
        vt = sb2("vt", [128, 16, 1040], BF16)
        PT = sb2("PTt", [128, 4, 1024], BF16)
        qraw = sb2("qraw", [128, 2, 512], BF16)
        rsw = sb2("rsw", [128, 2, 512], BF16)
        tmpA = sb2("tmpA", [128, 512], BF16)
        tmpC = sb2("tmpC", [128, 512], BF16)
        nqrow = sb2("nqrow", [1, 16], F32)
        nkrow = sb2("nkrow", [1, 16], F32)
        nrm8 = sb2("nrm8", [1, 8], F32)
        brow = sb2("brow", [1, 16], F32)
        browb = sb2("browb", [1, 16], BF16)
        negbc = sb2("negbc", [128, 16], F32)
        recf = sb2("recf", [65, 1024], F32)
        rh16 = sb2("rh16", [65, 1024], BF16)
        atb = [sb2("atb0", [128, 512], BF16), sb2("atb1", [128, 512], BF16)]
        wqt = sb2("wqt", [128, 1, 1408], F32)
        wqo = sb2("wqo", [128, 1, 1408], BF16)
        wqr = sb2("wqr", [128, 1, 4], F32)
        nrm4 = sb2("nrm4", [2, 4], F32)

        dsem("cst", "sy")
        for t_, d_ in ((tcos, tcos_d), (tsin, tsin_d),
                       (permt, perm_d), (trit, tri_d)):
            inc(SY.dma_start(out=t_[:, :], in_=d_[:, :]), "cst", 16)
        ld_const2 = CN["cst"]

        w(SC, "gpc", gp_casts["gpc"])
        VE.memset(wvp[:, :, :], 0.0)
        wvp_ve0 = vemark()
        for c in range(DCH):
            w(SC, "ve", wvp_ve0)
            for (s, v) in [("ve", CN["ve"])] if c > 0 else []:
                SC.wait_ge(sems[s], v)
            tv = inc(SC.dma_start_transpose(vstage[:, :],
                                            wkvbf_d[2 * D:3 * D, 128 * c:128 * c + 128]),
                     "tr", 16)
            w(VE, "tr", tv)
            VE.tensor_copy(bass.AP(wvp, c * 1040, [[DCH * 1040, 128], [65, 16], [1, 64]]),
                           bass.AP(vstage, 0, [[D, 128], [64, 16], [1, 64]]))
            vsync()
        wvp_v = CN["ve"]

        def quant_weights(src_d, dst_d, sc_tile, nch, width, dequant=False):
            dsem("wql", "sy")
            dsem("wqs", "sy")
            toks = [[]]
            halves = [(0, width)] if width <= 1408 else [(0, width // 2), (width // 2, width)]
            src_rr = src_d.ap().rearrange("(m p) d -> m p d", p=128)
            dst_rr = dst_d.ap().rearrange("(m p) d -> m p d", p=128)
            for m in range(nch):
                sl = 0
                lvs = []
                red_ve = 0
                for hi_, (a, b) in enumerate(halves):
                    for (s, v) in toks[sl]:
                        SY.wait_ge(sems[s], v)
                    toks[sl] = []
                    w(SY, "ve", red_ve)
                    lv = inc(SY.dma_start(out=wqt[:, sl, 0:b - a], in_=src_rr[m][:, a:b]),
                             "wql", 16)
                    lvs.append(lv)
                    w(VE, "wql", lv)
                    VE.tensor_reduce(wqr[:, sl, hi_:hi_ + 1], wqt[:, sl, 0:b - a], AX,
                                     OP.max, apply_absolute_value=True)
                    VE.drain()
                    red_ve = vemark()
                    if hi_ == len(halves) - 1:
                        VE.tensor_reduce(wqr[:, sl, 2:3], wqr[:, sl, 0:len(halves)], AX, OP.max)
                        VE.drain()
                        VE.tensor_scalar(sc_tile[:, m:m + 1], wqr[:, sl, 2:3], 1.0 / QMAX,
                                         None, OP.mult, OP.bypass)
                        VE.drain()
                        VE.reciprocal(wqr[:, sl, 3:4], sc_tile[:, m:m + 1])
                        VE.drain()
                if len(halves) == 1:
                    VE.tensor_scalar(wqt[:, sl, 0:width], wqt[:, sl, 0:width],
                                     wqr[:, sl, 3:4], magic[:, 0:1], OP.mult, OP.add)
                    VE.drain()
                    if dequant:
                        VE.tensor_scalar(wqt[:, sl, 0:width], wqt[:, sl, 0:width],
                                         magic[:, 0:1], sc_tile[:, m:m + 1],
                                         OP.subtract, OP.mult)
                        VE.drain()
                        VE.tensor_copy(wqo[:, sl, 0:width], wqt[:, sl, 0:width])
                    else:
                        VE.tensor_scalar(wqo[:, sl, 0:width], wqt[:, sl, 0:width],
                                         magic[:, 0:1], None, OP.subtract, OP.bypass)
                    qv = vsync()
                    w(SY, "ve", qv)
                    sv = inc(SY.dma_start(out=dst_rr[m], in_=wqo[:, sl, 0:width]), "wqs", 16)
                    toks[sl] = [("wqs", sv)]
                else:
                    prev_sv = None
                    for hi_ in (1, 0):
                        a, b = halves[hi_]
                        if hi_ == 0:
                            if prev_sv is not None:
                                SY.wait_ge(sems["wqs"], prev_sv)
                            lv = inc(SY.dma_start(out=wqt[:, sl, 0:b - a],
                                                  in_=src_rr[m][:, a:b]), "wql", 16)
                            w(VE, "wql", lv)
                        VE.tensor_scalar(wqt[:, sl, 0:b - a], wqt[:, sl, 0:b - a],
                                         wqr[:, sl, 3:4], magic[:, 0:1], OP.mult, OP.add)
                        VE.drain()
                        if dequant:
                            VE.tensor_scalar(wqt[:, sl, 0:b - a], wqt[:, sl, 0:b - a],
                                             magic[:, 0:1], sc_tile[:, m:m + 1],
                                             OP.subtract, OP.mult)
                            VE.drain()
                            VE.tensor_copy(wqo[:, sl, 0:b - a], wqt[:, sl, 0:b - a])
                        else:
                            VE.tensor_scalar(wqo[:, sl, 0:b - a], wqt[:, sl, 0:b - a],
                                             magic[:, 0:1], None, OP.subtract, OP.bypass)
                        qv = vsync()
                        w(SY, "ve", qv)
                        sv = inc(SY.dma_start(out=dst_rr[m][:, a:b], in_=wqo[:, sl, 0:b - a]),
                                 "wqs", 16)
                        prev_sv = sv
                        toks[sl] = [("wqs", sv)]

        quant_weights(wg_d, wgi_d, wgsc, ICH, D)
        quant_weights(wu_d, wui_d, wusc, ICH, D)
        quant_weights(wd_d, wdi_d, wdsc, DCH, IFF, dequant=True)
        wq_done = dict(CN)
        if stop_after == "wq":
            barrier()
            return _finish()
        VE.tensor_scalar(scg[:, :], wgsc[:, :], sx_bc[:, 0:1], None, OP.mult, OP.bypass)
        VE.tensor_scalar(scu[:, :], wusc[:, :], sx_bc[:, 0:1], None, OP.mult, OP.bypass)
        scgu_v = vemark()

        xnbf_mat = xnbf_d
        flash_idx = 0
        xnf_tok = []
        qk_ps = [ps_avA, ps_avB]
        qk_tok = [[], []]
        rs_ps = [ps_xA, ps_xB]
        rs_tok = [[("ve", ex1_done["ve"])], []]
        qraw_tok = [[], []]
        pt_tok = [[], [], [], []]
        av_tok = [[], []]
        sx_tok = [[], []]
        at_tok = [[], []]
        prev_seq = None
        for s_i in range(NSEQ):
            t0 = s_i * S
            if prev_seq is not None:
                w(VE, "te", prev_seq["te"])
                w(VE, "ac", prev_seq["ac"])
            for (s, v) in xnf_tok:
                SC.wait_ge(sems[s], v)
            xnf_tok = []
            for s_ in ("sp16_0", "sp16_1"):
                SC.wait_ge(sems[s_], p1_done.get(s_, CN[s_]))
            for c in range(DCH):
                inc(SC.dma_start_transpose(xnf[:, c, :],
                                           xnbf_mat[t0:t0 + S, 128 * c:128 * c + 128]),
                    "tr", 16)
            xnf_ld = CN["tr"]

            wst_tok = [[], []]
            pend = None

            def rope_flush():
                nonlocal pend
                if pend is None:
                    return
                pi, dst, dch, cols, ct, st_, ev = pend
                pend = None
                w(TE, "ac", ev)
                rp = rs_ps[pi]
                for (s, v) in rs_tok[pi]:
                    TE.wait_ge(sems[s], v)
                rs_tok[pi] = []
                rv = inc(TE.matmul(rp[:, 0:512], permt[:, :], qraw[:, pi, :],
                                   start=True, stop=True), "te")
                w(VE, "te", rv)
                VE.tensor_copy(rsw[:, pi, :], rp[:, 0:512])
                rcv = vemark()
                rs_tok[pi].append(("ve", rcv))
                VE.tensor_tensor(tmpA[:, :], qraw[:, pi, :], ct[:, cols], OP.mult)
                VE.tensor_tensor(tmpC[:, :], rsw[:, pi, :], st_[:, cols], OP.mult)
                VE.tensor_tensor(dst[:, dch, cols], tmpA[:, :], tmpC[:, :], OP.add)
                dv = vsync()
                qk_tok[pi].append(("ve", dv))
                qraw_tok[pi].append(("ve", dv))

            for m in range(16):
                dst = qro if m < 8 else kro
                dch = m if m < 8 else m - 8
                ct, st_ = tcos, tsin
                msl = m % 2
                for (s, v) in wst_tok[msl]:
                    SC.wait_ge(sems[s], v)
                wst_tok[msl] = []
                wtv = inc(SC.dma_start_transpose(wqkvst[:, msl, :, :],
                                                 wkvbf_d[128 * m:128 * m + 128, :]),
                          "tr", 16)
                for tc in range(4):
                    pi = (m * 4 + tc) % 2
                    ps = qk_ps[pi]
                    w(TE, "tr", max(xnf_ld, wtv))
                    w(TE, "cst", ld_const2)
                    w(TE, "gp", bias_gp)
                    for (s, v) in qk_tok[pi]:
                        TE.wait_ge(sems[s], v)
                    qk_tok[pi] = []
                    for d in range(DCH):
                        TE.matmul(ps[:, 0:512], wqkvst[:, msl, d, :],
                                  xnf[:, d, 512 * tc:512 * tc + 512],
                                  start=(d == 0), stop=False)
                    mv = inc(TE.matmul(ps[:, 0:512], bhlqk[0:2, 128 * m:128 * m + 128],
                                       ones2[0:2, 0:512], start=False, stop=True), "te")
                    w(SC, "te", mv)
                    for (s, v) in qraw_tok[pi]:
                        SC.wait_ge(sems[s], v)
                    qraw_tok[pi] = []
                    ev = inc(SC.activation(qraw[:, pi, :], ps[:, 0:512], ACT.Copy), "ac")
                    qk_tok[pi].append(("ac", ev))
                    if tc == 3:
                        wst_tok[msl].append(("te", CN["te"]))
                    rope_flush()
                    pend = (pi, dst, dch, slice(512 * tc, 512 * tc + 512), ct, st_, ev)
            rope_flush()
            rope_done_v = CN["ve"]

            w(VE, "ve", wvp_v)
            vcnt = 0
            for tt in range(16):
                for (c0, c1) in ((0, 512), (512, 1024), (1024, 1040)):
                    pi = vcnt % 2
                    vcnt += 1
                    ps = qk_ps[pi]
                    for (s, v) in qk_tok[pi]:
                        TE.wait_ge(sems[s], v)
                    qk_tok[pi] = []
                    w(TE, "ve", wvp_v)
                    for d in range(DCH):
                        TE.matmul(ps[:, 0:c1 - c0], xnf[:, d, 128 * tt:128 * tt + 128],
                                  wvp[:, d, c0:c1], start=(d == 0), stop=False)
                    mv = inc(TE.matmul(ps[:, 0:c1 - c0], ones2[0:2, 0:128],
                                       vbp[0:2, c0:c1], start=False, stop=True), "te")
                    w(SC, "te", mv)
                    ev = inc(SC.activation(vt[:, tt, c0:c1], ps[:, 0:c1 - c0], ACT.Copy), "ac")
                    qk_tok[pi].append(("ac", ev))
            v_done = CN["ac"]

            sq_te = 0
            for (src, dstn) in ((qro, nqrow), (kro, nkrow)):
                for c in range(DCH):
                    w(VE, "ve", rope_done_v)
                    w(VE, "te", sq_te)
                    sqbf = bass.AP(PT, 0, [[4096, 128], [1, 2048]])
                    VE.tensor_tensor(sqbf, src[:, c, :], src[:, c, :], OP.mult)
                    sqv = vsync()
                    w(TE, "ve", sqv)
                    for tc in range(4):
                        for (s, v) in sx_tok[0]:
                            TE.wait_ge(sems[s], v)
                        sx_tok[0] = []
                        sq_sl = bass.AP(PT, 512 * tc, [[4096, 128], [1, 512]])
                        TE.matmul(ps_sA[0:1, 0:512], ind1e[:, 0:1],
                                  sq_sl, start=True, stop=True)
                        nv = inc(TE.matmul(ps_sA[0:1, 512:1024], ind1o[:, 0:1],
                                           sq_sl, start=True, stop=True), "te")
                        sq_te = nv
                        w(VE, "te", nv)
                        VE.tensor_reduce(nrm8[0:1, tc:tc + 1], ps_sA[0:1, 0:512], AX, OP.max)
                        VE.tensor_reduce(nrm8[0:1, 4 + tc:5 + tc], ps_sA[0:1, 512:1024], AX, OP.max)
                        rv = vemark()
                        sx_tok[0].append(("ve", rv))
                    VE.drain()
                    VE.tensor_reduce(dstn[0:1, 2 * c:2 * c + 1], nrm8[0:1, 0:4], AX, OP.max)
                    VE.tensor_reduce(dstn[0:1, 2 * c + 1:2 * c + 2], nrm8[0:1, 4:8], AX, OP.max)
                    VE.drain()
            VE.tensor_tensor(brow[:, :], nqrow[:, :], nkrow[:, :], OP.mult)
            nbv = vemark()
            w(SC, "ve", nbv)
            nsq = inc(SC.activation(brow[:, :], brow[:, :], ACT.Sqrt, bias=0.0), "ac")
            w(VE, "ac", nsq)
            VE.tensor_scalar(brow[:, :], brow[:, :], -0.125, None, OP.mult, OP.bypass)
            VE.drain()
            VE.tensor_copy(browb[:, :], brow[:, :])
            bbv = vsync()
            w(TE, "ve", bbv)
            for (s, v) in rs_tok[0]:
                TE.wait_ge(sems[s], v)
            rs_tok[0] = []
            nbm = inc(TE.matmul(ps_xA[0:128, 0:16], onesPb[0:1, 0:128], browb[0:1, 0:16],
                                start=True, stop=True), "te")
            w(VE, "te", nbm)
            VE.tensor_copy(negbc[:, :], ps_xA[0:128, 0:16])
            nbcv = vsync()
            rs_tok[0].append(("ve", nbcv))

            if stop_after == "rope0" and s_i == 0:
                w(SY, "ve", nbcv)
                for (t_, d_) in ((xnf, dxnf_d), (qro, dqro_d), (kro, dkro_d), (vt, dvt_d)):
                    inc(SY.dma_start(out=d_[:, :, :], in_=t_[:, :, :]), "sy", 16)
                inc(SY.dma_start(out=dnegb_d[:, :], in_=negbc[:, :]), "sy", 16)
                barrier()
                return _finish()
            w(SC, "ve", nbcv)
            ps_s = [ps_sA, ps_sB]
            def tewarm():
                TE.matmul(ps_xB[64:128, 0:64], onesPb[64:65, 64:128],
                          onesPb[64:65, 0:64], start=True, stop=True)

            def do_tail(u, chp, qr):
                w(TE, "ve", u.rhv)
                rbm = inc(TE.matmul(u.sps[0:64, 0:512], onesPb[64:65, 0:64],
                                    rh16[64:65, 512 * u.par:512 * u.par + 512],
                                    start=True, stop=True), "te")
                w(VE, "te", rbm)
                asl = u.par
                for (s, v) in at_tok[asl]:
                    VE.wait_ge(sems[s], v)
                at_tok[asl] = []
                VE.tensor_copy(recf[0:64, 512 * u.par:512 * u.par + 512],
                               u.sps[0:64, 0:512])
                rcv = vemark()
                sx_tok[u.par].append(("ve", rcv))
                VE.tensor_tensor(atb[asl][0:64, 0:512], u.av[0:64, 0:512],
                                 recf[0:64, 512 * u.par:512 * u.par + 512], OP.mult)
                ntv = vsync()
                av_tok[u.ai].append(("ve", ntv))
                ab_s = dsem(f"atb{asl}", "sy")
                w(SY, "ve", ntv)
                spv = inc(SY.dma_start(out=attnb_d[chp, u.r0:u.r1, t0 + 512 * qr:t0 + 512 * qr + 512],
                                       in_=atb[asl][0:64, 0:512]), ab_s, 16)
                at_tok[asl].append((ab_s, spv))

            class U:
                pass

            for chp in range(8):
                for qr in range(4):
                    qc = slice(512 * qr, 512 * qr + 512)
                    kts = list(range(4 * qr + 4))
                    groups = [kts[i:i + 2] for i in range(0, len(kts), 2)]
                    G = len(groups)
                    units = []
                    for par in range(2):
                        u = U()
                        u.h = 2 * chp + par
                        u.par = par
                        u.r0, u.r1 = 64 * par, 64 * par + 64
                        u.ai = par
                        u.av = qk_ps[par]
                        u.rxp = rs_ps[par]
                        u.sps = ps_s[par]
                        u.first_av = True
                        u.smv = [None] * G
                        u.epv = [None] * G
                        u.mkv = [None] * G
                        units.append(u)

                    def emit_scores(u, g):
                        grp = groups[g]
                        for (s, v) in sx_tok[u.par]:
                            TE.wait_ge(sems[s], v)
                        sx_tok[u.par] = []
                        w(TE, "ve", nbcv)
                        for j, kt in enumerate(grp):
                            mm_ = TE.matmul(u.sps[:, 512 * j:512 * j + 512],
                                            kro[u.r0:u.r1, chp, 128 * kt:128 * kt + 128],
                                            qro[u.r0:u.r1, chp, qc], start=True, stop=True)
                            if j == len(grp) - 1:
                                u.smv[g] = inc(mm_, "te")

                    def emit_exp(u, g):
                        grp = groups[g]
                        w(SC, "te", u.smv[g])
                        for (s, v) in pt_tok[2 * u.par + (g % 2)]:
                            SC.wait_ge(sems[s], v)
                        pt_tok[2 * u.par + (g % 2)] = []
                        pt = bass.AP(PT, (2 * u.par + (g % 2)) * 1024, [[4096, 128], [1, 512 * len(grp)]])
                        u.epv[g] = inc(SC.activation(pt, u.sps[:, 0:512 * len(grp)], ACT.Exp,
                                                     bias=negbc[:, u.h:u.h + 1], scale=0.125), "ac")
                        sx_tok[u.par].append(("ac", u.epv[g]))
                        need_ve = False
                        w(VE, "ac", u.epv[g])
                        base = (2 * u.par + (g % 2)) * 1024
                        for j, kt in enumerate(grp):
                            ja = kt - 4 * qr
                            if ja >= 0:
                                need_ve = True
                                if ja > 0:
                                    VE.memset(bass.AP(PT, base + 512 * j, [[4096, 128], [1, 128 * ja]]), 0.0)
                                mp = bass.AP(PT, base + 512 * j + 128 * ja, [[4096, 128], [1, 128]])
                                VE.tensor_tensor(mp, mp, trit[:, :], OP.mult)
                        u.mkv[g] = vsync() if need_ve else None

                    def emit_av(u, g):
                        grp = groups[g]
                        w(TE, "ac", u.epv[g])
                        if u.mkv[g] is not None:
                            w(TE, "ve", u.mkv[g])
                        w(TE, "ac", v_done)
                        if u.first_av:
                            for (s, v) in av_tok[u.ai]:
                                TE.wait_ge(sems[s], v)
                            av_tok[u.ai] = []
                        base = (2 * u.par + (g % 2)) * 1024
                        for j, kt in enumerate(grp):
                            last = (g == G - 1) and (j == len(grp) - 1)
                            mm_ = TE.matmul(u.av[0:65, 0:512], vt[:, kt, 65 * u.h:65 * u.h + 65],
                                            bass.AP(PT, base + 512 * j, [[4096, 128], [1, 512]]),
                                            start=u.first_av, stop=last)
                            u.first_av = False
                            if j == len(grp) - 1:
                                avv = inc(mm_, "te")
                        pt_tok[2 * u.par + (g % 2)].append(("te", avv))
                        u.avv = avv

                    uA, uB = units
                    emit_scores(uA, 0)
                    emit_scores(uB, 0)
                    for g in range(G):
                        emit_exp(uA, g)
                        if g + 1 < G:
                            emit_scores(uA, g + 1)
                        emit_exp(uB, g)
                        tewarm()
                        emit_av(uA, g)
                        if g + 1 < G:
                            emit_scores(uB, g + 1)
                        tewarm()
                        emit_av(uB, g)
                    for u in units:
                        w(VE, "te", u.avv)
                        VE.reciprocal(recf[64:65, 512 * u.par:512 * u.par + 512],
                                      u.av[64:65, 0:512])
                        VE.drain()
                        VE.tensor_copy(rh16[64:65, 512 * u.par:512 * u.par + 512],
                                       recf[64:65, 512 * u.par:512 * u.par + 512])
                        u.rhv = vsync()
                        flash_idx += 1
                    tewarm()
                    do_tail(uA, chp, qr)
                    tewarm()
                    do_tail(uB, chp, qr)
            qk_tok[0] += av_tok[0]
            qk_tok[1] += av_tok[1]
            av_tok = [[], []]
            xnf_tok = [("te", CN["te"])]
            prev_seq = dict(CN)
        p2_done = dict(CN)
        barrier()
        if stop_after == "p2":
            return _finish()

    late = contextlib.ExitStack()
    wpjf = late.enter_context(nc.sbuf_tensor("wpjf", [128, DCH, D], BF16))
    w(SC, "gpc", gp_casts["gpc"])
    for c in range(DCH):
        inc(SC.dma_start_transpose(wpjf[:, c, :], wpjbf_d[:, 128 * c:128 * c + 128]),
            "tr", 16)
    wpjf_tr = CN["tr"]
    with contextlib.ExitStack() as p25:
        def sb25(name, shape, dtype):
            return p25.enter_context(nc.sbuf_tensor(name, shape, dtype))
        xqf = sb25("xqf", [128, DCH, S], BF16)
        wgst = sb25("wgst", [128, 2, DCH, 128], BF16)
        wust = sb25("wust", [128, 2, DCH, 128], BF16)
        qx32 = sb25("qx32", [128, 2, 2048], F32)
        xq16 = sb25("xq16", [128, 2, 2048], BF16)
        gsb = sb25("gsb", [128, 2, 512], F32)
        silb = sb25("silb", [128, 2, 512], F32)
        usb = sb25("usb", [128, 2, 512], F32)
        hbuf = sb25("hbuf", [128, 2, 2048], F32)

        xn32_rr2 = xn32_d.ap().rearrange("(b m p) d -> b p m d", p=128, m=2)
        xqbf_rr = xqbf_d.ap().rearrange("(b m p) d -> b p m d", p=128, m=2)
        toks = [[], []]
        for tb in range(32):
            sl = tb % 2
            for (s, v) in toks[sl]:
                SY.wait_ge(sems[s], v)
            toks[sl] = []
            qx_s = dsem(f"qx{sl}", "sy")
            lv = inc(SY.dma_start(out=bass.AP(qx32, sl * 2048, [[4096, 128], [1024, 2], [1, 1024]]),
                                  in_=xn32_rr2[tb]), qx_s, 16)
            w(VE, qx_s, lv)
            VE.tensor_scalar(qx32[:, sl, :], qx32[:, sl, :], rsx_bc[:, 0:1], magic[:, 0:1],
                             OP.mult, OP.add)
            VE.tensor_scalar(qx32[:, sl, :], qx32[:, sl, :], magic[:, 0:1], QMAX,
                             OP.subtract, OP.min)
            VE.tensor_scalar(xq16[:, sl, :], qx32[:, sl, :], -QMAX - 1.0, None,
                             OP.max, OP.bypass)
            xqs_s = dsem(f"xqs{sl}", "sy")
            qv = vsync()
            w(SY, "ve", qv)
            sv = inc(SY.dma_start(out=xqbf_rr[tb],
                                  in_=bass.AP(xq16, sl * 2048, [[4096, 128], [1024, 2], [1, 1024]])),
                     xqs_s, 16)
            toks[sl] = [(xqs_s, sv), ("ve", qv)]
        xq_sp = (CN["xqs0"], CN["xqs1"])

        gu_ps = [ps_avA, ps_xA, ps_avB, ps_xB]
        gu_tok = [[], [], [], []]
        wgu_tok = [[], []]
        hb_tok = [[], []]
        xqf_tok = []
        hm_col = 0
        for b4 in range(4):
            tb0 = b4 * S
            for (s, v) in xqf_tok:
                SC.wait_ge(sems[s], v)
            xqf_tok = []
            SC.wait_ge(sems["xqs0"], xq_sp[0])
            SC.wait_ge(sems["xqs1"], xq_sp[1])
            for c in range(DCH):
                inc(SC.dma_start_transpose(xqf[:, c, :],
                                           xqbf_d[tb0:tb0 + S, 128 * c:128 * c + 128]),
                    "tr", 16)
            xqf_ld = CN["tr"]
            def gu_fetch(m):
                msl = m % 2
                for (s, v) in wgu_tok[msl]:
                    SC.wait_ge(sems[s], v)
                wgu_tok[msl] = []
                w(SC, "wqs", wq_done["wqs"])
                inc(SC.dma_start_transpose(wgst[:, msl, :, :],
                                           wgi_d[128 * m:128 * m + 128, :]), "tr", 16)
                return inc(SC.dma_start_transpose(wust[:, msl, :, :],
                                                  wui_d[128 * m:128 * m + 128, :]), "tr", 16)

            if b4 == 0:
                wtv_next = gu_fetch(0)
            for m in range(ICH):
                hsl = (b4 * ICH + m) % 2
                msl = m % 2
                wtv = wtv_next
                if m + 1 < ICH:
                    wtv_next = gu_fetch(m + 1)
                elif b4 < 3:
                    wtv_next = gu_fetch(0)
                for (s, v) in hb_tok[hsl]:
                    VE.wait_ge(sems[s], v)
                hb_tok[hsl] = []
                for tc in range(4):
                    gi = (2 * tc) % 4
                    ui = gi + 1
                    for (s, v) in gu_tok[gi]:
                        TE.wait_ge(sems[s], v)
                    gu_tok[gi] = []
                    for (s, v) in gu_tok[ui]:
                        TE.wait_ge(sems[s], v)
                    gu_tok[ui] = []
                    w(TE, "tr", max(xqf_ld, wtv))
                    cols = slice(512 * tc, 512 * tc + 512)
                    gv = uv = None
                    for d in range(DCH):
                        mm_ = TE.matmul(gu_ps[gi][:, 0:512], wgst[:, msl, d, :],
                                        xqf[:, d, cols], start=(d == 0), stop=(d == DCH - 1))
                        if d == DCH - 1:
                            gv = inc(mm_, "te")
                    for d in range(DCH):
                        mm_ = TE.matmul(gu_ps[ui][:, 0:512], wust[:, msl, d, :],
                                        xqf[:, d, cols], start=(d == 0), stop=(d == DCH - 1))
                        if d == DCH - 1:
                            uv = inc(mm_, "te")
                    esl = tc % 2
                    w(VE, "te", gv)
                    w(VE, "ve", scgu_v)
                    VE.tensor_scalar(gsb[:, esl, :], gu_ps[gi][:, 0:512], scg[:, m:m + 1],
                                     None, OP.mult, OP.bypass)
                    gsv = vsync()
                    gu_tok[gi].append(("ve", gsv))
                    w(SC, "ve", gsv)
                    slv = inc(SC.activation(silb[:, esl, :], gsb[:, esl, :], ACT.Silu), "ac")
                    w(VE, "te", uv)
                    VE.tensor_scalar(usb[:, esl, :], gu_ps[ui][:, 0:512], scu[:, m:m + 1],
                                     None, OP.mult, OP.bypass)
                    usv = vemark()
                    gu_tok[ui].append(("ve", usv))
                    w(VE, "ac", slv)
                    VE.tensor_tensor(hbuf[:, hsl, cols], silb[:, esl, :], usb[:, esl, :],
                                     OP.mult)
                    hv = vemark()
                VE.tensor_reduce(hmg[:, hm_col:hm_col + 1], hbuf[:, hsl, :], AX, OP.max,
                                 apply_absolute_value=True)
                hmv = vsync()
                hm_col += 1
                hb_s = dsem(f"hb{hsl}", "sy")
                w(SY, "ve", hmv)
                hs = inc(SY.dma_start(out=h32_d[m, :, tb0:tb0 + S], in_=hbuf[:, hsl, :]),
                         hb_s, 16)
                hb_tok[hsl] = [(hb_s, hs)]
                wgu_tok[msl].append(("te", CN["te"]))
            xqf_tok = [("te", CN["te"])]
        p25_done = dict(CN)
        barrier()
        if stop_after == "p25":
            return _finish()

    VE.tensor_reduce(onetmp[:, :], hmg[:, :], AX, OP.max)
    VE.drain()
    partition_max_to_scalar(onetmp[:, 0:1], tiny[:, 8:9])
    v0 = vemark()
    w(GP, "ve", v0)
    inc(GP.dma_start(out=cc2i_d[:, :], in_=tiny[:, 8:9]), "gpe", 16)
    w(GP, "gpe")
    ccv = inc(GP.collective_compute("AllReduce", OP.max,
                                    replica_groups=[list(range(NCORES))],
                                    ins=[cc2i_d.ap()], outs=[cc2o_d.ap()]), "cc")
    w(GP, "cc", ccv)
    g2 = inc(GP.dma_start(out=tiny[:, 9:10], in_=cc2o_d[:, :]), "gpe", 16)
    sgv = inc(SC.activation(tiny[:, 10:11], cl32[:, 1:2], ACT.Sigmoid), "ac")
    w(VE, "gpe", g2)
    w(VE, "ac", sgv)
    VE.tensor_tensor(tiny[:, 11:12], tiny[:, 9:10], tiny[:, 10:11], OP.mult)
    VE.drain()
    VE.tensor_scalar(tiny[:, 11:12], tiny[:, 11:12], 1.0 / QMAX, None, OP.mult, OP.bypass)
    VE.drain()
    VE.reciprocal(tiny[:, 12:13], tiny[:, 11:12])
    VE.drain()
    bcast128(tiny[:, 11:12], sh_bc[:, 0:1], 4)
    bcast128(tiny[:, 12:13], rsh_bc[:, 0:1], 6)

    with contextlib.ExitStack() as p3:
        def sb3(name, shape, dtype):
            return p3.enter_context(nc.sbuf_tensor(name, shape, dtype))
        wdf = sb3("wdf", [128, ICH, D], BF16)
        inth = sb3("inth", [128, 2, ICH, 512], BF16)
        atst = sb3("atst", [128, 2, DCH, 512], BF16)
        hld = sb3("hld", [128, 2, 1024], F32)
        xb3 = sb3("xb3", [128, 2, 4096], F32)
        outb = sb3("outb", [128, 2, 4096], F32)

        w(SC, "wqs", wq_done["wqs"])
        for i in range(ICH):
            inc(SC.dma_start_transpose(wdf[:, i, :], wdi_d[:, 128 * i:128 * i + 128]),
                "tr", 16)
        wdf_tr = CN["tr"]

        x_rr3 = x_d.ap().rearrange("(tc m p) d -> tc p m d", p=128, m=4)
        out_rr = out_d.ap().rearrange("(tc m p) d -> tc p m d", p=128, m=4)
        tc_tok = [[], []]
        hld_tok = [[], []]
        ps3 = [(ps_sA, ps_avA, ps_xA), (ps_sB, ps_avB, ps_xB)]
        ps3_tok = [[], []]
        ob_tok = [[], []]
        stage = {}

        def p3_load(tc):
            sl = tc % 2
            for (s, v) in tc_tok[sl]:
                SY.wait_ge(sems[s], v)
            tc_tok[sl] = []
            ats_s = dsem(f"ats{sl}", "sy")
            xb_s = dsem(f"xb{sl}", "sy")
            for c in range(DCH):
                inc(SY.dma_start(out=atst[:, sl, c, :],
                                 in_=attnb_d[c, :, 512 * tc:512 * tc + 512]), ats_s, 16)
            at_ld = CN[ats_s]
            xl = inc(SY.dma_start(out=bass.AP(xb3, sl * 4096, [[8192, 128], [1024, 4], [1, 1024]]),
                                  in_=x_rr3[tc]), xb_s, 16)
            for i2 in range(0, ICH, 2):
                npair = 2 if i2 + 1 < ICH else 1
                hsl = (i2 // 2) % 2
                for (s, v) in hld_tok[hsl]:
                    SY.wait_ge(sems[s], v)
                hld_tok[hsl] = []
                hl_s = dsem(f"hl{hsl}", "sy")
                lv = inc(SY.dma_start(out=bass.AP(hld, hsl * 1024, [[2048, 128], [1, 512 * npair]]),
                                      in_=bass.AP(h32_d, i2 * 128 * T + 512 * tc,
                                                  [[T, 128], [128 * T, npair], [1, 512]])),
                         hl_s, 16)
                w(VE, hl_s, lv)
                fd = 512 * npair
                hap = bass.AP(hld, hsl * 1024, [[2048, 128], [1, fd]])
                iap = bass.AP(inth, sl * (ICH * 512) + i2 * 512,
                              [[2 * ICH * 512, 128], [1, fd]])
                VE.tensor_scalar(hap, hap, rsh_bc[:, 0:1],
                                 magic[:, 0:1], OP.mult, OP.add)
                VE.tensor_scalar(hap, hap, magic[:, 0:1],
                                 QMAX, OP.subtract, OP.min)
                VE.tensor_scalar(iap, hap, -QMAX - 1.0,
                                 None, OP.max, OP.bypass)
                qv = vsync()
                hld_tok[hsl] = [("ve", qv)]
                tc_tok[sl].append(("ve", qv))
            stage[tc] = (CN["ve"], at_ld, xl, ats_s, xb_s)

        def p3_compute(tc):
            sl = tc % 2
            hq_done, at_ld, xl, ats_s, xb_s = stage.pop(tc)
            for tt in range(4):
                pp = (tc * 4 + tt) % 2
                prj, dn0, dn1 = ps3[pp]
                for (s, v) in ps3_tok[pp]:
                    TE.wait_ge(sems[s], v)
                ps3_tok[pp] = []
                w(TE, ats_s, at_ld)
                w(TE, "ve", hq_done)
                w(TE, "tr", wdf_tr)
                tsl = slice(128 * tt, 128 * tt + 128)
                for cc_i, cols in enumerate((slice(0, 512), slice(512, 1024))):
                    for d in range(DCH):
                        TE.matmul(prj[:, cols], atst[:, sl, d, tsl], wpjf[:, d, cols],
                                  start=(d == 0), stop=False)
                    TE.matmul(prj[:, cols], ones2[0:2, 0:128], pbhl[0:2, cols],
                              start=False, stop=True)
                    dps = dn0 if cc_i == 0 else dn1
                    for i in range(ICH):
                        TE.matmul(dps[:, 0:512], inth[:, sl, i, tsl], wdf[:, i, cols],
                                  start=(i == 0), stop=(i == ICH - 1))
                mv = temark()
                w(VE, "te", mv)
                w(VE, xb_s, xl)
                for (s, v) in (ob_tok[sl] if tt == 0 else []):
                    VE.wait_ge(sems[s], v)
                if tt == 0:
                    ob_tok[sl] = []
                for cc_i in range(2):
                    dps = (dn0, dn1)[cc_i]
                    cols = slice(512 * cc_i, 512 * cc_i + 512)
                    ocol = slice(1024 * tt + 512 * cc_i, 1024 * tt + 512 * cc_i + 512)
                    VE.tensor_scalar(outb[:, sl, ocol], dps[:, 0:512], sh_bc[:, 0:1], None,
                                     OP.mult, OP.bypass)
                    VE.tensor_tensor(outb[:, sl, ocol], outb[:, sl, ocol], prj[:, cols], OP.add)
                    VE.tensor_tensor(outb[:, sl, ocol], outb[:, sl, ocol],
                                     xb3[:, sl, ocol], OP.add)
                ev = vsync()
                ps3_tok[pp].append(("ve", ev))
            ov_s = dsem(f"ov{sl}", "sy")
            w(SY, "ve", CN["ve"])
            ov = inc(SY.dma_start(out=out_rr[tc],
                                  in_=bass.AP(outb, sl * 4096, [[8192, 128], [1024, 4], [1, 1024]])),
                     ov_s, 16)
            tc_tok[sl].append((ov_s, ov))
            tc_tok[sl].append(("te", CN["te"]))
            tc_tok[sl].append(("ve", CN["ve"]))
            ob_tok[sl].append((ov_s, ov))

        p3_load(0)
        for tc in range(16):
            if tc + 1 < 16:
                p3_load(tc + 1)
            p3_compute(tc)
            if stop_after == "p3serial":
                barrier()
        SY.wait_ge(sems["ov0"], CN["ov0"])
        SY.wait_ge(sems["ov1"], CN["ov1"])
    late.close()

    return _finish()


_CACHE = {}


def kernel(L0_attn_x, L0_attn_qkv_w, L0_attn_qkv_b, L0_attn_proj_w, L0_attn_proj_b,
           L0_ffn_clip_x, L0_ffn_clip_down, L0_ffn_w_gate, L0_ffn_w_up, L0_ffn_w_down):
    consts = _host_consts()
    if "nc" not in _CACHE:
        _CACHE["nc"] = build_program()
    nc, es = _CACHE["nc"]
    x = np.ascontiguousarray(np.asarray(L0_attn_x, dtype=np.float32))
    shared = {
        "qkv_w": np.ascontiguousarray(np.asarray(L0_attn_qkv_w, np.float32)),
        "qkv_b": np.ascontiguousarray(np.asarray(L0_attn_qkv_b, np.float32)),
        "proj_w": np.ascontiguousarray(np.asarray(L0_attn_proj_w, np.float32)),
        "proj_b": np.ascontiguousarray(np.asarray(L0_attn_proj_b, np.float32)),
        "clip_x": np.ascontiguousarray(np.asarray(L0_ffn_clip_x, np.float32)),
        "clip_down": np.ascontiguousarray(np.asarray(L0_ffn_clip_down, np.float32)),
        "w_gate": np.ascontiguousarray(np.asarray(L0_ffn_w_gate, np.float32)),
        "w_up": np.ascontiguousarray(np.asarray(L0_ffn_w_up, np.float32)),
        "w_down": np.ascontiguousarray(np.asarray(L0_ffn_w_down, np.float32)),
        **consts,
    }
    in_maps = []
    for c in range(NCORES):
        im = dict(shared)
        im["x"] = np.ascontiguousarray(x[NSEQ * c:NSEQ * (c + 1)].reshape(T, D))
        in_maps.append(im)
    res = run_bass_kernel_spmd(nc, in_maps, core_ids=list(range(NCORES)))
    out = np.concatenate(
        [np.asarray(r["out"], dtype=np.float32).reshape(NSEQ, S, D) for r in res.results],
        axis=0)
    return out


# version 11
# speedup vs baseline: 1.0416x; 1.0416x over previous
import contextlib

import numpy as np
import ml_dtypes

import concourse.bass as bass
import concourse.mybir as mybir
from concourse.alu_op_type import AluOpType as OP
from concourse.bass_utils import run_bass_kernel_spmd

F32 = mybir.dt.float32
BF16 = mybir.dt.bfloat16
ACT = mybir.ActivationFunctionType
AX = mybir.AxisListType.X
BF = ml_dtypes.bfloat16

NCORES = 8
B, S, D = 32, 2048, 1024
NSEQ = B // NCORES
T = NSEQ * S
H, HD = 16, 64
IFF = 2816
ICH = IFF // 128
DCH = D // 128
MAGIC32 = 12582912.0
CBIG = float(np.float32(1.5 * 2**82))
QMAX = 127.0


def _host_consts():
    half = 32
    freqs = 1.0 / 10000.0 ** (np.arange(half, dtype=np.float64) / half)
    pos = np.arange(S, dtype=np.float64)
    ang = pos[None, :] * freqs[:, None]
    cos32 = np.cos(ang)
    sin32 = np.sin(ang)
    r = np.arange(128)
    i_of_r = r % 32
    sgn = np.where((r % 64) < 32, -1.0, 1.0)
    tcos = cos32[i_of_r, :]
    tsin = sin32[i_of_r, :] * sgn[:, None]
    perm = np.zeros((128, 128), dtype=np.float64)
    for blk in range(2):
        b0 = 64 * blk
        perm[b0:b0 + 32, b0 + 32:b0 + 64] = np.eye(32)
        perm[b0 + 32:b0 + 64, b0:b0 + 32] = np.eye(32)
    rr, cc = np.meshgrid(np.arange(128), np.arange(128), indexing="ij")
    tri01 = (rr <= cc).astype(np.float64)
    c = {
        "tcos": tcos, "tsin": tsin,
        "permc": perm, "tri01": tri01,
    }
    return {k: np.asarray(v, dtype=np.float32).astype(BF) for k, v in c.items()}


def build_program(dbg=(), stop_after=None):
    nc = bass.Bass("TRN2", target_bir_lowering=False, debug=False)
    es = contextlib.ExitStack()

    def dram(name, shape, dtype, kind=None):
        if name in dbg:
            kind = "ExternalOutput"
        if kind is None:
            return nc.dram_tensor(name, shape, dtype)
        return nc.dram_tensor(name, shape, dtype, kind=kind)

    x_d = dram("x", [T, D], F32, "ExternalInput")
    qkvw_d = dram("qkv_w", [3 * D, D], F32, "ExternalInput")
    qkvb_d = dram("qkv_b", [3 * D], F32, "ExternalInput")
    pjw_d = dram("proj_w", [D, D], F32, "ExternalInput")
    pjb_d = dram("proj_b", [D], F32, "ExternalInput")
    clx_d = dram("clip_x", [1], F32, "ExternalInput")
    cld_d = dram("clip_down", [1], F32, "ExternalInput")
    wg_d = dram("w_gate", [IFF, D], F32, "ExternalInput")
    wu_d = dram("w_up", [IFF, D], F32, "ExternalInput")
    wd_d = dram("w_down", [D, IFF], F32, "ExternalInput")
    tcos_d = dram("tcos", [128, S], BF16, "ExternalInput")
    tsin_d = dram("tsin", [128, S], BF16, "ExternalInput")
    perm_d = dram("permc", [128, 128], BF16, "ExternalInput")
    tri_d = dram("tri01", [128, 128], BF16, "ExternalInput")
    out_d = dram("out", [T, D], F32, "ExternalOutput")
    xn32_d = dram("xn32", [T, D], F32)
    xnbf_d = dram("xnbf", [T, D], BF16)
    xqbf_d = dram("xqbf", [T, D], BF16)
    wkvbf_d = dram("wkvbf", [3 * D, D], BF16)
    wpjbf_d = dram("wpjbf", [D, D], BF16)
    wgi_d = dram("wgi", [IFF, D], BF16)
    wui_d = dram("wui", [IFF, D], BF16)
    wdi_d = dram("wdi", [D, IFF], BF16)
    h32_d = dram("h32", [ICH, 128, T], F32)
    attnb_d = dram("attnb", [DCH, 128, T], BF16)
    cc1i_d = dram("cc1i", [1, 1], F32)
    cc1o_d = dram("cc1o", [1, 1], F32)
    cc2i_d = dram("cc2i", [1, 1], F32)
    cc2o_d = dram("cc2o", [1, 1], F32)
    red_d = dram("redsc", [128], F32)
    dxnf_d = dram("dxnf", [128, DCH, S], BF16)
    dqro_d = dram("dqro", [128, DCH, S], BF16)
    dkro_d = dram("dkro", [128, DCH, S], BF16)
    dvt_d = dram("dvt", [128, 16, 1040], BF16)
    dnegb_d = dram("dnegb", [128, 16], F32)
    dpt0_d = dram("dpt0", [128, 1024], BF16)
    dpt1_d = dram("dpt1", [128, 1024], BF16)
    dav_d = dram("dav", [128, 512], F32)
    dwdf_d = dram("dwdf", [128, 22, 1024], BF16)
    dwpj_d = dram("dwpj", [128, 8, 1024], BF16)
    dith_d = dram("dith", [128, 22, 512], BF16)
    datst_d = dram("datst", [128, 8, 512], BF16)
    dbias_d = dram("dbias", [2, 4096], BF16)
    dsc_d = dram("dsc", [128, 8], F32)
    drr_d = dram("drr", [65, 512], F32)
    drcf_d = dram("drcf", [64, 512], F32)
    datb_d = dram("datb", [128, 512], BF16)
    hlscr_d = dram("hlscr", [2, 2048], BF16)

    sems = {}
    for s in ("sy", "tr", "gp", "te", "ve", "ac", "cc", "bar"):
        sems[s] = es.enter_context(nc.semaphore("sem_" + s))
    CN = {k: 0 for k in sems}
    DMA_FAM = {"sy": ["sy"], "tr": ["tr"], "gp": ["gp"]}

    def dsem(name, fam):
        if name not in sems:
            sems[name] = es.enter_context(nc.semaphore("sem_" + name))
            CN[name] = 0
            DMA_FAM[fam].append(name)
        return name

    def sb(name, shape, dtype):
        return es.enter_context(nc.sbuf_tensor(name, shape, dtype))

    ones2 = sb("ones2", [2, 512], BF16)
    magic = sb("magict", [128, 1], F32)
    epsc = sb("epsc", [128, 1], F32)
    onetmp = sb("onetmp", [128, 1], F32)
    bhlqk = sb("bhlqk", [2, 2 * D], BF16)
    pbhl = sb("pbhl", [2, D], BF16)
    vbp = sb("vbp", [2, 1040], BF16)
    cl32 = sb("cl32", [1, 2], F32)
    wgsc = sb("wgsc", [128, ICH], F32)
    wusc = sb("wusc", [128, ICH], F32)
    wdsc = sb("wdsc", [128, DCH], F32)
    scg = sb("scg", [128, ICH], F32)
    scu = sb("scu", [128, ICH], F32)
    rsx_bc = sb("rsx_bc", [128, 1], F32)
    sx_bc = sb("sx_bc", [128, 1], F32)
    rsh_bc = sb("rsh_bc", [128, 1], F32)
    sh_bc = sb("sh_bc", [128, 1], F32)
    tiny = sb("tinyt", [1, 16], F32)
    tinyb = sb("tinyb", [1, 16], BF16)
    trow = sb("trowt", [1, 128], F32)
    xmg = sb("xmg", [128, 64], F32)
    hmg = sb("hmg", [128, 88], F32)
    onesPb = sb("onesPb", [128, 128], BF16)
    ind1e = sb("ind1e", [128, 1], BF16)
    ind1o = sb("ind1o", [128, 1], BF16)

    ps_sA = es.enter_context(nc.psum_tensor("ps_sA", [128, 1024], F32))
    ps_sB = es.enter_context(nc.psum_tensor("ps_sB", [128, 1024], F32))
    ps_avA = es.enter_context(nc.psum_tensor("ps_avA", [128, 512], F32))
    ps_avB = es.enter_context(nc.psum_tensor("ps_avB", [128, 512], F32))
    ps_xA = es.enter_context(nc.psum_tensor("ps_xA", [128, 512], F32))
    ps_xB = es.enter_context(nc.psum_tensor("ps_xB", [128, 512], F32))

    block = es.enter_context(nc.Block())
    REC = {k: [] for k in ("sy", "sc", "ve", "te", "gp")}

    class _Pend:
        __slots__ = ("incs",)

        def __init__(self):
            self.incs = []

        def then_inc(self, sem, n=1):
            self.incs.append((sem, n))
            return self

    class _Eng:
        def __init__(self, key):
            self._k = key

        def __getattr__(self, name):
            q = REC[self._k]

            def call(*a, **kw):
                p = _Pend()
                q.append((name, a, kw, p))
                return p
            return call

    SY, SC, VE, TE, GP = _Eng("sy"), _Eng("sc"), _Eng("ve"), _Eng("te"), _Eng("gp")

    def inc(instr, s, n=1):
        instr.then_inc(sems[s], n)
        CN[s] += n
        return CN[s]

    def mark(key, s):
        name = REC[key][-1][0]
        assert name not in ("wait_ge",), name

        REC[key][-1][3].incs.append((sems[s], 1))
        CN[s] += 1
        return CN[s]

    def vemark():
        return mark("ve", "ve")

    def vsync():
        VE.drain()
        return mark("ve", "ve")

    def temark():
        return mark("te", "te")

    def w(eng, s, val=None):
        eng.wait_ge(sems[s], CN[s] if val is None else val)

    def barrier():
        for s in DMA_FAM["sy"]:
            SY.wait_ge(sems[s], CN[s])
        inc(SY.nop(), "bar")
        for s in DMA_FAM["tr"]:
            SC.wait_ge(sems[s], CN[s])
        inc(SC.nop(), "bar")
        for s in DMA_FAM["gp"]:
            GP.wait_ge(sems[s], CN[s])
        inc(GP.nop(), "bar")
        inc(TE.nop(), "bar")
        inc(VE.nop(), "bar")
        tgt = CN["bar"]
        for eng in (TE, VE, SC, GP, SY):
            eng.wait_ge(sems["bar"], tgt)

    def _finish():
        def _replay(key):
            def f(eng):
                for (name, a, kw, p) in REC[key]:
                    if key in ("ve", "te", "sc") and name == "nop" and p.incs:
                        eng.drain()
                    r = getattr(eng, name)(*a, **kw)
                    for (sem, n) in p.incs:
                        r.then_inc(sem, n)
            return f
        block.sync(_replay("sy"))
        block.scalar(_replay("sc"))
        block.vector(_replay("ve"))
        block.tensor(_replay("te"))
        block.gpsimd(_replay("gp"))
        return nc, es

    dsem("gpc", "gp")
    inc(GP.dma_start(out=wkvbf_d[:, :], in_=qkvw_d[:, :]), "gpc", 16)
    inc(GP.dma_start(out=wpjbf_d[:, :], in_=pjw_d[:, :]), "gpc", 16)
    gp_casts = dict(CN)

    VE.memset(ones2[:, :], 1.0)
    VE.memset(epsc[:, :], 1e-5)
    VE.memset(onesPb[:, :], 1.0)
    VE.memset(ind1e[0:64, 0:1], 1.0)
    VE.memset(ind1e[64:128, 0:1], 0.0)
    VE.memset(ind1o[0:64, 0:1], 0.0)
    VE.memset(ind1o[64:128, 0:1], 1.0)
    VE.memset(onetmp[:, :], 1.0)
    VE.drain()
    VE.tensor_scalar(magic[:, :], onetmp[:, :], float(2.0 ** -30), 1.0,
                     OP.add, OP.subtract)
    VE.drain()
    VE.tensor_scalar(magic[:, :], magic[:, :], CBIG, MAGIC32,
                     OP.mult, OP.add)
    VE.drain()

    p0 = contextlib.ExitStack()
    qb32 = p0.enter_context(nc.sbuf_tensor("qb32", [1, 3 * D], F32))
    hl16 = p0.enter_context(nc.sbuf_tensor("hl16", [1, 3 * D], BF16))
    hl32 = p0.enter_context(nc.sbuf_tensor("hl32", [1, 3 * D], F32))
    lo32 = p0.enter_context(nc.sbuf_tensor("lo32", [1, 3 * D], F32))
    lo16 = p0.enter_context(nc.sbuf_tensor("lo16", [1, 3 * D], BF16))
    vb32 = p0.enter_context(nc.sbuf_tensor("vb32", [1, 1040], F32))

    dsem("bia", "sy")
    inc(SY.dma_start(out=qb32[:, :], in_=qkvb_d.ap().rearrange("(a d) -> a d", a=1)), "bia", 16)
    inc(SY.dma_start(out=cl32[:, 0:1], in_=clx_d.ap().rearrange("(a d) -> a d", a=1)), "bia", 16)
    inc(SY.dma_start(out=cl32[:, 1:2], in_=cld_d.ap().rearrange("(a d) -> a d", a=1)), "bia", 16)
    ld_bias = CN["bia"]

    def hilo_row(src_ap, n, dst, dst_off):
        w(VE, "gpb") if "gpb" in sems else None
        VE.tensor_copy(hl16[:, 0:n], src_ap)
        VE.drain()
        VE.tensor_copy(hl32[:, 0:n], hl16[:, 0:n])
        VE.drain()
        VE.tensor_tensor(lo32[:, 0:n], src_ap, hl32[:, 0:n], OP.subtract)
        VE.drain()
        VE.tensor_copy(lo16[:, 0:n], lo32[:, 0:n])
        v = vsync()
        dsem("gpb", "gp")
        w(GP, "ve", v)
        inc(GP.dma_start(out=hlscr_d[0:1, 0:n], in_=hl16[:, 0:n]), "gpb", 16)
        inc(GP.dma_start(out=hlscr_d[1:2, 0:n], in_=lo16[:, 0:n]), "gpb", 16)
        w(GP, "gpb")
        inc(GP.dma_start(out=dst[0:2, dst_off:dst_off + n], in_=hlscr_d[0:2, 0:n]), "gpb", 16)

    w(VE, "bia", ld_bias)
    hilo_row(qb32[:, 0:2 * D], 2 * D, bhlqk, 0)
    VE.memset(vb32[:, :], 0.0)
    VE.tensor_copy(bass.AP(vb32, 0, [[1040, 1], [65, 16], [1, 64]]),
                   bass.AP(qb32, 2 * D, [[3 * D, 1], [64, 16], [1, 64]]))
    VE.memset(bass.AP(vb32, 64, [[1040, 1], [65, 16], [1, 1]]), 1.0)
    VE.drain()
    hilo_row(vb32[:, 0:1040], 1040, vbp, 0)
    pb32 = qb32
    pv = CN["ve"]
    w(SY, "ve", pv)
    w(SY, "gp")
    inc(SY.dma_start(out=pb32[:, 0:D], in_=pjb_d.ap().rearrange("(a d) -> a d", a=1)), "bia", 16)
    w(VE, "bia")
    hilo_row(pb32[:, 0:D], D, pbhl, 0)

    bias_gp = CN["gp"]
    barrier()
    p0.close()

    with contextlib.ExitStack() as p1:
        def sb1(name, shape, dtype):
            return p1.enter_context(nc.sbuf_tensor(name, shape, dtype))
        xb = sb1("xb", [128, 2, 4096], F32)
        sqb = sb1("sqb", [128, 2, 4096], F32)
        xn32b = sb1("xn32b", [128, 2, 4096], F32)
        xn16b = sb1("xn16b", [128, 2, 4096], BF16)
        st_sum = sb1("st_sum", [128, 2, 4], F32)
        st_sq = sb1("st_sq", [128, 2, 4], F32)
        st_m = sb1("st_m", [128, 2, 4], F32)
        st_e = sb1("st_e", [128, 2, 4], F32)
        st_v = sb1("st_v", [128, 2, 4], F32)
        st_sd = sb1("st_sd", [128, 2, 4], F32)
        st_r = sb1("st_r", [128, 2, 4], F32)
        st_n = sb1("st_n", [128, 2, 4], F32)

        slot_tok = [[], []]
        spill_tok = [[], []]
        x_rr = x_d.ap().rearrange("(b m p) d -> b p m d", p=128, m=4)
        xn32_rr = xn32_d.ap().rearrange("(b m p) d -> b p m d", p=128, m=4)
        xnbf_rr = xnbf_d.ap().rearrange("(b m p) d -> b p m d", p=128, m=4)
        stv = {}

        def p1_a(b):
            sl = b % 2
            for (s, v) in slot_tok[sl]:
                SY.wait_ge(sems[s], v)
            slot_tok[sl] = []
            xs_s = dsem(f"x{sl}", "sy")
            ldv = inc(SY.dma_start(out=bass.AP(xb, sl * 4096, [[8192, 128], [1024, 4], [1, 1024]]),
                                   in_=x_rr[b]), xs_s, 16)
            w(VE, xs_s, ldv)
            VE.tensor_reduce(st_sum[:, sl, :],
                             bass.AP(xb, sl * 4096, [[8192, 128], [1024, 4], [1, 1024]]),
                             AX, OP.add)
            w(SC, xs_s, ldv)
            sqv = inc(SC.activation(sqb[:, sl, :], xb[:, sl, :], ACT.Square), "ac")
            w(VE, "ac", sqv)
            VE.tensor_reduce(st_sq[:, sl, :],
                             bass.AP(sqb, sl * 4096, [[8192, 128], [1024, 4], [1, 1024]]),
                             AX, OP.add)
            VE.drain()
            VE.tensor_scalar(st_m[:, sl, :], st_sum[:, sl, :], 1.0 / D, None, OP.mult, OP.bypass)
            VE.tensor_scalar(st_e[:, sl, :], st_sq[:, sl, :], 1.0 / D, None, OP.mult, OP.bypass)
            VE.drain()
            VE.tensor_tensor(st_v[:, sl, :], st_m[:, sl, :], st_m[:, sl, :], OP.mult)
            VE.drain()
            VE.tensor_tensor(st_v[:, sl, :], st_e[:, sl, :], st_v[:, sl, :], OP.subtract)
            vv = vsync()
            w(SC, "ve", vv)
            stv[b] = inc(SC.activation(st_sd[:, sl, :], st_v[:, sl, :], ACT.Sqrt,
                                       bias=epsc[:, 0:1]), "ac")

        def p1_b(b):
            sl = b % 2
            sdv = stv.pop(b)
            w(VE, "ac", sdv)
            VE.reciprocal(st_r[:, sl, :], st_sd[:, sl, :])
            VE.drain()
            VE.tensor_tensor(st_n[:, sl, :], st_m[:, sl, :], st_r[:, sl, :], OP.mult)
            VE.drain()
            VE.tensor_scalar(st_n[:, sl, :], st_n[:, sl, :], -1.0, None, OP.mult, OP.bypass)
            VE.drain()
            for (s, v) in spill_tok[sl]:
                VE.wait_ge(sems[s], v)
            spill_tok[sl] = []
            for mm in range(4):
                VE.tensor_scalar(xn32b[:, sl, 1024 * mm:1024 * mm + 1024],
                                 xb[:, sl, 1024 * mm:1024 * mm + 1024],
                                 st_r[:, sl, mm:mm + 1], st_n[:, sl, mm:mm + 1],
                                 OP.mult, OP.add)
            xnv = vsync()
            VE.tensor_reduce(xmg[:, 4 * b:4 * b + 4],
                             bass.AP(xn32b, sl * 4096, [[8192, 128], [1024, 4], [1, 1024]]),
                             AX, OP.max, apply_absolute_value=True)
            xmv = vemark()
            w(SC, "ve", xnv)
            cav = inc(SC.activation(xn16b[:, sl, :], xn32b[:, sl, :], ACT.Copy), "ac")
            sp32 = dsem(f"sp32_{sl}", "sy")
            sp16 = dsem(f"sp16_{sl}", "sy")
            w(SY, "ve", xnv)
            s1 = inc(SY.dma_start(out=xn32_rr[b],
                                  in_=bass.AP(xn32b, sl * 4096, [[8192, 128], [1024, 4], [1, 1024]])),
                     sp32, 16)
            w(SY, "ac", cav)
            s2 = inc(SY.dma_start(out=xnbf_rr[b],
                                  in_=bass.AP(xn16b, sl * 4096, [[8192, 128], [1024, 4], [1, 1024]])),
                     sp16, 16)
            slot_tok[sl] = [(sp16, s2), ("ve", xmv), ("ac", cav)]
            spill_tok[sl] = [(sp32, s1), (sp16, s2)]

        p1_a(0)
        for b in range(16):
            if b + 1 < 16:
                p1_a(b + 1)
            p1_b(b)
        barrier()
        p1_done = dict(CN)
        if stop_after == "p1":
            barrier()
            return _finish()

    def partition_max_to_scalar(src_col, out11):
        dsem("gpe", "gp")
        v0 = vemark()
        w(GP, "ve", v0)
        inc(GP.dma_start(out=red_d[:], in_=src_col), "gpe", 16)
        w(GP, "gpe")
        inc(GP.dma_start(out=trow[:, :], in_=red_d.ap().rearrange("(a d) -> a d", a=1)), "gpe", 16)
        w(VE, "gpe")
        VE.tensor_reduce(out11, trow[:, :], AX, OP.max)
        VE.drain()

    def bcast128(src11_f32, dst_col, tmp_off):
        VE.tensor_copy(tinyb[:, tmp_off:tmp_off + 1], src11_f32)
        VE.drain()
        VE.tensor_copy(tiny[:, tmp_off:tmp_off + 1], tinyb[:, tmp_off:tmp_off + 1])
        VE.drain()
        VE.tensor_tensor(tiny[:, tmp_off + 1:tmp_off + 2], src11_f32,
                         tiny[:, tmp_off:tmp_off + 1], OP.subtract)
        VE.drain()
        VE.tensor_copy(tinyb[:, tmp_off + 1:tmp_off + 2], tiny[:, tmp_off + 1:tmp_off + 2])
        v1 = vsync()
        w(TE, "ve", v1)
        TE.matmul(ps_xA[0:128, 0:1], ones2[0:1, 0:128], tinyb[:, tmp_off:tmp_off + 1],
                  start=True, stop=False)
        t1 = inc(TE.matmul(ps_xA[0:128, 0:1], ones2[0:1, 0:128],
                           tinyb[:, tmp_off + 1:tmp_off + 2], start=False, stop=True), "te")
        w(VE, "te", t1)
        VE.tensor_copy(dst_col, ps_xA[0:128, 0:1])
        vd = vemark()
        return vd

    VE.tensor_reduce(onetmp[:, :], xmg[:, :], AX, OP.max)
    VE.drain()
    partition_max_to_scalar(onetmp[:, 0:1], tiny[:, 8:9])
    v0 = vemark()
    w(GP, "ve", v0)
    inc(GP.dma_start(out=cc1i_d[:, :], in_=tiny[:, 8:9]), "gpe", 16)
    w(GP, "gpe")
    ccv = inc(GP.collective_compute("AllReduce", OP.max,
                                    replica_groups=[list(range(NCORES))],
                                    ins=[cc1i_d.ap()], outs=[cc1o_d.ap()]), "cc")
    w(GP, "cc", ccv)
    g1 = inc(GP.dma_start(out=tiny[:, 9:10], in_=cc1o_d[:, :]), "gpe", 16)
    w(SC, "bia", ld_bias)
    sgv = inc(SC.activation(tiny[:, 10:11], cl32[:, 0:1], ACT.Sigmoid), "ac")
    w(VE, "gpe", g1)
    w(VE, "ac", sgv)
    VE.tensor_tensor(tiny[:, 11:12], tiny[:, 9:10], tiny[:, 10:11], OP.mult)
    VE.drain()
    VE.tensor_scalar(tiny[:, 11:12], tiny[:, 11:12], 1.0 / QMAX, None, OP.mult, OP.bypass)
    VE.drain()
    VE.reciprocal(tiny[:, 12:13], tiny[:, 11:12])
    VE.drain()
    bcast128(tiny[:, 11:12], sx_bc[:, 0:1], 0)
    bcast128(tiny[:, 12:13], rsx_bc[:, 0:1], 2)
    ex1_done = dict(CN)
    if stop_after == "ex1":
        barrier()
        return _finish()

    with contextlib.ExitStack() as p2:
        def sb2(name, shape, dtype):
            return p2.enter_context(nc.sbuf_tensor(name, shape, dtype))
        tcos = sb2("tcos_s", [128, S], BF16)
        tsin = sb2("tsin_s", [128, S], BF16)
        permt = sb2("permt", [128, 128], BF16)
        trit = sb2("trit", [128, 128], BF16)
        wqkvst = sb2("wqkvst", [128, 2, DCH, 128], BF16)
        wvp = sb2("wvp", [128, DCH, 1040], BF16)
        vstage = sb2("vstage", [128, D], BF16)
        xnf = sb2("xnf", [128, DCH, S], BF16)
        qro = sb2("qro", [128, DCH, S], BF16)
        kro = sb2("kro", [128, DCH, S], BF16)
        vt = sb2("vt", [128, 16, 1040], BF16)
        PT = sb2("PTt", [128, 4, 1024], BF16)
        qraw = sb2("qraw", [128, 2, 512], BF16)
        rsw = sb2("rsw", [128, 2, 512], BF16)
        tmpA = sb2("tmpA", [128, 512], BF16)
        tmpC = sb2("tmpC", [128, 512], BF16)
        nqrow = sb2("nqrow", [1, 16], F32)
        nkrow = sb2("nkrow", [1, 16], F32)
        nrm8 = sb2("nrm8", [1, 8], F32)
        brow = sb2("brow", [1, 16], F32)
        browb = sb2("browb", [1, 16], BF16)
        negbc = sb2("negbc", [128, 16], F32)
        recf = sb2("recf", [65, 1024], F32)
        rh16 = sb2("rh16", [65, 1024], BF16)
        atb = [sb2("atb0", [128, 512], BF16), sb2("atb1", [128, 512], BF16)]
        wqt = sb2("wqt", [128, 1, 1408], F32)
        wqo = sb2("wqo", [128, 1, 1408], BF16)
        wqr = sb2("wqr", [128, 1, 4], F32)
        nrm4 = sb2("nrm4", [2, 4], F32)

        dsem("cst", "sy")
        for t_, d_ in ((tcos, tcos_d), (tsin, tsin_d),
                       (permt, perm_d), (trit, tri_d)):
            inc(SY.dma_start(out=t_[:, :], in_=d_[:, :]), "cst", 16)
        ld_const2 = CN["cst"]

        w(SC, "gpc", gp_casts["gpc"])
        VE.memset(wvp[:, :, :], 0.0)
        wvp_ve0 = vemark()
        for c in range(DCH):
            w(SC, "ve", wvp_ve0)
            for (s, v) in [("ve", CN["ve"])] if c > 0 else []:
                SC.wait_ge(sems[s], v)
            tv = inc(SC.dma_start_transpose(vstage[:, :],
                                            wkvbf_d[2 * D:3 * D, 128 * c:128 * c + 128]),
                     "tr", 16)
            w(VE, "tr", tv)
            VE.tensor_copy(bass.AP(wvp, c * 1040, [[DCH * 1040, 128], [65, 16], [1, 64]]),
                           bass.AP(vstage, 0, [[D, 128], [64, 16], [1, 64]]))
            vsync()
        wvp_v = CN["ve"]

        def quant_weights(src_d, dst_d, sc_tile, nch, width, dequant=False):
            dsem("wql", "sy")
            dsem("wqs", "sy")
            toks = [[]]
            halves = [(0, width)] if width <= 1408 else [(0, width // 2), (width // 2, width)]
            src_rr = src_d.ap().rearrange("(m p) d -> m p d", p=128)
            dst_rr = dst_d.ap().rearrange("(m p) d -> m p d", p=128)
            for m in range(nch):
                sl = 0
                lvs = []
                red_ve = 0
                for hi_, (a, b) in enumerate(halves):
                    for (s, v) in toks[sl]:
                        SY.wait_ge(sems[s], v)
                    toks[sl] = []
                    w(SY, "ve", red_ve)
                    lv = inc(SY.dma_start(out=wqt[:, sl, 0:b - a], in_=src_rr[m][:, a:b]),
                             "wql", 16)
                    lvs.append(lv)
                    w(VE, "wql", lv)
                    VE.tensor_reduce(wqr[:, sl, hi_:hi_ + 1], wqt[:, sl, 0:b - a], AX,
                                     OP.max, apply_absolute_value=True)
                    VE.drain()
                    red_ve = vemark()
                    if hi_ == len(halves) - 1:
                        VE.tensor_reduce(wqr[:, sl, 2:3], wqr[:, sl, 0:len(halves)], AX, OP.max)
                        VE.drain()
                        VE.tensor_scalar(sc_tile[:, m:m + 1], wqr[:, sl, 2:3], 1.0 / QMAX,
                                         None, OP.mult, OP.bypass)
                        VE.drain()
                        VE.reciprocal(wqr[:, sl, 3:4], sc_tile[:, m:m + 1])
                        VE.drain()
                if len(halves) == 1:
                    VE.tensor_scalar(wqt[:, sl, 0:width], wqt[:, sl, 0:width],
                                     wqr[:, sl, 3:4], magic[:, 0:1], OP.mult, OP.add)
                    VE.drain()
                    if dequant:
                        VE.tensor_scalar(wqt[:, sl, 0:width], wqt[:, sl, 0:width],
                                         magic[:, 0:1], sc_tile[:, m:m + 1],
                                         OP.subtract, OP.mult)
                        VE.drain()
                        VE.tensor_copy(wqo[:, sl, 0:width], wqt[:, sl, 0:width])
                    else:
                        VE.tensor_scalar(wqo[:, sl, 0:width], wqt[:, sl, 0:width],
                                         magic[:, 0:1], None, OP.subtract, OP.bypass)
                    qv = vsync()
                    w(SY, "ve", qv)
                    sv = inc(SY.dma_start(out=dst_rr[m], in_=wqo[:, sl, 0:width]), "wqs", 16)
                    toks[sl] = [("wqs", sv)]
                else:
                    prev_sv = None
                    for hi_ in (1, 0):
                        a, b = halves[hi_]
                        if hi_ == 0:
                            if prev_sv is not None:
                                SY.wait_ge(sems["wqs"], prev_sv)
                            lv = inc(SY.dma_start(out=wqt[:, sl, 0:b - a],
                                                  in_=src_rr[m][:, a:b]), "wql", 16)
                            w(VE, "wql", lv)
                        VE.tensor_scalar(wqt[:, sl, 0:b - a], wqt[:, sl, 0:b - a],
                                         wqr[:, sl, 3:4], magic[:, 0:1], OP.mult, OP.add)
                        VE.drain()
                        if dequant:
                            VE.tensor_scalar(wqt[:, sl, 0:b - a], wqt[:, sl, 0:b - a],
                                             magic[:, 0:1], sc_tile[:, m:m + 1],
                                             OP.subtract, OP.mult)
                            VE.drain()
                            VE.tensor_copy(wqo[:, sl, 0:b - a], wqt[:, sl, 0:b - a])
                        else:
                            VE.tensor_scalar(wqo[:, sl, 0:b - a], wqt[:, sl, 0:b - a],
                                             magic[:, 0:1], None, OP.subtract, OP.bypass)
                        qv = vsync()
                        w(SY, "ve", qv)
                        sv = inc(SY.dma_start(out=dst_rr[m][:, a:b], in_=wqo[:, sl, 0:b - a]),
                                 "wqs", 16)
                        prev_sv = sv
                        toks[sl] = [("wqs", sv)]

        quant_weights(wg_d, wgi_d, wgsc, ICH, D)
        quant_weights(wu_d, wui_d, wusc, ICH, D)
        quant_weights(wd_d, wdi_d, wdsc, DCH, IFF, dequant=True)
        wq_done = dict(CN)
        if stop_after == "wq":
            barrier()
            return _finish()
        VE.tensor_scalar(scg[:, :], wgsc[:, :], sx_bc[:, 0:1], None, OP.mult, OP.bypass)
        VE.tensor_scalar(scu[:, :], wusc[:, :], sx_bc[:, 0:1], None, OP.mult, OP.bypass)
        scgu_v = vemark()

        xnbf_mat = xnbf_d
        flash_idx = 0
        xnf_tok = []
        qk_ps = [ps_avA, ps_avB]
        qk_tok = [[], []]
        rs_ps = [ps_xA, ps_xB]
        rs_tok = [[("ve", ex1_done["ve"])], []]
        qraw_tok = [[], []]
        pt_tok = [[], [], [], []]
        av_tok = [[], []]
        sx_tok = [[], []]
        at_tok = [[], []]
        prev_seq = None
        for s_i in range(NSEQ):
            t0 = s_i * S
            if prev_seq is not None:
                w(VE, "te", prev_seq["te"])
                w(VE, "ac", prev_seq["ac"])
            for (s, v) in xnf_tok:
                SC.wait_ge(sems[s], v)
            xnf_tok = []
            for s_ in ("sp16_0", "sp16_1"):
                SC.wait_ge(sems[s_], p1_done.get(s_, CN[s_]))
            for c in range(DCH):
                inc(SC.dma_start_transpose(xnf[:, c, :],
                                           xnbf_mat[t0:t0 + S, 128 * c:128 * c + 128]),
                    "tr", 16)
            xnf_ld = CN["tr"]

            wst_tok = [[], []]
            pend = None

            def rope_flush():
                nonlocal pend
                if pend is None:
                    return
                pi, dst, dch, cols, ct, st_, ev = pend
                pend = None
                w(TE, "ac", ev)
                rp = rs_ps[pi]
                for (s, v) in rs_tok[pi]:
                    TE.wait_ge(sems[s], v)
                rs_tok[pi] = []
                rv = inc(TE.matmul(rp[:, 0:512], permt[:, :], qraw[:, pi, :],
                                   start=True, stop=True), "te")
                w(VE, "te", rv)
                VE.tensor_copy(rsw[:, pi, :], rp[:, 0:512])
                rcv = vemark()
                rs_tok[pi].append(("ve", rcv))
                VE.tensor_tensor(tmpA[:, :], qraw[:, pi, :], ct[:, cols], OP.mult)
                VE.tensor_tensor(tmpC[:, :], rsw[:, pi, :], st_[:, cols], OP.mult)
                VE.tensor_tensor(dst[:, dch, cols], tmpA[:, :], tmpC[:, :], OP.add)
                dv = vsync()
                qk_tok[pi].append(("ve", dv))
                qraw_tok[pi].append(("ve", dv))

            for m in range(16):
                dst = qro if m < 8 else kro
                dch = m if m < 8 else m - 8
                ct, st_ = tcos, tsin
                msl = m % 2
                for (s, v) in wst_tok[msl]:
                    SC.wait_ge(sems[s], v)
                wst_tok[msl] = []
                wtv = inc(SC.dma_start_transpose(wqkvst[:, msl, :, :],
                                                 wkvbf_d[128 * m:128 * m + 128, :]),
                          "tr", 16)
                for tc in range(4):
                    pi = (m * 4 + tc) % 2
                    ps = qk_ps[pi]
                    w(TE, "tr", max(xnf_ld, wtv))
                    w(TE, "cst", ld_const2)
                    w(TE, "gp", bias_gp)
                    for (s, v) in qk_tok[pi]:
                        TE.wait_ge(sems[s], v)
                    qk_tok[pi] = []
                    for d in range(DCH):
                        TE.matmul(ps[:, 0:512], wqkvst[:, msl, d, :],
                                  xnf[:, d, 512 * tc:512 * tc + 512],
                                  start=(d == 0), stop=False)
                    mv = inc(TE.matmul(ps[:, 0:512], bhlqk[0:2, 128 * m:128 * m + 128],
                                       ones2[0:2, 0:512], start=False, stop=True), "te")
                    w(SC, "te", mv)
                    for (s, v) in qraw_tok[pi]:
                        SC.wait_ge(sems[s], v)
                    qraw_tok[pi] = []
                    ev = inc(SC.activation(qraw[:, pi, :], ps[:, 0:512], ACT.Copy), "ac")
                    qk_tok[pi].append(("ac", ev))
                    if tc == 3:
                        wst_tok[msl].append(("te", CN["te"]))
                    rope_flush()
                    pend = (pi, dst, dch, slice(512 * tc, 512 * tc + 512), ct, st_, ev)
            rope_flush()
            rope_done_v = CN["ve"]

            w(VE, "ve", wvp_v)
            vcnt = 0
            for tt in range(16):
                for (c0, c1) in ((0, 512), (512, 1024), (1024, 1040)):
                    pi = vcnt % 2
                    vcnt += 1
                    ps = qk_ps[pi]
                    for (s, v) in qk_tok[pi]:
                        TE.wait_ge(sems[s], v)
                    qk_tok[pi] = []
                    w(TE, "ve", wvp_v)
                    for d in range(DCH):
                        TE.matmul(ps[:, 0:c1 - c0], xnf[:, d, 128 * tt:128 * tt + 128],
                                  wvp[:, d, c0:c1], start=(d == 0), stop=False)
                    mv = inc(TE.matmul(ps[:, 0:c1 - c0], ones2[0:2, 0:128],
                                       vbp[0:2, c0:c1], start=False, stop=True), "te")
                    w(SC, "te", mv)
                    ev = inc(SC.activation(vt[:, tt, c0:c1], ps[:, 0:c1 - c0], ACT.Copy), "ac")
                    qk_tok[pi].append(("ac", ev))
            v_done = CN["ac"]

            sq_te = 0
            for (src, dstn) in ((qro, nqrow), (kro, nkrow)):
                for c in range(DCH):
                    w(VE, "ve", rope_done_v)
                    w(VE, "te", sq_te)
                    sqbf = bass.AP(PT, 0, [[4096, 128], [1, 2048]])
                    VE.tensor_tensor(sqbf, src[:, c, :], src[:, c, :], OP.mult)
                    sqv = vsync()
                    w(TE, "ve", sqv)
                    for tc in range(4):
                        for (s, v) in sx_tok[0]:
                            TE.wait_ge(sems[s], v)
                        sx_tok[0] = []
                        sq_sl = bass.AP(PT, 512 * tc, [[4096, 128], [1, 512]])
                        TE.matmul(ps_sA[0:1, 0:512], ind1e[:, 0:1],
                                  sq_sl, start=True, stop=True)
                        nv = inc(TE.matmul(ps_sA[0:1, 512:1024], ind1o[:, 0:1],
                                           sq_sl, start=True, stop=True), "te")
                        sq_te = nv
                        w(VE, "te", nv)
                        VE.tensor_reduce(nrm8[0:1, tc:tc + 1], ps_sA[0:1, 0:512], AX, OP.max)
                        VE.tensor_reduce(nrm8[0:1, 4 + tc:5 + tc], ps_sA[0:1, 512:1024], AX, OP.max)
                        rv = vemark()
                        sx_tok[0].append(("ve", rv))
                    VE.drain()
                    VE.tensor_reduce(dstn[0:1, 2 * c:2 * c + 1], nrm8[0:1, 0:4], AX, OP.max)
                    VE.tensor_reduce(dstn[0:1, 2 * c + 1:2 * c + 2], nrm8[0:1, 4:8], AX, OP.max)
                    VE.drain()
            VE.tensor_tensor(brow[:, :], nqrow[:, :], nkrow[:, :], OP.mult)
            nbv = vemark()
            w(SC, "ve", nbv)
            nsq = inc(SC.activation(brow[:, :], brow[:, :], ACT.Sqrt, bias=0.0), "ac")
            w(VE, "ac", nsq)
            VE.tensor_scalar(brow[:, :], brow[:, :], -0.125, None, OP.mult, OP.bypass)
            VE.drain()
            VE.tensor_copy(browb[:, :], brow[:, :])
            bbv = vsync()
            w(TE, "ve", bbv)
            for (s, v) in rs_tok[0]:
                TE.wait_ge(sems[s], v)
            rs_tok[0] = []
            nbm = inc(TE.matmul(ps_xA[0:128, 0:16], onesPb[0:1, 0:128], browb[0:1, 0:16],
                                start=True, stop=True), "te")
            w(VE, "te", nbm)
            VE.tensor_copy(negbc[:, :], ps_xA[0:128, 0:16])
            nbcv = vsync()
            rs_tok[0].append(("ve", nbcv))

            if stop_after == "rope0" and s_i == 0:
                w(SY, "ve", nbcv)
                for (t_, d_) in ((xnf, dxnf_d), (qro, dqro_d), (kro, dkro_d), (vt, dvt_d)):
                    inc(SY.dma_start(out=d_[:, :, :], in_=t_[:, :, :]), "sy", 16)
                inc(SY.dma_start(out=dnegb_d[:, :], in_=negbc[:, :]), "sy", 16)
                barrier()
                return _finish()
            w(SC, "ve", nbcv)
            ps_s = [ps_sA, ps_sB]
            pend_tails = []

            def tail_flush():
                while pend_tails:
                    ai, av, rxp, h_, ch_, r0_, r1_, qr_, t0_, asl, rhv = pend_tails.pop(0)
                    w(TE, "ve", rhv)
                    for (s, v) in rs_tok[ai]:
                        TE.wait_ge(sems[s], v)
                    rs_tok[ai] = []
                    rbm = inc(TE.matmul(rxp[0:64, 0:512], onesPb[64:65, 0:64],
                                        rh16[64:65, 512 * ai:512 * ai + 512],
                                        start=True, stop=True), "te")
                    w(VE, "te", rbm)
                    for (s, v) in at_tok[asl]:
                        VE.wait_ge(sems[s], v)
                    at_tok[asl] = []
                    VE.tensor_copy(recf[0:64, 512 * ai:512 * ai + 512], rxp[0:64, 0:512])
                    rcv = vemark()
                    rs_tok[ai].append(("ve", rcv))
                    VE.tensor_tensor(atb[asl][0:64, 0:512], av[0:64, 0:512],
                                     recf[0:64, 512 * ai:512 * ai + 512], OP.mult)
                    ntv = vsync()
                    av_tok[ai].append(("ve", ntv))
                    ab_s = dsem(f"atb{asl}", "sy")
                    w(SY, "ve", ntv)
                    spv = inc(SY.dma_start(out=attnb_d[ch_, r0_:r1_, t0_ + 512 * qr_:t0_ + 512 * qr_ + 512],
                                           in_=atb[asl][0:64, 0:512]), ab_s, 16)
                    at_tok[asl].append((ab_s, spv))

            class U:
                pass

            for chp in range(8):
                for qr in range(4):
                    qc = slice(512 * qr, 512 * qr + 512)
                    kts = list(range(4 * qr + 4))
                    groups = [kts[i:i + 2] for i in range(0, len(kts), 2)]
                    G = len(groups)
                    units = []
                    for par in range(2):
                        u = U()
                        u.h = 2 * chp + par
                        u.par = par
                        u.r0, u.r1 = 64 * par, 64 * par + 64
                        u.ai = par
                        u.av = qk_ps[par]
                        u.rxp = rs_ps[par]
                        u.sps = ps_s[par]
                        u.first_av = True
                        u.smv = [None] * G
                        u.epv = [None] * G
                        u.mkv = [None] * G
                        units.append(u)

                    def emit_scores(u, g):
                        grp = groups[g]
                        for (s, v) in sx_tok[u.par]:
                            TE.wait_ge(sems[s], v)
                        sx_tok[u.par] = []
                        w(TE, "ve", nbcv)
                        for j, kt in enumerate(grp):
                            mm_ = TE.matmul(u.sps[:, 512 * j:512 * j + 512],
                                            kro[u.r0:u.r1, chp, 128 * kt:128 * kt + 128],
                                            qro[u.r0:u.r1, chp, qc], start=True, stop=True)
                            if j == len(grp) - 1:
                                u.smv[g] = inc(mm_, "te")

                    def emit_exp(u, g):
                        grp = groups[g]
                        w(SC, "te", u.smv[g])
                        for (s, v) in pt_tok[2 * u.par + (g % 2)]:
                            SC.wait_ge(sems[s], v)
                        pt_tok[2 * u.par + (g % 2)] = []
                        pt = bass.AP(PT, (2 * u.par + (g % 2)) * 1024, [[4096, 128], [1, 512 * len(grp)]])
                        u.epv[g] = inc(SC.activation(pt, u.sps[:, 0:512 * len(grp)], ACT.Exp,
                                                     bias=negbc[:, u.h:u.h + 1], scale=0.125), "ac")
                        sx_tok[u.par].append(("ac", u.epv[g]))
                        need_ve = False
                        w(VE, "ac", u.epv[g])
                        base = (2 * u.par + (g % 2)) * 1024
                        for j, kt in enumerate(grp):
                            ja = kt - 4 * qr
                            if ja >= 0:
                                need_ve = True
                                if ja > 0:
                                    VE.memset(bass.AP(PT, base + 512 * j, [[4096, 128], [1, 128 * ja]]), 0.0)
                                mp = bass.AP(PT, base + 512 * j + 128 * ja, [[4096, 128], [1, 128]])
                                VE.tensor_tensor(mp, mp, trit[:, :], OP.mult)
                        u.mkv[g] = vsync() if need_ve else None

                    def emit_av(u, g):
                        grp = groups[g]
                        w(TE, "ac", u.epv[g])
                        if u.mkv[g] is not None:
                            w(TE, "ve", u.mkv[g])
                        w(TE, "ac", v_done)
                        if u.first_av:
                            for (s, v) in av_tok[u.ai]:
                                TE.wait_ge(sems[s], v)
                            av_tok[u.ai] = []
                        base = (2 * u.par + (g % 2)) * 1024
                        for j, kt in enumerate(grp):
                            last = (g == G - 1) and (j == len(grp) - 1)
                            mm_ = TE.matmul(u.av[0:65, 0:512], vt[:, kt, 65 * u.h:65 * u.h + 65],
                                            bass.AP(PT, base + 512 * j, [[4096, 128], [1, 512]]),
                                            start=u.first_av, stop=last)
                            u.first_av = False
                            if j == len(grp) - 1:
                                avv = inc(mm_, "te")
                        pt_tok[2 * u.par + (g % 2)].append(("te", avv))
                        u.avv = avv

                    uA, uB = units
                    emit_scores(uA, 0)
                    emit_scores(uB, 0)
                    tail_flush()
                    for g in range(G):
                        emit_exp(uA, g)
                        if g + 1 < G:
                            emit_scores(uA, g + 1)
                        emit_exp(uB, g)
                        emit_av(uA, g)
                        if g + 1 < G:
                            emit_scores(uB, g + 1)
                        emit_av(uB, g)
                    for u in units:
                        w(VE, "te", u.avv)
                        VE.reciprocal(recf[64:65, 512 * u.par:512 * u.par + 512],
                                      u.av[64:65, 0:512])
                        VE.drain()
                        VE.tensor_copy(rh16[64:65, 512 * u.par:512 * u.par + 512],
                                       recf[64:65, 512 * u.par:512 * u.par + 512])
                        rhv = vsync()
                        asl = u.par
                        pend_tails.append((u.ai, u.av, u.rxp, u.h, chp, u.r0, u.r1,
                                           qr, t0, asl, rhv))
                        flash_idx += 1
            tail_flush()
            qk_tok[0] += av_tok[0]
            qk_tok[1] += av_tok[1]
            av_tok = [[], []]
            xnf_tok = [("te", CN["te"])]
            prev_seq = dict(CN)
        p2_done = dict(CN)
        barrier()
        if stop_after == "p2":
            return _finish()

    late = contextlib.ExitStack()
    wpjf = late.enter_context(nc.sbuf_tensor("wpjf", [128, DCH, D], BF16))
    w(SC, "gpc", gp_casts["gpc"])
    for c in range(DCH):
        inc(SC.dma_start_transpose(wpjf[:, c, :], wpjbf_d[:, 128 * c:128 * c + 128]),
            "tr", 16)
    wpjf_tr = CN["tr"]
    with contextlib.ExitStack() as p25:
        def sb25(name, shape, dtype):
            return p25.enter_context(nc.sbuf_tensor(name, shape, dtype))
        xqf = sb25("xqf", [128, DCH, S], BF16)
        wgst = sb25("wgst", [128, 2, DCH, 128], BF16)
        wust = sb25("wust", [128, 2, DCH, 128], BF16)
        qx32 = sb25("qx32", [128, 2, 2048], F32)
        xq16 = sb25("xq16", [128, 2, 2048], BF16)
        gsb = sb25("gsb", [128, 2, 512], F32)
        silb = sb25("silb", [128, 2, 512], F32)
        usb = sb25("usb", [128, 2, 512], F32)
        hbuf = sb25("hbuf", [128, 2, 2048], F32)

        xn32_rr2 = xn32_d.ap().rearrange("(b m p) d -> b p m d", p=128, m=2)
        xqbf_rr = xqbf_d.ap().rearrange("(b m p) d -> b p m d", p=128, m=2)
        toks = [[], []]
        for tb in range(32):
            sl = tb % 2
            for (s, v) in toks[sl]:
                SY.wait_ge(sems[s], v)
            toks[sl] = []
            qx_s = dsem(f"qx{sl}", "sy")
            lv = inc(SY.dma_start(out=bass.AP(qx32, sl * 2048, [[4096, 128], [1024, 2], [1, 1024]]),
                                  in_=xn32_rr2[tb]), qx_s, 16)
            w(VE, qx_s, lv)
            VE.tensor_scalar(qx32[:, sl, :], qx32[:, sl, :], rsx_bc[:, 0:1], magic[:, 0:1],
                             OP.mult, OP.add)
            VE.tensor_scalar(qx32[:, sl, :], qx32[:, sl, :], magic[:, 0:1], QMAX,
                             OP.subtract, OP.min)
            VE.tensor_scalar(xq16[:, sl, :], qx32[:, sl, :], -QMAX - 1.0, None,
                             OP.max, OP.bypass)
            xqs_s = dsem(f"xqs{sl}", "sy")
            qv = vsync()
            w(SY, "ve", qv)
            sv = inc(SY.dma_start(out=xqbf_rr[tb],
                                  in_=bass.AP(xq16, sl * 2048, [[4096, 128], [1024, 2], [1, 1024]])),
                     xqs_s, 16)
            toks[sl] = [(xqs_s, sv), ("ve", qv)]
        xq_sp = (CN["xqs0"], CN["xqs1"])

        gu_ps = [ps_avA, ps_xA, ps_avB, ps_xB]
        gu_tok = [[], [], [], []]
        wgu_tok = [[], []]
        hb_tok = [[], []]
        xqf_tok = []
        hm_col = 0
        for b4 in range(4):
            tb0 = b4 * S
            for (s, v) in xqf_tok:
                SC.wait_ge(sems[s], v)
            xqf_tok = []
            SC.wait_ge(sems["xqs0"], xq_sp[0])
            SC.wait_ge(sems["xqs1"], xq_sp[1])
            for c in range(DCH):
                inc(SC.dma_start_transpose(xqf[:, c, :],
                                           xqbf_d[tb0:tb0 + S, 128 * c:128 * c + 128]),
                    "tr", 16)
            xqf_ld = CN["tr"]
            def gu_fetch(m):
                msl = m % 2
                for (s, v) in wgu_tok[msl]:
                    SC.wait_ge(sems[s], v)
                wgu_tok[msl] = []
                w(SC, "wqs", wq_done["wqs"])
                inc(SC.dma_start_transpose(wgst[:, msl, :, :],
                                           wgi_d[128 * m:128 * m + 128, :]), "tr", 16)
                return inc(SC.dma_start_transpose(wust[:, msl, :, :],
                                                  wui_d[128 * m:128 * m + 128, :]), "tr", 16)

            if b4 == 0:
                wtv_next = gu_fetch(0)
            for m in range(ICH):
                hsl = (b4 * ICH + m) % 2
                msl = m % 2
                wtv = wtv_next
                if m + 1 < ICH:
                    wtv_next = gu_fetch(m + 1)
                elif b4 < 3:
                    wtv_next = gu_fetch(0)
                for (s, v) in hb_tok[hsl]:
                    VE.wait_ge(sems[s], v)
                hb_tok[hsl] = []
                for tc in range(4):
                    gi = (2 * tc) % 4
                    ui = gi + 1
                    for (s, v) in gu_tok[gi]:
                        TE.wait_ge(sems[s], v)
                    gu_tok[gi] = []
                    for (s, v) in gu_tok[ui]:
                        TE.wait_ge(sems[s], v)
                    gu_tok[ui] = []
                    w(TE, "tr", max(xqf_ld, wtv))
                    cols = slice(512 * tc, 512 * tc + 512)
                    gv = uv = None
                    for d in range(DCH):
                        mm_ = TE.matmul(gu_ps[gi][:, 0:512], wgst[:, msl, d, :],
                                        xqf[:, d, cols], start=(d == 0), stop=(d == DCH - 1))
                        if d == DCH - 1:
                            gv = inc(mm_, "te")
                    for d in range(DCH):
                        mm_ = TE.matmul(gu_ps[ui][:, 0:512], wust[:, msl, d, :],
                                        xqf[:, d, cols], start=(d == 0), stop=(d == DCH - 1))
                        if d == DCH - 1:
                            uv = inc(mm_, "te")
                    esl = tc % 2
                    w(VE, "te", gv)
                    w(VE, "ve", scgu_v)
                    VE.tensor_scalar(gsb[:, esl, :], gu_ps[gi][:, 0:512], scg[:, m:m + 1],
                                     None, OP.mult, OP.bypass)
                    gsv = vsync()
                    gu_tok[gi].append(("ve", gsv))
                    w(SC, "ve", gsv)
                    slv = inc(SC.activation(silb[:, esl, :], gsb[:, esl, :], ACT.Silu), "ac")
                    w(VE, "te", uv)
                    VE.tensor_scalar(usb[:, esl, :], gu_ps[ui][:, 0:512], scu[:, m:m + 1],
                                     None, OP.mult, OP.bypass)
                    usv = vemark()
                    gu_tok[ui].append(("ve", usv))
                    w(VE, "ac", slv)
                    VE.tensor_tensor(hbuf[:, hsl, cols], silb[:, esl, :], usb[:, esl, :],
                                     OP.mult)
                    hv = vemark()
                VE.tensor_reduce(hmg[:, hm_col:hm_col + 1], hbuf[:, hsl, :], AX, OP.max,
                                 apply_absolute_value=True)
                hmv = vsync()
                hm_col += 1
                hb_s = dsem(f"hb{hsl}", "sy")
                w(SY, "ve", hmv)
                hs = inc(SY.dma_start(out=h32_d[m, :, tb0:tb0 + S], in_=hbuf[:, hsl, :]),
                         hb_s, 16)
                hb_tok[hsl] = [(hb_s, hs)]
                wgu_tok[msl].append(("te", CN["te"]))
            xqf_tok = [("te", CN["te"])]
        p25_done = dict(CN)
        barrier()
        if stop_after == "p25":
            return _finish()

    VE.tensor_reduce(onetmp[:, :], hmg[:, :], AX, OP.max)
    VE.drain()
    partition_max_to_scalar(onetmp[:, 0:1], tiny[:, 8:9])
    v0 = vemark()
    w(GP, "ve", v0)
    inc(GP.dma_start(out=cc2i_d[:, :], in_=tiny[:, 8:9]), "gpe", 16)
    w(GP, "gpe")
    ccv = inc(GP.collective_compute("AllReduce", OP.max,
                                    replica_groups=[list(range(NCORES))],
                                    ins=[cc2i_d.ap()], outs=[cc2o_d.ap()]), "cc")
    w(GP, "cc", ccv)
    g2 = inc(GP.dma_start(out=tiny[:, 9:10], in_=cc2o_d[:, :]), "gpe", 16)
    sgv = inc(SC.activation(tiny[:, 10:11], cl32[:, 1:2], ACT.Sigmoid), "ac")
    w(VE, "gpe", g2)
    w(VE, "ac", sgv)
    VE.tensor_tensor(tiny[:, 11:12], tiny[:, 9:10], tiny[:, 10:11], OP.mult)
    VE.drain()
    VE.tensor_scalar(tiny[:, 11:12], tiny[:, 11:12], 1.0 / QMAX, None, OP.mult, OP.bypass)
    VE.drain()
    VE.reciprocal(tiny[:, 12:13], tiny[:, 11:12])
    VE.drain()
    bcast128(tiny[:, 11:12], sh_bc[:, 0:1], 4)
    bcast128(tiny[:, 12:13], rsh_bc[:, 0:1], 6)

    with contextlib.ExitStack() as p3:
        def sb3(name, shape, dtype):
            return p3.enter_context(nc.sbuf_tensor(name, shape, dtype))
        wdf = sb3("wdf", [128, ICH, D], BF16)
        inth = sb3("inth", [128, 2, ICH, 512], BF16)
        atst = sb3("atst", [128, 2, DCH, 512], BF16)
        hld = sb3("hld", [128, 2, 1024], F32)
        xb3 = sb3("xb3", [128, 2, 4096], F32)
        outb = sb3("outb", [128, 2, 4096], F32)

        w(SC, "wqs", wq_done["wqs"])
        for i in range(ICH):
            inc(SC.dma_start_transpose(wdf[:, i, :], wdi_d[:, 128 * i:128 * i + 128]),
                "tr", 16)
        wdf_tr = CN["tr"]

        x_rr3 = x_d.ap().rearrange("(tc m p) d -> tc p m d", p=128, m=4)
        out_rr = out_d.ap().rearrange("(tc m p) d -> tc p m d", p=128, m=4)
        tc_tok = [[], []]
        hld_tok = [[], []]
        ps3 = [(ps_sA, ps_avA, ps_xA), (ps_sB, ps_avB, ps_xB)]
        ps3_tok = [[], []]
        ob_tok = [[], []]
        stage = {}

        def p3_load(tc):
            sl = tc % 2
            for (s, v) in tc_tok[sl]:
                SY.wait_ge(sems[s], v)
            tc_tok[sl] = []
            ats_s = dsem(f"ats{sl}", "sy")
            xb_s = dsem(f"xb{sl}", "sy")
            for c in range(DCH):
                inc(SY.dma_start(out=atst[:, sl, c, :],
                                 in_=attnb_d[c, :, 512 * tc:512 * tc + 512]), ats_s, 16)
            at_ld = CN[ats_s]
            xl = inc(SY.dma_start(out=bass.AP(xb3, sl * 4096, [[8192, 128], [1024, 4], [1, 1024]]),
                                  in_=x_rr3[tc]), xb_s, 16)
            for i2 in range(0, ICH, 2):
                npair = 2 if i2 + 1 < ICH else 1
                hsl = (i2 // 2) % 2
                for (s, v) in hld_tok[hsl]:
                    SY.wait_ge(sems[s], v)
                hld_tok[hsl] = []
                hl_s = dsem(f"hl{hsl}", "sy")
                lv = inc(SY.dma_start(out=bass.AP(hld, hsl * 1024, [[2048, 128], [1, 512 * npair]]),
                                      in_=bass.AP(h32_d, i2 * 128 * T + 512 * tc,
                                                  [[T, 128], [128 * T, npair], [1, 512]])),
                         hl_s, 16)
                w(VE, hl_s, lv)
                fd = 512 * npair
                hap = bass.AP(hld, hsl * 1024, [[2048, 128], [1, fd]])
                iap = bass.AP(inth, sl * (ICH * 512) + i2 * 512,
                              [[2 * ICH * 512, 128], [1, fd]])
                VE.tensor_scalar(hap, hap, rsh_bc[:, 0:1],
                                 magic[:, 0:1], OP.mult, OP.add)
                VE.tensor_scalar(hap, hap, magic[:, 0:1],
                                 QMAX, OP.subtract, OP.min)
                VE.tensor_scalar(iap, hap, -QMAX - 1.0,
                                 None, OP.max, OP.bypass)
                qv = vsync()
                hld_tok[hsl] = [("ve", qv)]
                tc_tok[sl].append(("ve", qv))
            stage[tc] = (CN["ve"], at_ld, xl, ats_s, xb_s)

        def p3_compute(tc):
            sl = tc % 2
            hq_done, at_ld, xl, ats_s, xb_s = stage.pop(tc)
            for tt in range(4):
                pp = (tc * 4 + tt) % 2
                prj, dn0, dn1 = ps3[pp]
                for (s, v) in ps3_tok[pp]:
                    TE.wait_ge(sems[s], v)
                ps3_tok[pp] = []
                w(TE, ats_s, at_ld)
                w(TE, "ve", hq_done)
                w(TE, "tr", wdf_tr)
                tsl = slice(128 * tt, 128 * tt + 128)
                for cc_i, cols in enumerate((slice(0, 512), slice(512, 1024))):
                    for d in range(DCH):
                        TE.matmul(prj[:, cols], atst[:, sl, d, tsl], wpjf[:, d, cols],
                                  start=(d == 0), stop=False)
                    TE.matmul(prj[:, cols], ones2[0:2, 0:128], pbhl[0:2, cols],
                              start=False, stop=True)
                    dps = dn0 if cc_i == 0 else dn1
                    for i in range(ICH):
                        TE.matmul(dps[:, 0:512], inth[:, sl, i, tsl], wdf[:, i, cols],
                                  start=(i == 0), stop=(i == ICH - 1))
                mv = temark()
                w(VE, "te", mv)
                w(VE, xb_s, xl)
                for (s, v) in (ob_tok[sl] if tt == 0 else []):
                    VE.wait_ge(sems[s], v)
                if tt == 0:
                    ob_tok[sl] = []
                for cc_i in range(2):
                    dps = (dn0, dn1)[cc_i]
                    cols = slice(512 * cc_i, 512 * cc_i + 512)
                    ocol = slice(1024 * tt + 512 * cc_i, 1024 * tt + 512 * cc_i + 512)
                    VE.tensor_scalar(outb[:, sl, ocol], dps[:, 0:512], sh_bc[:, 0:1], None,
                                     OP.mult, OP.bypass)
                    VE.tensor_tensor(outb[:, sl, ocol], outb[:, sl, ocol], prj[:, cols], OP.add)
                    VE.tensor_tensor(outb[:, sl, ocol], outb[:, sl, ocol],
                                     xb3[:, sl, ocol], OP.add)
                ev = vsync()
                ps3_tok[pp].append(("ve", ev))
            ov_s = dsem(f"ov{sl}", "sy")
            w(SY, "ve", CN["ve"])
            ov = inc(SY.dma_start(out=out_rr[tc],
                                  in_=bass.AP(outb, sl * 4096, [[8192, 128], [1024, 4], [1, 1024]])),
                     ov_s, 16)
            tc_tok[sl].append((ov_s, ov))
            tc_tok[sl].append(("te", CN["te"]))
            tc_tok[sl].append(("ve", CN["ve"]))
            ob_tok[sl].append((ov_s, ov))

        p3_load(0)
        for tc in range(16):
            if tc + 1 < 16:
                p3_load(tc + 1)
            p3_compute(tc)
            if stop_after == "p3serial":
                barrier()
        SY.wait_ge(sems["ov0"], CN["ov0"])
        SY.wait_ge(sems["ov1"], CN["ov1"])
    late.close()

    return _finish()


_CACHE = {}


def kernel(L0_attn_x, L0_attn_qkv_w, L0_attn_qkv_b, L0_attn_proj_w, L0_attn_proj_b,
           L0_ffn_clip_x, L0_ffn_clip_down, L0_ffn_w_gate, L0_ffn_w_up, L0_ffn_w_down):
    consts = _host_consts()
    if "nc" not in _CACHE:
        _CACHE["nc"] = build_program()
    nc, es = _CACHE["nc"]
    x = np.ascontiguousarray(np.asarray(L0_attn_x, dtype=np.float32))
    shared = {
        "qkv_w": np.ascontiguousarray(np.asarray(L0_attn_qkv_w, np.float32)),
        "qkv_b": np.ascontiguousarray(np.asarray(L0_attn_qkv_b, np.float32)),
        "proj_w": np.ascontiguousarray(np.asarray(L0_attn_proj_w, np.float32)),
        "proj_b": np.ascontiguousarray(np.asarray(L0_attn_proj_b, np.float32)),
        "clip_x": np.ascontiguousarray(np.asarray(L0_ffn_clip_x, np.float32)),
        "clip_down": np.ascontiguousarray(np.asarray(L0_ffn_clip_down, np.float32)),
        "w_gate": np.ascontiguousarray(np.asarray(L0_ffn_w_gate, np.float32)),
        "w_up": np.ascontiguousarray(np.asarray(L0_ffn_w_up, np.float32)),
        "w_down": np.ascontiguousarray(np.asarray(L0_ffn_w_down, np.float32)),
        **consts,
    }
    in_maps = []
    for c in range(NCORES):
        im = dict(shared)
        im["x"] = np.ascontiguousarray(x[NSEQ * c:NSEQ * (c + 1)].reshape(T, D))
        in_maps.append(im)
    res = run_bass_kernel_spmd(nc, in_maps, core_ids=list(range(NCORES)))
    out = np.concatenate(
        [np.asarray(r["out"], dtype=np.float32).reshape(NSEQ, S, D) for r in res.results],
        axis=0)
    return out


# version 13
# speedup vs baseline: 1.0450x; 1.0033x over previous
import contextlib

import numpy as np
import ml_dtypes

import concourse.bass as bass
import concourse.mybir as mybir
from concourse.alu_op_type import AluOpType as OP
from concourse.bass_utils import run_bass_kernel_spmd

F32 = mybir.dt.float32
BF16 = mybir.dt.bfloat16
ACT = mybir.ActivationFunctionType
AX = mybir.AxisListType.X
BF = ml_dtypes.bfloat16

NCORES = 8
B, S, D = 32, 2048, 1024
NSEQ = B // NCORES
T = NSEQ * S
H, HD = 16, 64
IFF = 2816
ICH = IFF // 128
DCH = D // 128
MAGIC32 = 12582912.0
CBIG = float(np.float32(1.5 * 2**82))
QMAX = 127.0


def _host_consts():
    half = 32
    freqs = 1.0 / 10000.0 ** (np.arange(half, dtype=np.float64) / half)
    pos = np.arange(S, dtype=np.float64)
    ang = pos[None, :] * freqs[:, None]
    cos32 = np.cos(ang)
    sin32 = np.sin(ang)
    r = np.arange(128)
    i_of_r = r % 32
    sgn = np.where((r % 64) < 32, -1.0, 1.0)
    tcos = cos32[i_of_r, :]
    tsin = sin32[i_of_r, :] * sgn[:, None]
    perm = np.zeros((128, 128), dtype=np.float64)
    for blk in range(2):
        b0 = 64 * blk
        perm[b0:b0 + 32, b0 + 32:b0 + 64] = np.eye(32)
        perm[b0 + 32:b0 + 64, b0:b0 + 32] = np.eye(32)
    rr, cc = np.meshgrid(np.arange(128), np.arange(128), indexing="ij")
    tri01 = (rr <= cc).astype(np.float64)
    c = {
        "tcos": tcos, "tsin": tsin,
        "permc": perm, "tri01": tri01,
    }
    return {k: np.asarray(v, dtype=np.float32).astype(BF) for k, v in c.items()}


def build_program(dbg=(), stop_after=None):
    nc = bass.Bass("TRN2", target_bir_lowering=False, debug=False)
    es = contextlib.ExitStack()

    def dram(name, shape, dtype, kind=None):
        if name in dbg:
            kind = "ExternalOutput"
        if kind is None:
            return nc.dram_tensor(name, shape, dtype)
        return nc.dram_tensor(name, shape, dtype, kind=kind)

    x_d = dram("x", [T, D], F32, "ExternalInput")
    qkvw_d = dram("qkv_w", [3 * D, D], F32, "ExternalInput")
    qkvb_d = dram("qkv_b", [3 * D], F32, "ExternalInput")
    pjw_d = dram("proj_w", [D, D], F32, "ExternalInput")
    pjb_d = dram("proj_b", [D], F32, "ExternalInput")
    clx_d = dram("clip_x", [1], F32, "ExternalInput")
    cld_d = dram("clip_down", [1], F32, "ExternalInput")
    wg_d = dram("w_gate", [IFF, D], F32, "ExternalInput")
    wu_d = dram("w_up", [IFF, D], F32, "ExternalInput")
    wd_d = dram("w_down", [D, IFF], F32, "ExternalInput")
    tcos_d = dram("tcos", [128, S], BF16, "ExternalInput")
    tsin_d = dram("tsin", [128, S], BF16, "ExternalInput")
    perm_d = dram("permc", [128, 128], BF16, "ExternalInput")
    tri_d = dram("tri01", [128, 128], BF16, "ExternalInput")
    out_d = dram("out", [T, D], F32, "ExternalOutput")
    xn32_d = dram("xn32", [T, D], F32)
    xnbf_d = dram("xnbf", [T, D], BF16)
    xqbf_d = dram("xqbf", [T, D], BF16)
    wkvbf_d = dram("wkvbf", [3 * D, D], BF16)
    wpjbf_d = dram("wpjbf", [D, D], BF16)
    wgi_d = dram("wgi", [IFF, D], BF16)
    wui_d = dram("wui", [IFF, D], BF16)
    wdi_d = dram("wdi", [D, IFF], BF16)
    h32_d = dram("h32", [ICH, 128, T], F32)
    attnb_d = dram("attnb", [DCH, 128, T], BF16)
    cc1i_d = dram("cc1i", [1, 1], F32)
    cc1o_d = dram("cc1o", [1, 1], F32)
    cc2i_d = dram("cc2i", [1, 1], F32)
    cc2o_d = dram("cc2o", [1, 1], F32)
    red_d = dram("redsc", [128], F32)
    dxnf_d = dram("dxnf", [128, DCH, S], BF16)
    dqro_d = dram("dqro", [128, DCH, S], BF16)
    dkro_d = dram("dkro", [128, DCH, S], BF16)
    dvt_d = dram("dvt", [128, 16, 1040], BF16)
    dnegb_d = dram("dnegb", [128, 16], F32)
    dpt0_d = dram("dpt0", [128, 1024], BF16)
    dpt1_d = dram("dpt1", [128, 1024], BF16)
    dav_d = dram("dav", [128, 512], F32)
    dwdf_d = dram("dwdf", [128, 22, 1024], BF16)
    dwpj_d = dram("dwpj", [128, 8, 1024], BF16)
    dith_d = dram("dith", [128, 22, 512], BF16)
    datst_d = dram("datst", [128, 8, 512], BF16)
    dbias_d = dram("dbias", [2, 4096], BF16)
    dsc_d = dram("dsc", [128, 8], F32)
    drr_d = dram("drr", [65, 512], F32)
    drcf_d = dram("drcf", [64, 512], F32)
    datb_d = dram("datb", [128, 512], BF16)
    hlscr_d = dram("hlscr", [2, 2048], BF16)

    sems = {}
    for s in ("sy", "tr", "gp", "te", "ve", "ac", "cc", "bar"):
        sems[s] = es.enter_context(nc.semaphore("sem_" + s))
    CN = {k: 0 for k in sems}
    DMA_FAM = {"sy": ["sy"], "tr": ["tr"], "gp": ["gp"]}

    def dsem(name, fam):
        if name not in sems:
            sems[name] = es.enter_context(nc.semaphore("sem_" + name))
            CN[name] = 0
            DMA_FAM[fam].append(name)
        return name

    def sb(name, shape, dtype):
        return es.enter_context(nc.sbuf_tensor(name, shape, dtype))

    ones2 = sb("ones2", [2, 512], BF16)
    magic = sb("magict", [128, 1], F32)
    epsc = sb("epsc", [128, 1], F32)
    onetmp = sb("onetmp", [128, 1], F32)
    bhlqk = sb("bhlqk", [2, 2 * D], BF16)
    pbhl = sb("pbhl", [2, D], BF16)
    vbp = sb("vbp", [2, 1040], BF16)
    cl32 = sb("cl32", [1, 2], F32)
    wgsc = sb("wgsc", [128, ICH], F32)
    wusc = sb("wusc", [128, ICH], F32)
    wdsc = sb("wdsc", [128, DCH], F32)
    scg = sb("scg", [128, ICH], F32)
    scu = sb("scu", [128, ICH], F32)
    rsx_bc = sb("rsx_bc", [128, 1], F32)
    sx_bc = sb("sx_bc", [128, 1], F32)
    rsh_bc = sb("rsh_bc", [128, 1], F32)
    sh_bc = sb("sh_bc", [128, 1], F32)
    tiny = sb("tinyt", [1, 16], F32)
    tinyb = sb("tinyb", [1, 16], BF16)
    trow = sb("trowt", [1, 128], F32)
    xmg = sb("xmg", [128, 64], F32)
    hmg = sb("hmg", [128, 88], F32)
    onesPb = sb("onesPb", [128, 128], BF16)
    ind1e = sb("ind1e", [128, 1], BF16)
    ind1o = sb("ind1o", [128, 1], BF16)

    ps_sA = es.enter_context(nc.psum_tensor("ps_sA", [128, 1024], F32))
    ps_sB = es.enter_context(nc.psum_tensor("ps_sB", [128, 1024], F32))
    ps_avA = es.enter_context(nc.psum_tensor("ps_avA", [128, 512], F32))
    ps_avB = es.enter_context(nc.psum_tensor("ps_avB", [128, 512], F32))
    ps_xA = es.enter_context(nc.psum_tensor("ps_xA", [128, 512], F32))
    ps_xB = es.enter_context(nc.psum_tensor("ps_xB", [128, 512], F32))

    block = es.enter_context(nc.Block())
    REC = {k: [] for k in ("sy", "sc", "ve", "te", "gp")}

    class _Pend:
        __slots__ = ("incs",)

        def __init__(self):
            self.incs = []

        def then_inc(self, sem, n=1):
            self.incs.append((sem, n))
            return self

    class _Eng:
        def __init__(self, key):
            self._k = key

        def __getattr__(self, name):
            q = REC[self._k]

            def call(*a, **kw):
                p = _Pend()
                q.append((name, a, kw, p))
                return p
            return call

    SY, SC, VE, TE, GP = _Eng("sy"), _Eng("sc"), _Eng("ve"), _Eng("te"), _Eng("gp")

    def inc(instr, s, n=1):
        instr.then_inc(sems[s], n)
        CN[s] += n
        return CN[s]

    def mark(key, s):
        name = REC[key][-1][0]
        assert name not in ("wait_ge",), name

        REC[key][-1][3].incs.append((sems[s], 1))
        CN[s] += 1
        return CN[s]

    def vemark():
        return mark("ve", "ve")

    def vsync():
        VE.drain()
        return mark("ve", "ve")

    def temark():
        return mark("te", "te")

    def w(eng, s, val=None):
        eng.wait_ge(sems[s], CN[s] if val is None else val)

    def barrier():
        for s in DMA_FAM["sy"]:
            SY.wait_ge(sems[s], CN[s])
        inc(SY.nop(), "bar")
        for s in DMA_FAM["tr"]:
            SC.wait_ge(sems[s], CN[s])
        inc(SC.nop(), "bar")
        for s in DMA_FAM["gp"]:
            GP.wait_ge(sems[s], CN[s])
        inc(GP.nop(), "bar")
        inc(TE.nop(), "bar")
        inc(VE.nop(), "bar")
        tgt = CN["bar"]
        for eng in (TE, VE, SC, GP, SY):
            eng.wait_ge(sems["bar"], tgt)

    def _finish():
        def _replay(key):
            def f(eng):
                for (name, a, kw, p) in REC[key]:
                    if key in ("ve", "te", "sc") and name == "nop" and p.incs:
                        eng.drain()
                    r = getattr(eng, name)(*a, **kw)
                    for (sem, n) in p.incs:
                        r.then_inc(sem, n)
            return f
        block.sync(_replay("sy"))
        block.scalar(_replay("sc"))
        block.vector(_replay("ve"))
        block.tensor(_replay("te"))
        block.gpsimd(_replay("gp"))
        return nc, es

    dsem("gpc", "gp")
    inc(GP.dma_start(out=wkvbf_d[:, :], in_=qkvw_d[:, :]), "gpc", 16)
    inc(GP.dma_start(out=wpjbf_d[:, :], in_=pjw_d[:, :]), "gpc", 16)
    gp_casts = dict(CN)

    VE.memset(ones2[:, :], 1.0)
    VE.memset(epsc[:, :], 1e-5)
    VE.memset(onesPb[:, :], 1.0)
    VE.memset(ind1e[0:64, 0:1], 1.0)
    VE.memset(ind1e[64:128, 0:1], 0.0)
    VE.memset(ind1o[0:64, 0:1], 0.0)
    VE.memset(ind1o[64:128, 0:1], 1.0)
    VE.memset(onetmp[:, :], 1.0)
    VE.drain()
    VE.tensor_scalar(magic[:, :], onetmp[:, :], float(2.0 ** -30), 1.0,
                     OP.add, OP.subtract)
    VE.drain()
    VE.tensor_scalar(magic[:, :], magic[:, :], CBIG, MAGIC32,
                     OP.mult, OP.add)
    VE.drain()

    p0 = contextlib.ExitStack()
    qb32 = p0.enter_context(nc.sbuf_tensor("qb32", [1, 3 * D], F32))
    hl16 = p0.enter_context(nc.sbuf_tensor("hl16", [1, 3 * D], BF16))
    hl32 = p0.enter_context(nc.sbuf_tensor("hl32", [1, 3 * D], F32))
    lo32 = p0.enter_context(nc.sbuf_tensor("lo32", [1, 3 * D], F32))
    lo16 = p0.enter_context(nc.sbuf_tensor("lo16", [1, 3 * D], BF16))
    vb32 = p0.enter_context(nc.sbuf_tensor("vb32", [1, 1040], F32))

    dsem("bia", "sy")
    inc(SY.dma_start(out=qb32[:, :], in_=qkvb_d.ap().rearrange("(a d) -> a d", a=1)), "bia", 16)
    inc(SY.dma_start(out=cl32[:, 0:1], in_=clx_d.ap().rearrange("(a d) -> a d", a=1)), "bia", 16)
    inc(SY.dma_start(out=cl32[:, 1:2], in_=cld_d.ap().rearrange("(a d) -> a d", a=1)), "bia", 16)
    ld_bias = CN["bia"]

    def hilo_row(src_ap, n, dst, dst_off):
        w(VE, "gpb") if "gpb" in sems else None
        VE.tensor_copy(hl16[:, 0:n], src_ap)
        VE.drain()
        VE.tensor_copy(hl32[:, 0:n], hl16[:, 0:n])
        VE.drain()
        VE.tensor_tensor(lo32[:, 0:n], src_ap, hl32[:, 0:n], OP.subtract)
        VE.drain()
        VE.tensor_copy(lo16[:, 0:n], lo32[:, 0:n])
        v = vsync()
        dsem("gpb", "gp")
        w(GP, "ve", v)
        inc(GP.dma_start(out=hlscr_d[0:1, 0:n], in_=hl16[:, 0:n]), "gpb", 16)
        inc(GP.dma_start(out=hlscr_d[1:2, 0:n], in_=lo16[:, 0:n]), "gpb", 16)
        w(GP, "gpb")
        inc(GP.dma_start(out=dst[0:2, dst_off:dst_off + n], in_=hlscr_d[0:2, 0:n]), "gpb", 16)

    w(VE, "bia", ld_bias)
    hilo_row(qb32[:, 0:2 * D], 2 * D, bhlqk, 0)
    VE.memset(vb32[:, :], 0.0)
    VE.tensor_copy(bass.AP(vb32, 0, [[1040, 1], [65, 16], [1, 64]]),
                   bass.AP(qb32, 2 * D, [[3 * D, 1], [64, 16], [1, 64]]))
    VE.memset(bass.AP(vb32, 64, [[1040, 1], [65, 16], [1, 1]]), 1.0)
    VE.drain()
    hilo_row(vb32[:, 0:1040], 1040, vbp, 0)
    pb32 = qb32
    pv = CN["ve"]
    w(SY, "ve", pv)
    w(SY, "gp")
    inc(SY.dma_start(out=pb32[:, 0:D], in_=pjb_d.ap().rearrange("(a d) -> a d", a=1)), "bia", 16)
    w(VE, "bia")
    hilo_row(pb32[:, 0:D], D, pbhl, 0)

    bias_gp = CN["gp"]
    barrier()
    p0.close()

    with contextlib.ExitStack() as p1:
        def sb1(name, shape, dtype):
            return p1.enter_context(nc.sbuf_tensor(name, shape, dtype))
        xb = sb1("xb", [128, 3, 4096], F32)
        sqb = sb1("sqb", [128, 3, 4096], F32)
        xn32b = sb1("xn32b", [128, 3, 4096], F32)
        xn16b = sb1("xn16b", [128, 3, 4096], BF16)
        st_sum = sb1("st_sum", [128, 3, 4], F32)
        st_sq = sb1("st_sq", [128, 3, 4], F32)
        st_m = sb1("st_m", [128, 3, 4], F32)
        st_e = sb1("st_e", [128, 3, 4], F32)
        st_v = sb1("st_v", [128, 3, 4], F32)
        st_sd = sb1("st_sd", [128, 3, 4], F32)
        st_r = sb1("st_r", [128, 3, 4], F32)
        st_n = sb1("st_n", [128, 3, 4], F32)

        slot_tok = [[], [], []]
        spill_tok = [[], [], []]
        x_rr = x_d.ap().rearrange("(b m p) d -> b p m d", p=128, m=4)
        xn32_rr = xn32_d.ap().rearrange("(b m p) d -> b p m d", p=128, m=4)
        xnbf_rr = xnbf_d.ap().rearrange("(b m p) d -> b p m d", p=128, m=4)
        stv = {}

        def p1_a(b):
            sl = b % 3
            for (s, v) in slot_tok[sl]:
                SY.wait_ge(sems[s], v)
            slot_tok[sl] = []
            xs_s = dsem(f"x{sl}", "sy")
            ldv = inc(SY.dma_start(out=bass.AP(xb, sl * 4096, [[12288, 128], [1024, 4], [1, 1024]]),
                                   in_=x_rr[b]), xs_s, 16)
            w(VE, xs_s, ldv)
            VE.tensor_reduce(st_sum[:, sl, :],
                             bass.AP(xb, sl * 4096, [[12288, 128], [1024, 4], [1, 1024]]),
                             AX, OP.add)
            w(SC, xs_s, ldv)
            sqv = inc(SC.activation(sqb[:, sl, :], xb[:, sl, :], ACT.Square), "ac")
            w(VE, "ac", sqv)
            VE.tensor_reduce(st_sq[:, sl, :],
                             bass.AP(sqb, sl * 4096, [[12288, 128], [1024, 4], [1, 1024]]),
                             AX, OP.add)
            VE.drain()
            VE.tensor_scalar(st_m[:, sl, :], st_sum[:, sl, :], 1.0 / D, None, OP.mult, OP.bypass)
            VE.tensor_scalar(st_e[:, sl, :], st_sq[:, sl, :], 1.0 / D, None, OP.mult, OP.bypass)
            VE.drain()
            VE.tensor_tensor(st_v[:, sl, :], st_m[:, sl, :], st_m[:, sl, :], OP.mult)
            VE.drain()
            VE.tensor_tensor(st_v[:, sl, :], st_e[:, sl, :], st_v[:, sl, :], OP.subtract)
            vv = vsync()
            w(SC, "ve", vv)
            stv[b] = inc(SC.activation(st_sd[:, sl, :], st_v[:, sl, :], ACT.Sqrt,
                                       bias=epsc[:, 0:1]), "ac")

        def p1_b(b):
            sl = b % 3
            sdv = stv.pop(b)
            w(VE, "ac", sdv)
            VE.reciprocal(st_r[:, sl, :], st_sd[:, sl, :])
            VE.drain()
            VE.tensor_tensor(st_n[:, sl, :], st_m[:, sl, :], st_r[:, sl, :], OP.mult)
            VE.drain()
            VE.tensor_scalar(st_n[:, sl, :], st_n[:, sl, :], -1.0, None, OP.mult, OP.bypass)
            VE.drain()
            for (s, v) in spill_tok[sl]:
                VE.wait_ge(sems[s], v)
            spill_tok[sl] = []
            for mm in range(4):
                VE.tensor_scalar(xn32b[:, sl, 1024 * mm:1024 * mm + 1024],
                                 xb[:, sl, 1024 * mm:1024 * mm + 1024],
                                 st_r[:, sl, mm:mm + 1], st_n[:, sl, mm:mm + 1],
                                 OP.mult, OP.add)
            xnv = vsync()
            VE.tensor_reduce(xmg[:, 4 * b:4 * b + 4],
                             bass.AP(xn32b, sl * 4096, [[12288, 128], [1024, 4], [1, 1024]]),
                             AX, OP.max, apply_absolute_value=True)
            xmv = vemark()
            w(SC, "ve", xnv)
            cav = inc(SC.activation(xn16b[:, sl, :], xn32b[:, sl, :], ACT.Copy), "ac")
            sp32 = dsem(f"sp32_{sl}", "sy")
            sp16 = dsem(f"sp16_{sl}", "sy")
            w(SY, "ve", xnv)
            s1 = inc(SY.dma_start(out=xn32_rr[b],
                                  in_=bass.AP(xn32b, sl * 4096, [[12288, 128], [1024, 4], [1, 1024]])),
                     sp32, 16)
            w(SY, "ac", cav)
            s2 = inc(SY.dma_start(out=xnbf_rr[b],
                                  in_=bass.AP(xn16b, sl * 4096, [[12288, 128], [1024, 4], [1, 1024]])),
                     sp16, 16)
            slot_tok[sl] = [(sp16, s2), ("ve", xmv), ("ac", cav)]
            spill_tok[sl] = [(sp32, s1), (sp16, s2)]

        p1_a(0)
        p1_a(1)
        for b in range(16):
            if b + 2 < 16:
                p1_a(b + 2)
            p1_b(b)
        barrier()
        p1_done = dict(CN)
        if stop_after == "p1":
            barrier()
            return _finish()

    def partition_max_to_scalar(src_col, out11):
        dsem("gpe", "gp")
        v0 = vemark()
        w(GP, "ve", v0)
        inc(GP.dma_start(out=red_d[:], in_=src_col), "gpe", 16)
        w(GP, "gpe")
        inc(GP.dma_start(out=trow[:, :], in_=red_d.ap().rearrange("(a d) -> a d", a=1)), "gpe", 16)
        w(VE, "gpe")
        VE.tensor_reduce(out11, trow[:, :], AX, OP.max)
        VE.drain()

    def bcast128(src11_f32, dst_col, tmp_off):
        VE.tensor_copy(tinyb[:, tmp_off:tmp_off + 1], src11_f32)
        VE.drain()
        VE.tensor_copy(tiny[:, tmp_off:tmp_off + 1], tinyb[:, tmp_off:tmp_off + 1])
        VE.drain()
        VE.tensor_tensor(tiny[:, tmp_off + 1:tmp_off + 2], src11_f32,
                         tiny[:, tmp_off:tmp_off + 1], OP.subtract)
        VE.drain()
        VE.tensor_copy(tinyb[:, tmp_off + 1:tmp_off + 2], tiny[:, tmp_off + 1:tmp_off + 2])
        v1 = vsync()
        w(TE, "ve", v1)
        TE.matmul(ps_xA[0:128, 0:1], ones2[0:1, 0:128], tinyb[:, tmp_off:tmp_off + 1],
                  start=True, stop=False)
        t1 = inc(TE.matmul(ps_xA[0:128, 0:1], ones2[0:1, 0:128],
                           tinyb[:, tmp_off + 1:tmp_off + 2], start=False, stop=True), "te")
        w(VE, "te", t1)
        VE.tensor_copy(dst_col, ps_xA[0:128, 0:1])
        vd = vemark()
        return vd

    VE.tensor_reduce(onetmp[:, :], xmg[:, :], AX, OP.max)
    VE.drain()
    partition_max_to_scalar(onetmp[:, 0:1], tiny[:, 8:9])
    v0 = vemark()
    w(GP, "ve", v0)
    inc(GP.dma_start(out=cc1i_d[:, :], in_=tiny[:, 8:9]), "gpe", 16)
    w(GP, "gpe")
    ccv = inc(GP.collective_compute("AllReduce", OP.max,
                                    replica_groups=[list(range(NCORES))],
                                    ins=[cc1i_d.ap()], outs=[cc1o_d.ap()]), "cc")
    w(GP, "cc", ccv)
    g1 = inc(GP.dma_start(out=tiny[:, 9:10], in_=cc1o_d[:, :]), "gpe", 16)
    w(SC, "bia", ld_bias)
    sgv = inc(SC.activation(tiny[:, 10:11], cl32[:, 0:1], ACT.Sigmoid), "ac")
    w(VE, "gpe", g1)
    w(VE, "ac", sgv)
    VE.tensor_tensor(tiny[:, 11:12], tiny[:, 9:10], tiny[:, 10:11], OP.mult)
    VE.drain()
    VE.tensor_scalar(tiny[:, 11:12], tiny[:, 11:12], 1.0 / QMAX, None, OP.mult, OP.bypass)
    VE.drain()
    VE.reciprocal(tiny[:, 12:13], tiny[:, 11:12])
    VE.drain()
    bcast128(tiny[:, 11:12], sx_bc[:, 0:1], 0)
    bcast128(tiny[:, 12:13], rsx_bc[:, 0:1], 2)
    ex1_done = dict(CN)
    if stop_after == "ex1":
        barrier()
        return _finish()

    with contextlib.ExitStack() as p2:
        def sb2(name, shape, dtype):
            return p2.enter_context(nc.sbuf_tensor(name, shape, dtype))
        tcos = sb2("tcos_s", [128, S], BF16)
        tsin = sb2("tsin_s", [128, S], BF16)
        permt = sb2("permt", [128, 128], BF16)
        trit = sb2("trit", [128, 128], BF16)
        wqkvst = sb2("wqkvst", [128, 2, DCH, 128], BF16)
        wvp = sb2("wvp", [128, DCH, 1040], BF16)
        vstage = sb2("vstage", [128, D], BF16)
        xnf = sb2("xnf", [128, DCH, S], BF16)
        qro = sb2("qro", [128, DCH, S], BF16)
        kro = sb2("kro", [128, DCH, S], BF16)
        vt = sb2("vt", [128, 16, 1040], BF16)
        PT = sb2("PTt", [128, 4, 1024], BF16)
        qraw = sb2("qraw", [128, 2, 512], BF16)
        rsw = sb2("rsw", [128, 2, 512], BF16)
        tmpA = sb2("tmpA", [128, 512], BF16)
        tmpC = sb2("tmpC", [128, 512], BF16)
        nqrow = sb2("nqrow", [1, 16], F32)
        nkrow = sb2("nkrow", [1, 16], F32)
        nrm8 = sb2("nrm8", [1, 8], F32)
        brow = sb2("brow", [1, 16], F32)
        browb = sb2("browb", [1, 16], BF16)
        negbc = sb2("negbc", [128, 16], F32)
        recf = sb2("recf", [65, 1024], F32)
        rh16 = sb2("rh16", [65, 1024], BF16)
        atb = [sb2("atb0", [128, 512], BF16), sb2("atb1", [128, 512], BF16)]
        wqt = sb2("wqt", [128, 1, 1408], F32)
        wqo = sb2("wqo", [128, 1, 1408], BF16)
        wqr = sb2("wqr", [128, 1, 4], F32)
        nrm4 = sb2("nrm4", [2, 4], F32)

        dsem("cst", "sy")
        for t_, d_ in ((tcos, tcos_d), (tsin, tsin_d),
                       (permt, perm_d), (trit, tri_d)):
            inc(SY.dma_start(out=t_[:, :], in_=d_[:, :]), "cst", 16)
        ld_const2 = CN["cst"]

        w(SC, "gpc", gp_casts["gpc"])
        VE.memset(wvp[:, :, :], 0.0)
        wvp_ve0 = vemark()
        for c in range(DCH):
            w(SC, "ve", wvp_ve0)
            for (s, v) in [("ve", CN["ve"])] if c > 0 else []:
                SC.wait_ge(sems[s], v)
            tv = inc(SC.dma_start_transpose(vstage[:, :],
                                            wkvbf_d[2 * D:3 * D, 128 * c:128 * c + 128]),
                     "tr", 16)
            w(VE, "tr", tv)
            VE.tensor_copy(bass.AP(wvp, c * 1040, [[DCH * 1040, 128], [65, 16], [1, 64]]),
                           bass.AP(vstage, 0, [[D, 128], [64, 16], [1, 64]]))
            vsync()
        wvp_v = CN["ve"]

        def quant_weights(src_d, dst_d, sc_tile, nch, width, dequant=False):
            dsem("wql", "sy")
            dsem("wqs", "sy")
            toks = [[]]
            halves = [(0, width)] if width <= 1408 else [(0, width // 2), (width // 2, width)]
            src_rr = src_d.ap().rearrange("(m p) d -> m p d", p=128)
            dst_rr = dst_d.ap().rearrange("(m p) d -> m p d", p=128)
            for m in range(nch):
                sl = 0
                lvs = []
                red_ve = 0
                for hi_, (a, b) in enumerate(halves):
                    for (s, v) in toks[sl]:
                        SY.wait_ge(sems[s], v)
                    toks[sl] = []
                    w(SY, "ve", red_ve)
                    lv = inc(SY.dma_start(out=wqt[:, sl, 0:b - a], in_=src_rr[m][:, a:b]),
                             "wql", 16)
                    lvs.append(lv)
                    w(VE, "wql", lv)
                    VE.tensor_reduce(wqr[:, sl, hi_:hi_ + 1], wqt[:, sl, 0:b - a], AX,
                                     OP.max, apply_absolute_value=True)
                    VE.drain()
                    red_ve = vemark()
                    if hi_ == len(halves) - 1:
                        VE.tensor_reduce(wqr[:, sl, 2:3], wqr[:, sl, 0:len(halves)], AX, OP.max)
                        VE.drain()
                        VE.tensor_scalar(sc_tile[:, m:m + 1], wqr[:, sl, 2:3], 1.0 / QMAX,
                                         None, OP.mult, OP.bypass)
                        VE.drain()
                        VE.reciprocal(wqr[:, sl, 3:4], sc_tile[:, m:m + 1])
                        VE.drain()
                if len(halves) == 1:
                    VE.tensor_scalar(wqt[:, sl, 0:width], wqt[:, sl, 0:width],
                                     wqr[:, sl, 3:4], magic[:, 0:1], OP.mult, OP.add)
                    VE.drain()
                    if dequant:
                        VE.tensor_scalar(wqt[:, sl, 0:width], wqt[:, sl, 0:width],
                                         magic[:, 0:1], sc_tile[:, m:m + 1],
                                         OP.subtract, OP.mult)
                        VE.drain()
                        VE.tensor_copy(wqo[:, sl, 0:width], wqt[:, sl, 0:width])
                    else:
                        VE.tensor_scalar(wqo[:, sl, 0:width], wqt[:, sl, 0:width],
                                         magic[:, 0:1], None, OP.subtract, OP.bypass)
                    qv = vsync()
                    w(SY, "ve", qv)
                    sv = inc(SY.dma_start(out=dst_rr[m], in_=wqo[:, sl, 0:width]), "wqs", 16)
                    toks[sl] = [("wqs", sv)]
                else:
                    prev_sv = None
                    for hi_ in (1, 0):
                        a, b = halves[hi_]
                        if hi_ == 0:
                            if prev_sv is not None:
                                SY.wait_ge(sems["wqs"], prev_sv)
                            lv = inc(SY.dma_start(out=wqt[:, sl, 0:b - a],
                                                  in_=src_rr[m][:, a:b]), "wql", 16)
                            w(VE, "wql", lv)
                        VE.tensor_scalar(wqt[:, sl, 0:b - a], wqt[:, sl, 0:b - a],
                                         wqr[:, sl, 3:4], magic[:, 0:1], OP.mult, OP.add)
                        VE.drain()
                        if dequant:
                            VE.tensor_scalar(wqt[:, sl, 0:b - a], wqt[:, sl, 0:b - a],
                                             magic[:, 0:1], sc_tile[:, m:m + 1],
                                             OP.subtract, OP.mult)
                            VE.drain()
                            VE.tensor_copy(wqo[:, sl, 0:b - a], wqt[:, sl, 0:b - a])
                        else:
                            VE.tensor_scalar(wqo[:, sl, 0:b - a], wqt[:, sl, 0:b - a],
                                             magic[:, 0:1], None, OP.subtract, OP.bypass)
                        qv = vsync()
                        w(SY, "ve", qv)
                        sv = inc(SY.dma_start(out=dst_rr[m][:, a:b], in_=wqo[:, sl, 0:b - a]),
                                 "wqs", 16)
                        prev_sv = sv
                        toks[sl] = [("wqs", sv)]

        quant_weights(wg_d, wgi_d, wgsc, ICH, D)
        quant_weights(wu_d, wui_d, wusc, ICH, D)
        quant_weights(wd_d, wdi_d, wdsc, DCH, IFF, dequant=True)
        wq_done = dict(CN)
        if stop_after == "wq":
            barrier()
            return _finish()
        VE.tensor_scalar(scg[:, :], wgsc[:, :], sx_bc[:, 0:1], None, OP.mult, OP.bypass)
        VE.tensor_scalar(scu[:, :], wusc[:, :], sx_bc[:, 0:1], None, OP.mult, OP.bypass)
        scgu_v = vemark()

        xnbf_mat = xnbf_d
        flash_idx = 0
        xnf_tok = []
        qk_ps = [ps_avA, ps_avB]
        qk_tok = [[], []]
        rs_ps = [ps_xA, ps_xB]
        rs_tok = [[("ve", ex1_done["ve"])], []]
        qraw_tok = [[], []]
        pt_tok = [[], [], [], []]
        av_tok = [[], []]
        sx_tok = [[], []]
        at_tok = [[], []]
        prev_seq = None
        for s_i in range(NSEQ):
            t0 = s_i * S
            if prev_seq is not None:
                w(VE, "te", prev_seq["te"])
                w(VE, "ac", prev_seq["ac"])
            for (s, v) in xnf_tok:
                SC.wait_ge(sems[s], v)
            xnf_tok = []
            for s_ in ("sp16_0", "sp16_1"):
                SC.wait_ge(sems[s_], p1_done.get(s_, CN[s_]))
            for c in range(DCH):
                inc(SC.dma_start_transpose(xnf[:, c, :],
                                           xnbf_mat[t0:t0 + S, 128 * c:128 * c + 128]),
                    "tr", 16)
            xnf_ld = CN["tr"]

            wst_tok = [[], []]
            pend = None

            def rope_flush():
                nonlocal pend
                if pend is None:
                    return
                pi, dst, dch, cols, ct, st_, ev = pend
                pend = None
                w(TE, "ac", ev)
                rp = rs_ps[pi]
                for (s, v) in rs_tok[pi]:
                    TE.wait_ge(sems[s], v)
                rs_tok[pi] = []
                rv = inc(TE.matmul(rp[:, 0:512], permt[:, :], qraw[:, pi, :],
                                   start=True, stop=True), "te")
                w(VE, "te", rv)
                VE.tensor_copy(rsw[:, pi, :], rp[:, 0:512])
                rcv = vemark()
                rs_tok[pi].append(("ve", rcv))
                VE.tensor_tensor(tmpA[:, :], qraw[:, pi, :], ct[:, cols], OP.mult)
                VE.tensor_tensor(tmpC[:, :], rsw[:, pi, :], st_[:, cols], OP.mult)
                VE.tensor_tensor(dst[:, dch, cols], tmpA[:, :], tmpC[:, :], OP.add)
                dv = vsync()
                qk_tok[pi].append(("ve", dv))
                qraw_tok[pi].append(("ve", dv))

            for m in range(16):
                dst = qro if m < 8 else kro
                dch = m if m < 8 else m - 8
                ct, st_ = tcos, tsin
                msl = m % 2
                for (s, v) in wst_tok[msl]:
                    SC.wait_ge(sems[s], v)
                wst_tok[msl] = []
                wtv = inc(SC.dma_start_transpose(wqkvst[:, msl, :, :],
                                                 wkvbf_d[128 * m:128 * m + 128, :]),
                          "tr", 16)
                for tc in range(4):
                    pi = (m * 4 + tc) % 2
                    ps = qk_ps[pi]
                    w(TE, "tr", max(xnf_ld, wtv))
                    w(TE, "cst", ld_const2)
                    w(TE, "gp", bias_gp)
                    for (s, v) in qk_tok[pi]:
                        TE.wait_ge(sems[s], v)
                    qk_tok[pi] = []
                    for d in range(DCH):
                        TE.matmul(ps[:, 0:512], wqkvst[:, msl, d, :],
                                  xnf[:, d, 512 * tc:512 * tc + 512],
                                  start=(d == 0), stop=False)
                    mv = inc(TE.matmul(ps[:, 0:512], bhlqk[0:2, 128 * m:128 * m + 128],
                                       ones2[0:2, 0:512], start=False, stop=True), "te")
                    w(SC, "te", mv)
                    for (s, v) in qraw_tok[pi]:
                        SC.wait_ge(sems[s], v)
                    qraw_tok[pi] = []
                    ev = inc(SC.activation(qraw[:, pi, :], ps[:, 0:512], ACT.Copy), "ac")
                    qk_tok[pi].append(("ac", ev))
                    if tc == 3:
                        wst_tok[msl].append(("te", CN["te"]))
                    rope_flush()
                    pend = (pi, dst, dch, slice(512 * tc, 512 * tc + 512), ct, st_, ev)
            rope_flush()
            rope_done_v = CN["ve"]

            w(VE, "ve", wvp_v)
            vcnt = 0
            for tt in range(16):
                for (c0, c1) in ((0, 512), (512, 1024), (1024, 1040)):
                    pi = vcnt % 2
                    vcnt += 1
                    ps = qk_ps[pi]
                    for (s, v) in qk_tok[pi]:
                        TE.wait_ge(sems[s], v)
                    qk_tok[pi] = []
                    w(TE, "ve", wvp_v)
                    for d in range(DCH):
                        TE.matmul(ps[:, 0:c1 - c0], xnf[:, d, 128 * tt:128 * tt + 128],
                                  wvp[:, d, c0:c1], start=(d == 0), stop=False)
                    mv = inc(TE.matmul(ps[:, 0:c1 - c0], ones2[0:2, 0:128],
                                       vbp[0:2, c0:c1], start=False, stop=True), "te")
                    w(SC, "te", mv)
                    ev = inc(SC.activation(vt[:, tt, c0:c1], ps[:, 0:c1 - c0], ACT.Copy), "ac")
                    qk_tok[pi].append(("ac", ev))
            v_done = CN["ac"]

            sq_te = 0
            for (src, dstn) in ((qro, nqrow), (kro, nkrow)):
                for c in range(DCH):
                    w(VE, "ve", rope_done_v)
                    w(VE, "te", sq_te)
                    sqbf = bass.AP(PT, 0, [[4096, 128], [1, 2048]])
                    VE.tensor_tensor(sqbf, src[:, c, :], src[:, c, :], OP.mult)
                    sqv = vsync()
                    w(TE, "ve", sqv)
                    for tc in range(4):
                        for (s, v) in sx_tok[0]:
                            TE.wait_ge(sems[s], v)
                        sx_tok[0] = []
                        sq_sl = bass.AP(PT, 512 * tc, [[4096, 128], [1, 512]])
                        TE.matmul(ps_sA[0:1, 0:512], ind1e[:, 0:1],
                                  sq_sl, start=True, stop=True)
                        nv = inc(TE.matmul(ps_sA[0:1, 512:1024], ind1o[:, 0:1],
                                           sq_sl, start=True, stop=True), "te")
                        sq_te = nv
                        w(VE, "te", nv)
                        VE.tensor_reduce(nrm8[0:1, tc:tc + 1], ps_sA[0:1, 0:512], AX, OP.max)
                        VE.tensor_reduce(nrm8[0:1, 4 + tc:5 + tc], ps_sA[0:1, 512:1024], AX, OP.max)
                        rv = vemark()
                        sx_tok[0].append(("ve", rv))
                    VE.drain()
                    VE.tensor_reduce(dstn[0:1, 2 * c:2 * c + 1], nrm8[0:1, 0:4], AX, OP.max)
                    VE.tensor_reduce(dstn[0:1, 2 * c + 1:2 * c + 2], nrm8[0:1, 4:8], AX, OP.max)
                    VE.drain()
            VE.tensor_tensor(brow[:, :], nqrow[:, :], nkrow[:, :], OP.mult)
            nbv = vemark()
            w(SC, "ve", nbv)
            nsq = inc(SC.activation(brow[:, :], brow[:, :], ACT.Sqrt, bias=0.0), "ac")
            w(VE, "ac", nsq)
            VE.tensor_scalar(brow[:, :], brow[:, :], -0.125, None, OP.mult, OP.bypass)
            VE.drain()
            VE.tensor_copy(browb[:, :], brow[:, :])
            bbv = vsync()
            w(TE, "ve", bbv)
            for (s, v) in rs_tok[0]:
                TE.wait_ge(sems[s], v)
            rs_tok[0] = []
            nbm = inc(TE.matmul(ps_xA[0:128, 0:16], onesPb[0:1, 0:128], browb[0:1, 0:16],
                                start=True, stop=True), "te")
            w(VE, "te", nbm)
            VE.tensor_copy(negbc[:, :], ps_xA[0:128, 0:16])
            nbcv = vsync()
            rs_tok[0].append(("ve", nbcv))

            if stop_after == "rope0" and s_i == 0:
                w(SY, "ve", nbcv)
                for (t_, d_) in ((xnf, dxnf_d), (qro, dqro_d), (kro, dkro_d), (vt, dvt_d)):
                    inc(SY.dma_start(out=d_[:, :, :], in_=t_[:, :, :]), "sy", 16)
                inc(SY.dma_start(out=dnegb_d[:, :], in_=negbc[:, :]), "sy", 16)
                barrier()
                return _finish()
            w(SC, "ve", nbcv)
            ps_s = [ps_sA, ps_sB]
            pend_tails = []

            def tail_flush():
                while pend_tails:
                    ai, av, rxp, h_, ch_, r0_, r1_, qr_, t0_, asl, rhv = pend_tails.pop(0)
                    w(TE, "ve", rhv)
                    for (s, v) in rs_tok[ai]:
                        TE.wait_ge(sems[s], v)
                    rs_tok[ai] = []
                    rbm = inc(TE.matmul(rxp[0:64, 0:512], onesPb[64:65, 0:64],
                                        rh16[64:65, 512 * ai:512 * ai + 512],
                                        start=True, stop=True), "te")
                    w(VE, "te", rbm)
                    for (s, v) in at_tok[asl]:
                        VE.wait_ge(sems[s], v)
                    at_tok[asl] = []
                    VE.tensor_copy(recf[0:64, 512 * ai:512 * ai + 512], rxp[0:64, 0:512])
                    rcv = vemark()
                    rs_tok[ai].append(("ve", rcv))
                    VE.tensor_tensor(atb[asl][0:64, 0:512], av[0:64, 0:512],
                                     recf[0:64, 512 * ai:512 * ai + 512], OP.mult)
                    ntv = vsync()
                    av_tok[ai].append(("ve", ntv))
                    ab_s = dsem(f"atb{asl}", "sy")
                    w(SY, "ve", ntv)
                    spv = inc(SY.dma_start(out=attnb_d[ch_, r0_:r1_, t0_ + 512 * qr_:t0_ + 512 * qr_ + 512],
                                           in_=atb[asl][0:64, 0:512]), ab_s, 16)
                    at_tok[asl].append((ab_s, spv))

            class U:
                pass

            for chp in range(8):
                for qr in range(4):
                    qc = slice(512 * qr, 512 * qr + 512)
                    kts = list(range(4 * qr + 4))
                    groups = [kts[i:i + 2] for i in range(0, len(kts), 2)]
                    G = len(groups)
                    units = []
                    for par in range(2):
                        u = U()
                        u.h = 2 * chp + par
                        u.par = par
                        u.r0, u.r1 = 64 * par, 64 * par + 64
                        u.ai = par
                        u.av = qk_ps[par]
                        u.rxp = rs_ps[par]
                        u.sps = ps_s[par]
                        u.first_av = True
                        u.smv = [None] * G
                        u.epv = [None] * G
                        u.mkv = [None] * G
                        units.append(u)

                    def emit_scores(u, g):
                        grp = groups[g]
                        for (s, v) in sx_tok[u.par]:
                            TE.wait_ge(sems[s], v)
                        sx_tok[u.par] = []
                        w(TE, "ve", nbcv)
                        for j, kt in enumerate(grp):
                            mm_ = TE.matmul(u.sps[:, 512 * j:512 * j + 512],
                                            kro[u.r0:u.r1, chp, 128 * kt:128 * kt + 128],
                                            qro[u.r0:u.r1, chp, qc], start=True, stop=True)
                            if j == len(grp) - 1:
                                u.smv[g] = inc(mm_, "te")

                    def emit_exp(u, g):
                        grp = groups[g]
                        w(SC, "te", u.smv[g])
                        for (s, v) in pt_tok[2 * u.par + (g % 2)]:
                            SC.wait_ge(sems[s], v)
                        pt_tok[2 * u.par + (g % 2)] = []
                        pt = bass.AP(PT, (2 * u.par + (g % 2)) * 1024, [[4096, 128], [1, 512 * len(grp)]])
                        u.epv[g] = inc(SC.activation(pt, u.sps[:, 0:512 * len(grp)], ACT.Exp,
                                                     bias=negbc[:, u.h:u.h + 1], scale=0.125), "ac")
                        sx_tok[u.par].append(("ac", u.epv[g]))
                        need_ve = False
                        w(VE, "ac", u.epv[g])
                        base = (2 * u.par + (g % 2)) * 1024
                        for j, kt in enumerate(grp):
                            ja = kt - 4 * qr
                            if ja >= 0:
                                need_ve = True
                                if ja > 0:
                                    VE.memset(bass.AP(PT, base + 512 * j, [[4096, 128], [1, 128 * ja]]), 0.0)
                                mp = bass.AP(PT, base + 512 * j + 128 * ja, [[4096, 128], [1, 128]])
                                VE.tensor_tensor(mp, mp, trit[:, :], OP.mult)
                        u.mkv[g] = vsync() if need_ve else None

                    def emit_av(u, g):
                        grp = groups[g]
                        w(TE, "ac", u.epv[g])
                        if u.mkv[g] is not None:
                            w(TE, "ve", u.mkv[g])
                        w(TE, "ac", v_done)
                        if u.first_av:
                            for (s, v) in av_tok[u.ai]:
                                TE.wait_ge(sems[s], v)
                            av_tok[u.ai] = []
                        base = (2 * u.par + (g % 2)) * 1024
                        for j, kt in enumerate(grp):
                            last = (g == G - 1) and (j == len(grp) - 1)
                            mm_ = TE.matmul(u.av[0:65, 0:512], vt[:, kt, 65 * u.h:65 * u.h + 65],
                                            bass.AP(PT, base + 512 * j, [[4096, 128], [1, 512]]),
                                            start=u.first_av, stop=last)
                            u.first_av = False
                            if j == len(grp) - 1:
                                avv = inc(mm_, "te")
                        pt_tok[2 * u.par + (g % 2)].append(("te", avv))
                        u.avv = avv

                    uA, uB = units
                    emit_scores(uA, 0)
                    emit_scores(uB, 0)
                    tail_flush()
                    for g in range(G):
                        emit_exp(uA, g)
                        if g + 1 < G:
                            emit_scores(uA, g + 1)
                        emit_exp(uB, g)
                        emit_av(uA, g)
                        if g + 1 < G:
                            emit_scores(uB, g + 1)
                        emit_av(uB, g)
                    for u in units:
                        w(VE, "te", u.avv)
                        VE.reciprocal(recf[64:65, 512 * u.par:512 * u.par + 512],
                                      u.av[64:65, 0:512])
                        VE.drain()
                        VE.tensor_copy(rh16[64:65, 512 * u.par:512 * u.par + 512],
                                       recf[64:65, 512 * u.par:512 * u.par + 512])
                        rhv = vsync()
                        asl = u.par
                        pend_tails.append((u.ai, u.av, u.rxp, u.h, chp, u.r0, u.r1,
                                           qr, t0, asl, rhv))
                        flash_idx += 1
            tail_flush()
            qk_tok[0] += av_tok[0]
            qk_tok[1] += av_tok[1]
            av_tok = [[], []]
            xnf_tok = [("te", CN["te"])]
            prev_seq = dict(CN)
        p2_done = dict(CN)
        barrier()
        if stop_after == "p2":
            return _finish()

    late = contextlib.ExitStack()
    wpjf = late.enter_context(nc.sbuf_tensor("wpjf", [128, DCH, D], BF16))
    w(SC, "gpc", gp_casts["gpc"])
    for c in range(DCH):
        inc(SC.dma_start_transpose(wpjf[:, c, :], wpjbf_d[:, 128 * c:128 * c + 128]),
            "tr", 16)
    wpjf_tr = CN["tr"]
    with contextlib.ExitStack() as p25:
        def sb25(name, shape, dtype):
            return p25.enter_context(nc.sbuf_tensor(name, shape, dtype))
        xqf = sb25("xqf", [128, DCH, S], BF16)
        wgst = sb25("wgst", [128, 2, DCH, 128], BF16)
        wust = sb25("wust", [128, 2, DCH, 128], BF16)
        qx32 = sb25("qx32", [128, 2, 2048], F32)
        xq16 = sb25("xq16", [128, 2, 2048], BF16)
        gsb = sb25("gsb", [128, 2, 512], F32)
        silb = sb25("silb", [128, 2, 512], F32)
        usb = sb25("usb", [128, 2, 512], F32)
        hbuf = sb25("hbuf", [128, 2, 2048], F32)

        xn32_rr2 = xn32_d.ap().rearrange("(b m p) d -> b p m d", p=128, m=2)
        xqbf_rr = xqbf_d.ap().rearrange("(b m p) d -> b p m d", p=128, m=2)
        toks = [[], []]
        for tb in range(32):
            sl = tb % 2
            for (s, v) in toks[sl]:
                SY.wait_ge(sems[s], v)
            toks[sl] = []
            qx_s = dsem(f"qx{sl}", "sy")
            lv = inc(SY.dma_start(out=bass.AP(qx32, sl * 2048, [[4096, 128], [1024, 2], [1, 1024]]),
                                  in_=xn32_rr2[tb]), qx_s, 16)
            w(VE, qx_s, lv)
            VE.tensor_scalar(qx32[:, sl, :], qx32[:, sl, :], rsx_bc[:, 0:1], magic[:, 0:1],
                             OP.mult, OP.add)
            VE.tensor_scalar(qx32[:, sl, :], qx32[:, sl, :], magic[:, 0:1], QMAX,
                             OP.subtract, OP.min)
            VE.tensor_scalar(xq16[:, sl, :], qx32[:, sl, :], -QMAX - 1.0, None,
                             OP.max, OP.bypass)
            xqs_s = dsem(f"xqs{sl}", "sy")
            qv = vsync()
            w(SY, "ve", qv)
            sv = inc(SY.dma_start(out=xqbf_rr[tb],
                                  in_=bass.AP(xq16, sl * 2048, [[4096, 128], [1024, 2], [1, 1024]])),
                     xqs_s, 16)
            toks[sl] = [(xqs_s, sv), ("ve", qv)]
        xq_sp = (CN["xqs0"], CN["xqs1"])

        gu_ps = [ps_avA, ps_xA, ps_avB, ps_xB]
        gu_tok = [[], [], [], []]
        wgu_tok = [[], []]
        hb_tok = [[], []]
        xqf_tok = []
        hm_col = 0
        for b4 in range(4):
            tb0 = b4 * S
            for (s, v) in xqf_tok:
                SC.wait_ge(sems[s], v)
            xqf_tok = []
            SC.wait_ge(sems["xqs0"], xq_sp[0])
            SC.wait_ge(sems["xqs1"], xq_sp[1])
            for c in range(DCH):
                inc(SC.dma_start_transpose(xqf[:, c, :],
                                           xqbf_d[tb0:tb0 + S, 128 * c:128 * c + 128]),
                    "tr", 16)
            xqf_ld = CN["tr"]
            def gu_fetch(m):
                msl = m % 2
                for (s, v) in wgu_tok[msl]:
                    SC.wait_ge(sems[s], v)
                wgu_tok[msl] = []
                w(SC, "wqs", wq_done["wqs"])
                inc(SC.dma_start_transpose(wgst[:, msl, :, :],
                                           wgi_d[128 * m:128 * m + 128, :]), "tr", 16)
                return inc(SC.dma_start_transpose(wust[:, msl, :, :],
                                                  wui_d[128 * m:128 * m + 128, :]), "tr", 16)

            if b4 == 0:
                wtv_next = gu_fetch(0)
            for m in range(ICH):
                hsl = (b4 * ICH + m) % 2
                msl = m % 2
                wtv = wtv_next
                if m + 1 < ICH:
                    wtv_next = gu_fetch(m + 1)
                elif b4 < 3:
                    wtv_next = gu_fetch(0)
                for (s, v) in hb_tok[hsl]:
                    VE.wait_ge(sems[s], v)
                hb_tok[hsl] = []
                for tc in range(4):
                    gi = (2 * tc) % 4
                    ui = gi + 1
                    for (s, v) in gu_tok[gi]:
                        TE.wait_ge(sems[s], v)
                    gu_tok[gi] = []
                    for (s, v) in gu_tok[ui]:
                        TE.wait_ge(sems[s], v)
                    gu_tok[ui] = []
                    w(TE, "tr", max(xqf_ld, wtv))
                    cols = slice(512 * tc, 512 * tc + 512)
                    gv = uv = None
                    for d in range(DCH):
                        mm_ = TE.matmul(gu_ps[gi][:, 0:512], wgst[:, msl, d, :],
                                        xqf[:, d, cols], start=(d == 0), stop=(d == DCH - 1))
                        if d == DCH - 1:
                            gv = inc(mm_, "te")
                    for d in range(DCH):
                        mm_ = TE.matmul(gu_ps[ui][:, 0:512], wust[:, msl, d, :],
                                        xqf[:, d, cols], start=(d == 0), stop=(d == DCH - 1))
                        if d == DCH - 1:
                            uv = inc(mm_, "te")
                    esl = tc % 2
                    w(VE, "te", gv)
                    w(VE, "ve", scgu_v)
                    VE.tensor_scalar(gsb[:, esl, :], gu_ps[gi][:, 0:512], scg[:, m:m + 1],
                                     None, OP.mult, OP.bypass)
                    gsv = vsync()
                    gu_tok[gi].append(("ve", gsv))
                    w(SC, "ve", gsv)
                    slv = inc(SC.activation(silb[:, esl, :], gsb[:, esl, :], ACT.Silu), "ac")
                    w(VE, "te", uv)
                    VE.tensor_scalar(usb[:, esl, :], gu_ps[ui][:, 0:512], scu[:, m:m + 1],
                                     None, OP.mult, OP.bypass)
                    usv = vemark()
                    gu_tok[ui].append(("ve", usv))
                    w(VE, "ac", slv)
                    VE.tensor_tensor(hbuf[:, hsl, cols], silb[:, esl, :], usb[:, esl, :],
                                     OP.mult)
                    hv = vemark()
                VE.tensor_reduce(hmg[:, hm_col:hm_col + 1], hbuf[:, hsl, :], AX, OP.max,
                                 apply_absolute_value=True)
                hmv = vsync()
                hm_col += 1
                hb_s = dsem(f"hb{hsl}", "sy")
                w(SY, "ve", hmv)
                hs = inc(SY.dma_start(out=h32_d[m, :, tb0:tb0 + S], in_=hbuf[:, hsl, :]),
                         hb_s, 16)
                hb_tok[hsl] = [(hb_s, hs)]
                wgu_tok[msl].append(("te", CN["te"]))
            xqf_tok = [("te", CN["te"])]
        p25_done = dict(CN)
        barrier()
        if stop_after == "p25":
            return _finish()

    VE.tensor_reduce(onetmp[:, :], hmg[:, :], AX, OP.max)
    VE.drain()
    partition_max_to_scalar(onetmp[:, 0:1], tiny[:, 8:9])
    v0 = vemark()
    w(GP, "ve", v0)
    inc(GP.dma_start(out=cc2i_d[:, :], in_=tiny[:, 8:9]), "gpe", 16)
    w(GP, "gpe")
    ccv = inc(GP.collective_compute("AllReduce", OP.max,
                                    replica_groups=[list(range(NCORES))],
                                    ins=[cc2i_d.ap()], outs=[cc2o_d.ap()]), "cc")
    w(GP, "cc", ccv)
    g2 = inc(GP.dma_start(out=tiny[:, 9:10], in_=cc2o_d[:, :]), "gpe", 16)
    sgv = inc(SC.activation(tiny[:, 10:11], cl32[:, 1:2], ACT.Sigmoid), "ac")
    w(VE, "gpe", g2)
    w(VE, "ac", sgv)
    VE.tensor_tensor(tiny[:, 11:12], tiny[:, 9:10], tiny[:, 10:11], OP.mult)
    VE.drain()
    VE.tensor_scalar(tiny[:, 11:12], tiny[:, 11:12], 1.0 / QMAX, None, OP.mult, OP.bypass)
    VE.drain()
    VE.reciprocal(tiny[:, 12:13], tiny[:, 11:12])
    VE.drain()
    bcast128(tiny[:, 11:12], sh_bc[:, 0:1], 4)
    bcast128(tiny[:, 12:13], rsh_bc[:, 0:1], 6)

    with contextlib.ExitStack() as p3:
        def sb3(name, shape, dtype):
            return p3.enter_context(nc.sbuf_tensor(name, shape, dtype))
        wdf = sb3("wdf", [128, ICH, D], BF16)
        inth = sb3("inth", [128, 2, ICH, 512], BF16)
        atst = sb3("atst", [128, 2, DCH, 512], BF16)
        hld = sb3("hld", [128, 2, 1024], F32)
        xb3 = sb3("xb3", [128, 2, 4096], F32)
        outb = sb3("outb", [128, 2, 4096], F32)

        w(SC, "wqs", wq_done["wqs"])
        for i in range(ICH):
            inc(SC.dma_start_transpose(wdf[:, i, :], wdi_d[:, 128 * i:128 * i + 128]),
                "tr", 16)
        wdf_tr = CN["tr"]

        x_rr3 = x_d.ap().rearrange("(tc m p) d -> tc p m d", p=128, m=4)
        out_rr = out_d.ap().rearrange("(tc m p) d -> tc p m d", p=128, m=4)
        tc_tok = [[], []]
        hld_tok = [[], []]
        ps3 = [(ps_sA, ps_avA, ps_xA), (ps_sB, ps_avB, ps_xB)]
        ps3_tok = [[], []]
        ob_tok = [[], []]
        stage = {}

        def p3_load(tc):
            sl = tc % 2
            for (s, v) in tc_tok[sl]:
                SY.wait_ge(sems[s], v)
            tc_tok[sl] = []
            ats_s = dsem(f"ats{sl}", "sy")
            xb_s = dsem(f"xb{sl}", "sy")
            for c in range(DCH):
                inc(SY.dma_start(out=atst[:, sl, c, :],
                                 in_=attnb_d[c, :, 512 * tc:512 * tc + 512]), ats_s, 16)
            at_ld = CN[ats_s]
            xl = inc(SY.dma_start(out=bass.AP(xb3, sl * 4096, [[8192, 128], [1024, 4], [1, 1024]]),
                                  in_=x_rr3[tc]), xb_s, 16)
            for i2 in range(0, ICH, 2):
                npair = 2 if i2 + 1 < ICH else 1
                hsl = (i2 // 2) % 2
                for (s, v) in hld_tok[hsl]:
                    SY.wait_ge(sems[s], v)
                hld_tok[hsl] = []
                hl_s = dsem(f"hl{hsl}", "sy")
                lv = inc(SY.dma_start(out=bass.AP(hld, hsl * 1024, [[2048, 128], [1, 512 * npair]]),
                                      in_=bass.AP(h32_d, i2 * 128 * T + 512 * tc,
                                                  [[T, 128], [128 * T, npair], [1, 512]])),
                         hl_s, 16)
                w(VE, hl_s, lv)
                fd = 512 * npair
                hap = bass.AP(hld, hsl * 1024, [[2048, 128], [1, fd]])
                iap = bass.AP(inth, sl * (ICH * 512) + i2 * 512,
                              [[2 * ICH * 512, 128], [1, fd]])
                VE.tensor_scalar(hap, hap, rsh_bc[:, 0:1],
                                 magic[:, 0:1], OP.mult, OP.add)
                VE.tensor_scalar(hap, hap, magic[:, 0:1],
                                 QMAX, OP.subtract, OP.min)
                VE.tensor_scalar(iap, hap, -QMAX - 1.0,
                                 None, OP.max, OP.bypass)
                qv = vsync()
                hld_tok[hsl] = [("ve", qv)]
                tc_tok[sl].append(("ve", qv))
            stage[tc] = (CN["ve"], at_ld, xl, ats_s, xb_s)

        def p3_compute(tc):
            sl = tc % 2
            hq_done, at_ld, xl, ats_s, xb_s = stage.pop(tc)
            for tt in range(4):
                pp = (tc * 4 + tt) % 2
                prj, dn0, dn1 = ps3[pp]
                for (s, v) in ps3_tok[pp]:
                    TE.wait_ge(sems[s], v)
                ps3_tok[pp] = []
                w(TE, ats_s, at_ld)
                w(TE, "ve", hq_done)
                w(TE, "tr", wdf_tr)
                tsl = slice(128 * tt, 128 * tt + 128)
                for cc_i, cols in enumerate((slice(0, 512), slice(512, 1024))):
                    for d in range(DCH):
                        TE.matmul(prj[:, cols], atst[:, sl, d, tsl], wpjf[:, d, cols],
                                  start=(d == 0), stop=False)
                    TE.matmul(prj[:, cols], ones2[0:2, 0:128], pbhl[0:2, cols],
                              start=False, stop=True)
                    dps = dn0 if cc_i == 0 else dn1
                    for i in range(ICH):
                        TE.matmul(dps[:, 0:512], inth[:, sl, i, tsl], wdf[:, i, cols],
                                  start=(i == 0), stop=(i == ICH - 1))
                mv = temark()
                w(VE, "te", mv)
                w(VE, xb_s, xl)
                for (s, v) in (ob_tok[sl] if tt == 0 else []):
                    VE.wait_ge(sems[s], v)
                if tt == 0:
                    ob_tok[sl] = []
                for cc_i in range(2):
                    dps = (dn0, dn1)[cc_i]
                    cols = slice(512 * cc_i, 512 * cc_i + 512)
                    ocol = slice(1024 * tt + 512 * cc_i, 1024 * tt + 512 * cc_i + 512)
                    VE.tensor_scalar(outb[:, sl, ocol], dps[:, 0:512], sh_bc[:, 0:1], None,
                                     OP.mult, OP.bypass)
                    VE.tensor_tensor(outb[:, sl, ocol], outb[:, sl, ocol], prj[:, cols], OP.add)
                    VE.tensor_tensor(outb[:, sl, ocol], outb[:, sl, ocol],
                                     xb3[:, sl, ocol], OP.add)
                ev = vsync()
                ps3_tok[pp].append(("ve", ev))
            ov_s = dsem(f"ov{sl}", "sy")
            w(SY, "ve", CN["ve"])
            ov = inc(SY.dma_start(out=out_rr[tc],
                                  in_=bass.AP(outb, sl * 4096, [[8192, 128], [1024, 4], [1, 1024]])),
                     ov_s, 16)
            tc_tok[sl].append((ov_s, ov))
            tc_tok[sl].append(("te", CN["te"]))
            tc_tok[sl].append(("ve", CN["ve"]))
            ob_tok[sl].append((ov_s, ov))

        p3_load(0)
        for tc in range(16):
            if tc + 1 < 16:
                p3_load(tc + 1)
            p3_compute(tc)
            if stop_after == "p3serial":
                barrier()
        SY.wait_ge(sems["ov0"], CN["ov0"])
        SY.wait_ge(sems["ov1"], CN["ov1"])
    late.close()

    return _finish()


_CACHE = {}


def kernel(L0_attn_x, L0_attn_qkv_w, L0_attn_qkv_b, L0_attn_proj_w, L0_attn_proj_b,
           L0_ffn_clip_x, L0_ffn_clip_down, L0_ffn_w_gate, L0_ffn_w_up, L0_ffn_w_down):
    consts = _host_consts()
    if "nc" not in _CACHE:
        _CACHE["nc"] = build_program()
    nc, es = _CACHE["nc"]
    x = np.ascontiguousarray(np.asarray(L0_attn_x, dtype=np.float32))
    shared = {
        "qkv_w": np.ascontiguousarray(np.asarray(L0_attn_qkv_w, np.float32)),
        "qkv_b": np.ascontiguousarray(np.asarray(L0_attn_qkv_b, np.float32)),
        "proj_w": np.ascontiguousarray(np.asarray(L0_attn_proj_w, np.float32)),
        "proj_b": np.ascontiguousarray(np.asarray(L0_attn_proj_b, np.float32)),
        "clip_x": np.ascontiguousarray(np.asarray(L0_ffn_clip_x, np.float32)),
        "clip_down": np.ascontiguousarray(np.asarray(L0_ffn_clip_down, np.float32)),
        "w_gate": np.ascontiguousarray(np.asarray(L0_ffn_w_gate, np.float32)),
        "w_up": np.ascontiguousarray(np.asarray(L0_ffn_w_up, np.float32)),
        "w_down": np.ascontiguousarray(np.asarray(L0_ffn_w_down, np.float32)),
        **consts,
    }
    in_maps = []
    for c in range(NCORES):
        im = dict(shared)
        im["x"] = np.ascontiguousarray(x[NSEQ * c:NSEQ * (c + 1)].reshape(T, D))
        in_maps.append(im)
    res = run_bass_kernel_spmd(nc, in_maps, core_ids=list(range(NCORES)))
    out = np.concatenate(
        [np.asarray(r["out"], dtype=np.float32).reshape(NSEQ, S, D) for r in res.results],
        axis=0)
    return out


# version 14
# speedup vs baseline: 1.0497x; 1.0045x over previous
import contextlib

import numpy as np
import ml_dtypes

import concourse.bass as bass
import concourse.mybir as mybir
from concourse.alu_op_type import AluOpType as OP
from concourse.bass_utils import run_bass_kernel_spmd

F32 = mybir.dt.float32
BF16 = mybir.dt.bfloat16
ACT = mybir.ActivationFunctionType
AX = mybir.AxisListType.X
BF = ml_dtypes.bfloat16

NCORES = 8
B, S, D = 32, 2048, 1024
NSEQ = B // NCORES
T = NSEQ * S
H, HD = 16, 64
IFF = 2816
ICH = IFF // 128
DCH = D // 128
MAGIC32 = 12582912.0
CBIG = float(np.float32(1.5 * 2**82))
QMAX = 127.0


def _host_consts():
    half = 32
    freqs = 1.0 / 10000.0 ** (np.arange(half, dtype=np.float64) / half)
    pos = np.arange(S, dtype=np.float64)
    ang = pos[None, :] * freqs[:, None]
    cos32 = np.cos(ang)
    sin32 = np.sin(ang)
    r = np.arange(128)
    i_of_r = r % 32
    sgn = np.where((r % 64) < 32, -1.0, 1.0)
    tcos = cos32[i_of_r, :]
    tsin = sin32[i_of_r, :] * sgn[:, None]
    perm = np.zeros((128, 128), dtype=np.float64)
    for blk in range(2):
        b0 = 64 * blk
        perm[b0:b0 + 32, b0 + 32:b0 + 64] = np.eye(32)
        perm[b0 + 32:b0 + 64, b0:b0 + 32] = np.eye(32)
    rr, cc = np.meshgrid(np.arange(128), np.arange(128), indexing="ij")
    tri01 = (rr <= cc).astype(np.float64)
    c = {
        "tcos": tcos, "tsin": tsin,
        "permc": perm, "tri01": tri01,
    }
    return {k: np.asarray(v, dtype=np.float32).astype(BF) for k, v in c.items()}


def build_program(dbg=(), stop_after=None):
    nc = bass.Bass("TRN2", target_bir_lowering=False, debug=False)
    es = contextlib.ExitStack()

    def dram(name, shape, dtype, kind=None):
        if name in dbg:
            kind = "ExternalOutput"
        if kind is None:
            return nc.dram_tensor(name, shape, dtype)
        return nc.dram_tensor(name, shape, dtype, kind=kind)

    x_d = dram("x", [T, D], F32, "ExternalInput")
    qkvw_d = dram("qkv_w", [3 * D, D], F32, "ExternalInput")
    qkvb_d = dram("qkv_b", [3 * D], F32, "ExternalInput")
    pjw_d = dram("proj_w", [D, D], F32, "ExternalInput")
    pjb_d = dram("proj_b", [D], F32, "ExternalInput")
    clx_d = dram("clip_x", [1], F32, "ExternalInput")
    cld_d = dram("clip_down", [1], F32, "ExternalInput")
    wg_d = dram("w_gate", [IFF, D], F32, "ExternalInput")
    wu_d = dram("w_up", [IFF, D], F32, "ExternalInput")
    wd_d = dram("w_down", [D, IFF], F32, "ExternalInput")
    tcos_d = dram("tcos", [128, S], BF16, "ExternalInput")
    tsin_d = dram("tsin", [128, S], BF16, "ExternalInput")
    perm_d = dram("permc", [128, 128], BF16, "ExternalInput")
    tri_d = dram("tri01", [128, 128], BF16, "ExternalInput")
    out_d = dram("out", [T, D], F32, "ExternalOutput")
    xn32_d = dram("xn32", [T, D], F32)
    xnbf_d = dram("xnbf", [T, D], BF16)
    xqbf_d = dram("xqbf", [T, D], BF16)
    wkvbf_d = dram("wkvbf", [3 * D, D], BF16)
    wpjbf_d = dram("wpjbf", [D, D], BF16)
    wgi_d = dram("wgi", [IFF, D], BF16)
    wui_d = dram("wui", [IFF, D], BF16)
    wdi_d = dram("wdi", [D, IFF], BF16)
    h32_d = dram("h32", [ICH, 128, T], F32)
    attnb_d = dram("attnb", [DCH, 128, T], BF16)
    cc1i_d = dram("cc1i", [1, 1], F32)
    cc1o_d = dram("cc1o", [1, 1], F32)
    cc2i_d = dram("cc2i", [1, 1], F32)
    cc2o_d = dram("cc2o", [1, 1], F32)
    red_d = dram("redsc", [128], F32)
    dxnf_d = dram("dxnf", [128, DCH, S], BF16)
    dqro_d = dram("dqro", [128, DCH, S], BF16)
    dkro_d = dram("dkro", [128, DCH, S], BF16)
    dvt_d = dram("dvt", [128, 16, 1040], BF16)
    dnegb_d = dram("dnegb", [128, 16], F32)
    dpt0_d = dram("dpt0", [128, 1024], BF16)
    dpt1_d = dram("dpt1", [128, 1024], BF16)
    dav_d = dram("dav", [128, 512], F32)
    dwdf_d = dram("dwdf", [128, 22, 1024], BF16)
    dwpj_d = dram("dwpj", [128, 8, 1024], BF16)
    dith_d = dram("dith", [128, 22, 512], BF16)
    datst_d = dram("datst", [128, 8, 512], BF16)
    dbias_d = dram("dbias", [2, 4096], BF16)
    dsc_d = dram("dsc", [128, 8], F32)
    drr_d = dram("drr", [65, 512], F32)
    drcf_d = dram("drcf", [64, 512], F32)
    datb_d = dram("datb", [128, 512], BF16)
    hlscr_d = dram("hlscr", [2, 2048], BF16)

    sems = {}
    for s in ("sy", "tr", "gp", "te", "ve", "ac", "cc", "bar"):
        sems[s] = es.enter_context(nc.semaphore("sem_" + s))
    CN = {k: 0 for k in sems}
    DMA_FAM = {"sy": ["sy"], "tr": ["tr"], "gp": ["gp"]}

    def dsem(name, fam):
        if name not in sems:
            sems[name] = es.enter_context(nc.semaphore("sem_" + name))
            CN[name] = 0
            DMA_FAM[fam].append(name)
        return name

    def sb(name, shape, dtype):
        return es.enter_context(nc.sbuf_tensor(name, shape, dtype))

    ones2 = sb("ones2", [2, 512], BF16)
    magic = sb("magict", [128, 1], F32)
    epsc = sb("epsc", [128, 1], F32)
    onetmp = sb("onetmp", [128, 1], F32)
    bhlqk = sb("bhlqk", [2, 2 * D], BF16)
    pbhl = sb("pbhl", [2, D], BF16)
    vbp = sb("vbp", [2, 1040], BF16)
    cl32 = sb("cl32", [1, 2], F32)
    wgsc = sb("wgsc", [128, ICH], F32)
    wusc = sb("wusc", [128, ICH], F32)
    wdsc = sb("wdsc", [128, DCH], F32)
    scg = sb("scg", [128, ICH], F32)
    scu = sb("scu", [128, ICH], F32)
    rsx_bc = sb("rsx_bc", [128, 1], F32)
    sx_bc = sb("sx_bc", [128, 1], F32)
    rsh_bc = sb("rsh_bc", [128, 1], F32)
    sh_bc = sb("sh_bc", [128, 1], F32)
    tiny = sb("tinyt", [1, 16], F32)
    tinyb = sb("tinyb", [1, 16], BF16)
    trow = sb("trowt", [1, 128], F32)
    xmg = sb("xmg", [128, 64], F32)
    hmg = sb("hmg", [128, 88], F32)
    onesPb = sb("onesPb", [128, 128], BF16)
    ind1e = sb("ind1e", [128, 1], BF16)
    ind1o = sb("ind1o", [128, 1], BF16)

    ps_sA = es.enter_context(nc.psum_tensor("ps_sA", [128, 1024], F32))
    ps_sB = es.enter_context(nc.psum_tensor("ps_sB", [128, 1024], F32))
    ps_avA = es.enter_context(nc.psum_tensor("ps_avA", [128, 512], F32))
    ps_avB = es.enter_context(nc.psum_tensor("ps_avB", [128, 512], F32))
    ps_xA = es.enter_context(nc.psum_tensor("ps_xA", [128, 512], F32))
    ps_xB = es.enter_context(nc.psum_tensor("ps_xB", [128, 512], F32))

    block = es.enter_context(nc.Block())
    REC = {k: [] for k in ("sy", "sc", "ve", "te", "gp")}

    class _Pend:
        __slots__ = ("incs",)

        def __init__(self):
            self.incs = []

        def then_inc(self, sem, n=1):
            self.incs.append((sem, n))
            return self

    class _Eng:
        def __init__(self, key):
            self._k = key

        def __getattr__(self, name):
            q = REC[self._k]

            def call(*a, **kw):
                p = _Pend()
                q.append((name, a, kw, p))
                return p
            return call

    SY, SC, VE, TE, GP = _Eng("sy"), _Eng("sc"), _Eng("ve"), _Eng("te"), _Eng("gp")

    def inc(instr, s, n=1):
        instr.then_inc(sems[s], n)
        CN[s] += n
        return CN[s]

    def mark(key, s):
        name = REC[key][-1][0]
        assert name not in ("wait_ge",), name

        REC[key][-1][3].incs.append((sems[s], 1))
        CN[s] += 1
        return CN[s]

    def vemark():
        return mark("ve", "ve")

    def vsync():
        VE.drain()
        return mark("ve", "ve")

    def temark():
        return mark("te", "te")

    def w(eng, s, val=None):
        eng.wait_ge(sems[s], CN[s] if val is None else val)

    def barrier():
        for s in DMA_FAM["sy"]:
            SY.wait_ge(sems[s], CN[s])
        inc(SY.nop(), "bar")
        for s in DMA_FAM["tr"]:
            SC.wait_ge(sems[s], CN[s])
        inc(SC.nop(), "bar")
        for s in DMA_FAM["gp"]:
            GP.wait_ge(sems[s], CN[s])
        inc(GP.nop(), "bar")
        inc(TE.nop(), "bar")
        inc(VE.nop(), "bar")
        tgt = CN["bar"]
        for eng in (TE, VE, SC, GP, SY):
            eng.wait_ge(sems["bar"], tgt)

    def _finish():
        def _replay(key):
            def f(eng):
                for (name, a, kw, p) in REC[key]:
                    if key in ("ve", "te", "sc") and name == "nop" and p.incs:
                        eng.drain()
                    r = getattr(eng, name)(*a, **kw)
                    for (sem, n) in p.incs:
                        r.then_inc(sem, n)
            return f
        block.sync(_replay("sy"))
        block.scalar(_replay("sc"))
        block.vector(_replay("ve"))
        block.tensor(_replay("te"))
        block.gpsimd(_replay("gp"))
        return nc, es

    dsem("gpc", "gp")
    inc(GP.dma_start(out=wkvbf_d[:, :], in_=qkvw_d[:, :]), "gpc", 16)
    inc(GP.dma_start(out=wpjbf_d[:, :], in_=pjw_d[:, :]), "gpc", 16)
    gp_casts = dict(CN)

    VE.memset(ones2[:, :], 1.0)
    VE.memset(epsc[:, :], 1e-5)
    VE.memset(onesPb[:, :], 1.0)
    VE.memset(ind1e[0:64, 0:1], 1.0)
    VE.memset(ind1e[64:128, 0:1], 0.0)
    VE.memset(ind1o[0:64, 0:1], 0.0)
    VE.memset(ind1o[64:128, 0:1], 1.0)
    VE.memset(onetmp[:, :], 1.0)
    VE.drain()
    VE.tensor_scalar(magic[:, :], onetmp[:, :], float(2.0 ** -30), 1.0,
                     OP.add, OP.subtract)
    VE.drain()
    VE.tensor_scalar(magic[:, :], magic[:, :], CBIG, MAGIC32,
                     OP.mult, OP.add)
    VE.drain()

    p0 = contextlib.ExitStack()
    qb32 = p0.enter_context(nc.sbuf_tensor("qb32", [1, 3 * D], F32))
    hl16 = p0.enter_context(nc.sbuf_tensor("hl16", [1, 3 * D], BF16))
    hl32 = p0.enter_context(nc.sbuf_tensor("hl32", [1, 3 * D], F32))
    lo32 = p0.enter_context(nc.sbuf_tensor("lo32", [1, 3 * D], F32))
    lo16 = p0.enter_context(nc.sbuf_tensor("lo16", [1, 3 * D], BF16))
    vb32 = p0.enter_context(nc.sbuf_tensor("vb32", [1, 1040], F32))

    dsem("bia", "sy")
    inc(SY.dma_start(out=qb32[:, :], in_=qkvb_d.ap().rearrange("(a d) -> a d", a=1)), "bia", 16)
    inc(SY.dma_start(out=cl32[:, 0:1], in_=clx_d.ap().rearrange("(a d) -> a d", a=1)), "bia", 16)
    inc(SY.dma_start(out=cl32[:, 1:2], in_=cld_d.ap().rearrange("(a d) -> a d", a=1)), "bia", 16)
    ld_bias = CN["bia"]

    def hilo_row(src_ap, n, dst, dst_off):
        w(VE, "gpb") if "gpb" in sems else None
        VE.tensor_copy(hl16[:, 0:n], src_ap)
        VE.drain()
        VE.tensor_copy(hl32[:, 0:n], hl16[:, 0:n])
        VE.drain()
        VE.tensor_tensor(lo32[:, 0:n], src_ap, hl32[:, 0:n], OP.subtract)
        VE.drain()
        VE.tensor_copy(lo16[:, 0:n], lo32[:, 0:n])
        v = vsync()
        dsem("gpb", "gp")
        w(GP, "ve", v)
        inc(GP.dma_start(out=hlscr_d[0:1, 0:n], in_=hl16[:, 0:n]), "gpb", 16)
        inc(GP.dma_start(out=hlscr_d[1:2, 0:n], in_=lo16[:, 0:n]), "gpb", 16)
        w(GP, "gpb")
        inc(GP.dma_start(out=dst[0:2, dst_off:dst_off + n], in_=hlscr_d[0:2, 0:n]), "gpb", 16)

    w(VE, "bia", ld_bias)
    hilo_row(qb32[:, 0:2 * D], 2 * D, bhlqk, 0)
    VE.memset(vb32[:, :], 0.0)
    VE.tensor_copy(bass.AP(vb32, 0, [[1040, 1], [65, 16], [1, 64]]),
                   bass.AP(qb32, 2 * D, [[3 * D, 1], [64, 16], [1, 64]]))
    VE.memset(bass.AP(vb32, 64, [[1040, 1], [65, 16], [1, 1]]), 1.0)
    VE.drain()
    hilo_row(vb32[:, 0:1040], 1040, vbp, 0)
    pb32 = qb32
    pv = CN["ve"]
    w(SY, "ve", pv)
    w(SY, "gp")
    inc(SY.dma_start(out=pb32[:, 0:D], in_=pjb_d.ap().rearrange("(a d) -> a d", a=1)), "bia", 16)
    w(VE, "bia")
    hilo_row(pb32[:, 0:D], D, pbhl, 0)

    bias_gp = CN["gp"]
    barrier()
    p0.close()

    with contextlib.ExitStack() as p1:
        def sb1(name, shape, dtype):
            return p1.enter_context(nc.sbuf_tensor(name, shape, dtype))
        xb = sb1("xb", [128, 3, 4096], F32)
        sqb = sb1("sqb", [128, 3, 4096], F32)
        xn32b = sb1("xn32b", [128, 3, 4096], F32)
        xn16b = sb1("xn16b", [128, 3, 4096], BF16)
        st_sum = sb1("st_sum", [128, 3, 4], F32)
        st_sq = sb1("st_sq", [128, 3, 4], F32)
        st_m = sb1("st_m", [128, 3, 4], F32)
        st_e = sb1("st_e", [128, 3, 4], F32)
        st_v = sb1("st_v", [128, 3, 4], F32)
        st_sd = sb1("st_sd", [128, 3, 4], F32)
        st_r = sb1("st_r", [128, 3, 4], F32)
        st_n = sb1("st_n", [128, 3, 4], F32)

        slot_tok = [[], [], []]
        spill_tok = [[], [], []]
        x_rr = x_d.ap().rearrange("(b m p) d -> b p m d", p=128, m=4)
        xn32_rr = xn32_d.ap().rearrange("(b m p) d -> b p m d", p=128, m=4)
        xnbf_rr = xnbf_d.ap().rearrange("(b m p) d -> b p m d", p=128, m=4)
        stv = {}

        def p1_a(b):
            sl = b % 3
            for (s, v) in slot_tok[sl]:
                SY.wait_ge(sems[s], v)
            slot_tok[sl] = []
            xs_s = dsem(f"x{sl}", "sy")
            ldv = inc(SY.dma_start(out=bass.AP(xb, sl * 4096, [[12288, 128], [1024, 4], [1, 1024]]),
                                   in_=x_rr[b]), xs_s, 16)
            w(VE, xs_s, ldv)
            VE.tensor_reduce(st_sum[:, sl, :],
                             bass.AP(xb, sl * 4096, [[12288, 128], [1024, 4], [1, 1024]]),
                             AX, OP.add)
            w(SC, xs_s, ldv)
            sqv = inc(SC.activation(sqb[:, sl, :], xb[:, sl, :], ACT.Square), "ac")
            w(VE, "ac", sqv)
            VE.tensor_reduce(st_sq[:, sl, :],
                             bass.AP(sqb, sl * 4096, [[12288, 128], [1024, 4], [1, 1024]]),
                             AX, OP.add)
            VE.drain()
            VE.tensor_scalar(st_m[:, sl, :], st_sum[:, sl, :], 1.0 / D, None, OP.mult, OP.bypass)
            VE.tensor_scalar(st_e[:, sl, :], st_sq[:, sl, :], 1.0 / D, None, OP.mult, OP.bypass)
            VE.drain()
            VE.tensor_tensor(st_v[:, sl, :], st_m[:, sl, :], st_m[:, sl, :], OP.mult)
            VE.drain()
            VE.tensor_tensor(st_v[:, sl, :], st_e[:, sl, :], st_v[:, sl, :], OP.subtract)
            vv = vsync()
            w(SC, "ve", vv)
            stv[b] = inc(SC.activation(st_sd[:, sl, :], st_v[:, sl, :], ACT.Sqrt,
                                       bias=epsc[:, 0:1]), "ac")

        def p1_b(b):
            sl = b % 3
            sdv = stv.pop(b)
            w(VE, "ac", sdv)
            VE.reciprocal(st_r[:, sl, :], st_sd[:, sl, :])
            VE.drain()
            VE.tensor_tensor(st_n[:, sl, :], st_m[:, sl, :], st_r[:, sl, :], OP.mult)
            VE.drain()
            VE.tensor_scalar(st_n[:, sl, :], st_n[:, sl, :], -1.0, None, OP.mult, OP.bypass)
            VE.drain()
            for (s, v) in spill_tok[sl]:
                VE.wait_ge(sems[s], v)
            spill_tok[sl] = []
            for mm in range(4):
                VE.tensor_scalar(xn32b[:, sl, 1024 * mm:1024 * mm + 1024],
                                 xb[:, sl, 1024 * mm:1024 * mm + 1024],
                                 st_r[:, sl, mm:mm + 1], st_n[:, sl, mm:mm + 1],
                                 OP.mult, OP.add)
            xnv = vsync()
            VE.tensor_reduce(xmg[:, 4 * b:4 * b + 4],
                             bass.AP(xn32b, sl * 4096, [[12288, 128], [1024, 4], [1, 1024]]),
                             AX, OP.max, apply_absolute_value=True)
            xmv = vemark()
            w(SC, "ve", xnv)
            cav = inc(SC.activation(xn16b[:, sl, :], xn32b[:, sl, :], ACT.Copy), "ac")
            sp32 = dsem(f"sp32_{sl}", "sy")
            sp16 = dsem(f"sp16_{sl}", "sy")
            w(SY, "ve", xnv)
            s1 = inc(SY.dma_start(out=xn32_rr[b],
                                  in_=bass.AP(xn32b, sl * 4096, [[12288, 128], [1024, 4], [1, 1024]])),
                     sp32, 16)
            w(SY, "ac", cav)
            s2 = inc(SY.dma_start(out=xnbf_rr[b],
                                  in_=bass.AP(xn16b, sl * 4096, [[12288, 128], [1024, 4], [1, 1024]])),
                     sp16, 16)
            slot_tok[sl] = [(sp16, s2), ("ve", xmv), ("ac", cav)]
            spill_tok[sl] = [(sp32, s1), (sp16, s2)]

        p1_a(0)
        p1_a(1)
        for b in range(16):
            if b + 2 < 16:
                p1_a(b + 2)
            p1_b(b)
        barrier()
        p1_done = dict(CN)
        if stop_after == "p1":
            barrier()
            return _finish()

    def partition_max_to_scalar(src_col, out11):
        dsem("gpe", "gp")
        v0 = vemark()
        w(GP, "ve", v0)
        inc(GP.dma_start(out=red_d[:], in_=src_col), "gpe", 16)
        w(GP, "gpe")
        inc(GP.dma_start(out=trow[:, :], in_=red_d.ap().rearrange("(a d) -> a d", a=1)), "gpe", 16)
        w(VE, "gpe")
        VE.tensor_reduce(out11, trow[:, :], AX, OP.max)
        VE.drain()

    def bcast128(src11_f32, dst_col, tmp_off):
        VE.tensor_copy(tinyb[:, tmp_off:tmp_off + 1], src11_f32)
        VE.drain()
        VE.tensor_copy(tiny[:, tmp_off:tmp_off + 1], tinyb[:, tmp_off:tmp_off + 1])
        VE.drain()
        VE.tensor_tensor(tiny[:, tmp_off + 1:tmp_off + 2], src11_f32,
                         tiny[:, tmp_off:tmp_off + 1], OP.subtract)
        VE.drain()
        VE.tensor_copy(tinyb[:, tmp_off + 1:tmp_off + 2], tiny[:, tmp_off + 1:tmp_off + 2])
        v1 = vsync()
        w(TE, "ve", v1)
        TE.matmul(ps_xA[0:128, 0:1], ones2[0:1, 0:128], tinyb[:, tmp_off:tmp_off + 1],
                  start=True, stop=False)
        t1 = inc(TE.matmul(ps_xA[0:128, 0:1], ones2[0:1, 0:128],
                           tinyb[:, tmp_off + 1:tmp_off + 2], start=False, stop=True), "te")
        w(VE, "te", t1)
        VE.tensor_copy(dst_col, ps_xA[0:128, 0:1])
        vd = vemark()
        return vd

    VE.tensor_reduce(onetmp[:, :], xmg[:, :], AX, OP.max)
    VE.drain()
    partition_max_to_scalar(onetmp[:, 0:1], tiny[:, 8:9])
    v0 = vemark()
    w(GP, "ve", v0)
    inc(GP.dma_start(out=cc1i_d[:, :], in_=tiny[:, 8:9]), "gpe", 16)
    w(GP, "gpe")
    ccv = inc(GP.collective_compute("AllReduce", OP.max,
                                    replica_groups=[list(range(NCORES))],
                                    ins=[cc1i_d.ap()], outs=[cc1o_d.ap()]), "cc")
    w(GP, "cc", ccv)
    g1 = inc(GP.dma_start(out=tiny[:, 9:10], in_=cc1o_d[:, :]), "gpe", 16)
    w(SC, "bia", ld_bias)
    sgv = inc(SC.activation(tiny[:, 10:11], cl32[:, 0:1], ACT.Sigmoid), "ac")
    w(VE, "gpe", g1)
    w(VE, "ac", sgv)
    VE.tensor_tensor(tiny[:, 11:12], tiny[:, 9:10], tiny[:, 10:11], OP.mult)
    VE.drain()
    VE.tensor_scalar(tiny[:, 11:12], tiny[:, 11:12], 1.0 / QMAX, None, OP.mult, OP.bypass)
    VE.drain()
    VE.reciprocal(tiny[:, 12:13], tiny[:, 11:12])
    VE.drain()
    bcast128(tiny[:, 11:12], sx_bc[:, 0:1], 0)
    bcast128(tiny[:, 12:13], rsx_bc[:, 0:1], 2)
    ex1_done = dict(CN)
    if stop_after == "ex1":
        barrier()
        return _finish()

    with contextlib.ExitStack() as p2:
        def sb2(name, shape, dtype):
            return p2.enter_context(nc.sbuf_tensor(name, shape, dtype))
        tcos = sb2("tcos_s", [128, S], BF16)
        tsin = sb2("tsin_s", [128, S], BF16)
        permt = sb2("permt", [128, 128], BF16)
        trit = sb2("trit", [128, 128], BF16)
        wqkvst = sb2("wqkvst", [128, 2, DCH, 128], BF16)
        wvp = sb2("wvp", [128, DCH, 1040], BF16)
        vstage = sb2("vstage", [128, D], BF16)
        xnf = sb2("xnf", [128, DCH, S], BF16)
        qro = sb2("qro", [128, DCH, S], BF16)
        kro = sb2("kro", [128, DCH, S], BF16)
        vt = sb2("vt", [128, 16, 1040], BF16)
        PT = sb2("PTt", [128, 4, 1024], BF16)
        qraw = sb2("qraw", [128, 2, 512], BF16)
        rsw = sb2("rsw", [128, 2, 512], BF16)
        tmpA = sb2("tmpA", [128, 512], BF16)
        tmpC = sb2("tmpC", [128, 512], BF16)
        nqrow = sb2("nqrow", [1, 16], F32)
        nkrow = sb2("nkrow", [1, 16], F32)
        nrm8 = sb2("nrm8", [1, 8], F32)
        brow = sb2("brow", [1, 16], F32)
        browb = sb2("browb", [1, 16], BF16)
        negbc = sb2("negbc", [128, 16], F32)
        recf = sb2("recf", [65, 1024], F32)
        rh16 = sb2("rh16", [65, 1024], BF16)
        atb = [sb2("atb0", [128, 512], BF16), sb2("atb1", [128, 512], BF16)]
        wqt = sb2("wqt", [128, 1, 1408], F32)
        wqo = sb2("wqo", [128, 1, 1408], BF16)
        wqr = sb2("wqr", [128, 1, 4], F32)
        nrm4 = sb2("nrm4", [2, 4], F32)

        dsem("cst", "sy")
        for t_, d_ in ((tcos, tcos_d), (tsin, tsin_d),
                       (permt, perm_d), (trit, tri_d)):
            inc(SY.dma_start(out=t_[:, :], in_=d_[:, :]), "cst", 16)
        ld_const2 = CN["cst"]

        w(SC, "gpc", gp_casts["gpc"])
        VE.memset(wvp[:, :, :], 0.0)
        wvp_ve0 = vemark()
        for c in range(DCH):
            w(SC, "ve", wvp_ve0)
            for (s, v) in [("ve", CN["ve"])] if c > 0 else []:
                SC.wait_ge(sems[s], v)
            tv = inc(SC.dma_start_transpose(vstage[:, :],
                                            wkvbf_d[2 * D:3 * D, 128 * c:128 * c + 128]),
                     "tr", 16)
            w(VE, "tr", tv)
            VE.tensor_copy(bass.AP(wvp, c * 1040, [[DCH * 1040, 128], [65, 16], [1, 64]]),
                           bass.AP(vstage, 0, [[D, 128], [64, 16], [1, 64]]))
            vsync()
        wvp_v = CN["ve"]

        dsem("wql", "sy")
        dsem("wqs", "sy")
        wq_toks = [[]]

        def quant_weights(src_d, dst_d, sc_tile, monly, width, dequant=False):
            nch = monly + 1
            toks = wq_toks
            halves = [(0, width)] if width <= 1408 else [(0, width // 2), (width // 2, width)]
            src_rr = src_d.ap().rearrange("(m p) d -> m p d", p=128)
            dst_rr = dst_d.ap().rearrange("(m p) d -> m p d", p=128)
            for m in (monly,):
                sl = 0
                lvs = []
                red_ve = 0
                for hi_, (a, b) in enumerate(halves):
                    for (s, v) in toks[sl]:
                        SY.wait_ge(sems[s], v)
                    toks[sl] = []
                    w(SY, "ve", red_ve)
                    lv = inc(SY.dma_start(out=wqt[:, sl, 0:b - a], in_=src_rr[m][:, a:b]),
                             "wql", 16)
                    lvs.append(lv)
                    w(VE, "wql", lv)
                    VE.tensor_reduce(wqr[:, sl, hi_:hi_ + 1], wqt[:, sl, 0:b - a], AX,
                                     OP.max, apply_absolute_value=True)
                    VE.drain()
                    red_ve = vemark()
                    if hi_ == len(halves) - 1:
                        VE.tensor_reduce(wqr[:, sl, 2:3], wqr[:, sl, 0:len(halves)], AX, OP.max)
                        VE.drain()
                        VE.tensor_scalar(sc_tile[:, m:m + 1], wqr[:, sl, 2:3], 1.0 / QMAX,
                                         None, OP.mult, OP.bypass)
                        VE.drain()
                        VE.reciprocal(wqr[:, sl, 3:4], sc_tile[:, m:m + 1])
                        VE.drain()
                if len(halves) == 1:
                    VE.tensor_scalar(wqt[:, sl, 0:width], wqt[:, sl, 0:width],
                                     wqr[:, sl, 3:4], magic[:, 0:1], OP.mult, OP.add)
                    VE.drain()
                    if dequant:
                        VE.tensor_scalar(wqt[:, sl, 0:width], wqt[:, sl, 0:width],
                                         magic[:, 0:1], sc_tile[:, m:m + 1],
                                         OP.subtract, OP.mult)
                        VE.drain()
                        VE.tensor_copy(wqo[:, sl, 0:width], wqt[:, sl, 0:width])
                    else:
                        VE.tensor_scalar(wqo[:, sl, 0:width], wqt[:, sl, 0:width],
                                         magic[:, 0:1], None, OP.subtract, OP.bypass)
                    qv = vsync()
                    w(SY, "ve", qv)
                    sv = inc(SY.dma_start(out=dst_rr[m], in_=wqo[:, sl, 0:width]), "wqs", 16)
                    toks[sl] = [("wqs", sv)]
                else:
                    prev_sv = None
                    for hi_ in (1, 0):
                        a, b = halves[hi_]
                        if hi_ == 0:
                            if prev_sv is not None:
                                SY.wait_ge(sems["wqs"], prev_sv)
                            lv = inc(SY.dma_start(out=wqt[:, sl, 0:b - a],
                                                  in_=src_rr[m][:, a:b]), "wql", 16)
                            w(VE, "wql", lv)
                        VE.tensor_scalar(wqt[:, sl, 0:b - a], wqt[:, sl, 0:b - a],
                                         wqr[:, sl, 3:4], magic[:, 0:1], OP.mult, OP.add)
                        VE.drain()
                        if dequant:
                            VE.tensor_scalar(wqt[:, sl, 0:b - a], wqt[:, sl, 0:b - a],
                                             magic[:, 0:1], sc_tile[:, m:m + 1],
                                             OP.subtract, OP.mult)
                            VE.drain()
                            VE.tensor_copy(wqo[:, sl, 0:b - a], wqt[:, sl, 0:b - a])
                        else:
                            VE.tensor_scalar(wqo[:, sl, 0:b - a], wqt[:, sl, 0:b - a],
                                             magic[:, 0:1], None, OP.subtract, OP.bypass)
                        qv = vsync()
                        w(SY, "ve", qv)
                        sv = inc(SY.dma_start(out=dst_rr[m][:, a:b], in_=wqo[:, sl, 0:b - a]),
                                 "wqs", 16)
                        prev_sv = sv
                        toks[sl] = [("wqs", sv)]

        wq_tasks = ([(wg_d, wgi_d, wgsc, m, D, False) for m in range(ICH)]
                    + [(wu_d, wui_d, wusc, m, D, False) for m in range(ICH)]
                    + [(wd_d, wdi_d, wdsc, m, IFF, True) for m in range(DCH)])


        def wq_emit(n):
            for _ in range(n):
                if not wq_tasks:
                    return
                sd, dd, sct, m, wdt, dq = wq_tasks.pop(0)
                quant_weights(sd, dd, sct, m, wdt, dequant=dq)

        xnbf_mat = xnbf_d
        flash_idx = 0
        xnf_tok = []
        qk_ps = [ps_avA, ps_avB]
        qk_tok = [[], []]
        rs_ps = [ps_xA, ps_xB]
        rs_tok = [[("ve", ex1_done["ve"])], []]
        qraw_tok = [[], []]
        pt_tok = [[], [], [], []]
        av_tok = [[], []]
        sx_tok = [[], []]
        at_tok = [[], []]
        prev_seq = None
        for s_i in range(NSEQ):
            t0 = s_i * S
            if prev_seq is not None:
                w(VE, "te", prev_seq["te"])
                w(VE, "ac", prev_seq["ac"])
            for (s, v) in xnf_tok:
                SC.wait_ge(sems[s], v)
            xnf_tok = []
            for s_ in ("sp16_0", "sp16_1"):
                SC.wait_ge(sems[s_], p1_done.get(s_, CN[s_]))
            for c in range(DCH):
                inc(SC.dma_start_transpose(xnf[:, c, :],
                                           xnbf_mat[t0:t0 + S, 128 * c:128 * c + 128]),
                    "tr", 16)
            xnf_ld = CN["tr"]

            wst_tok = [[], []]
            pend = None

            def rope_flush():
                nonlocal pend
                if pend is None:
                    return
                pi, dst, dch, cols, ct, st_, ev = pend
                pend = None
                w(TE, "ac", ev)
                rp = rs_ps[pi]
                for (s, v) in rs_tok[pi]:
                    TE.wait_ge(sems[s], v)
                rs_tok[pi] = []
                rv = inc(TE.matmul(rp[:, 0:512], permt[:, :], qraw[:, pi, :],
                                   start=True, stop=True), "te")
                w(VE, "te", rv)
                VE.tensor_copy(rsw[:, pi, :], rp[:, 0:512])
                rcv = vemark()
                rs_tok[pi].append(("ve", rcv))
                VE.tensor_tensor(tmpA[:, :], qraw[:, pi, :], ct[:, cols], OP.mult)
                VE.tensor_tensor(tmpC[:, :], rsw[:, pi, :], st_[:, cols], OP.mult)
                VE.tensor_tensor(dst[:, dch, cols], tmpA[:, :], tmpC[:, :], OP.add)
                dv = vsync()
                qk_tok[pi].append(("ve", dv))
                qraw_tok[pi].append(("ve", dv))

            for m in range(16):
                dst = qro if m < 8 else kro
                dch = m if m < 8 else m - 8
                ct, st_ = tcos, tsin
                msl = m % 2
                for (s, v) in wst_tok[msl]:
                    SC.wait_ge(sems[s], v)
                wst_tok[msl] = []
                wtv = inc(SC.dma_start_transpose(wqkvst[:, msl, :, :],
                                                 wkvbf_d[128 * m:128 * m + 128, :]),
                          "tr", 16)
                for tc in range(4):
                    pi = (m * 4 + tc) % 2
                    ps = qk_ps[pi]
                    w(TE, "tr", max(xnf_ld, wtv))
                    w(TE, "cst", ld_const2)
                    w(TE, "gp", bias_gp)
                    for (s, v) in qk_tok[pi]:
                        TE.wait_ge(sems[s], v)
                    qk_tok[pi] = []
                    for d in range(DCH):
                        TE.matmul(ps[:, 0:512], wqkvst[:, msl, d, :],
                                  xnf[:, d, 512 * tc:512 * tc + 512],
                                  start=(d == 0), stop=False)
                    mv = inc(TE.matmul(ps[:, 0:512], bhlqk[0:2, 128 * m:128 * m + 128],
                                       ones2[0:2, 0:512], start=False, stop=True), "te")
                    w(SC, "te", mv)
                    for (s, v) in qraw_tok[pi]:
                        SC.wait_ge(sems[s], v)
                    qraw_tok[pi] = []
                    ev = inc(SC.activation(qraw[:, pi, :], ps[:, 0:512], ACT.Copy), "ac")
                    qk_tok[pi].append(("ac", ev))
                    if tc == 3:
                        wst_tok[msl].append(("te", CN["te"]))
                    rope_flush()
                    pend = (pi, dst, dch, slice(512 * tc, 512 * tc + 512), ct, st_, ev)
            rope_flush()
            rope_done_v = CN["ve"]

            w(VE, "ve", wvp_v)
            vcnt = 0
            for tt in range(16):
                for (c0, c1) in ((0, 512), (512, 1024), (1024, 1040)):
                    pi = vcnt % 2
                    vcnt += 1
                    ps = qk_ps[pi]
                    for (s, v) in qk_tok[pi]:
                        TE.wait_ge(sems[s], v)
                    qk_tok[pi] = []
                    w(TE, "ve", wvp_v)
                    for d in range(DCH):
                        TE.matmul(ps[:, 0:c1 - c0], xnf[:, d, 128 * tt:128 * tt + 128],
                                  wvp[:, d, c0:c1], start=(d == 0), stop=False)
                    mv = inc(TE.matmul(ps[:, 0:c1 - c0], ones2[0:2, 0:128],
                                       vbp[0:2, c0:c1], start=False, stop=True), "te")
                    w(SC, "te", mv)
                    ev = inc(SC.activation(vt[:, tt, c0:c1], ps[:, 0:c1 - c0], ACT.Copy), "ac")
                    qk_tok[pi].append(("ac", ev))
            v_done = CN["ac"]

            sq_te = 0
            for (src, dstn) in ((qro, nqrow), (kro, nkrow)):
                for c in range(DCH):
                    w(VE, "ve", rope_done_v)
                    w(VE, "te", sq_te)
                    sqbf = bass.AP(PT, 0, [[4096, 128], [1, 2048]])
                    VE.tensor_tensor(sqbf, src[:, c, :], src[:, c, :], OP.mult)
                    sqv = vsync()
                    w(TE, "ve", sqv)
                    for tc in range(4):
                        for (s, v) in sx_tok[0]:
                            TE.wait_ge(sems[s], v)
                        sx_tok[0] = []
                        sq_sl = bass.AP(PT, 512 * tc, [[4096, 128], [1, 512]])
                        TE.matmul(ps_sA[0:1, 0:512], ind1e[:, 0:1],
                                  sq_sl, start=True, stop=True)
                        nv = inc(TE.matmul(ps_sA[0:1, 512:1024], ind1o[:, 0:1],
                                           sq_sl, start=True, stop=True), "te")
                        sq_te = nv
                        w(VE, "te", nv)
                        VE.tensor_reduce(nrm8[0:1, tc:tc + 1], ps_sA[0:1, 0:512], AX, OP.max)
                        VE.tensor_reduce(nrm8[0:1, 4 + tc:5 + tc], ps_sA[0:1, 512:1024], AX, OP.max)
                        rv = vemark()
                        sx_tok[0].append(("ve", rv))
                    VE.drain()
                    VE.tensor_reduce(dstn[0:1, 2 * c:2 * c + 1], nrm8[0:1, 0:4], AX, OP.max)
                    VE.tensor_reduce(dstn[0:1, 2 * c + 1:2 * c + 2], nrm8[0:1, 4:8], AX, OP.max)
                    VE.drain()
            VE.tensor_tensor(brow[:, :], nqrow[:, :], nkrow[:, :], OP.mult)
            nbv = vemark()
            w(SC, "ve", nbv)
            nsq = inc(SC.activation(brow[:, :], brow[:, :], ACT.Sqrt, bias=0.0), "ac")
            w(VE, "ac", nsq)
            VE.tensor_scalar(brow[:, :], brow[:, :], -0.125, None, OP.mult, OP.bypass)
            VE.drain()
            VE.tensor_copy(browb[:, :], brow[:, :])
            bbv = vsync()
            w(TE, "ve", bbv)
            for (s, v) in rs_tok[0]:
                TE.wait_ge(sems[s], v)
            rs_tok[0] = []
            nbm = inc(TE.matmul(ps_xA[0:128, 0:16], onesPb[0:1, 0:128], browb[0:1, 0:16],
                                start=True, stop=True), "te")
            w(VE, "te", nbm)
            VE.tensor_copy(negbc[:, :], ps_xA[0:128, 0:16])
            nbcv = vsync()
            rs_tok[0].append(("ve", nbcv))

            if stop_after == "rope0" and s_i == 0:
                w(SY, "ve", nbcv)
                for (t_, d_) in ((xnf, dxnf_d), (qro, dqro_d), (kro, dkro_d), (vt, dvt_d)):
                    inc(SY.dma_start(out=d_[:, :, :], in_=t_[:, :, :]), "sy", 16)
                inc(SY.dma_start(out=dnegb_d[:, :], in_=negbc[:, :]), "sy", 16)
                barrier()
                return _finish()
            w(SC, "ve", nbcv)
            ps_s = [ps_sA, ps_sB]
            pend_tails = []

            def tail_flush():
                while pend_tails:
                    ai, av, rxp, h_, ch_, r0_, r1_, qr_, t0_, asl, rhv = pend_tails.pop(0)
                    w(TE, "ve", rhv)
                    for (s, v) in rs_tok[ai]:
                        TE.wait_ge(sems[s], v)
                    rs_tok[ai] = []
                    rbm = inc(TE.matmul(rxp[0:64, 0:512], onesPb[64:65, 0:64],
                                        rh16[64:65, 512 * ai:512 * ai + 512],
                                        start=True, stop=True), "te")
                    w(VE, "te", rbm)
                    for (s, v) in at_tok[asl]:
                        VE.wait_ge(sems[s], v)
                    at_tok[asl] = []
                    VE.tensor_copy(recf[0:64, 512 * ai:512 * ai + 512], rxp[0:64, 0:512])
                    rcv = vemark()
                    rs_tok[ai].append(("ve", rcv))
                    VE.tensor_tensor(atb[asl][0:64, 0:512], av[0:64, 0:512],
                                     recf[0:64, 512 * ai:512 * ai + 512], OP.mult)
                    ntv = vsync()
                    av_tok[ai].append(("ve", ntv))
                    ab_s = dsem(f"atb{asl}", "sy")
                    w(SY, "ve", ntv)
                    spv = inc(SY.dma_start(out=attnb_d[ch_, r0_:r1_, t0_ + 512 * qr_:t0_ + 512 * qr_ + 512],
                                           in_=atb[asl][0:64, 0:512]), ab_s, 16)
                    at_tok[asl].append((ab_s, spv))

            class U:
                pass

            for chp in range(8):
                for qr in range(4):
                    qc = slice(512 * qr, 512 * qr + 512)
                    kts = list(range(4 * qr + 4))
                    groups = [kts[i:i + 2] for i in range(0, len(kts), 2)]
                    G = len(groups)
                    units = []
                    for par in range(2):
                        u = U()
                        u.h = 2 * chp + par
                        u.par = par
                        u.r0, u.r1 = 64 * par, 64 * par + 64
                        u.ai = par
                        u.av = qk_ps[par]
                        u.rxp = rs_ps[par]
                        u.sps = ps_s[par]
                        u.first_av = True
                        u.smv = [None] * G
                        u.epv = [None] * G
                        u.mkv = [None] * G
                        units.append(u)

                    def emit_scores(u, g):
                        grp = groups[g]
                        for (s, v) in sx_tok[u.par]:
                            TE.wait_ge(sems[s], v)
                        sx_tok[u.par] = []
                        w(TE, "ve", nbcv)
                        for j, kt in enumerate(grp):
                            mm_ = TE.matmul(u.sps[:, 512 * j:512 * j + 512],
                                            kro[u.r0:u.r1, chp, 128 * kt:128 * kt + 128],
                                            qro[u.r0:u.r1, chp, qc], start=True, stop=True)
                            if j == len(grp) - 1:
                                u.smv[g] = inc(mm_, "te")

                    def emit_exp(u, g):
                        grp = groups[g]
                        w(SC, "te", u.smv[g])
                        for (s, v) in pt_tok[2 * u.par + (g % 2)]:
                            SC.wait_ge(sems[s], v)
                        pt_tok[2 * u.par + (g % 2)] = []
                        pt = bass.AP(PT, (2 * u.par + (g % 2)) * 1024, [[4096, 128], [1, 512 * len(grp)]])
                        u.epv[g] = inc(SC.activation(pt, u.sps[:, 0:512 * len(grp)], ACT.Exp,
                                                     bias=negbc[:, u.h:u.h + 1], scale=0.125), "ac")
                        sx_tok[u.par].append(("ac", u.epv[g]))
                        need_ve = False
                        w(VE, "ac", u.epv[g])
                        base = (2 * u.par + (g % 2)) * 1024
                        for j, kt in enumerate(grp):
                            ja = kt - 4 * qr
                            if ja >= 0:
                                need_ve = True
                                if ja > 0:
                                    VE.memset(bass.AP(PT, base + 512 * j, [[4096, 128], [1, 128 * ja]]), 0.0)
                                mp = bass.AP(PT, base + 512 * j + 128 * ja, [[4096, 128], [1, 128]])
                                VE.tensor_tensor(mp, mp, trit[:, :], OP.mult)
                        u.mkv[g] = vsync() if need_ve else None

                    def emit_av(u, g):
                        grp = groups[g]
                        w(TE, "ac", u.epv[g])
                        if u.mkv[g] is not None:
                            w(TE, "ve", u.mkv[g])
                        w(TE, "ac", v_done)
                        if u.first_av:
                            for (s, v) in av_tok[u.ai]:
                                TE.wait_ge(sems[s], v)
                            av_tok[u.ai] = []
                        base = (2 * u.par + (g % 2)) * 1024
                        for j, kt in enumerate(grp):
                            last = (g == G - 1) and (j == len(grp) - 1)
                            mm_ = TE.matmul(u.av[0:65, 0:512], vt[:, kt, 65 * u.h:65 * u.h + 65],
                                            bass.AP(PT, base + 512 * j, [[4096, 128], [1, 512]]),
                                            start=u.first_av, stop=last)
                            u.first_av = False
                            if j == len(grp) - 1:
                                avv = inc(mm_, "te")
                        pt_tok[2 * u.par + (g % 2)].append(("te", avv))
                        u.avv = avv

                    uA, uB = units
                    emit_scores(uA, 0)
                    emit_scores(uB, 0)
                    tail_flush()
                    for g in range(G):
                        emit_exp(uA, g)
                        if g + 1 < G:
                            emit_scores(uA, g + 1)
                        emit_exp(uB, g)
                        emit_av(uA, g)
                        if g + 1 < G:
                            emit_scores(uB, g + 1)
                        emit_av(uB, g)
                    for u in units:
                        w(VE, "te", u.avv)
                        VE.reciprocal(recf[64:65, 512 * u.par:512 * u.par + 512],
                                      u.av[64:65, 0:512])
                        VE.drain()
                        VE.tensor_copy(rh16[64:65, 512 * u.par:512 * u.par + 512],
                                       recf[64:65, 512 * u.par:512 * u.par + 512])
                        rhv = vsync()
                        asl = u.par
                        pend_tails.append((u.ai, u.av, u.rxp, u.h, chp, u.r0, u.r1,
                                           qr, t0, asl, rhv))
                        flash_idx += 1
            tail_flush()
            qk_tok[0] += av_tok[0]
            qk_tok[1] += av_tok[1]
            av_tok = [[], []]
            xnf_tok = [("te", CN["te"])]
            prev_seq = dict(CN)
            wq_emit(14)
        wq_emit(99)
        wq_done = dict(CN)
        VE.tensor_scalar(scg[:, :], wgsc[:, :], sx_bc[:, 0:1], None, OP.mult, OP.bypass)
        VE.tensor_scalar(scu[:, :], wusc[:, :], sx_bc[:, 0:1], None, OP.mult, OP.bypass)
        scgu_v = vemark()
        p2_done = dict(CN)
        barrier()
        if stop_after == "p2":
            return _finish()

    late = contextlib.ExitStack()
    wpjf = late.enter_context(nc.sbuf_tensor("wpjf", [128, DCH, D], BF16))
    w(SC, "gpc", gp_casts["gpc"])
    for c in range(DCH):
        inc(SC.dma_start_transpose(wpjf[:, c, :], wpjbf_d[:, 128 * c:128 * c + 128]),
            "tr", 16)
    wpjf_tr = CN["tr"]
    with contextlib.ExitStack() as p25:
        def sb25(name, shape, dtype):
            return p25.enter_context(nc.sbuf_tensor(name, shape, dtype))
        xqf = sb25("xqf", [128, DCH, S], BF16)
        wgst = sb25("wgst", [128, 2, DCH, 128], BF16)
        wust = sb25("wust", [128, 2, DCH, 128], BF16)
        qx32 = sb25("qx32", [128, 2, 2048], F32)
        xq16 = sb25("xq16", [128, 2, 2048], BF16)
        gsb = sb25("gsb", [128, 2, 512], F32)
        silb = sb25("silb", [128, 2, 512], F32)
        usb = sb25("usb", [128, 2, 512], F32)
        hbuf = sb25("hbuf", [128, 2, 2048], F32)

        xn32_rr2 = xn32_d.ap().rearrange("(b m p) d -> b p m d", p=128, m=2)
        xqbf_rr = xqbf_d.ap().rearrange("(b m p) d -> b p m d", p=128, m=2)
        toks = [[], []]
        for tb in range(32):
            sl = tb % 2
            for (s, v) in toks[sl]:
                SY.wait_ge(sems[s], v)
            toks[sl] = []
            qx_s = dsem(f"qx{sl}", "sy")
            lv = inc(SY.dma_start(out=bass.AP(qx32, sl * 2048, [[4096, 128], [1024, 2], [1, 1024]]),
                                  in_=xn32_rr2[tb]), qx_s, 16)
            w(VE, qx_s, lv)
            VE.tensor_scalar(qx32[:, sl, :], qx32[:, sl, :], rsx_bc[:, 0:1], magic[:, 0:1],
                             OP.mult, OP.add)
            VE.tensor_scalar(qx32[:, sl, :], qx32[:, sl, :], magic[:, 0:1], QMAX,
                             OP.subtract, OP.min)
            VE.tensor_scalar(xq16[:, sl, :], qx32[:, sl, :], -QMAX - 1.0, None,
                             OP.max, OP.bypass)
            xqs_s = dsem(f"xqs{sl}", "sy")
            qv = vsync()
            w(SY, "ve", qv)
            sv = inc(SY.dma_start(out=xqbf_rr[tb],
                                  in_=bass.AP(xq16, sl * 2048, [[4096, 128], [1024, 2], [1, 1024]])),
                     xqs_s, 16)
            toks[sl] = [(xqs_s, sv), ("ve", qv)]
        xq_sp = (CN["xqs0"], CN["xqs1"])

        gu_ps = [ps_avA, ps_xA, ps_avB, ps_xB]
        gu_tok = [[], [], [], []]
        wgu_tok = [[], []]
        hb_tok = [[], []]
        xqf_tok = []
        hm_col = 0
        for b4 in range(4):
            tb0 = b4 * S
            for (s, v) in xqf_tok:
                SC.wait_ge(sems[s], v)
            xqf_tok = []
            SC.wait_ge(sems["xqs0"], xq_sp[0])
            SC.wait_ge(sems["xqs1"], xq_sp[1])
            for c in range(DCH):
                inc(SC.dma_start_transpose(xqf[:, c, :],
                                           xqbf_d[tb0:tb0 + S, 128 * c:128 * c + 128]),
                    "tr", 16)
            xqf_ld = CN["tr"]
            def gu_fetch(m):
                msl = m % 2
                for (s, v) in wgu_tok[msl]:
                    SC.wait_ge(sems[s], v)
                wgu_tok[msl] = []
                w(SC, "wqs", wq_done["wqs"])
                inc(SC.dma_start_transpose(wgst[:, msl, :, :],
                                           wgi_d[128 * m:128 * m + 128, :]), "tr", 16)
                return inc(SC.dma_start_transpose(wust[:, msl, :, :],
                                                  wui_d[128 * m:128 * m + 128, :]), "tr", 16)

            if b4 == 0:
                wtv_next = gu_fetch(0)
            for m in range(ICH):
                hsl = (b4 * ICH + m) % 2
                msl = m % 2
                wtv = wtv_next
                if m + 1 < ICH:
                    wtv_next = gu_fetch(m + 1)
                elif b4 < 3:
                    wtv_next = gu_fetch(0)
                for (s, v) in hb_tok[hsl]:
                    VE.wait_ge(sems[s], v)
                hb_tok[hsl] = []
                for tc in range(4):
                    gi = (2 * tc) % 4
                    ui = gi + 1
                    for (s, v) in gu_tok[gi]:
                        TE.wait_ge(sems[s], v)
                    gu_tok[gi] = []
                    for (s, v) in gu_tok[ui]:
                        TE.wait_ge(sems[s], v)
                    gu_tok[ui] = []
                    w(TE, "tr", max(xqf_ld, wtv))
                    cols = slice(512 * tc, 512 * tc + 512)
                    gv = uv = None
                    for d in range(DCH):
                        mm_ = TE.matmul(gu_ps[gi][:, 0:512], wgst[:, msl, d, :],
                                        xqf[:, d, cols], start=(d == 0), stop=(d == DCH - 1))
                        if d == DCH - 1:
                            gv = inc(mm_, "te")
                    for d in range(DCH):
                        mm_ = TE.matmul(gu_ps[ui][:, 0:512], wust[:, msl, d, :],
                                        xqf[:, d, cols], start=(d == 0), stop=(d == DCH - 1))
                        if d == DCH - 1:
                            uv = inc(mm_, "te")
                    esl = tc % 2
                    w(VE, "te", gv)
                    w(VE, "ve", scgu_v)
                    VE.tensor_scalar(gsb[:, esl, :], gu_ps[gi][:, 0:512], scg[:, m:m + 1],
                                     None, OP.mult, OP.bypass)
                    gsv = vsync()
                    gu_tok[gi].append(("ve", gsv))
                    w(SC, "ve", gsv)
                    slv = inc(SC.activation(silb[:, esl, :], gsb[:, esl, :], ACT.Silu), "ac")
                    w(VE, "te", uv)
                    VE.tensor_scalar(usb[:, esl, :], gu_ps[ui][:, 0:512], scu[:, m:m + 1],
                                     None, OP.mult, OP.bypass)
                    usv = vemark()
                    gu_tok[ui].append(("ve", usv))
                    w(VE, "ac", slv)
                    VE.tensor_tensor(hbuf[:, hsl, cols], silb[:, esl, :], usb[:, esl, :],
                                     OP.mult)
                    hv = vemark()
                VE.tensor_reduce(hmg[:, hm_col:hm_col + 1], hbuf[:, hsl, :], AX, OP.max,
                                 apply_absolute_value=True)
                hmv = vsync()
                hm_col += 1
                hb_s = dsem(f"hb{hsl}", "sy")
                w(SY, "ve", hmv)
                hs = inc(SY.dma_start(out=h32_d[m, :, tb0:tb0 + S], in_=hbuf[:, hsl, :]),
                         hb_s, 16)
                hb_tok[hsl] = [(hb_s, hs)]
                wgu_tok[msl].append(("te", CN["te"]))
            xqf_tok = [("te", CN["te"])]
        p25_done = dict(CN)
        barrier()
        if stop_after == "p25":
            return _finish()

    VE.tensor_reduce(onetmp[:, :], hmg[:, :], AX, OP.max)
    VE.drain()
    partition_max_to_scalar(onetmp[:, 0:1], tiny[:, 8:9])
    v0 = vemark()
    w(GP, "ve", v0)
    inc(GP.dma_start(out=cc2i_d[:, :], in_=tiny[:, 8:9]), "gpe", 16)
    w(GP, "gpe")
    ccv = inc(GP.collective_compute("AllReduce", OP.max,
                                    replica_groups=[list(range(NCORES))],
                                    ins=[cc2i_d.ap()], outs=[cc2o_d.ap()]), "cc")
    w(GP, "cc", ccv)
    g2 = inc(GP.dma_start(out=tiny[:, 9:10], in_=cc2o_d[:, :]), "gpe", 16)
    sgv = inc(SC.activation(tiny[:, 10:11], cl32[:, 1:2], ACT.Sigmoid), "ac")
    w(VE, "gpe", g2)
    w(VE, "ac", sgv)
    VE.tensor_tensor(tiny[:, 11:12], tiny[:, 9:10], tiny[:, 10:11], OP.mult)
    VE.drain()
    VE.tensor_scalar(tiny[:, 11:12], tiny[:, 11:12], 1.0 / QMAX, None, OP.mult, OP.bypass)
    VE.drain()
    VE.reciprocal(tiny[:, 12:13], tiny[:, 11:12])
    VE.drain()
    bcast128(tiny[:, 11:12], sh_bc[:, 0:1], 4)
    bcast128(tiny[:, 12:13], rsh_bc[:, 0:1], 6)

    with contextlib.ExitStack() as p3:
        def sb3(name, shape, dtype):
            return p3.enter_context(nc.sbuf_tensor(name, shape, dtype))
        wdf = sb3("wdf", [128, ICH, D], BF16)
        inth = sb3("inth", [128, 2, ICH, 512], BF16)
        atst = sb3("atst", [128, 2, DCH, 512], BF16)
        hld = sb3("hld", [128, 2, 1024], F32)
        xb3 = sb3("xb3", [128, 2, 4096], F32)
        outb = sb3("outb", [128, 2, 4096], F32)

        w(SC, "wqs", wq_done["wqs"])
        for i in range(ICH):
            inc(SC.dma_start_transpose(wdf[:, i, :], wdi_d[:, 128 * i:128 * i + 128]),
                "tr", 16)
        wdf_tr = CN["tr"]

        x_rr3 = x_d.ap().rearrange("(tc m p) d -> tc p m d", p=128, m=4)
        out_rr = out_d.ap().rearrange("(tc m p) d -> tc p m d", p=128, m=4)
        tc_tok = [[], []]
        hld_tok = [[], []]
        ps3 = [(ps_sA, ps_avA, ps_xA), (ps_sB, ps_avB, ps_xB)]
        ps3_tok = [[], []]
        ob_tok = [[], []]
        stage = {}

        def p3_load(tc):
            sl = tc % 2
            for (s, v) in tc_tok[sl]:
                SY.wait_ge(sems[s], v)
            tc_tok[sl] = []
            ats_s = dsem(f"ats{sl}", "sy")
            xb_s = dsem(f"xb{sl}", "sy")
            for c in range(DCH):
                inc(SY.dma_start(out=atst[:, sl, c, :],
                                 in_=attnb_d[c, :, 512 * tc:512 * tc + 512]), ats_s, 16)
            at_ld = CN[ats_s]
            xl = inc(SY.dma_start(out=bass.AP(xb3, sl * 4096, [[8192, 128], [1024, 4], [1, 1024]]),
                                  in_=x_rr3[tc]), xb_s, 16)
            for i2 in range(0, ICH, 2):
                npair = 2 if i2 + 1 < ICH else 1
                hsl = (i2 // 2) % 2
                for (s, v) in hld_tok[hsl]:
                    SY.wait_ge(sems[s], v)
                hld_tok[hsl] = []
                hl_s = dsem(f"hl{hsl}", "sy")
                lv = inc(SY.dma_start(out=bass.AP(hld, hsl * 1024, [[2048, 128], [1, 512 * npair]]),
                                      in_=bass.AP(h32_d, i2 * 128 * T + 512 * tc,
                                                  [[T, 128], [128 * T, npair], [1, 512]])),
                         hl_s, 16)
                w(VE, hl_s, lv)
                fd = 512 * npair
                hap = bass.AP(hld, hsl * 1024, [[2048, 128], [1, fd]])
                iap = bass.AP(inth, sl * (ICH * 512) + i2 * 512,
                              [[2 * ICH * 512, 128], [1, fd]])
                VE.tensor_scalar(hap, hap, rsh_bc[:, 0:1],
                                 magic[:, 0:1], OP.mult, OP.add)
                VE.tensor_scalar(hap, hap, magic[:, 0:1],
                                 QMAX, OP.subtract, OP.min)
                VE.tensor_scalar(iap, hap, -QMAX - 1.0,
                                 None, OP.max, OP.bypass)
                qv = vsync()
                hld_tok[hsl] = [("ve", qv)]
                tc_tok[sl].append(("ve", qv))
            stage[tc] = (CN["ve"], at_ld, xl, ats_s, xb_s)

        def p3_compute(tc):
            sl = tc % 2
            hq_done, at_ld, xl, ats_s, xb_s = stage.pop(tc)
            for tt in range(4):
                pp = (tc * 4 + tt) % 2
                prj, dn0, dn1 = ps3[pp]
                for (s, v) in ps3_tok[pp]:
                    TE.wait_ge(sems[s], v)
                ps3_tok[pp] = []
                w(TE, ats_s, at_ld)
                w(TE, "ve", hq_done)
                w(TE, "tr", wdf_tr)
                tsl = slice(128 * tt, 128 * tt + 128)
                for cc_i, cols in enumerate((slice(0, 512), slice(512, 1024))):
                    for d in range(DCH):
                        TE.matmul(prj[:, cols], atst[:, sl, d, tsl], wpjf[:, d, cols],
                                  start=(d == 0), stop=False)
                    TE.matmul(prj[:, cols], ones2[0:2, 0:128], pbhl[0:2, cols],
                              start=False, stop=True)
                    dps = dn0 if cc_i == 0 else dn1
                    for i in range(ICH):
                        TE.matmul(dps[:, 0:512], inth[:, sl, i, tsl], wdf[:, i, cols],
                                  start=(i == 0), stop=(i == ICH - 1))
                mv = temark()
                w(VE, "te", mv)
                w(VE, xb_s, xl)
                for (s, v) in (ob_tok[sl] if tt == 0 else []):
                    VE.wait_ge(sems[s], v)
                if tt == 0:
                    ob_tok[sl] = []
                for cc_i in range(2):
                    dps = (dn0, dn1)[cc_i]
                    cols = slice(512 * cc_i, 512 * cc_i + 512)
                    ocol = slice(1024 * tt + 512 * cc_i, 1024 * tt + 512 * cc_i + 512)
                    VE.tensor_scalar(outb[:, sl, ocol], dps[:, 0:512], sh_bc[:, 0:1], None,
                                     OP.mult, OP.bypass)
                    VE.tensor_tensor(outb[:, sl, ocol], outb[:, sl, ocol], prj[:, cols], OP.add)
                    VE.tensor_tensor(outb[:, sl, ocol], outb[:, sl, ocol],
                                     xb3[:, sl, ocol], OP.add)
                ev = vsync()
                ps3_tok[pp].append(("ve", ev))
            ov_s = dsem(f"ov{sl}", "sy")
            w(SY, "ve", CN["ve"])
            ov = inc(SY.dma_start(out=out_rr[tc],
                                  in_=bass.AP(outb, sl * 4096, [[8192, 128], [1024, 4], [1, 1024]])),
                     ov_s, 16)
            tc_tok[sl].append((ov_s, ov))
            tc_tok[sl].append(("te", CN["te"]))
            tc_tok[sl].append(("ve", CN["ve"]))
            ob_tok[sl].append((ov_s, ov))

        p3_load(0)
        for tc in range(16):
            if tc + 1 < 16:
                p3_load(tc + 1)
            p3_compute(tc)
            if stop_after == "p3serial":
                barrier()
        SY.wait_ge(sems["ov0"], CN["ov0"])
        SY.wait_ge(sems["ov1"], CN["ov1"])
    late.close()

    return _finish()


_CACHE = {}


def kernel(L0_attn_x, L0_attn_qkv_w, L0_attn_qkv_b, L0_attn_proj_w, L0_attn_proj_b,
           L0_ffn_clip_x, L0_ffn_clip_down, L0_ffn_w_gate, L0_ffn_w_up, L0_ffn_w_down):
    consts = _host_consts()
    if "nc" not in _CACHE:
        _CACHE["nc"] = build_program()
    nc, es = _CACHE["nc"]
    x = np.ascontiguousarray(np.asarray(L0_attn_x, dtype=np.float32))
    shared = {
        "qkv_w": np.ascontiguousarray(np.asarray(L0_attn_qkv_w, np.float32)),
        "qkv_b": np.ascontiguousarray(np.asarray(L0_attn_qkv_b, np.float32)),
        "proj_w": np.ascontiguousarray(np.asarray(L0_attn_proj_w, np.float32)),
        "proj_b": np.ascontiguousarray(np.asarray(L0_attn_proj_b, np.float32)),
        "clip_x": np.ascontiguousarray(np.asarray(L0_ffn_clip_x, np.float32)),
        "clip_down": np.ascontiguousarray(np.asarray(L0_ffn_clip_down, np.float32)),
        "w_gate": np.ascontiguousarray(np.asarray(L0_ffn_w_gate, np.float32)),
        "w_up": np.ascontiguousarray(np.asarray(L0_ffn_w_up, np.float32)),
        "w_down": np.ascontiguousarray(np.asarray(L0_ffn_w_down, np.float32)),
        **consts,
    }
    in_maps = []
    for c in range(NCORES):
        im = dict(shared)
        im["x"] = np.ascontiguousarray(x[NSEQ * c:NSEQ * (c + 1)].reshape(T, D))
        in_maps.append(im)
    res = run_bass_kernel_spmd(nc, in_maps, core_ids=list(range(NCORES)))
    out = np.concatenate(
        [np.asarray(r["out"], dtype=np.float32).reshape(NSEQ, S, D) for r in res.results],
        axis=0)
    return out


# version 15
# speedup vs baseline: 1.0576x; 1.0076x over previous
import contextlib

import numpy as np
import ml_dtypes

import concourse.bass as bass
import concourse.mybir as mybir
from concourse.alu_op_type import AluOpType as OP
from concourse.bass_utils import run_bass_kernel_spmd

F32 = mybir.dt.float32
BF16 = mybir.dt.bfloat16
ACT = mybir.ActivationFunctionType
AX = mybir.AxisListType.X
BF = ml_dtypes.bfloat16

NCORES = 8
B, S, D = 32, 2048, 1024
NSEQ = B // NCORES
T = NSEQ * S
H, HD = 16, 64
IFF = 2816
ICH = IFF // 128
DCH = D // 128
MAGIC32 = 12582912.0
CBIG = float(np.float32(1.5 * 2**82))
QMAX = 127.0


def _host_consts():
    half = 32
    freqs = 1.0 / 10000.0 ** (np.arange(half, dtype=np.float64) / half)
    pos = np.arange(S, dtype=np.float64)
    ang = pos[None, :] * freqs[:, None]
    cos32 = np.cos(ang)
    sin32 = np.sin(ang)
    r = np.arange(128)
    i_of_r = r % 32
    sgn = np.where((r % 64) < 32, -1.0, 1.0)
    tcos = cos32[i_of_r, :]
    tsin = sin32[i_of_r, :] * sgn[:, None]
    perm = np.zeros((128, 128), dtype=np.float64)
    for blk in range(2):
        b0 = 64 * blk
        perm[b0:b0 + 32, b0 + 32:b0 + 64] = np.eye(32)
        perm[b0 + 32:b0 + 64, b0:b0 + 32] = np.eye(32)
    rr, cc = np.meshgrid(np.arange(128), np.arange(128), indexing="ij")
    tri01 = (rr <= cc).astype(np.float64)
    c = {
        "tcos": tcos, "tsin": tsin,
        "permc": perm, "tri01": tri01,
    }
    return {k: np.asarray(v, dtype=np.float32).astype(BF) for k, v in c.items()}


def build_program(dbg=(), stop_after=None):
    nc = bass.Bass("TRN2", target_bir_lowering=False, debug=False)
    es = contextlib.ExitStack()

    def dram(name, shape, dtype, kind=None):
        if name in dbg:
            kind = "ExternalOutput"
        if kind is None:
            return nc.dram_tensor(name, shape, dtype)
        return nc.dram_tensor(name, shape, dtype, kind=kind)

    x_d = dram("x", [T, D], F32, "ExternalInput")
    qkvw_d = dram("qkv_w", [3 * D, D], F32, "ExternalInput")
    qkvb_d = dram("qkv_b", [3 * D], F32, "ExternalInput")
    pjw_d = dram("proj_w", [D, D], F32, "ExternalInput")
    pjb_d = dram("proj_b", [D], F32, "ExternalInput")
    clx_d = dram("clip_x", [1], F32, "ExternalInput")
    cld_d = dram("clip_down", [1], F32, "ExternalInput")
    wg_d = dram("w_gate", [IFF, D], F32, "ExternalInput")
    wu_d = dram("w_up", [IFF, D], F32, "ExternalInput")
    wd_d = dram("w_down", [D, IFF], F32, "ExternalInput")
    tcos_d = dram("tcos", [128, S], BF16, "ExternalInput")
    tsin_d = dram("tsin", [128, S], BF16, "ExternalInput")
    perm_d = dram("permc", [128, 128], BF16, "ExternalInput")
    tri_d = dram("tri01", [128, 128], BF16, "ExternalInput")
    out_d = dram("out", [T, D], F32, "ExternalOutput")
    xn32_d = dram("xn32", [T, D], F32)
    xnbf_d = dram("xnbf", [T, D], BF16)
    xqbf_d = dram("xqbf", [T, D], BF16)
    wkvbf_d = dram("wkvbf", [3 * D, D], BF16)
    wpjbf_d = dram("wpjbf", [D, D], BF16)
    wgi_d = dram("wgi", [IFF, D], BF16)
    wui_d = dram("wui", [IFF, D], BF16)
    wdi_d = dram("wdi", [D, IFF], BF16)
    h32_d = dram("h32", [ICH, 128, T], F32)
    attnb_d = dram("attnb", [DCH, 128, T], BF16)
    cc1i_d = dram("cc1i", [1, 1], F32)
    cc1o_d = dram("cc1o", [1, 1], F32)
    cc2i_d = dram("cc2i", [1, 1], F32)
    cc2o_d = dram("cc2o", [1, 1], F32)
    red_d = dram("redsc", [128], F32)
    dxnf_d = dram("dxnf", [128, DCH, S], BF16)
    dqro_d = dram("dqro", [128, DCH, S], BF16)
    dkro_d = dram("dkro", [128, DCH, S], BF16)
    dvt_d = dram("dvt", [128, 16, 1040], BF16)
    dnegb_d = dram("dnegb", [128, 16], F32)
    dpt0_d = dram("dpt0", [128, 1024], BF16)
    dpt1_d = dram("dpt1", [128, 1024], BF16)
    dav_d = dram("dav", [128, 512], F32)
    dwdf_d = dram("dwdf", [128, 22, 1024], BF16)
    dwpj_d = dram("dwpj", [128, 8, 1024], BF16)
    dith_d = dram("dith", [128, 22, 512], BF16)
    datst_d = dram("datst", [128, 8, 512], BF16)
    dbias_d = dram("dbias", [2, 4096], BF16)
    dsc_d = dram("dsc", [128, 8], F32)
    drr_d = dram("drr", [65, 512], F32)
    drcf_d = dram("drcf", [64, 512], F32)
    datb_d = dram("datb", [128, 512], BF16)
    hlscr_d = dram("hlscr", [2, 2048], BF16)

    sems = {}
    for s in ("sy", "tr", "gp", "te", "ve", "ac", "cc", "bar"):
        sems[s] = es.enter_context(nc.semaphore("sem_" + s))
    CN = {k: 0 for k in sems}
    DMA_FAM = {"sy": ["sy"], "tr": ["tr"], "gp": ["gp"]}

    def dsem(name, fam):
        if name not in sems:
            sems[name] = es.enter_context(nc.semaphore("sem_" + name))
            CN[name] = 0
            DMA_FAM[fam].append(name)
        return name

    def sb(name, shape, dtype):
        return es.enter_context(nc.sbuf_tensor(name, shape, dtype))

    ones2 = sb("ones2", [2, 512], BF16)
    magic = sb("magict", [128, 1], F32)
    epsc = sb("epsc", [128, 1], F32)
    onetmp = sb("onetmp", [128, 1], F32)
    bhlqk = sb("bhlqk", [2, 2 * D], BF16)
    pbhl = sb("pbhl", [2, D], BF16)
    vbp = sb("vbp", [2, 1040], BF16)
    cl32 = sb("cl32", [1, 2], F32)
    wgsc = sb("wgsc", [128, ICH], F32)
    wusc = sb("wusc", [128, ICH], F32)
    wdsc = sb("wdsc", [128, DCH], F32)
    scg = sb("scg", [128, ICH], F32)
    scu = sb("scu", [128, ICH], F32)
    rsx_bc = sb("rsx_bc", [128, 1], F32)
    sx_bc = sb("sx_bc", [128, 1], F32)
    rsh_bc = sb("rsh_bc", [128, 1], F32)
    sh_bc = sb("sh_bc", [128, 1], F32)
    tiny = sb("tinyt", [1, 16], F32)
    tinyb = sb("tinyb", [1, 16], BF16)
    trow = sb("trowt", [1, 128], F32)
    xmg = sb("xmg", [128, 64], F32)
    hmg = sb("hmg", [128, 88], F32)
    onesPb = sb("onesPb", [128, 128], BF16)
    ind1e = sb("ind1e", [128, 1], BF16)
    ind1o = sb("ind1o", [128, 1], BF16)

    ps_sA = es.enter_context(nc.psum_tensor("ps_sA", [128, 1024], F32))
    ps_sB = es.enter_context(nc.psum_tensor("ps_sB", [128, 1024], F32))
    ps_avA = es.enter_context(nc.psum_tensor("ps_avA", [128, 512], F32))
    ps_avB = es.enter_context(nc.psum_tensor("ps_avB", [128, 512], F32))
    ps_xA = es.enter_context(nc.psum_tensor("ps_xA", [128, 512], F32))
    ps_xB = es.enter_context(nc.psum_tensor("ps_xB", [128, 512], F32))

    block = es.enter_context(nc.Block())
    REC = {k: [] for k in ("sy", "sc", "ve", "te", "gp")}

    class _Pend:
        __slots__ = ("incs",)

        def __init__(self):
            self.incs = []

        def then_inc(self, sem, n=1):
            self.incs.append((sem, n))
            return self

    class _Eng:
        def __init__(self, key):
            self._k = key

        def __getattr__(self, name):
            q = REC[self._k]

            def call(*a, **kw):
                p = _Pend()
                q.append((name, a, kw, p))
                return p
            return call

    SY, SC, VE, TE, GP = _Eng("sy"), _Eng("sc"), _Eng("ve"), _Eng("te"), _Eng("gp")

    def inc(instr, s, n=1):
        instr.then_inc(sems[s], n)
        CN[s] += n
        return CN[s]

    def mark(key, s):
        name = REC[key][-1][0]
        assert name not in ("wait_ge",), name

        REC[key][-1][3].incs.append((sems[s], 1))
        CN[s] += 1
        return CN[s]

    def vemark():
        return mark("ve", "ve")

    def vsync():
        VE.drain()
        return mark("ve", "ve")

    def temark():
        return mark("te", "te")

    def w(eng, s, val=None):
        eng.wait_ge(sems[s], CN[s] if val is None else val)

    def barrier():
        for s in DMA_FAM["sy"]:
            SY.wait_ge(sems[s], CN[s])
        inc(SY.nop(), "bar")
        for s in DMA_FAM["tr"]:
            SC.wait_ge(sems[s], CN[s])
        inc(SC.nop(), "bar")
        for s in DMA_FAM["gp"]:
            GP.wait_ge(sems[s], CN[s])
        inc(GP.nop(), "bar")
        inc(TE.nop(), "bar")
        inc(VE.nop(), "bar")
        tgt = CN["bar"]
        for eng in (TE, VE, SC, GP, SY):
            eng.wait_ge(sems["bar"], tgt)

    def _finish():
        def _replay(key):
            def f(eng):
                for (name, a, kw, p) in REC[key]:
                    if key in ("ve", "te", "sc") and name == "nop" and p.incs:
                        eng.drain()
                    r = getattr(eng, name)(*a, **kw)
                    for (sem, n) in p.incs:
                        r.then_inc(sem, n)
            return f
        block.sync(_replay("sy"))
        block.scalar(_replay("sc"))
        block.vector(_replay("ve"))
        block.tensor(_replay("te"))
        block.gpsimd(_replay("gp"))
        return nc, es

    dsem("gpc", "gp")
    inc(GP.dma_start(out=wkvbf_d[:, :], in_=qkvw_d[:, :]), "gpc", 16)
    inc(GP.dma_start(out=wpjbf_d[:, :], in_=pjw_d[:, :]), "gpc", 16)
    gp_casts = dict(CN)

    VE.memset(ones2[:, :], 1.0)
    VE.memset(epsc[:, :], 1e-5)
    VE.memset(onesPb[:, :], 1.0)
    VE.memset(ind1e[0:64, 0:1], 1.0)
    VE.memset(ind1e[64:128, 0:1], 0.0)
    VE.memset(ind1o[0:64, 0:1], 0.0)
    VE.memset(ind1o[64:128, 0:1], 1.0)
    VE.memset(onetmp[:, :], 1.0)
    VE.drain()
    VE.tensor_scalar(magic[:, :], onetmp[:, :], float(2.0 ** -30), 1.0,
                     OP.add, OP.subtract)
    VE.drain()
    VE.tensor_scalar(magic[:, :], magic[:, :], CBIG, MAGIC32,
                     OP.mult, OP.add)
    VE.drain()

    p0 = contextlib.ExitStack()
    qb32 = p0.enter_context(nc.sbuf_tensor("qb32", [1, 3 * D], F32))
    hl16 = p0.enter_context(nc.sbuf_tensor("hl16", [1, 3 * D], BF16))
    hl32 = p0.enter_context(nc.sbuf_tensor("hl32", [1, 3 * D], F32))
    lo32 = p0.enter_context(nc.sbuf_tensor("lo32", [1, 3 * D], F32))
    lo16 = p0.enter_context(nc.sbuf_tensor("lo16", [1, 3 * D], BF16))
    vb32 = p0.enter_context(nc.sbuf_tensor("vb32", [1, 1040], F32))

    dsem("bia", "sy")
    inc(SY.dma_start(out=qb32[:, :], in_=qkvb_d.ap().rearrange("(a d) -> a d", a=1)), "bia", 16)
    inc(SY.dma_start(out=cl32[:, 0:1], in_=clx_d.ap().rearrange("(a d) -> a d", a=1)), "bia", 16)
    inc(SY.dma_start(out=cl32[:, 1:2], in_=cld_d.ap().rearrange("(a d) -> a d", a=1)), "bia", 16)
    ld_bias = CN["bia"]

    def hilo_row(src_ap, n, dst, dst_off):
        w(VE, "gpb") if "gpb" in sems else None
        VE.tensor_copy(hl16[:, 0:n], src_ap)
        VE.drain()
        VE.tensor_copy(hl32[:, 0:n], hl16[:, 0:n])
        VE.drain()
        VE.tensor_tensor(lo32[:, 0:n], src_ap, hl32[:, 0:n], OP.subtract)
        VE.drain()
        VE.tensor_copy(lo16[:, 0:n], lo32[:, 0:n])
        v = vsync()
        dsem("gpb", "gp")
        w(GP, "ve", v)
        inc(GP.dma_start(out=hlscr_d[0:1, 0:n], in_=hl16[:, 0:n]), "gpb", 16)
        inc(GP.dma_start(out=hlscr_d[1:2, 0:n], in_=lo16[:, 0:n]), "gpb", 16)
        w(GP, "gpb")
        inc(GP.dma_start(out=dst[0:2, dst_off:dst_off + n], in_=hlscr_d[0:2, 0:n]), "gpb", 16)

    w(VE, "bia", ld_bias)
    hilo_row(qb32[:, 0:2 * D], 2 * D, bhlqk, 0)
    VE.memset(vb32[:, :], 0.0)
    VE.tensor_copy(bass.AP(vb32, 0, [[1040, 1], [65, 16], [1, 64]]),
                   bass.AP(qb32, 2 * D, [[3 * D, 1], [64, 16], [1, 64]]))
    VE.memset(bass.AP(vb32, 64, [[1040, 1], [65, 16], [1, 1]]), 1.0)
    VE.drain()
    hilo_row(vb32[:, 0:1040], 1040, vbp, 0)
    pb32 = qb32
    pv = CN["ve"]
    w(SY, "ve", pv)
    w(SY, "gp")
    inc(SY.dma_start(out=pb32[:, 0:D], in_=pjb_d.ap().rearrange("(a d) -> a d", a=1)), "bia", 16)
    w(VE, "bia")
    hilo_row(pb32[:, 0:D], D, pbhl, 0)

    bias_gp = CN["gp"]
    barrier()
    p0.close()

    with contextlib.ExitStack() as p1:
        def sb1(name, shape, dtype):
            return p1.enter_context(nc.sbuf_tensor(name, shape, dtype))
        xb = sb1("xb", [128, 2, 4096], F32)
        sqb = sb1("sqb", [128, 2, 4096], F32)
        xn32b = sb1("xn32b", [128, 2, 4096], F32)
        xn16b = sb1("xn16b", [128, 2, 4096], BF16)
        st_sum = sb1("st_sum", [128, 2, 4], F32)
        st_sq = sb1("st_sq", [128, 2, 4], F32)
        st_m = sb1("st_m", [128, 2, 4], F32)
        st_e = sb1("st_e", [128, 2, 4], F32)
        st_v = sb1("st_v", [128, 2, 4], F32)
        st_sd = sb1("st_sd", [128, 2, 4], F32)
        st_r = sb1("st_r", [128, 2, 4], F32)
        st_n = sb1("st_n", [128, 2, 4], F32)

        slot_tok = [[], []]
        spill_tok = [[], []]
        x_rr = x_d.ap().rearrange("(b m p) d -> b p m d", p=128, m=4)
        xn32_rr = xn32_d.ap().rearrange("(b m p) d -> b p m d", p=128, m=4)
        xnbf_rr = xnbf_d.ap().rearrange("(b m p) d -> b p m d", p=128, m=4)

        for b in range(16):
            sl = b % 2
            for (s, v) in slot_tok[sl]:
                SY.wait_ge(sems[s], v)
            slot_tok[sl] = []
            xs_s = dsem(f"x{sl}", "sy")
            ldv = inc(SY.dma_start(out=bass.AP(xb, sl * 4096, [[8192, 128], [1024, 4], [1, 1024]]),
                                   in_=x_rr[b]), xs_s, 16)
            w(VE, xs_s, ldv)
            VE.tensor_reduce(st_sum[:, sl, :],
                             bass.AP(xb, sl * 4096, [[8192, 128], [1024, 4], [1, 1024]]),
                             AX, OP.add)
            w(SC, xs_s, ldv)
            sqv = inc(SC.activation(sqb[:, sl, :], xb[:, sl, :], ACT.Square), "ac")
            w(VE, "ac", sqv)
            VE.tensor_reduce(st_sq[:, sl, :],
                             bass.AP(sqb, sl * 4096, [[8192, 128], [1024, 4], [1, 1024]]),
                             AX, OP.add)
            VE.drain()
            VE.tensor_scalar(st_m[:, sl, :], st_sum[:, sl, :], 1.0 / D, None, OP.mult, OP.bypass)
            VE.tensor_scalar(st_e[:, sl, :], st_sq[:, sl, :], 1.0 / D, None, OP.mult, OP.bypass)
            VE.drain()
            VE.tensor_tensor(st_v[:, sl, :], st_m[:, sl, :], st_m[:, sl, :], OP.mult)
            VE.drain()
            VE.tensor_tensor(st_v[:, sl, :], st_e[:, sl, :], st_v[:, sl, :], OP.subtract)
            vv = vemark()
            w(SC, "ve", vv)
            sdv = inc(SC.activation(st_sd[:, sl, :], st_v[:, sl, :], ACT.Sqrt, bias=epsc[:, 0:1]), "ac")
            w(VE, "ac", sdv)
            VE.reciprocal(st_r[:, sl, :], st_sd[:, sl, :])
            VE.drain()
            VE.tensor_tensor(st_n[:, sl, :], st_m[:, sl, :], st_r[:, sl, :], OP.mult)
            VE.drain()
            VE.tensor_scalar(st_n[:, sl, :], st_n[:, sl, :], -1.0, None, OP.mult, OP.bypass)
            VE.drain()
            for (s, v) in spill_tok[sl]:
                VE.wait_ge(sems[s], v)
            for mm in range(4):
                VE.tensor_scalar(xn32b[:, sl, 1024 * mm:1024 * mm + 1024],
                                 xb[:, sl, 1024 * mm:1024 * mm + 1024],
                                 st_r[:, sl, mm:mm + 1], st_n[:, sl, mm:mm + 1],
                                 OP.mult, OP.add)
            xnv = vsync()
            VE.tensor_reduce(xmg[:, 4 * b:4 * b + 4],
                             bass.AP(xn32b, sl * 4096, [[8192, 128], [1024, 4], [1, 1024]]),
                             AX, OP.max, apply_absolute_value=True)
            xmv = vemark()
            w(SC, "ve", xnv)
            cav = inc(SC.activation(xn16b[:, sl, :], xn32b[:, sl, :], ACT.Copy), "ac")
            sp32 = dsem(f"sp32_{sl}", "sy")
            sp16 = dsem(f"sp16_{sl}", "sy")
            w(SY, "ve", xnv)
            s1 = inc(SY.dma_start(out=xn32_rr[b],
                                  in_=bass.AP(xn32b, sl * 4096, [[8192, 128], [1024, 4], [1, 1024]])),
                     sp32, 16)
            w(SY, "ac", cav)
            s2 = inc(SY.dma_start(out=xnbf_rr[b],
                                  in_=bass.AP(xn16b, sl * 4096, [[8192, 128], [1024, 4], [1, 1024]])),
                     sp16, 16)
            slot_tok[sl] = [(sp16, s2), ("ve", xmv), ("ac", cav)]
            spill_tok[sl] = [(sp32, s1), (sp16, s2)]
        barrier()
        p1_done = dict(CN)
        if stop_after == "p1":
            barrier()
            return _finish()

    def partition_max_to_scalar(src_col, out11):
        dsem("gpe", "gp")
        v0 = vemark()
        w(GP, "ve", v0)
        inc(GP.dma_start(out=red_d[:], in_=src_col), "gpe", 16)
        w(GP, "gpe")
        inc(GP.dma_start(out=trow[:, :], in_=red_d.ap().rearrange("(a d) -> a d", a=1)), "gpe", 16)
        w(VE, "gpe")
        VE.tensor_reduce(out11, trow[:, :], AX, OP.max)
        VE.drain()

    def bcast128(src11_f32, dst_col, tmp_off):
        VE.tensor_copy(tinyb[:, tmp_off:tmp_off + 1], src11_f32)
        VE.drain()
        VE.tensor_copy(tiny[:, tmp_off:tmp_off + 1], tinyb[:, tmp_off:tmp_off + 1])
        VE.drain()
        VE.tensor_tensor(tiny[:, tmp_off + 1:tmp_off + 2], src11_f32,
                         tiny[:, tmp_off:tmp_off + 1], OP.subtract)
        VE.drain()
        VE.tensor_copy(tinyb[:, tmp_off + 1:tmp_off + 2], tiny[:, tmp_off + 1:tmp_off + 2])
        v1 = vsync()
        w(TE, "ve", v1)
        TE.matmul(ps_xA[0:128, 0:1], ones2[0:1, 0:128], tinyb[:, tmp_off:tmp_off + 1],
                  start=True, stop=False)
        t1 = inc(TE.matmul(ps_xA[0:128, 0:1], ones2[0:1, 0:128],
                           tinyb[:, tmp_off + 1:tmp_off + 2], start=False, stop=True), "te")
        w(VE, "te", t1)
        VE.tensor_copy(dst_col, ps_xA[0:128, 0:1])
        vd = vemark()
        return vd

    VE.tensor_reduce(onetmp[:, :], xmg[:, :], AX, OP.max)
    VE.drain()
    partition_max_to_scalar(onetmp[:, 0:1], tiny[:, 8:9])
    v0 = vemark()
    w(GP, "ve", v0)
    inc(GP.dma_start(out=cc1i_d[:, :], in_=tiny[:, 8:9]), "gpe", 16)
    w(GP, "gpe")
    ccv = inc(GP.collective_compute("AllReduce", OP.max,
                                    replica_groups=[list(range(NCORES))],
                                    ins=[cc1i_d.ap()], outs=[cc1o_d.ap()]), "cc")
    w(GP, "cc", ccv)
    g1 = inc(GP.dma_start(out=tiny[:, 9:10], in_=cc1o_d[:, :]), "gpe", 16)
    w(SC, "bia", ld_bias)
    sgv = inc(SC.activation(tiny[:, 10:11], cl32[:, 0:1], ACT.Sigmoid), "ac")
    w(VE, "gpe", g1)
    w(VE, "ac", sgv)
    VE.tensor_tensor(tiny[:, 11:12], tiny[:, 9:10], tiny[:, 10:11], OP.mult)
    VE.drain()
    VE.tensor_scalar(tiny[:, 11:12], tiny[:, 11:12], 1.0 / QMAX, None, OP.mult, OP.bypass)
    VE.drain()
    VE.reciprocal(tiny[:, 12:13], tiny[:, 11:12])
    VE.drain()
    bcast128(tiny[:, 11:12], sx_bc[:, 0:1], 0)
    bcast128(tiny[:, 12:13], rsx_bc[:, 0:1], 2)
    ex1_done = dict(CN)
    if stop_after == "ex1":
        barrier()
        return _finish()

    with contextlib.ExitStack() as p2:
        def sb2(name, shape, dtype):
            return p2.enter_context(nc.sbuf_tensor(name, shape, dtype))
        tcos = sb2("tcos_s", [128, S], BF16)
        tsin = sb2("tsin_s", [128, S], BF16)
        permt = sb2("permt", [128, 128], BF16)
        trit = sb2("trit", [128, 128], BF16)
        wqkvst = sb2("wqkvst", [128, 2, DCH, 128], BF16)
        wvp = sb2("wvp", [128, DCH, 1040], BF16)
        vstage = sb2("vstage", [128, D], BF16)
        xnf = sb2("xnf", [128, DCH, S], BF16)
        qro = sb2("qro", [128, DCH, S], BF16)
        kro = sb2("kro", [128, DCH, S], BF16)
        vt = sb2("vt", [128, 16, 1040], BF16)
        PT = sb2("PTt", [128, 4, 1024], BF16)
        qraw = sb2("qraw", [128, 2, 512], BF16)
        rsw = sb2("rsw", [128, 2, 512], BF16)
        tmpA = sb2("tmpA", [128, 512], BF16)
        tmpC = sb2("tmpC", [128, 512], BF16)
        nqrow = sb2("nqrow", [1, 16], F32)
        nkrow = sb2("nkrow", [1, 16], F32)
        nrm8 = sb2("nrm8", [1, 8], F32)
        brow = sb2("brow", [1, 16], F32)
        browb = sb2("browb", [1, 16], BF16)
        negbc = sb2("negbc", [128, 16], F32)
        recf = sb2("recf", [65, 1024], F32)
        rh16 = sb2("rh16", [65, 1024], BF16)
        atb = [sb2("atb0", [128, 512], BF16), sb2("atb1", [128, 512], BF16)]
        wqt = sb2("wqt", [128, 1, 1408], F32)
        wqo = sb2("wqo", [128, 1, 1408], BF16)
        wqr = sb2("wqr", [128, 1, 4], F32)
        nrm4 = sb2("nrm4", [2, 4], F32)

        dsem("cst", "sy")
        for t_, d_ in ((tcos, tcos_d), (tsin, tsin_d),
                       (permt, perm_d), (trit, tri_d)):
            inc(SY.dma_start(out=t_[:, :], in_=d_[:, :]), "cst", 16)
        ld_const2 = CN["cst"]

        w(SC, "gpc", gp_casts["gpc"])
        VE.memset(wvp[:, :, :], 0.0)
        wvp_ve0 = vemark()
        for c in range(DCH):
            w(SC, "ve", wvp_ve0)
            for (s, v) in [("ve", CN["ve"])] if c > 0 else []:
                SC.wait_ge(sems[s], v)
            tv = inc(SC.dma_start_transpose(vstage[:, :],
                                            wkvbf_d[2 * D:3 * D, 128 * c:128 * c + 128]),
                     "tr", 16)
            w(VE, "tr", tv)
            VE.tensor_copy(bass.AP(wvp, c * 1040, [[DCH * 1040, 128], [65, 16], [1, 64]]),
                           bass.AP(vstage, 0, [[D, 128], [64, 16], [1, 64]]))
            vsync()
        wvp_v = CN["ve"]

        dsem("wql", "sy")
        dsem("wqs", "sy")
        wq_toks = [[]]

        def quant_weights(src_d, dst_d, sc_tile, monly, width, dequant=False):
            nch = monly + 1
            toks = wq_toks
            halves = [(0, width)] if width <= 1408 else [(0, width // 2), (width // 2, width)]
            src_rr = src_d.ap().rearrange("(m p) d -> m p d", p=128)
            dst_rr = dst_d.ap().rearrange("(m p) d -> m p d", p=128)
            for m in (monly,):
                sl = 0
                lvs = []
                red_ve = 0
                for hi_, (a, b) in enumerate(halves):
                    for (s, v) in toks[sl]:
                        SY.wait_ge(sems[s], v)
                    toks[sl] = []
                    w(SY, "ve", red_ve)
                    lv = inc(SY.dma_start(out=wqt[:, sl, 0:b - a], in_=src_rr[m][:, a:b]),
                             "wql", 16)
                    lvs.append(lv)
                    w(VE, "wql", lv)
                    VE.tensor_reduce(wqr[:, sl, hi_:hi_ + 1], wqt[:, sl, 0:b - a], AX,
                                     OP.max, apply_absolute_value=True)
                    VE.drain()
                    red_ve = vemark()
                    if hi_ == len(halves) - 1:
                        VE.tensor_reduce(wqr[:, sl, 2:3], wqr[:, sl, 0:len(halves)], AX, OP.max)
                        VE.drain()
                        VE.tensor_scalar(sc_tile[:, m:m + 1], wqr[:, sl, 2:3], 1.0 / QMAX,
                                         None, OP.mult, OP.bypass)
                        VE.drain()
                        VE.reciprocal(wqr[:, sl, 3:4], sc_tile[:, m:m + 1])
                        VE.drain()
                if len(halves) == 1:
                    VE.tensor_scalar(wqt[:, sl, 0:width], wqt[:, sl, 0:width],
                                     wqr[:, sl, 3:4], magic[:, 0:1], OP.mult, OP.add)
                    VE.drain()
                    if dequant:
                        VE.tensor_scalar(wqt[:, sl, 0:width], wqt[:, sl, 0:width],
                                         magic[:, 0:1], sc_tile[:, m:m + 1],
                                         OP.subtract, OP.mult)
                        VE.drain()
                        VE.tensor_copy(wqo[:, sl, 0:width], wqt[:, sl, 0:width])
                    else:
                        VE.tensor_scalar(wqo[:, sl, 0:width], wqt[:, sl, 0:width],
                                         magic[:, 0:1], None, OP.subtract, OP.bypass)
                    qv = vsync()
                    w(SY, "ve", qv)
                    sv = inc(SY.dma_start(out=dst_rr[m], in_=wqo[:, sl, 0:width]), "wqs", 16)
                    toks[sl] = [("wqs", sv)]
                else:
                    prev_sv = None
                    for hi_ in (1, 0):
                        a, b = halves[hi_]
                        if hi_ == 0:
                            if prev_sv is not None:
                                SY.wait_ge(sems["wqs"], prev_sv)
                            lv = inc(SY.dma_start(out=wqt[:, sl, 0:b - a],
                                                  in_=src_rr[m][:, a:b]), "wql", 16)
                            w(VE, "wql", lv)
                        VE.tensor_scalar(wqt[:, sl, 0:b - a], wqt[:, sl, 0:b - a],
                                         wqr[:, sl, 3:4], magic[:, 0:1], OP.mult, OP.add)
                        VE.drain()
                        if dequant:
                            VE.tensor_scalar(wqt[:, sl, 0:b - a], wqt[:, sl, 0:b - a],
                                             magic[:, 0:1], sc_tile[:, m:m + 1],
                                             OP.subtract, OP.mult)
                            VE.drain()
                            VE.tensor_copy(wqo[:, sl, 0:b - a], wqt[:, sl, 0:b - a])
                        else:
                            VE.tensor_scalar(wqo[:, sl, 0:b - a], wqt[:, sl, 0:b - a],
                                             magic[:, 0:1], None, OP.subtract, OP.bypass)
                        qv = vsync()
                        w(SY, "ve", qv)
                        sv = inc(SY.dma_start(out=dst_rr[m][:, a:b], in_=wqo[:, sl, 0:b - a]),
                                 "wqs", 16)
                        prev_sv = sv
                        toks[sl] = [("wqs", sv)]

        wq_tasks = ([(wg_d, wgi_d, wgsc, m, D, False) for m in range(ICH)]
                    + [(wu_d, wui_d, wusc, m, D, False) for m in range(ICH)]
                    + [(wd_d, wdi_d, wdsc, m, IFF, True) for m in range(DCH)])


        def wq_emit(n):
            for _ in range(n):
                if not wq_tasks:
                    return
                sd, dd, sct, m, wdt, dq = wq_tasks.pop(0)
                quant_weights(sd, dd, sct, m, wdt, dequant=dq)

        xnbf_mat = xnbf_d
        flash_idx = 0
        xnf_tok = []
        qk_ps = [ps_avA, ps_avB]
        qk_tok = [[], []]
        rs_ps = [ps_xA, ps_xB]
        rs_tok = [[("ve", ex1_done["ve"])], []]
        qraw_tok = [[], []]
        pt_tok = [[], [], [], []]
        av_tok = [[], []]
        sx_tok = [[], []]
        at_tok = [[], []]
        prev_seq = None
        for s_i in range(NSEQ):
            t0 = s_i * S
            if prev_seq is not None:
                w(VE, "te", prev_seq["te"])
                w(VE, "ac", prev_seq["ac"])
            for (s, v) in xnf_tok:
                SC.wait_ge(sems[s], v)
            xnf_tok = []
            for s_ in ("sp16_0", "sp16_1"):
                SC.wait_ge(sems[s_], p1_done.get(s_, CN[s_]))
            for c in range(DCH):
                inc(SC.dma_start_transpose(xnf[:, c, :],
                                           xnbf_mat[t0:t0 + S, 128 * c:128 * c + 128]),
                    "tr", 16)
            xnf_ld = CN["tr"]

            wst_tok = [[], []]
            pend = None

            def rope_flush():
                nonlocal pend
                if pend is None:
                    return
                pi, dst, dch, cols, ct, st_, ev = pend
                pend = None
                w(TE, "ac", ev)
                rp = rs_ps[pi]
                for (s, v) in rs_tok[pi]:
                    TE.wait_ge(sems[s], v)
                rs_tok[pi] = []
                rv = inc(TE.matmul(rp[:, 0:512], permt[:, :], qraw[:, pi, :],
                                   start=True, stop=True), "te")
                w(VE, "te", rv)
                VE.tensor_copy(rsw[:, pi, :], rp[:, 0:512])
                rcv = vemark()
                rs_tok[pi].append(("ve", rcv))
                VE.tensor_tensor(tmpA[:, :], qraw[:, pi, :], ct[:, cols], OP.mult)
                VE.tensor_tensor(tmpC[:, :], rsw[:, pi, :], st_[:, cols], OP.mult)
                VE.tensor_tensor(dst[:, dch, cols], tmpA[:, :], tmpC[:, :], OP.add)
                dv = vsync()
                qk_tok[pi].append(("ve", dv))
                qraw_tok[pi].append(("ve", dv))

            for m in range(16):
                dst = qro if m < 8 else kro
                dch = m if m < 8 else m - 8
                ct, st_ = tcos, tsin
                msl = m % 2
                for (s, v) in wst_tok[msl]:
                    SC.wait_ge(sems[s], v)
                wst_tok[msl] = []
                wtv = inc(SC.dma_start_transpose(wqkvst[:, msl, :, :],
                                                 wkvbf_d[128 * m:128 * m + 128, :]),
                          "tr", 16)
                for tc in range(4):
                    pi = (m * 4 + tc) % 2
                    ps = qk_ps[pi]
                    w(TE, "tr", max(xnf_ld, wtv))
                    w(TE, "cst", ld_const2)
                    w(TE, "gp", bias_gp)
                    for (s, v) in qk_tok[pi]:
                        TE.wait_ge(sems[s], v)
                    qk_tok[pi] = []
                    for d in range(DCH):
                        TE.matmul(ps[:, 0:512], wqkvst[:, msl, d, :],
                                  xnf[:, d, 512 * tc:512 * tc + 512],
                                  start=(d == 0), stop=False)
                    mv = inc(TE.matmul(ps[:, 0:512], bhlqk[0:2, 128 * m:128 * m + 128],
                                       ones2[0:2, 0:512], start=False, stop=True), "te")
                    w(SC, "te", mv)
                    for (s, v) in qraw_tok[pi]:
                        SC.wait_ge(sems[s], v)
                    qraw_tok[pi] = []
                    ev = inc(SC.activation(qraw[:, pi, :], ps[:, 0:512], ACT.Copy), "ac")
                    qk_tok[pi].append(("ac", ev))
                    if tc == 3:
                        wst_tok[msl].append(("te", CN["te"]))
                    rope_flush()
                    pend = (pi, dst, dch, slice(512 * tc, 512 * tc + 512), ct, st_, ev)
            rope_flush()
            rope_done_v = CN["ve"]

            w(VE, "ve", wvp_v)
            vcnt = 0
            for tt in range(16):
                for (c0, c1) in ((0, 512), (512, 1024), (1024, 1040)):
                    pi = vcnt % 2
                    vcnt += 1
                    ps = qk_ps[pi]
                    for (s, v) in qk_tok[pi]:
                        TE.wait_ge(sems[s], v)
                    qk_tok[pi] = []
                    w(TE, "ve", wvp_v)
                    for d in range(DCH):
                        TE.matmul(ps[:, 0:c1 - c0], xnf[:, d, 128 * tt:128 * tt + 128],
                                  wvp[:, d, c0:c1], start=(d == 0), stop=False)
                    mv = inc(TE.matmul(ps[:, 0:c1 - c0], ones2[0:2, 0:128],
                                       vbp[0:2, c0:c1], start=False, stop=True), "te")
                    w(SC, "te", mv)
                    ev = inc(SC.activation(vt[:, tt, c0:c1], ps[:, 0:c1 - c0], ACT.Copy), "ac")
                    qk_tok[pi].append(("ac", ev))
            v_done = CN["ac"]

            sq_te = 0
            for (src, dstn) in ((qro, nqrow), (kro, nkrow)):
                for c in range(DCH):
                    w(VE, "ve", rope_done_v)
                    w(VE, "te", sq_te)
                    sqbf = bass.AP(PT, 0, [[4096, 128], [1, 2048]])
                    VE.tensor_tensor(sqbf, src[:, c, :], src[:, c, :], OP.mult)
                    sqv = vsync()
                    w(TE, "ve", sqv)
                    for tc in range(4):
                        for (s, v) in sx_tok[0]:
                            TE.wait_ge(sems[s], v)
                        sx_tok[0] = []
                        sq_sl = bass.AP(PT, 512 * tc, [[4096, 128], [1, 512]])
                        TE.matmul(ps_sA[0:1, 0:512], ind1e[:, 0:1],
                                  sq_sl, start=True, stop=True)
                        nv = inc(TE.matmul(ps_sA[0:1, 512:1024], ind1o[:, 0:1],
                                           sq_sl, start=True, stop=True), "te")
                        sq_te = nv
                        w(VE, "te", nv)
                        VE.tensor_reduce(nrm8[0:1, tc:tc + 1], ps_sA[0:1, 0:512], AX, OP.max)
                        VE.tensor_reduce(nrm8[0:1, 4 + tc:5 + tc], ps_sA[0:1, 512:1024], AX, OP.max)
                        rv = vemark()
                        sx_tok[0].append(("ve", rv))
                    VE.drain()
                    VE.tensor_reduce(dstn[0:1, 2 * c:2 * c + 1], nrm8[0:1, 0:4], AX, OP.max)
                    VE.tensor_reduce(dstn[0:1, 2 * c + 1:2 * c + 2], nrm8[0:1, 4:8], AX, OP.max)
                    VE.drain()
            VE.tensor_tensor(brow[:, :], nqrow[:, :], nkrow[:, :], OP.mult)
            nbv = vemark()
            w(SC, "ve", nbv)
            nsq = inc(SC.activation(brow[:, :], brow[:, :], ACT.Sqrt, bias=0.0), "ac")
            w(VE, "ac", nsq)
            VE.tensor_scalar(brow[:, :], brow[:, :], -0.125, None, OP.mult, OP.bypass)
            VE.drain()
            VE.tensor_copy(browb[:, :], brow[:, :])
            bbv = vsync()
            w(TE, "ve", bbv)
            for (s, v) in rs_tok[0]:
                TE.wait_ge(sems[s], v)
            rs_tok[0] = []
            nbm = inc(TE.matmul(ps_xA[0:128, 0:16], onesPb[0:1, 0:128], browb[0:1, 0:16],
                                start=True, stop=True), "te")
            w(VE, "te", nbm)
            VE.tensor_copy(negbc[:, :], ps_xA[0:128, 0:16])
            nbcv = vsync()
            rs_tok[0].append(("ve", nbcv))

            if stop_after == "rope0" and s_i == 0:
                w(SY, "ve", nbcv)
                for (t_, d_) in ((xnf, dxnf_d), (qro, dqro_d), (kro, dkro_d), (vt, dvt_d)):
                    inc(SY.dma_start(out=d_[:, :, :], in_=t_[:, :, :]), "sy", 16)
                inc(SY.dma_start(out=dnegb_d[:, :], in_=negbc[:, :]), "sy", 16)
                barrier()
                return _finish()
            w(SC, "ve", nbcv)
            ps_s = [ps_sA, ps_sB]
            pend_tails = []

            def tail_flush():
                while pend_tails:
                    ai, av, rxp, h_, ch_, r0_, r1_, qr_, t0_, asl, rhv = pend_tails.pop(0)
                    w(TE, "ve", rhv)
                    for (s, v) in rs_tok[ai]:
                        TE.wait_ge(sems[s], v)
                    rs_tok[ai] = []
                    rbm = inc(TE.matmul(rxp[0:64, 0:512], onesPb[64:65, 0:64],
                                        rh16[64:65, 512 * ai:512 * ai + 512],
                                        start=True, stop=True), "te")
                    w(VE, "te", rbm)
                    for (s, v) in at_tok[asl]:
                        VE.wait_ge(sems[s], v)
                    at_tok[asl] = []
                    VE.tensor_copy(recf[0:64, 512 * ai:512 * ai + 512], rxp[0:64, 0:512])
                    rcv = vemark()
                    rs_tok[ai].append(("ve", rcv))
                    VE.tensor_tensor(atb[asl][0:64, 0:512], av[0:64, 0:512],
                                     recf[0:64, 512 * ai:512 * ai + 512], OP.mult)
                    ntv = vsync()
                    av_tok[ai].append(("ve", ntv))
                    ab_s = dsem(f"atb{asl}", "sy")
                    w(SY, "ve", ntv)
                    spv = inc(SY.dma_start(out=attnb_d[ch_, r0_:r1_, t0_ + 512 * qr_:t0_ + 512 * qr_ + 512],
                                           in_=atb[asl][0:64, 0:512]), ab_s, 16)
                    at_tok[asl].append((ab_s, spv))

            class U:
                pass

            for chp in range(8):
                for qr in range(4):
                    qc = slice(512 * qr, 512 * qr + 512)
                    kts = list(range(4 * qr + 4))
                    groups = [kts[i:i + 2] for i in range(0, len(kts), 2)]
                    G = len(groups)
                    units = []
                    for par in range(2):
                        u = U()
                        u.h = 2 * chp + par
                        u.par = par
                        u.r0, u.r1 = 64 * par, 64 * par + 64
                        u.ai = par
                        u.av = qk_ps[par]
                        u.rxp = rs_ps[par]
                        u.sps = ps_s[par]
                        u.first_av = True
                        u.smv = [None] * G
                        u.epv = [None] * G
                        u.mkv = [None] * G
                        units.append(u)

                    def emit_scores(u, g):
                        grp = groups[g]
                        for (s, v) in sx_tok[u.par]:
                            TE.wait_ge(sems[s], v)
                        sx_tok[u.par] = []
                        w(TE, "ve", nbcv)
                        for j, kt in enumerate(grp):
                            mm_ = TE.matmul(u.sps[:, 512 * j:512 * j + 512],
                                            kro[u.r0:u.r1, chp, 128 * kt:128 * kt + 128],
                                            qro[u.r0:u.r1, chp, qc], start=True, stop=True)
                            if j == len(grp) - 1:
                                u.smv[g] = inc(mm_, "te")

                    def emit_exp(u, g):
                        grp = groups[g]
                        w(SC, "te", u.smv[g])
                        for (s, v) in pt_tok[2 * u.par + (g % 2)]:
                            SC.wait_ge(sems[s], v)
                        pt_tok[2 * u.par + (g % 2)] = []
                        pt = bass.AP(PT, (2 * u.par + (g % 2)) * 1024, [[4096, 128], [1, 512 * len(grp)]])
                        u.epv[g] = inc(SC.activation(pt, u.sps[:, 0:512 * len(grp)], ACT.Exp,
                                                     bias=negbc[:, u.h:u.h + 1], scale=0.125), "ac")
                        sx_tok[u.par].append(("ac", u.epv[g]))
                        need_ve = False
                        w(VE, "ac", u.epv[g])
                        base = (2 * u.par + (g % 2)) * 1024
                        for j, kt in enumerate(grp):
                            ja = kt - 4 * qr
                            if ja >= 0:
                                need_ve = True
                                if ja > 0:
                                    VE.memset(bass.AP(PT, base + 512 * j, [[4096, 128], [1, 128 * ja]]), 0.0)
                                mp = bass.AP(PT, base + 512 * j + 128 * ja, [[4096, 128], [1, 128]])
                                VE.tensor_tensor(mp, mp, trit[:, :], OP.mult)
                        u.mkv[g] = vsync() if need_ve else None

                    def emit_av(u, g):
                        grp = groups[g]
                        w(TE, "ac", u.epv[g])
                        if u.mkv[g] is not None:
                            w(TE, "ve", u.mkv[g])
                        w(TE, "ac", v_done)
                        if u.first_av:
                            for (s, v) in av_tok[u.ai]:
                                TE.wait_ge(sems[s], v)
                            av_tok[u.ai] = []
                        base = (2 * u.par + (g % 2)) * 1024
                        for j, kt in enumerate(grp):
                            last = (g == G - 1) and (j == len(grp) - 1)
                            mm_ = TE.matmul(u.av[0:65, 0:512], vt[:, kt, 65 * u.h:65 * u.h + 65],
                                            bass.AP(PT, base + 512 * j, [[4096, 128], [1, 512]]),
                                            start=u.first_av, stop=last)
                            u.first_av = False
                            if j == len(grp) - 1:
                                avv = inc(mm_, "te")
                        pt_tok[2 * u.par + (g % 2)].append(("te", avv))
                        u.avv = avv

                    uA, uB = units
                    emit_scores(uA, 0)
                    emit_scores(uB, 0)
                    tail_flush()
                    for g in range(G):
                        emit_exp(uA, g)
                        if g + 1 < G:
                            emit_scores(uA, g + 1)
                        emit_exp(uB, g)
                        emit_av(uA, g)
                        if g + 1 < G:
                            emit_scores(uB, g + 1)
                        emit_av(uB, g)
                    for u in units:
                        w(VE, "te", u.avv)
                        VE.reciprocal(recf[64:65, 512 * u.par:512 * u.par + 512],
                                      u.av[64:65, 0:512])
                        VE.drain()
                        VE.tensor_copy(rh16[64:65, 512 * u.par:512 * u.par + 512],
                                       recf[64:65, 512 * u.par:512 * u.par + 512])
                        rhv = vsync()
                        asl = u.par
                        pend_tails.append((u.ai, u.av, u.rxp, u.h, chp, u.r0, u.r1,
                                           qr, t0, asl, rhv))
                        flash_idx += 1
            tail_flush()
            qk_tok[0] += av_tok[0]
            qk_tok[1] += av_tok[1]
            av_tok = [[], []]
            xnf_tok = [("te", CN["te"])]
            prev_seq = dict(CN)
            wq_emit(14)
        wq_emit(99)
        wq_done = dict(CN)
        VE.tensor_scalar(scg[:, :], wgsc[:, :], sx_bc[:, 0:1], None, OP.mult, OP.bypass)
        VE.tensor_scalar(scu[:, :], wusc[:, :], sx_bc[:, 0:1], None, OP.mult, OP.bypass)
        scgu_v = vemark()
        p2_done = dict(CN)
        barrier()
        if stop_after == "p2":
            return _finish()

    late = contextlib.ExitStack()
    wpjf = late.enter_context(nc.sbuf_tensor("wpjf", [128, DCH, D], BF16))
    w(SC, "gpc", gp_casts["gpc"])
    for c in range(DCH):
        inc(SC.dma_start_transpose(wpjf[:, c, :], wpjbf_d[:, 128 * c:128 * c + 128]),
            "tr", 16)
    wpjf_tr = CN["tr"]
    with contextlib.ExitStack() as p25:
        def sb25(name, shape, dtype):
            return p25.enter_context(nc.sbuf_tensor(name, shape, dtype))
        xqf = sb25("xqf", [128, DCH, S], BF16)
        wgst = sb25("wgst", [128, 2, DCH, 128], BF16)
        wust = sb25("wust", [128, 2, DCH, 128], BF16)
        qx32 = sb25("qx32", [128, 2, 2048], F32)
        xq16 = sb25("xq16", [128, 2, 2048], BF16)
        gsb = sb25("gsb", [128, 2, 512], F32)
        silb = sb25("silb", [128, 2, 512], F32)
        usb = sb25("usb", [128, 2, 512], F32)
        hbuf = sb25("hbuf", [128, 2, 2048], F32)

        xn32_rr2 = xn32_d.ap().rearrange("(b m p) d -> b p m d", p=128, m=2)
        xqbf_rr = xqbf_d.ap().rearrange("(b m p) d -> b p m d", p=128, m=2)
        toks = [[], []]
        for tb in range(32):
            sl = tb % 2
            for (s, v) in toks[sl]:
                SY.wait_ge(sems[s], v)
            toks[sl] = []
            qx_s = dsem(f"qx{sl}", "sy")
            lv = inc(SY.dma_start(out=bass.AP(qx32, sl * 2048, [[4096, 128], [1024, 2], [1, 1024]]),
                                  in_=xn32_rr2[tb]), qx_s, 16)
            w(VE, qx_s, lv)
            VE.tensor_scalar(qx32[:, sl, :], qx32[:, sl, :], rsx_bc[:, 0:1], magic[:, 0:1],
                             OP.mult, OP.add)
            VE.tensor_scalar(qx32[:, sl, :], qx32[:, sl, :], magic[:, 0:1], QMAX,
                             OP.subtract, OP.min)
            VE.tensor_scalar(xq16[:, sl, :], qx32[:, sl, :], -QMAX - 1.0, None,
                             OP.max, OP.bypass)
            xqs_s = dsem(f"xqs{sl}", "sy")
            qv = vsync()
            w(SY, "ve", qv)
            sv = inc(SY.dma_start(out=xqbf_rr[tb],
                                  in_=bass.AP(xq16, sl * 2048, [[4096, 128], [1024, 2], [1, 1024]])),
                     xqs_s, 16)
            toks[sl] = [(xqs_s, sv), ("ve", qv)]
        xq_sp = (CN["xqs0"], CN["xqs1"])

        gu_ps = [ps_avA, ps_xA, ps_avB, ps_xB]
        gu_tok = [[], [], [], []]
        wgu_tok = [[], []]
        hb_tok = [[], []]
        xqf_tok = []
        hm_col = 0
        for b4 in range(4):
            tb0 = b4 * S
            for (s, v) in xqf_tok:
                SC.wait_ge(sems[s], v)
            xqf_tok = []
            SC.wait_ge(sems["xqs0"], xq_sp[0])
            SC.wait_ge(sems["xqs1"], xq_sp[1])
            for c in range(DCH):
                inc(SC.dma_start_transpose(xqf[:, c, :],
                                           xqbf_d[tb0:tb0 + S, 128 * c:128 * c + 128]),
                    "tr", 16)
            xqf_ld = CN["tr"]
            def gu_fetch(m):
                msl = m % 2
                for (s, v) in wgu_tok[msl]:
                    SC.wait_ge(sems[s], v)
                wgu_tok[msl] = []
                w(SC, "wqs", wq_done["wqs"])
                inc(SC.dma_start_transpose(wgst[:, msl, :, :],
                                           wgi_d[128 * m:128 * m + 128, :]), "tr", 16)
                return inc(SC.dma_start_transpose(wust[:, msl, :, :],
                                                  wui_d[128 * m:128 * m + 128, :]), "tr", 16)

            if b4 == 0:
                wtv_next = gu_fetch(0)
            for m in range(ICH):
                hsl = (b4 * ICH + m) % 2
                msl = m % 2
                wtv = wtv_next
                if m + 1 < ICH:
                    wtv_next = gu_fetch(m + 1)
                elif b4 < 3:
                    wtv_next = gu_fetch(0)
                for (s, v) in hb_tok[hsl]:
                    VE.wait_ge(sems[s], v)
                hb_tok[hsl] = []
                for tc in range(4):
                    gi = (2 * tc) % 4
                    ui = gi + 1
                    for (s, v) in gu_tok[gi]:
                        TE.wait_ge(sems[s], v)
                    gu_tok[gi] = []
                    for (s, v) in gu_tok[ui]:
                        TE.wait_ge(sems[s], v)
                    gu_tok[ui] = []
                    w(TE, "tr", max(xqf_ld, wtv))
                    cols = slice(512 * tc, 512 * tc + 512)
                    gv = uv = None
                    for d in range(DCH):
                        mm_ = TE.matmul(gu_ps[gi][:, 0:512], wgst[:, msl, d, :],
                                        xqf[:, d, cols], start=(d == 0), stop=(d == DCH - 1))
                        if d == DCH - 1:
                            gv = inc(mm_, "te")
                    for d in range(DCH):
                        mm_ = TE.matmul(gu_ps[ui][:, 0:512], wust[:, msl, d, :],
                                        xqf[:, d, cols], start=(d == 0), stop=(d == DCH - 1))
                        if d == DCH - 1:
                            uv = inc(mm_, "te")
                    esl = tc % 2
                    w(VE, "te", gv)
                    w(VE, "ve", scgu_v)
                    VE.tensor_scalar(gsb[:, esl, :], gu_ps[gi][:, 0:512], scg[:, m:m + 1],
                                     None, OP.mult, OP.bypass)
                    gsv = vsync()
                    gu_tok[gi].append(("ve", gsv))
                    w(SC, "ve", gsv)
                    slv = inc(SC.activation(silb[:, esl, :], gsb[:, esl, :], ACT.Silu), "ac")
                    w(VE, "te", uv)
                    VE.tensor_scalar(usb[:, esl, :], gu_ps[ui][:, 0:512], scu[:, m:m + 1],
                                     None, OP.mult, OP.bypass)
                    usv = vemark()
                    gu_tok[ui].append(("ve", usv))
                    w(VE, "ac", slv)
                    VE.tensor_tensor(hbuf[:, hsl, cols], silb[:, esl, :], usb[:, esl, :],
                                     OP.mult)
                    hv = vemark()
                VE.tensor_reduce(hmg[:, hm_col:hm_col + 1], hbuf[:, hsl, :], AX, OP.max,
                                 apply_absolute_value=True)
                hmv = vsync()
                hm_col += 1
                hb_s = dsem(f"hb{hsl}", "sy")
                w(SY, "ve", hmv)
                hs = inc(SY.dma_start(out=h32_d[m, :, tb0:tb0 + S], in_=hbuf[:, hsl, :]),
                         hb_s, 16)
                hb_tok[hsl] = [(hb_s, hs)]
                wgu_tok[msl].append(("te", CN["te"]))
            xqf_tok = [("te", CN["te"])]
        p25_done = dict(CN)
        barrier()
        if stop_after == "p25":
            return _finish()

    VE.tensor_reduce(onetmp[:, :], hmg[:, :], AX, OP.max)
    VE.drain()
    partition_max_to_scalar(onetmp[:, 0:1], tiny[:, 8:9])
    v0 = vemark()
    w(GP, "ve", v0)
    inc(GP.dma_start(out=cc2i_d[:, :], in_=tiny[:, 8:9]), "gpe", 16)
    w(GP, "gpe")
    ccv = inc(GP.collective_compute("AllReduce", OP.max,
                                    replica_groups=[list(range(NCORES))],
                                    ins=[cc2i_d.ap()], outs=[cc2o_d.ap()]), "cc")
    w(GP, "cc", ccv)
    g2 = inc(GP.dma_start(out=tiny[:, 9:10], in_=cc2o_d[:, :]), "gpe", 16)
    sgv = inc(SC.activation(tiny[:, 10:11], cl32[:, 1:2], ACT.Sigmoid), "ac")
    w(VE, "gpe", g2)
    w(VE, "ac", sgv)
    VE.tensor_tensor(tiny[:, 11:12], tiny[:, 9:10], tiny[:, 10:11], OP.mult)
    VE.drain()
    VE.tensor_scalar(tiny[:, 11:12], tiny[:, 11:12], 1.0 / QMAX, None, OP.mult, OP.bypass)
    VE.drain()
    VE.reciprocal(tiny[:, 12:13], tiny[:, 11:12])
    VE.drain()
    bcast128(tiny[:, 11:12], sh_bc[:, 0:1], 4)
    bcast128(tiny[:, 12:13], rsh_bc[:, 0:1], 6)

    with contextlib.ExitStack() as p3:
        def sb3(name, shape, dtype):
            return p3.enter_context(nc.sbuf_tensor(name, shape, dtype))
        wdf = sb3("wdf", [128, ICH, D], BF16)
        inth = sb3("inth", [128, 2, ICH, 512], BF16)
        atst = sb3("atst", [128, 2, DCH, 512], BF16)
        hld = sb3("hld", [128, 2, 1024], F32)
        xb3 = sb3("xb3", [128, 2, 4096], F32)
        outb = sb3("outb", [128, 2, 4096], F32)

        w(SC, "wqs", wq_done["wqs"])
        for i in range(ICH):
            inc(SC.dma_start_transpose(wdf[:, i, :], wdi_d[:, 128 * i:128 * i + 128]),
                "tr", 16)
        wdf_tr = CN["tr"]

        x_rr3 = x_d.ap().rearrange("(tc m p) d -> tc p m d", p=128, m=4)
        out_rr = out_d.ap().rearrange("(tc m p) d -> tc p m d", p=128, m=4)
        tc_tok = [[], []]
        hld_tok = [[], []]
        ps3 = [(ps_sA, ps_avA, ps_xA), (ps_sB, ps_avB, ps_xB)]
        ps3_tok = [[], []]
        ob_tok = [[], []]
        stage = {}

        def p3_load(tc):
            sl = tc % 2
            for (s, v) in tc_tok[sl]:
                SY.wait_ge(sems[s], v)
            tc_tok[sl] = []
            ats_s = dsem(f"ats{sl}", "sy")
            xb_s = dsem(f"xb{sl}", "sy")
            for c in range(DCH):
                inc(SY.dma_start(out=atst[:, sl, c, :],
                                 in_=attnb_d[c, :, 512 * tc:512 * tc + 512]), ats_s, 16)
            at_ld = CN[ats_s]
            xl = inc(SY.dma_start(out=bass.AP(xb3, sl * 4096, [[8192, 128], [1024, 4], [1, 1024]]),
                                  in_=x_rr3[tc]), xb_s, 16)
            for i2 in range(0, ICH, 2):
                npair = 2 if i2 + 1 < ICH else 1
                hsl = (i2 // 2) % 2
                for (s, v) in hld_tok[hsl]:
                    SY.wait_ge(sems[s], v)
                hld_tok[hsl] = []
                hl_s = dsem(f"hl{hsl}", "sy")
                lv = inc(SY.dma_start(out=bass.AP(hld, hsl * 1024, [[2048, 128], [1, 512 * npair]]),
                                      in_=bass.AP(h32_d, i2 * 128 * T + 512 * tc,
                                                  [[T, 128], [128 * T, npair], [1, 512]])),
                         hl_s, 16)
                w(VE, hl_s, lv)
                fd = 512 * npair
                hap = bass.AP(hld, hsl * 1024, [[2048, 128], [1, fd]])
                iap = bass.AP(inth, sl * (ICH * 512) + i2 * 512,
                              [[2 * ICH * 512, 128], [1, fd]])
                VE.tensor_scalar(hap, hap, rsh_bc[:, 0:1],
                                 magic[:, 0:1], OP.mult, OP.add)
                VE.tensor_scalar(hap, hap, magic[:, 0:1],
                                 QMAX, OP.subtract, OP.min)
                VE.tensor_scalar(iap, hap, -QMAX - 1.0,
                                 None, OP.max, OP.bypass)
                qv = vsync()
                hld_tok[hsl] = [("ve", qv)]
                tc_tok[sl].append(("ve", qv))
            stage[tc] = (CN["ve"], at_ld, xl, ats_s, xb_s)

        def p3_compute(tc):
            sl = tc % 2
            hq_done, at_ld, xl, ats_s, xb_s = stage.pop(tc)
            for tt in range(4):
                pp = (tc * 4 + tt) % 2
                prj, dn0, dn1 = ps3[pp]
                for (s, v) in ps3_tok[pp]:
                    TE.wait_ge(sems[s], v)
                ps3_tok[pp] = []
                w(TE, ats_s, at_ld)
                w(TE, "ve", hq_done)
                w(TE, "tr", wdf_tr)
                tsl = slice(128 * tt, 128 * tt + 128)
                for cc_i, cols in enumerate((slice(0, 512), slice(512, 1024))):
                    for d in range(DCH):
                        TE.matmul(prj[:, cols], atst[:, sl, d, tsl], wpjf[:, d, cols],
                                  start=(d == 0), stop=False)
                    TE.matmul(prj[:, cols], ones2[0:2, 0:128], pbhl[0:2, cols],
                              start=False, stop=True)
                    dps = dn0 if cc_i == 0 else dn1
                    for i in range(ICH):
                        TE.matmul(dps[:, 0:512], inth[:, sl, i, tsl], wdf[:, i, cols],
                                  start=(i == 0), stop=(i == ICH - 1))
                mv = temark()
                w(VE, "te", mv)
                w(VE, xb_s, xl)
                for (s, v) in (ob_tok[sl] if tt == 0 else []):
                    VE.wait_ge(sems[s], v)
                if tt == 0:
                    ob_tok[sl] = []
                for cc_i in range(2):
                    dps = (dn0, dn1)[cc_i]
                    cols = slice(512 * cc_i, 512 * cc_i + 512)
                    ocol = slice(1024 * tt + 512 * cc_i, 1024 * tt + 512 * cc_i + 512)
                    VE.tensor_scalar(outb[:, sl, ocol], dps[:, 0:512], sh_bc[:, 0:1], None,
                                     OP.mult, OP.bypass)
                    VE.tensor_tensor(outb[:, sl, ocol], outb[:, sl, ocol], prj[:, cols], OP.add)
                    VE.tensor_tensor(outb[:, sl, ocol], outb[:, sl, ocol],
                                     xb3[:, sl, ocol], OP.add)
                ev = vsync()
                ps3_tok[pp].append(("ve", ev))
            ov_s = dsem(f"ov{sl}", "sy")
            w(SY, "ve", CN["ve"])
            ov = inc(SY.dma_start(out=out_rr[tc],
                                  in_=bass.AP(outb, sl * 4096, [[8192, 128], [1024, 4], [1, 1024]])),
                     ov_s, 16)
            tc_tok[sl].append((ov_s, ov))
            tc_tok[sl].append(("te", CN["te"]))
            tc_tok[sl].append(("ve", CN["ve"]))
            ob_tok[sl].append((ov_s, ov))

        p3_load(0)
        for tc in range(16):
            if tc + 1 < 16:
                p3_load(tc + 1)
            p3_compute(tc)
            if stop_after == "p3serial":
                barrier()
        SY.wait_ge(sems["ov0"], CN["ov0"])
        SY.wait_ge(sems["ov1"], CN["ov1"])
    late.close()

    return _finish()


_CACHE = {}


def kernel(L0_attn_x, L0_attn_qkv_w, L0_attn_qkv_b, L0_attn_proj_w, L0_attn_proj_b,
           L0_ffn_clip_x, L0_ffn_clip_down, L0_ffn_w_gate, L0_ffn_w_up, L0_ffn_w_down):
    consts = _host_consts()
    if "nc" not in _CACHE:
        _CACHE["nc"] = build_program()
    nc, es = _CACHE["nc"]
    x = np.ascontiguousarray(np.asarray(L0_attn_x, dtype=np.float32))
    shared = {
        "qkv_w": np.ascontiguousarray(np.asarray(L0_attn_qkv_w, np.float32)),
        "qkv_b": np.ascontiguousarray(np.asarray(L0_attn_qkv_b, np.float32)),
        "proj_w": np.ascontiguousarray(np.asarray(L0_attn_proj_w, np.float32)),
        "proj_b": np.ascontiguousarray(np.asarray(L0_attn_proj_b, np.float32)),
        "clip_x": np.ascontiguousarray(np.asarray(L0_ffn_clip_x, np.float32)),
        "clip_down": np.ascontiguousarray(np.asarray(L0_ffn_clip_down, np.float32)),
        "w_gate": np.ascontiguousarray(np.asarray(L0_ffn_w_gate, np.float32)),
        "w_up": np.ascontiguousarray(np.asarray(L0_ffn_w_up, np.float32)),
        "w_down": np.ascontiguousarray(np.asarray(L0_ffn_w_down, np.float32)),
        **consts,
    }
    in_maps = []
    for c in range(NCORES):
        im = dict(shared)
        im["x"] = np.ascontiguousarray(x[NSEQ * c:NSEQ * (c + 1)].reshape(T, D))
        in_maps.append(im)
    res = run_bass_kernel_spmd(nc, in_maps, core_ids=list(range(NCORES)))
    out = np.concatenate(
        [np.asarray(r["out"], dtype=np.float32).reshape(NSEQ, S, D) for r in res.results],
        axis=0)
    return out


# version 16
# speedup vs baseline: 1.0648x; 1.0068x over previous
import contextlib

import numpy as np
import ml_dtypes

import concourse.bass as bass
import concourse.mybir as mybir
from concourse.alu_op_type import AluOpType as OP
from concourse.bass_utils import run_bass_kernel_spmd

F32 = mybir.dt.float32
BF16 = mybir.dt.bfloat16
ACT = mybir.ActivationFunctionType
AX = mybir.AxisListType.X
BF = ml_dtypes.bfloat16

NCORES = 8
B, S, D = 32, 2048, 1024
NSEQ = B // NCORES
T = NSEQ * S
H, HD = 16, 64
IFF = 2816
ICH = IFF // 128
DCH = D // 128
MAGIC32 = 12582912.0
CBIG = float(np.float32(1.5 * 2**82))
QMAX = 127.0


def _host_consts():
    half = 32
    freqs = 1.0 / 10000.0 ** (np.arange(half, dtype=np.float64) / half)
    pos = np.arange(S, dtype=np.float64)
    ang = pos[None, :] * freqs[:, None]
    cos32 = np.cos(ang)
    sin32 = np.sin(ang)
    r = np.arange(128)
    i_of_r = r % 32
    sgn = np.where((r % 64) < 32, -1.0, 1.0)
    tcos = cos32[i_of_r, :]
    tsin = sin32[i_of_r, :] * sgn[:, None]
    perm = np.zeros((128, 128), dtype=np.float64)
    for blk in range(2):
        b0 = 64 * blk
        perm[b0:b0 + 32, b0 + 32:b0 + 64] = np.eye(32)
        perm[b0 + 32:b0 + 64, b0:b0 + 32] = np.eye(32)
    rr, cc = np.meshgrid(np.arange(128), np.arange(128), indexing="ij")
    tri01 = (rr <= cc).astype(np.float64)
    c = {
        "tcos": tcos, "tsin": tsin,
        "permc": perm, "tri01": tri01,
    }
    return {k: np.asarray(v, dtype=np.float32).astype(BF) for k, v in c.items()}


def build_program(dbg=(), stop_after=None):
    nc = bass.Bass("TRN2", target_bir_lowering=False, debug=False)
    es = contextlib.ExitStack()

    def dram(name, shape, dtype, kind=None):
        if name in dbg:
            kind = "ExternalOutput"
        if kind is None:
            return nc.dram_tensor(name, shape, dtype)
        return nc.dram_tensor(name, shape, dtype, kind=kind)

    x_d = dram("x", [T, D], F32, "ExternalInput")
    qkvw_d = dram("qkv_w", [3 * D, D], F32, "ExternalInput")
    qkvb_d = dram("qkv_b", [3 * D], F32, "ExternalInput")
    pjw_d = dram("proj_w", [D, D], F32, "ExternalInput")
    pjb_d = dram("proj_b", [D], F32, "ExternalInput")
    clx_d = dram("clip_x", [1], F32, "ExternalInput")
    cld_d = dram("clip_down", [1], F32, "ExternalInput")
    wg_d = dram("w_gate", [IFF, D], F32, "ExternalInput")
    wu_d = dram("w_up", [IFF, D], F32, "ExternalInput")
    wd_d = dram("w_down", [D, IFF], F32, "ExternalInput")
    tcos_d = dram("tcos", [128, S], BF16, "ExternalInput")
    tsin_d = dram("tsin", [128, S], BF16, "ExternalInput")
    perm_d = dram("permc", [128, 128], BF16, "ExternalInput")
    tri_d = dram("tri01", [128, 128], BF16, "ExternalInput")
    out_d = dram("out", [T, D], F32, "ExternalOutput")
    xn32_d = dram("xn32", [T, D], F32)
    xnbf_d = dram("xnbf", [T, D], BF16)
    xqbf_d = dram("xqbf", [T, D], BF16)
    wkvbf_d = dram("wkvbf", [3 * D, D], BF16)
    wpjbf_d = dram("wpjbf", [D, D], BF16)
    wgi_d = dram("wgi", [IFF, D], BF16)
    wui_d = dram("wui", [IFF, D], BF16)
    wdi_d = dram("wdi", [D, IFF], BF16)
    h32_d = dram("h32", [ICH, 128, T], F32)
    attnb_d = dram("attnb", [DCH, 128, T], BF16)
    cc1i_d = dram("cc1i", [1, 1], F32)
    cc1o_d = dram("cc1o", [1, 1], F32)
    cc2i_d = dram("cc2i", [1, 1], F32)
    cc2o_d = dram("cc2o", [1, 1], F32)
    red_d = dram("redsc", [128], F32)
    dxnf_d = dram("dxnf", [128, DCH, S], BF16)
    dqro_d = dram("dqro", [128, DCH, S], BF16)
    dkro_d = dram("dkro", [128, DCH, S], BF16)
    dvt_d = dram("dvt", [128, 16, 1040], BF16)
    dnegb_d = dram("dnegb", [128, 16], F32)
    dpt0_d = dram("dpt0", [128, 1024], BF16)
    dpt1_d = dram("dpt1", [128, 1024], BF16)
    dav_d = dram("dav", [128, 512], F32)
    dwdf_d = dram("dwdf", [128, 22, 1024], BF16)
    dwpj_d = dram("dwpj", [128, 8, 1024], BF16)
    dith_d = dram("dith", [128, 22, 512], BF16)
    datst_d = dram("datst", [128, 8, 512], BF16)
    dbias_d = dram("dbias", [2, 4096], BF16)
    dsc_d = dram("dsc", [128, 8], F32)
    drr_d = dram("drr", [65, 512], F32)
    drcf_d = dram("drcf", [64, 512], F32)
    datb_d = dram("datb", [128, 512], BF16)
    hlscr_d = dram("hlscr", [2, 2048], BF16)

    sems = {}
    for s in ("sy", "tr", "gp", "te", "ve", "ac", "cc", "bar"):
        sems[s] = es.enter_context(nc.semaphore("sem_" + s))
    CN = {k: 0 for k in sems}
    DMA_FAM = {"sy": ["sy"], "tr": ["tr"], "gp": ["gp"]}

    def dsem(name, fam):
        if name not in sems:
            sems[name] = es.enter_context(nc.semaphore("sem_" + name))
            CN[name] = 0
            DMA_FAM[fam].append(name)
        return name

    def sb(name, shape, dtype):
        return es.enter_context(nc.sbuf_tensor(name, shape, dtype))

    ones2 = sb("ones2", [2, 512], BF16)
    magic = sb("magict", [128, 1], F32)
    epsc = sb("epsc", [128, 1], F32)
    onetmp = sb("onetmp", [128, 1], F32)
    bhlqk = sb("bhlqk", [2, 2 * D], BF16)
    pbhl = sb("pbhl", [2, D], BF16)
    vbp = sb("vbp", [2, 1040], BF16)
    cl32 = sb("cl32", [1, 2], F32)
    wgsc = sb("wgsc", [128, ICH], F32)
    wusc = sb("wusc", [128, ICH], F32)
    wdsc = sb("wdsc", [128, DCH], F32)
    scg = sb("scg", [128, ICH], F32)
    scu = sb("scu", [128, ICH], F32)
    rsx_bc = sb("rsx_bc", [128, 1], F32)
    sx_bc = sb("sx_bc", [128, 1], F32)
    rsh_bc = sb("rsh_bc", [128, 1], F32)
    sh_bc = sb("sh_bc", [128, 1], F32)
    tiny = sb("tinyt", [1, 16], F32)
    tinyb = sb("tinyb", [1, 16], BF16)
    trow = sb("trowt", [1, 128], F32)
    xmg = sb("xmg", [128, 64], F32)
    hmg = sb("hmg", [128, 88], F32)
    onesPb = sb("onesPb", [128, 128], BF16)
    ind1e = sb("ind1e", [128, 1], BF16)
    ind1o = sb("ind1o", [128, 1], BF16)

    ps_sA = es.enter_context(nc.psum_tensor("ps_sA", [128, 1024], F32))
    ps_sB = es.enter_context(nc.psum_tensor("ps_sB", [128, 1024], F32))
    ps_avA = es.enter_context(nc.psum_tensor("ps_avA", [128, 512], F32))
    ps_avB = es.enter_context(nc.psum_tensor("ps_avB", [128, 512], F32))
    ps_xA = es.enter_context(nc.psum_tensor("ps_xA", [128, 512], F32))
    ps_xB = es.enter_context(nc.psum_tensor("ps_xB", [128, 512], F32))

    block = es.enter_context(nc.Block())
    REC = {k: [] for k in ("sy", "sc", "ve", "te", "gp")}

    class _Pend:
        __slots__ = ("incs",)

        def __init__(self):
            self.incs = []

        def then_inc(self, sem, n=1):
            self.incs.append((sem, n))
            return self

    class _Eng:
        def __init__(self, key):
            self._k = key

        def __getattr__(self, name):
            q = REC[self._k]

            def call(*a, **kw):
                p = _Pend()
                q.append((name, a, kw, p))
                return p
            return call

    SY, SC, VE, TE, GP = _Eng("sy"), _Eng("sc"), _Eng("ve"), _Eng("te"), _Eng("gp")

    def inc(instr, s, n=1):
        instr.then_inc(sems[s], n)
        CN[s] += n
        return CN[s]

    def mark(key, s):
        name = REC[key][-1][0]
        assert name not in ("wait_ge",), name

        REC[key][-1][3].incs.append((sems[s], 1))
        CN[s] += 1
        return CN[s]

    def vemark():
        return mark("ve", "ve")

    def vsync():
        VE.drain()
        return mark("ve", "ve")

    def temark():
        return mark("te", "te")

    def w(eng, s, val=None):
        eng.wait_ge(sems[s], CN[s] if val is None else val)

    def barrier():
        for s in DMA_FAM["sy"]:
            SY.wait_ge(sems[s], CN[s])
        inc(SY.nop(), "bar")
        for s in DMA_FAM["tr"]:
            SC.wait_ge(sems[s], CN[s])
        inc(SC.nop(), "bar")
        for s in DMA_FAM["gp"]:
            GP.wait_ge(sems[s], CN[s])
        inc(GP.nop(), "bar")
        inc(TE.nop(), "bar")
        inc(VE.nop(), "bar")
        tgt = CN["bar"]
        for eng in (TE, VE, SC, GP, SY):
            eng.wait_ge(sems["bar"], tgt)

    def _finish():
        def _replay(key):
            def f(eng):
                for (name, a, kw, p) in REC[key]:
                    if key in ("ve", "te", "sc") and name == "nop" and p.incs:
                        eng.drain()
                    r = getattr(eng, name)(*a, **kw)
                    for (sem, n) in p.incs:
                        r.then_inc(sem, n)
            return f
        block.sync(_replay("sy"))
        block.scalar(_replay("sc"))
        block.vector(_replay("ve"))
        block.tensor(_replay("te"))
        block.gpsimd(_replay("gp"))
        return nc, es

    dsem("gpc", "gp")
    inc(GP.dma_start(out=wkvbf_d[:, :], in_=qkvw_d[:, :]), "gpc", 16)
    inc(GP.dma_start(out=wpjbf_d[:, :], in_=pjw_d[:, :]), "gpc", 16)
    gp_casts = dict(CN)

    VE.memset(ones2[:, :], 1.0)
    VE.memset(epsc[:, :], 1e-5)
    VE.memset(onesPb[:, :], 1.0)
    VE.memset(ind1e[0:64, 0:1], 1.0)
    VE.memset(ind1e[64:128, 0:1], 0.0)
    VE.memset(ind1o[0:64, 0:1], 0.0)
    VE.memset(ind1o[64:128, 0:1], 1.0)
    VE.memset(onetmp[:, :], 1.0)
    VE.drain()
    VE.tensor_scalar(magic[:, :], onetmp[:, :], float(2.0 ** -30), 1.0,
                     OP.add, OP.subtract)
    VE.drain()
    VE.tensor_scalar(magic[:, :], magic[:, :], CBIG, MAGIC32,
                     OP.mult, OP.add)
    VE.drain()

    p0 = contextlib.ExitStack()
    qb32 = p0.enter_context(nc.sbuf_tensor("qb32", [1, 3 * D], F32))
    hl16 = p0.enter_context(nc.sbuf_tensor("hl16", [1, 3 * D], BF16))
    hl32 = p0.enter_context(nc.sbuf_tensor("hl32", [1, 3 * D], F32))
    lo32 = p0.enter_context(nc.sbuf_tensor("lo32", [1, 3 * D], F32))
    lo16 = p0.enter_context(nc.sbuf_tensor("lo16", [1, 3 * D], BF16))
    vb32 = p0.enter_context(nc.sbuf_tensor("vb32", [1, 1040], F32))

    dsem("bia", "sy")
    inc(SY.dma_start(out=qb32[:, :], in_=qkvb_d.ap().rearrange("(a d) -> a d", a=1)), "bia", 16)
    inc(SY.dma_start(out=cl32[:, 0:1], in_=clx_d.ap().rearrange("(a d) -> a d", a=1)), "bia", 16)
    inc(SY.dma_start(out=cl32[:, 1:2], in_=cld_d.ap().rearrange("(a d) -> a d", a=1)), "bia", 16)
    ld_bias = CN["bia"]

    def hilo_row(src_ap, n, dst, dst_off):
        w(VE, "gpb") if "gpb" in sems else None
        VE.tensor_copy(hl16[:, 0:n], src_ap)
        VE.drain()
        VE.tensor_copy(hl32[:, 0:n], hl16[:, 0:n])
        VE.drain()
        VE.tensor_tensor(lo32[:, 0:n], src_ap, hl32[:, 0:n], OP.subtract)
        VE.drain()
        VE.tensor_copy(lo16[:, 0:n], lo32[:, 0:n])
        v = vsync()
        dsem("gpb", "gp")
        w(GP, "ve", v)
        inc(GP.dma_start(out=hlscr_d[0:1, 0:n], in_=hl16[:, 0:n]), "gpb", 16)
        inc(GP.dma_start(out=hlscr_d[1:2, 0:n], in_=lo16[:, 0:n]), "gpb", 16)
        w(GP, "gpb")
        inc(GP.dma_start(out=dst[0:2, dst_off:dst_off + n], in_=hlscr_d[0:2, 0:n]), "gpb", 16)

    w(VE, "bia", ld_bias)
    hilo_row(qb32[:, 0:2 * D], 2 * D, bhlqk, 0)
    VE.memset(vb32[:, :], 0.0)
    VE.tensor_copy(bass.AP(vb32, 0, [[1040, 1], [65, 16], [1, 64]]),
                   bass.AP(qb32, 2 * D, [[3 * D, 1], [64, 16], [1, 64]]))
    VE.memset(bass.AP(vb32, 64, [[1040, 1], [65, 16], [1, 1]]), 1.0)
    VE.drain()
    hilo_row(vb32[:, 0:1040], 1040, vbp, 0)
    pb32 = qb32
    pv = CN["ve"]
    w(SY, "ve", pv)
    w(SY, "gp")
    inc(SY.dma_start(out=pb32[:, 0:D], in_=pjb_d.ap().rearrange("(a d) -> a d", a=1)), "bia", 16)
    w(VE, "bia")
    hilo_row(pb32[:, 0:D], D, pbhl, 0)

    bias_gp = CN["gp"]
    barrier()
    p0.close()

    with contextlib.ExitStack() as p1:
        def sb1(name, shape, dtype):
            return p1.enter_context(nc.sbuf_tensor(name, shape, dtype))
        xb = sb1("xb", [128, 3, 4096], F32)
        sqb = sb1("sqb", [128, 3, 4096], F32)
        xn32b = sb1("xn32b", [128, 3, 4096], F32)
        xn16b = sb1("xn16b", [128, 3, 4096], BF16)
        st_sum = sb1("st_sum", [128, 3, 4], F32)
        st_sq = sb1("st_sq", [128, 3, 4], F32)
        st_m = sb1("st_m", [128, 3, 4], F32)
        st_e = sb1("st_e", [128, 3, 4], F32)
        st_v = sb1("st_v", [128, 3, 4], F32)
        st_sd = sb1("st_sd", [128, 3, 4], F32)
        st_r = sb1("st_r", [128, 3, 4], F32)
        st_n = sb1("st_n", [128, 3, 4], F32)

        slot_tok = [[], [], []]
        spill_tok = [[], [], []]
        x_rr = x_d.ap().rearrange("(b m p) d -> b p m d", p=128, m=4)
        xn32_rr = xn32_d.ap().rearrange("(b m p) d -> b p m d", p=128, m=4)
        xnbf_rr = xnbf_d.ap().rearrange("(b m p) d -> b p m d", p=128, m=4)
        stv = {}

        def p1_a(b):
            sl = b % 3
            for (s, v) in slot_tok[sl]:
                SY.wait_ge(sems[s], v)
            slot_tok[sl] = []
            xs_s = dsem(f"x{sl}", "sy")
            ldv = inc(SY.dma_start(out=bass.AP(xb, sl * 4096, [[12288, 128], [1024, 4], [1, 1024]]),
                                   in_=x_rr[b]), xs_s, 16)
            w(VE, xs_s, ldv)
            VE.tensor_reduce(st_sum[:, sl, :],
                             bass.AP(xb, sl * 4096, [[12288, 128], [1024, 4], [1, 1024]]),
                             AX, OP.add)
            w(SC, xs_s, ldv)
            sqv = inc(SC.activation(sqb[:, sl, :], xb[:, sl, :], ACT.Square), "ac")
            w(VE, "ac", sqv)
            VE.tensor_reduce(st_sq[:, sl, :],
                             bass.AP(sqb, sl * 4096, [[12288, 128], [1024, 4], [1, 1024]]),
                             AX, OP.add)
            VE.drain()
            VE.tensor_scalar(st_m[:, sl, :], st_sum[:, sl, :], 1.0 / D, None, OP.mult, OP.bypass)
            VE.tensor_scalar(st_e[:, sl, :], st_sq[:, sl, :], 1.0 / D, None, OP.mult, OP.bypass)
            VE.drain()
            VE.tensor_tensor(st_v[:, sl, :], st_m[:, sl, :], st_m[:, sl, :], OP.mult)
            VE.drain()
            VE.tensor_tensor(st_v[:, sl, :], st_e[:, sl, :], st_v[:, sl, :], OP.subtract)
            vv = vsync()
            w(SC, "ve", vv)
            stv[b] = inc(SC.activation(st_sd[:, sl, :], st_v[:, sl, :], ACT.Sqrt,
                                       bias=epsc[:, 0:1]), "ac")

        def p1_b(b):
            sl = b % 3
            sdv = stv.pop(b)
            w(VE, "ac", sdv)
            VE.reciprocal(st_r[:, sl, :], st_sd[:, sl, :])
            VE.drain()
            VE.tensor_tensor(st_n[:, sl, :], st_m[:, sl, :], st_r[:, sl, :], OP.mult)
            VE.drain()
            VE.tensor_scalar(st_n[:, sl, :], st_n[:, sl, :], -1.0, None, OP.mult, OP.bypass)
            VE.drain()
            for (s, v) in spill_tok[sl]:
                VE.wait_ge(sems[s], v)
            spill_tok[sl] = []
            for mm in range(4):
                VE.tensor_scalar(xn32b[:, sl, 1024 * mm:1024 * mm + 1024],
                                 xb[:, sl, 1024 * mm:1024 * mm + 1024],
                                 st_r[:, sl, mm:mm + 1], st_n[:, sl, mm:mm + 1],
                                 OP.mult, OP.add)
            xnv = vsync()
            VE.tensor_reduce(xmg[:, 4 * b:4 * b + 4],
                             bass.AP(xn32b, sl * 4096, [[12288, 128], [1024, 4], [1, 1024]]),
                             AX, OP.max, apply_absolute_value=True)
            xmv = vemark()
            w(SC, "ve", xnv)
            cav = inc(SC.activation(xn16b[:, sl, :], xn32b[:, sl, :], ACT.Copy), "ac")
            sp32 = dsem(f"sp32_{sl}", "sy")
            sp16 = dsem(f"sp16_{sl}", "sy")
            w(SY, "ve", xnv)
            s1 = inc(SY.dma_start(out=xn32_rr[b],
                                  in_=bass.AP(xn32b, sl * 4096, [[12288, 128], [1024, 4], [1, 1024]])),
                     sp32, 16)
            w(SY, "ac", cav)
            s2 = inc(SY.dma_start(out=xnbf_rr[b],
                                  in_=bass.AP(xn16b, sl * 4096, [[12288, 128], [1024, 4], [1, 1024]])),
                     sp16, 16)
            slot_tok[sl] = [(sp16, s2), ("ve", xmv), ("ac", cav)]
            spill_tok[sl] = [(sp32, s1), (sp16, s2)]

        p1_a(0)
        p1_a(1)
        for b in range(16):
            if b + 2 < 16:
                p1_a(b + 2)
            p1_b(b)
        barrier()
        p1_done = dict(CN)
        if stop_after == "p1":
            barrier()
            return _finish()

    def partition_max_to_scalar(src_col, out11):
        dsem("gpe", "gp")
        v0 = vemark()
        w(GP, "ve", v0)
        inc(GP.dma_start(out=red_d[:], in_=src_col), "gpe", 16)
        w(GP, "gpe")
        inc(GP.dma_start(out=trow[:, :], in_=red_d.ap().rearrange("(a d) -> a d", a=1)), "gpe", 16)
        w(VE, "gpe")
        VE.tensor_reduce(out11, trow[:, :], AX, OP.max)
        VE.drain()

    def bcast128(src11_f32, dst_col, tmp_off):
        VE.tensor_copy(tinyb[:, tmp_off:tmp_off + 1], src11_f32)
        VE.drain()
        VE.tensor_copy(tiny[:, tmp_off:tmp_off + 1], tinyb[:, tmp_off:tmp_off + 1])
        VE.drain()
        VE.tensor_tensor(tiny[:, tmp_off + 1:tmp_off + 2], src11_f32,
                         tiny[:, tmp_off:tmp_off + 1], OP.subtract)
        VE.drain()
        VE.tensor_copy(tinyb[:, tmp_off + 1:tmp_off + 2], tiny[:, tmp_off + 1:tmp_off + 2])
        v1 = vsync()
        w(TE, "ve", v1)
        TE.matmul(ps_xA[0:128, 0:1], ones2[0:1, 0:128], tinyb[:, tmp_off:tmp_off + 1],
                  start=True, stop=False)
        t1 = inc(TE.matmul(ps_xA[0:128, 0:1], ones2[0:1, 0:128],
                           tinyb[:, tmp_off + 1:tmp_off + 2], start=False, stop=True), "te")
        w(VE, "te", t1)
        VE.tensor_copy(dst_col, ps_xA[0:128, 0:1])
        vd = vemark()
        return vd

    VE.tensor_reduce(onetmp[:, :], xmg[:, :], AX, OP.max)
    VE.drain()
    partition_max_to_scalar(onetmp[:, 0:1], tiny[:, 8:9])
    v0 = vemark()
    w(GP, "ve", v0)
    inc(GP.dma_start(out=cc1i_d[:, :], in_=tiny[:, 8:9]), "gpe", 16)
    w(GP, "gpe")
    ccv = inc(GP.collective_compute("AllReduce", OP.max,
                                    replica_groups=[list(range(NCORES))],
                                    ins=[cc1i_d.ap()], outs=[cc1o_d.ap()]), "cc")
    w(GP, "cc", ccv)
    g1 = inc(GP.dma_start(out=tiny[:, 9:10], in_=cc1o_d[:, :]), "gpe", 16)
    w(SC, "bia", ld_bias)
    sgv = inc(SC.activation(tiny[:, 10:11], cl32[:, 0:1], ACT.Sigmoid), "ac")
    w(VE, "gpe", g1)
    w(VE, "ac", sgv)
    VE.tensor_tensor(tiny[:, 11:12], tiny[:, 9:10], tiny[:, 10:11], OP.mult)
    VE.drain()
    VE.tensor_scalar(tiny[:, 11:12], tiny[:, 11:12], 1.0 / QMAX, None, OP.mult, OP.bypass)
    VE.drain()
    VE.reciprocal(tiny[:, 12:13], tiny[:, 11:12])
    VE.drain()
    bcast128(tiny[:, 11:12], sx_bc[:, 0:1], 0)
    bcast128(tiny[:, 12:13], rsx_bc[:, 0:1], 2)
    ex1_done = dict(CN)
    if stop_after == "ex1":
        barrier()
        return _finish()

    with contextlib.ExitStack() as p2:
        def sb2(name, shape, dtype):
            return p2.enter_context(nc.sbuf_tensor(name, shape, dtype))
        tcos = sb2("tcos_s", [128, S], BF16)
        tsin = sb2("tsin_s", [128, S], BF16)
        permt = sb2("permt", [128, 128], BF16)
        trit = sb2("trit", [128, 128], BF16)
        wqkvst = sb2("wqkvst", [128, 2, DCH, 128], BF16)
        wvp = sb2("wvp", [128, DCH, 1040], BF16)
        vstage = sb2("vstage", [128, D], BF16)
        xnf = sb2("xnf", [128, DCH, S], BF16)
        qro = sb2("qro", [128, DCH, S], BF16)
        kro = sb2("kro", [128, DCH, S], BF16)
        vt = sb2("vt", [128, 16, 1040], BF16)
        PT = sb2("PTt", [128, 4, 1024], BF16)
        qraw = sb2("qraw", [128, 2, 512], BF16)
        rsw = sb2("rsw", [128, 2, 512], BF16)
        tmpA = sb2("tmpA", [128, 512], BF16)
        tmpC = sb2("tmpC", [128, 512], BF16)
        nqrow = sb2("nqrow", [1, 16], F32)
        nkrow = sb2("nkrow", [1, 16], F32)
        nrm8 = sb2("nrm8", [1, 8], F32)
        brow = sb2("brow", [1, 16], F32)
        browb = sb2("browb", [1, 16], BF16)
        negbc = sb2("negbc", [128, 16], F32)
        recf = sb2("recf", [65, 1024], F32)
        rh16 = sb2("rh16", [65, 1024], BF16)
        atb = [sb2("atb0", [128, 512], BF16), sb2("atb1", [128, 512], BF16)]
        wqt = sb2("wqt", [128, 1, 1408], F32)
        wqo = sb2("wqo", [128, 1, 1408], BF16)
        wqr = sb2("wqr", [128, 1, 4], F32)
        nrm4 = sb2("nrm4", [2, 4], F32)

        dsem("cst", "sy")
        for t_, d_ in ((tcos, tcos_d), (tsin, tsin_d),
                       (permt, perm_d), (trit, tri_d)):
            inc(SY.dma_start(out=t_[:, :], in_=d_[:, :]), "cst", 16)
        ld_const2 = CN["cst"]

        w(SC, "gpc", gp_casts["gpc"])
        VE.memset(wvp[:, :, :], 0.0)
        wvp_ve0 = vemark()
        for c in range(DCH):
            w(SC, "ve", wvp_ve0)
            for (s, v) in [("ve", CN["ve"])] if c > 0 else []:
                SC.wait_ge(sems[s], v)
            tv = inc(SC.dma_start_transpose(vstage[:, :],
                                            wkvbf_d[2 * D:3 * D, 128 * c:128 * c + 128]),
                     "tr", 16)
            w(VE, "tr", tv)
            VE.tensor_copy(bass.AP(wvp, c * 1040, [[DCH * 1040, 128], [65, 16], [1, 64]]),
                           bass.AP(vstage, 0, [[D, 128], [64, 16], [1, 64]]))
            vsync()
        wvp_v = CN["ve"]

        dsem("wql", "sy")
        dsem("wqs", "sy")
        wq_toks = [[]]

        def quant_weights(src_d, dst_d, sc_tile, monly, width, dequant=False):
            nch = monly + 1
            toks = wq_toks
            halves = [(0, width)] if width <= 1408 else [(0, width // 2), (width // 2, width)]
            src_rr = src_d.ap().rearrange("(m p) d -> m p d", p=128)
            dst_rr = dst_d.ap().rearrange("(m p) d -> m p d", p=128)
            for m in (monly,):
                sl = 0
                lvs = []
                red_ve = 0
                for hi_, (a, b) in enumerate(halves):
                    for (s, v) in toks[sl]:
                        SY.wait_ge(sems[s], v)
                    toks[sl] = []
                    w(SY, "ve", red_ve)
                    lv = inc(SY.dma_start(out=wqt[:, sl, 0:b - a], in_=src_rr[m][:, a:b]),
                             "wql", 16)
                    lvs.append(lv)
                    w(VE, "wql", lv)
                    VE.tensor_reduce(wqr[:, sl, hi_:hi_ + 1], wqt[:, sl, 0:b - a], AX,
                                     OP.max, apply_absolute_value=True)
                    VE.drain()
                    red_ve = vemark()
                    if hi_ == len(halves) - 1:
                        VE.tensor_reduce(wqr[:, sl, 2:3], wqr[:, sl, 0:len(halves)], AX, OP.max)
                        VE.drain()
                        VE.tensor_scalar(sc_tile[:, m:m + 1], wqr[:, sl, 2:3], 1.0 / QMAX,
                                         None, OP.mult, OP.bypass)
                        VE.drain()
                        VE.reciprocal(wqr[:, sl, 3:4], sc_tile[:, m:m + 1])
                        VE.drain()
                if len(halves) == 1:
                    VE.tensor_scalar(wqt[:, sl, 0:width], wqt[:, sl, 0:width],
                                     wqr[:, sl, 3:4], magic[:, 0:1], OP.mult, OP.add)
                    VE.drain()
                    if dequant:
                        VE.tensor_scalar(wqt[:, sl, 0:width], wqt[:, sl, 0:width],
                                         magic[:, 0:1], sc_tile[:, m:m + 1],
                                         OP.subtract, OP.mult)
                        VE.drain()
                        VE.tensor_copy(wqo[:, sl, 0:width], wqt[:, sl, 0:width])
                    else:
                        VE.tensor_scalar(wqo[:, sl, 0:width], wqt[:, sl, 0:width],
                                         magic[:, 0:1], None, OP.subtract, OP.bypass)
                    qv = vsync()
                    w(SY, "ve", qv)
                    sv = inc(SY.dma_start(out=dst_rr[m], in_=wqo[:, sl, 0:width]), "wqs", 16)
                    toks[sl] = [("wqs", sv)]
                else:
                    prev_sv = None
                    for hi_ in (1, 0):
                        a, b = halves[hi_]
                        if hi_ == 0:
                            if prev_sv is not None:
                                SY.wait_ge(sems["wqs"], prev_sv)
                            lv = inc(SY.dma_start(out=wqt[:, sl, 0:b - a],
                                                  in_=src_rr[m][:, a:b]), "wql", 16)
                            w(VE, "wql", lv)
                        VE.tensor_scalar(wqt[:, sl, 0:b - a], wqt[:, sl, 0:b - a],
                                         wqr[:, sl, 3:4], magic[:, 0:1], OP.mult, OP.add)
                        VE.drain()
                        if dequant:
                            VE.tensor_scalar(wqt[:, sl, 0:b - a], wqt[:, sl, 0:b - a],
                                             magic[:, 0:1], sc_tile[:, m:m + 1],
                                             OP.subtract, OP.mult)
                            VE.drain()
                            VE.tensor_copy(wqo[:, sl, 0:b - a], wqt[:, sl, 0:b - a])
                        else:
                            VE.tensor_scalar(wqo[:, sl, 0:b - a], wqt[:, sl, 0:b - a],
                                             magic[:, 0:1], None, OP.subtract, OP.bypass)
                        qv = vsync()
                        w(SY, "ve", qv)
                        sv = inc(SY.dma_start(out=dst_rr[m][:, a:b], in_=wqo[:, sl, 0:b - a]),
                                 "wqs", 16)
                        prev_sv = sv
                        toks[sl] = [("wqs", sv)]

        wq_tasks = ([(wg_d, wgi_d, wgsc, m, D, False) for m in range(ICH)]
                    + [(wu_d, wui_d, wusc, m, D, False) for m in range(ICH)]
                    + [(wd_d, wdi_d, wdsc, m, IFF, True) for m in range(DCH)])


        def wq_emit(n):
            for _ in range(n):
                if not wq_tasks:
                    return
                sd, dd, sct, m, wdt, dq = wq_tasks.pop(0)
                quant_weights(sd, dd, sct, m, wdt, dequant=dq)

        xnbf_mat = xnbf_d
        flash_idx = 0
        xnf_tok = []
        qk_ps = [ps_avA, ps_avB]
        qk_tok = [[], []]
        rs_ps = [ps_xA, ps_xB]
        rs_tok = [[("ve", ex1_done["ve"])], []]
        qraw_tok = [[], []]
        pt_tok = [[], [], [], []]
        av_tok = [[], []]
        sx_tok = [[], []]
        at_tok = [[], []]
        prev_seq = None
        for s_i in range(NSEQ):
            t0 = s_i * S
            if prev_seq is not None:
                w(VE, "te", prev_seq["te"])
                w(VE, "ac", prev_seq["ac"])
            for (s, v) in xnf_tok:
                SC.wait_ge(sems[s], v)
            xnf_tok = []
            for s_ in ("sp16_0", "sp16_1"):
                SC.wait_ge(sems[s_], p1_done.get(s_, CN[s_]))
            for c in range(DCH):
                inc(SC.dma_start_transpose(xnf[:, c, :],
                                           xnbf_mat[t0:t0 + S, 128 * c:128 * c + 128]),
                    "tr", 16)
            xnf_ld = CN["tr"]

            wst_tok = [[], []]
            pend = None

            def rope_flush():
                nonlocal pend
                if pend is None:
                    return
                pi, dst, dch, cols, ct, st_, ev = pend
                pend = None
                w(TE, "ac", ev)
                rp = rs_ps[pi]
                for (s, v) in rs_tok[pi]:
                    TE.wait_ge(sems[s], v)
                rs_tok[pi] = []
                rv = inc(TE.matmul(rp[:, 0:512], permt[:, :], qraw[:, pi, :],
                                   start=True, stop=True), "te")
                w(VE, "te", rv)
                VE.tensor_copy(rsw[:, pi, :], rp[:, 0:512])
                rcv = vemark()
                rs_tok[pi].append(("ve", rcv))
                VE.tensor_tensor(tmpA[:, :], qraw[:, pi, :], ct[:, cols], OP.mult)
                VE.tensor_tensor(tmpC[:, :], rsw[:, pi, :], st_[:, cols], OP.mult)
                VE.tensor_tensor(dst[:, dch, cols], tmpA[:, :], tmpC[:, :], OP.add)
                dv = vsync()
                qk_tok[pi].append(("ve", dv))
                qraw_tok[pi].append(("ve", dv))

            for m in range(16):
                dst = qro if m < 8 else kro
                dch = m if m < 8 else m - 8
                ct, st_ = tcos, tsin
                msl = m % 2
                for (s, v) in wst_tok[msl]:
                    SC.wait_ge(sems[s], v)
                wst_tok[msl] = []
                wtv = inc(SC.dma_start_transpose(wqkvst[:, msl, :, :],
                                                 wkvbf_d[128 * m:128 * m + 128, :]),
                          "tr", 16)
                for tc in range(4):
                    pi = (m * 4 + tc) % 2
                    ps = qk_ps[pi]
                    w(TE, "tr", max(xnf_ld, wtv))
                    w(TE, "cst", ld_const2)
                    w(TE, "gp", bias_gp)
                    for (s, v) in qk_tok[pi]:
                        TE.wait_ge(sems[s], v)
                    qk_tok[pi] = []
                    for d in range(DCH):
                        TE.matmul(ps[:, 0:512], wqkvst[:, msl, d, :],
                                  xnf[:, d, 512 * tc:512 * tc + 512],
                                  start=(d == 0), stop=False)
                    mv = inc(TE.matmul(ps[:, 0:512], bhlqk[0:2, 128 * m:128 * m + 128],
                                       ones2[0:2, 0:512], start=False, stop=True), "te")
                    w(SC, "te", mv)
                    for (s, v) in qraw_tok[pi]:
                        SC.wait_ge(sems[s], v)
                    qraw_tok[pi] = []
                    ev = inc(SC.activation(qraw[:, pi, :], ps[:, 0:512], ACT.Copy), "ac")
                    qk_tok[pi].append(("ac", ev))
                    if tc == 3:
                        wst_tok[msl].append(("te", CN["te"]))
                    rope_flush()
                    pend = (pi, dst, dch, slice(512 * tc, 512 * tc + 512), ct, st_, ev)
            rope_flush()
            rope_done_v = CN["ve"]

            w(VE, "ve", wvp_v)
            vcnt = 0
            for tt in range(16):
                for (c0, c1) in ((0, 512), (512, 1024), (1024, 1040)):
                    pi = vcnt % 2
                    vcnt += 1
                    ps = qk_ps[pi]
                    for (s, v) in qk_tok[pi]:
                        TE.wait_ge(sems[s], v)
                    qk_tok[pi] = []
                    w(TE, "ve", wvp_v)
                    for d in range(DCH):
                        TE.matmul(ps[:, 0:c1 - c0], xnf[:, d, 128 * tt:128 * tt + 128],
                                  wvp[:, d, c0:c1], start=(d == 0), stop=False)
                    mv = inc(TE.matmul(ps[:, 0:c1 - c0], ones2[0:2, 0:128],
                                       vbp[0:2, c0:c1], start=False, stop=True), "te")
                    w(SC, "te", mv)
                    ev = inc(SC.activation(vt[:, tt, c0:c1], ps[:, 0:c1 - c0], ACT.Copy), "ac")
                    qk_tok[pi].append(("ac", ev))
            v_done = CN["ac"]

            sq_te = 0
            for (src, dstn) in ((qro, nqrow), (kro, nkrow)):
                for c in range(DCH):
                    w(VE, "ve", rope_done_v)
                    w(VE, "te", sq_te)
                    sqbf = bass.AP(PT, 0, [[4096, 128], [1, 2048]])
                    VE.tensor_tensor(sqbf, src[:, c, :], src[:, c, :], OP.mult)
                    sqv = vsync()
                    w(TE, "ve", sqv)
                    for tc in range(4):
                        for (s, v) in sx_tok[0]:
                            TE.wait_ge(sems[s], v)
                        sx_tok[0] = []
                        sq_sl = bass.AP(PT, 512 * tc, [[4096, 128], [1, 512]])
                        TE.matmul(ps_sA[0:1, 0:512], ind1e[:, 0:1],
                                  sq_sl, start=True, stop=True)
                        nv = inc(TE.matmul(ps_sA[0:1, 512:1024], ind1o[:, 0:1],
                                           sq_sl, start=True, stop=True), "te")
                        sq_te = nv
                        w(VE, "te", nv)
                        VE.tensor_reduce(nrm8[0:1, tc:tc + 1], ps_sA[0:1, 0:512], AX, OP.max)
                        VE.tensor_reduce(nrm8[0:1, 4 + tc:5 + tc], ps_sA[0:1, 512:1024], AX, OP.max)
                        rv = vemark()
                        sx_tok[0].append(("ve", rv))
                    VE.drain()
                    VE.tensor_reduce(dstn[0:1, 2 * c:2 * c + 1], nrm8[0:1, 0:4], AX, OP.max)
                    VE.tensor_reduce(dstn[0:1, 2 * c + 1:2 * c + 2], nrm8[0:1, 4:8], AX, OP.max)
                    VE.drain()
            VE.tensor_tensor(brow[:, :], nqrow[:, :], nkrow[:, :], OP.mult)
            nbv = vemark()
            w(SC, "ve", nbv)
            nsq = inc(SC.activation(brow[:, :], brow[:, :], ACT.Sqrt, bias=0.0), "ac")
            w(VE, "ac", nsq)
            VE.tensor_scalar(brow[:, :], brow[:, :], -0.125, None, OP.mult, OP.bypass)
            VE.drain()
            VE.tensor_copy(browb[:, :], brow[:, :])
            bbv = vsync()
            w(TE, "ve", bbv)
            for (s, v) in rs_tok[0]:
                TE.wait_ge(sems[s], v)
            rs_tok[0] = []
            nbm = inc(TE.matmul(ps_xA[0:128, 0:16], onesPb[0:1, 0:128], browb[0:1, 0:16],
                                start=True, stop=True), "te")
            w(VE, "te", nbm)
            VE.tensor_copy(negbc[:, :], ps_xA[0:128, 0:16])
            nbcv = vsync()
            rs_tok[0].append(("ve", nbcv))

            if stop_after == "rope0" and s_i == 0:
                w(SY, "ve", nbcv)
                for (t_, d_) in ((xnf, dxnf_d), (qro, dqro_d), (kro, dkro_d), (vt, dvt_d)):
                    inc(SY.dma_start(out=d_[:, :, :], in_=t_[:, :, :]), "sy", 16)
                inc(SY.dma_start(out=dnegb_d[:, :], in_=negbc[:, :]), "sy", 16)
                barrier()
                return _finish()
            w(SC, "ve", nbcv)
            ps_s = [ps_sA, ps_sB]
            pend_tails = []

            def tail_flush():
                while pend_tails:
                    ai, av, rxp, h_, ch_, r0_, r1_, qr_, t0_, asl, rhv = pend_tails.pop(0)
                    w(TE, "ve", rhv)
                    for (s, v) in rs_tok[ai]:
                        TE.wait_ge(sems[s], v)
                    rs_tok[ai] = []
                    rbm = inc(TE.matmul(rxp[0:64, 0:512], onesPb[64:65, 0:64],
                                        rh16[64:65, 512 * ai:512 * ai + 512],
                                        start=True, stop=True), "te")
                    w(VE, "te", rbm)
                    for (s, v) in at_tok[asl]:
                        VE.wait_ge(sems[s], v)
                    at_tok[asl] = []
                    VE.tensor_copy(recf[0:64, 512 * ai:512 * ai + 512], rxp[0:64, 0:512])
                    rcv = vemark()
                    rs_tok[ai].append(("ve", rcv))
                    VE.tensor_tensor(atb[asl][0:64, 0:512], av[0:64, 0:512],
                                     recf[0:64, 512 * ai:512 * ai + 512], OP.mult)
                    ntv = vsync()
                    av_tok[ai].append(("ve", ntv))
                    ab_s = dsem(f"atb{asl}", "sy")
                    w(SY, "ve", ntv)
                    spv = inc(SY.dma_start(out=attnb_d[ch_, r0_:r1_, t0_ + 512 * qr_:t0_ + 512 * qr_ + 512],
                                           in_=atb[asl][0:64, 0:512]), ab_s, 16)
                    at_tok[asl].append((ab_s, spv))

            class U:
                pass

            for chp in range(8):
                for qr in range(4):
                    qc = slice(512 * qr, 512 * qr + 512)
                    kts = list(range(4 * qr + 4))
                    groups = [kts[i:i + 2] for i in range(0, len(kts), 2)]
                    G = len(groups)
                    units = []
                    for par in range(2):
                        u = U()
                        u.h = 2 * chp + par
                        u.par = par
                        u.r0, u.r1 = 64 * par, 64 * par + 64
                        u.ai = par
                        u.av = qk_ps[par]
                        u.rxp = rs_ps[par]
                        u.sps = ps_s[par]
                        u.first_av = True
                        u.smv = [None] * G
                        u.epv = [None] * G
                        u.mkv = [None] * G
                        units.append(u)

                    def emit_scores(u, g):
                        grp = groups[g]
                        for (s, v) in sx_tok[u.par]:
                            TE.wait_ge(sems[s], v)
                        sx_tok[u.par] = []
                        w(TE, "ve", nbcv)
                        for j, kt in enumerate(grp):
                            mm_ = TE.matmul(u.sps[:, 512 * j:512 * j + 512],
                                            kro[u.r0:u.r1, chp, 128 * kt:128 * kt + 128],
                                            qro[u.r0:u.r1, chp, qc], start=True, stop=True)
                            if j == len(grp) - 1:
                                u.smv[g] = inc(mm_, "te")

                    def emit_exp(u, g):
                        grp = groups[g]
                        w(SC, "te", u.smv[g])
                        for (s, v) in pt_tok[2 * u.par + (g % 2)]:
                            SC.wait_ge(sems[s], v)
                        pt_tok[2 * u.par + (g % 2)] = []
                        pt = bass.AP(PT, (2 * u.par + (g % 2)) * 1024, [[4096, 128], [1, 512 * len(grp)]])
                        u.epv[g] = inc(SC.activation(pt, u.sps[:, 0:512 * len(grp)], ACT.Exp,
                                                     bias=negbc[:, u.h:u.h + 1], scale=0.125), "ac")
                        sx_tok[u.par].append(("ac", u.epv[g]))
                        need_ve = False
                        w(VE, "ac", u.epv[g])
                        base = (2 * u.par + (g % 2)) * 1024
                        for j, kt in enumerate(grp):
                            ja = kt - 4 * qr
                            if ja >= 0:
                                need_ve = True
                                if ja > 0:
                                    VE.memset(bass.AP(PT, base + 512 * j, [[4096, 128], [1, 128 * ja]]), 0.0)
                                mp = bass.AP(PT, base + 512 * j + 128 * ja, [[4096, 128], [1, 128]])
                                VE.tensor_tensor(mp, mp, trit[:, :], OP.mult)
                        u.mkv[g] = vsync() if need_ve else None

                    def emit_av(u, g):
                        grp = groups[g]
                        w(TE, "ac", u.epv[g])
                        if u.mkv[g] is not None:
                            w(TE, "ve", u.mkv[g])
                        w(TE, "ac", v_done)
                        if u.first_av:
                            for (s, v) in av_tok[u.ai]:
                                TE.wait_ge(sems[s], v)
                            av_tok[u.ai] = []
                        base = (2 * u.par + (g % 2)) * 1024
                        for j, kt in enumerate(grp):
                            last = (g == G - 1) and (j == len(grp) - 1)
                            mm_ = TE.matmul(u.av[0:65, 0:512], vt[:, kt, 65 * u.h:65 * u.h + 65],
                                            bass.AP(PT, base + 512 * j, [[4096, 128], [1, 512]]),
                                            start=u.first_av, stop=last)
                            u.first_av = False
                            if j == len(grp) - 1:
                                avv = inc(mm_, "te")
                        pt_tok[2 * u.par + (g % 2)].append(("te", avv))
                        u.avv = avv

                    uA, uB = units
                    emit_scores(uA, 0)
                    emit_scores(uB, 0)
                    tail_flush()
                    for g in range(G):
                        emit_exp(uA, g)
                        if g + 1 < G:
                            emit_scores(uA, g + 1)
                        emit_exp(uB, g)
                        emit_av(uA, g)
                        if g + 1 < G:
                            emit_scores(uB, g + 1)
                        emit_av(uB, g)
                    for u in units:
                        w(VE, "te", u.avv)
                        VE.reciprocal(recf[64:65, 512 * u.par:512 * u.par + 512],
                                      u.av[64:65, 0:512])
                        VE.drain()
                        VE.tensor_copy(rh16[64:65, 512 * u.par:512 * u.par + 512],
                                       recf[64:65, 512 * u.par:512 * u.par + 512])
                        rhv = vsync()
                        asl = u.par
                        pend_tails.append((u.ai, u.av, u.rxp, u.h, chp, u.r0, u.r1,
                                           qr, t0, asl, rhv))
                        flash_idx += 1
            tail_flush()
            qk_tok[0] += av_tok[0]
            qk_tok[1] += av_tok[1]
            av_tok = [[], []]
            xnf_tok = [("te", CN["te"])]
            prev_seq = dict(CN)
            wq_emit(14)
        wq_emit(99)
        wq_done = dict(CN)
        VE.tensor_scalar(scg[:, :], wgsc[:, :], sx_bc[:, 0:1], None, OP.mult, OP.bypass)
        VE.tensor_scalar(scu[:, :], wusc[:, :], sx_bc[:, 0:1], None, OP.mult, OP.bypass)
        scgu_v = vemark()
        p2_done = dict(CN)
        barrier()
        if stop_after == "p2":
            return _finish()

    late = contextlib.ExitStack()
    wpjf = late.enter_context(nc.sbuf_tensor("wpjf", [128, DCH, D], BF16))
    w(SC, "gpc", gp_casts["gpc"])
    for c in range(DCH):
        inc(SC.dma_start_transpose(wpjf[:, c, :], wpjbf_d[:, 128 * c:128 * c + 128]),
            "tr", 16)
    wpjf_tr = CN["tr"]
    with contextlib.ExitStack() as p25:
        def sb25(name, shape, dtype):
            return p25.enter_context(nc.sbuf_tensor(name, shape, dtype))
        xqf = sb25("xqf", [128, DCH, S], BF16)
        wgst = sb25("wgst", [128, 2, DCH, 128], BF16)
        wust = sb25("wust", [128, 2, DCH, 128], BF16)
        qx32 = sb25("qx32", [128, 2, 2048], F32)
        xq16 = sb25("xq16", [128, 2, 2048], BF16)
        gsb = sb25("gsb", [128, 2, 512], F32)
        silb = sb25("silb", [128, 2, 512], F32)
        usb = sb25("usb", [128, 2, 512], F32)
        hbuf = sb25("hbuf", [128, 2, 2048], F32)

        xn32_rr2 = xn32_d.ap().rearrange("(b m p) d -> b p m d", p=128, m=2)
        xqbf_rr = xqbf_d.ap().rearrange("(b m p) d -> b p m d", p=128, m=2)
        toks = [[], []]
        qld = {}

        def quant_load(tb):
            sl = tb % 2
            for (s, v) in toks[sl]:
                SY.wait_ge(sems[s], v)
            toks[sl] = []
            qx_s = dsem(f"qx{sl}", "sy")
            qld[tb] = inc(SY.dma_start(out=bass.AP(qx32, sl * 2048,
                                                   [[4096, 128], [1024, 2], [1, 1024]]),
                                       in_=xn32_rr2[tb]), qx_s, 16)

        def quant_ve(tb):
            sl = tb % 2
            w(VE, f"qx{sl}", qld[tb])
            VE.tensor_scalar(qx32[:, sl, :], qx32[:, sl, :], rsx_bc[:, 0:1], magic[:, 0:1],
                             OP.mult, OP.add)
            VE.tensor_scalar(qx32[:, sl, :], qx32[:, sl, :], magic[:, 0:1], QMAX,
                             OP.subtract, OP.min)
            VE.tensor_scalar(xq16[:, sl, :], qx32[:, sl, :], -QMAX - 1.0, None,
                             OP.max, OP.bypass)
            xqs_s = dsem(f"xqs{sl}", "sy")
            qv = vsync()
            w(SY, "ve", qv)
            sv = inc(SY.dma_start(out=xqbf_rr[tb],
                                  in_=bass.AP(xq16, sl * 2048, [[4096, 128], [1024, 2], [1, 1024]])),
                     xqs_s, 16)
            toks[sl] = [(xqs_s, sv), ("ve", qv)]

        for tb in range(8):
            quant_load(tb)
            quant_ve(tb)
        xq_spg = [(CN["xqs0"], CN["xqs1"])]

        gu_ps = [ps_avA, ps_xA, ps_avB, ps_xB]
        gu_tok = [[], [], [], []]
        wgu_tok = [[], []]
        hb_tok = [[], []]
        xqf_tok = []
        hm_col = 0
        for b4 in range(4):
            tb0 = b4 * S
            for (s, v) in xqf_tok:
                SC.wait_ge(sems[s], v)
            xqf_tok = []
            SC.wait_ge(sems["xqs0"], xq_spg[b4][0])
            SC.wait_ge(sems["xqs1"], xq_spg[b4][1])
            for c in range(DCH):
                inc(SC.dma_start_transpose(xqf[:, c, :],
                                           xqbf_d[tb0:tb0 + S, 128 * c:128 * c + 128]),
                    "tr", 16)
            xqf_ld = CN["tr"]
            def gu_fetch(m):
                msl = m % 2
                for (s, v) in wgu_tok[msl]:
                    SC.wait_ge(sems[s], v)
                wgu_tok[msl] = []
                w(SC, "wqs", wq_done["wqs"])
                inc(SC.dma_start_transpose(wgst[:, msl, :, :],
                                           wgi_d[128 * m:128 * m + 128, :]), "tr", 16)
                return inc(SC.dma_start_transpose(wust[:, msl, :, :],
                                                  wui_d[128 * m:128 * m + 128, :]), "tr", 16)

            if b4 == 0:
                wtv_next = gu_fetch(0)
            for m in range(ICH):
                if b4 < 3:
                    if m < 8:
                        quant_load(8 * (b4 + 1) + m)
                    if 1 <= m <= 8:
                        quant_ve(8 * (b4 + 1) + m - 1)
                    if m == 8:
                        xq_spg.append((CN["xqs0"], CN["xqs1"]))
                hsl = (b4 * ICH + m) % 2
                msl = m % 2
                wtv = wtv_next
                if m + 1 < ICH:
                    wtv_next = gu_fetch(m + 1)
                elif b4 < 3:
                    wtv_next = gu_fetch(0)
                for (s, v) in hb_tok[hsl]:
                    VE.wait_ge(sems[s], v)
                hb_tok[hsl] = []
                for tc in range(4):
                    gi = (2 * tc) % 4
                    ui = gi + 1
                    for (s, v) in gu_tok[gi]:
                        TE.wait_ge(sems[s], v)
                    gu_tok[gi] = []
                    for (s, v) in gu_tok[ui]:
                        TE.wait_ge(sems[s], v)
                    gu_tok[ui] = []
                    w(TE, "tr", max(xqf_ld, wtv))
                    cols = slice(512 * tc, 512 * tc + 512)
                    gv = uv = None
                    for d in range(DCH):
                        mm_ = TE.matmul(gu_ps[gi][:, 0:512], wgst[:, msl, d, :],
                                        xqf[:, d, cols], start=(d == 0), stop=(d == DCH - 1))
                        if d == DCH - 1:
                            gv = inc(mm_, "te")
                    for d in range(DCH):
                        mm_ = TE.matmul(gu_ps[ui][:, 0:512], wust[:, msl, d, :],
                                        xqf[:, d, cols], start=(d == 0), stop=(d == DCH - 1))
                        if d == DCH - 1:
                            uv = inc(mm_, "te")
                    esl = tc % 2
                    w(VE, "te", gv)
                    w(VE, "ve", scgu_v)
                    VE.tensor_scalar(gsb[:, esl, :], gu_ps[gi][:, 0:512], scg[:, m:m + 1],
                                     None, OP.mult, OP.bypass)
                    gsv = vsync()
                    gu_tok[gi].append(("ve", gsv))
                    w(SC, "ve", gsv)
                    slv = inc(SC.activation(silb[:, esl, :], gsb[:, esl, :], ACT.Silu), "ac")
                    w(VE, "te", uv)
                    VE.tensor_scalar(usb[:, esl, :], gu_ps[ui][:, 0:512], scu[:, m:m + 1],
                                     None, OP.mult, OP.bypass)
                    usv = vemark()
                    gu_tok[ui].append(("ve", usv))
                    w(VE, "ac", slv)
                    VE.tensor_tensor(hbuf[:, hsl, cols], silb[:, esl, :], usb[:, esl, :],
                                     OP.mult)
                    hv = vemark()
                VE.tensor_reduce(hmg[:, hm_col:hm_col + 1], hbuf[:, hsl, :], AX, OP.max,
                                 apply_absolute_value=True)
                hmv = vsync()
                hm_col += 1
                hb_s = dsem(f"hb{hsl}", "sy")
                w(SY, "ve", hmv)
                hs = inc(SY.dma_start(out=h32_d[m, :, tb0:tb0 + S], in_=hbuf[:, hsl, :]),
                         hb_s, 16)
                hb_tok[hsl] = [(hb_s, hs)]
                wgu_tok[msl].append(("te", CN["te"]))
            xqf_tok = [("te", CN["te"])]
        p25_done = dict(CN)
        barrier()
        if stop_after == "p25":
            return _finish()

    VE.tensor_reduce(onetmp[:, :], hmg[:, :], AX, OP.max)
    VE.drain()
    partition_max_to_scalar(onetmp[:, 0:1], tiny[:, 8:9])
    v0 = vemark()
    w(GP, "ve", v0)
    inc(GP.dma_start(out=cc2i_d[:, :], in_=tiny[:, 8:9]), "gpe", 16)
    w(GP, "gpe")
    ccv = inc(GP.collective_compute("AllReduce", OP.max,
                                    replica_groups=[list(range(NCORES))],
                                    ins=[cc2i_d.ap()], outs=[cc2o_d.ap()]), "cc")
    w(GP, "cc", ccv)
    g2 = inc(GP.dma_start(out=tiny[:, 9:10], in_=cc2o_d[:, :]), "gpe", 16)
    sgv = inc(SC.activation(tiny[:, 10:11], cl32[:, 1:2], ACT.Sigmoid), "ac")
    w(VE, "gpe", g2)
    w(VE, "ac", sgv)
    VE.tensor_tensor(tiny[:, 11:12], tiny[:, 9:10], tiny[:, 10:11], OP.mult)
    VE.drain()
    VE.tensor_scalar(tiny[:, 11:12], tiny[:, 11:12], 1.0 / QMAX, None, OP.mult, OP.bypass)
    VE.drain()
    VE.reciprocal(tiny[:, 12:13], tiny[:, 11:12])
    VE.drain()
    bcast128(tiny[:, 11:12], sh_bc[:, 0:1], 4)
    bcast128(tiny[:, 12:13], rsh_bc[:, 0:1], 6)

    with contextlib.ExitStack() as p3:
        def sb3(name, shape, dtype):
            return p3.enter_context(nc.sbuf_tensor(name, shape, dtype))
        wdf = sb3("wdf", [128, ICH, D], BF16)
        inth = sb3("inth", [128, 2, ICH, 512], BF16)
        atst = sb3("atst", [128, 2, DCH, 512], BF16)
        hld = sb3("hld", [128, 2, 1024], F32)
        xb3 = sb3("xb3", [128, 2, 4096], F32)
        outb = sb3("outb", [128, 2, 4096], F32)

        w(SC, "wqs", wq_done["wqs"])
        for i in range(ICH):
            inc(SC.dma_start_transpose(wdf[:, i, :], wdi_d[:, 128 * i:128 * i + 128]),
                "tr", 16)
        wdf_tr = CN["tr"]

        x_rr3 = x_d.ap().rearrange("(tc m p) d -> tc p m d", p=128, m=4)
        out_rr = out_d.ap().rearrange("(tc m p) d -> tc p m d", p=128, m=4)
        tc_tok = [[], []]
        hld_tok = [[], []]
        ps3 = [(ps_sA, ps_avA, ps_xA), (ps_sB, ps_avB, ps_xB)]
        ps3_tok = [[], []]
        ob_tok = [[], []]
        stage = {}

        def p3_load(tc):
            sl = tc % 2
            for (s, v) in tc_tok[sl]:
                SY.wait_ge(sems[s], v)
            tc_tok[sl] = []
            ats_s = dsem(f"ats{sl}", "sy")
            xb_s = dsem(f"xb{sl}", "sy")
            for c in range(DCH):
                inc(SY.dma_start(out=atst[:, sl, c, :],
                                 in_=attnb_d[c, :, 512 * tc:512 * tc + 512]), ats_s, 16)
            at_ld = CN[ats_s]
            xl = inc(SY.dma_start(out=bass.AP(xb3, sl * 4096, [[8192, 128], [1024, 4], [1, 1024]]),
                                  in_=x_rr3[tc]), xb_s, 16)
            for i2 in range(0, ICH, 2):
                npair = 2 if i2 + 1 < ICH else 1
                hsl = (i2 // 2) % 2
                for (s, v) in hld_tok[hsl]:
                    SY.wait_ge(sems[s], v)
                hld_tok[hsl] = []
                hl_s = dsem(f"hl{hsl}", "sy")
                lv = inc(SY.dma_start(out=bass.AP(hld, hsl * 1024, [[2048, 128], [1, 512 * npair]]),
                                      in_=bass.AP(h32_d, i2 * 128 * T + 512 * tc,
                                                  [[T, 128], [128 * T, npair], [1, 512]])),
                         hl_s, 16)
                w(VE, hl_s, lv)
                fd = 512 * npair
                hap = bass.AP(hld, hsl * 1024, [[2048, 128], [1, fd]])
                iap = bass.AP(inth, sl * (ICH * 512) + i2 * 512,
                              [[2 * ICH * 512, 128], [1, fd]])
                VE.tensor_scalar(hap, hap, rsh_bc[:, 0:1],
                                 magic[:, 0:1], OP.mult, OP.add)
                VE.tensor_scalar(hap, hap, magic[:, 0:1],
                                 QMAX, OP.subtract, OP.min)
                VE.tensor_scalar(iap, hap, -QMAX - 1.0,
                                 None, OP.max, OP.bypass)
                qv = vsync()
                hld_tok[hsl] = [("ve", qv)]
                tc_tok[sl].append(("ve", qv))
            stage[tc] = (CN["ve"], at_ld, xl, ats_s, xb_s)

        def p3_compute(tc):
            sl = tc % 2
            hq_done, at_ld, xl, ats_s, xb_s = stage.pop(tc)
            for tt in range(4):
                pp = (tc * 4 + tt) % 2
                prj, dn0, dn1 = ps3[pp]
                for (s, v) in ps3_tok[pp]:
                    TE.wait_ge(sems[s], v)
                ps3_tok[pp] = []
                w(TE, ats_s, at_ld)
                w(TE, "ve", hq_done)
                w(TE, "tr", wdf_tr)
                tsl = slice(128 * tt, 128 * tt + 128)
                for cc_i, cols in enumerate((slice(0, 512), slice(512, 1024))):
                    for d in range(DCH):
                        TE.matmul(prj[:, cols], atst[:, sl, d, tsl], wpjf[:, d, cols],
                                  start=(d == 0), stop=False)
                    TE.matmul(prj[:, cols], ones2[0:2, 0:128], pbhl[0:2, cols],
                              start=False, stop=True)
                    dps = dn0 if cc_i == 0 else dn1
                    for i in range(ICH):
                        TE.matmul(dps[:, 0:512], inth[:, sl, i, tsl], wdf[:, i, cols],
                                  start=(i == 0), stop=(i == ICH - 1))
                mv = temark()
                w(VE, "te", mv)
                w(VE, xb_s, xl)
                for (s, v) in (ob_tok[sl] if tt == 0 else []):
                    VE.wait_ge(sems[s], v)
                if tt == 0:
                    ob_tok[sl] = []
                for cc_i in range(2):
                    dps = (dn0, dn1)[cc_i]
                    cols = slice(512 * cc_i, 512 * cc_i + 512)
                    ocol = slice(1024 * tt + 512 * cc_i, 1024 * tt + 512 * cc_i + 512)
                    VE.tensor_scalar(outb[:, sl, ocol], dps[:, 0:512], sh_bc[:, 0:1], None,
                                     OP.mult, OP.bypass)
                    VE.tensor_tensor(outb[:, sl, ocol], outb[:, sl, ocol], prj[:, cols], OP.add)
                    VE.tensor_tensor(outb[:, sl, ocol], outb[:, sl, ocol],
                                     xb3[:, sl, ocol], OP.add)
                ev = vsync()
                ps3_tok[pp].append(("ve", ev))
            ov_s = dsem(f"ov{sl}", "sy")
            w(SY, "ve", CN["ve"])
            ov = inc(SY.dma_start(out=out_rr[tc],
                                  in_=bass.AP(outb, sl * 4096, [[8192, 128], [1024, 4], [1, 1024]])),
                     ov_s, 16)
            tc_tok[sl].append((ov_s, ov))
            tc_tok[sl].append(("te", CN["te"]))
            tc_tok[sl].append(("ve", CN["ve"]))
            ob_tok[sl].append((ov_s, ov))

        p3_load(0)
        for tc in range(16):
            if tc + 1 < 16:
                p3_load(tc + 1)
            p3_compute(tc)
            if stop_after == "p3serial":
                barrier()
        SY.wait_ge(sems["ov0"], CN["ov0"])
        SY.wait_ge(sems["ov1"], CN["ov1"])
    late.close()

    return _finish()


_CACHE = {}


def kernel(L0_attn_x, L0_attn_qkv_w, L0_attn_qkv_b, L0_attn_proj_w, L0_attn_proj_b,
           L0_ffn_clip_x, L0_ffn_clip_down, L0_ffn_w_gate, L0_ffn_w_up, L0_ffn_w_down):
    consts = _host_consts()
    if "nc" not in _CACHE:
        _CACHE["nc"] = build_program()
    nc, es = _CACHE["nc"]
    x = np.ascontiguousarray(np.asarray(L0_attn_x, dtype=np.float32))
    shared = {
        "qkv_w": np.ascontiguousarray(np.asarray(L0_attn_qkv_w, np.float32)),
        "qkv_b": np.ascontiguousarray(np.asarray(L0_attn_qkv_b, np.float32)),
        "proj_w": np.ascontiguousarray(np.asarray(L0_attn_proj_w, np.float32)),
        "proj_b": np.ascontiguousarray(np.asarray(L0_attn_proj_b, np.float32)),
        "clip_x": np.ascontiguousarray(np.asarray(L0_ffn_clip_x, np.float32)),
        "clip_down": np.ascontiguousarray(np.asarray(L0_ffn_clip_down, np.float32)),
        "w_gate": np.ascontiguousarray(np.asarray(L0_ffn_w_gate, np.float32)),
        "w_up": np.ascontiguousarray(np.asarray(L0_ffn_w_up, np.float32)),
        "w_down": np.ascontiguousarray(np.asarray(L0_ffn_w_down, np.float32)),
        **consts,
    }
    in_maps = []
    for c in range(NCORES):
        im = dict(shared)
        im["x"] = np.ascontiguousarray(x[NSEQ * c:NSEQ * (c + 1)].reshape(T, D))
        in_maps.append(im)
    res = run_bass_kernel_spmd(nc, in_maps, core_ids=list(range(NCORES)))
    out = np.concatenate(
        [np.asarray(r["out"], dtype=np.float32).reshape(NSEQ, S, D) for r in res.results],
        axis=0)
    return out


# version 17
# speedup vs baseline: 1.0726x; 1.0073x over previous
import contextlib

import numpy as np
import ml_dtypes

import concourse.bass as bass
import concourse.mybir as mybir
from concourse.alu_op_type import AluOpType as OP
from concourse.bass_utils import run_bass_kernel_spmd

F32 = mybir.dt.float32
BF16 = mybir.dt.bfloat16
ACT = mybir.ActivationFunctionType
AX = mybir.AxisListType.X
BF = ml_dtypes.bfloat16

NCORES = 8
B, S, D = 32, 2048, 1024
NSEQ = B // NCORES
T = NSEQ * S
H, HD = 16, 64
IFF = 2816
ICH = IFF // 128
DCH = D // 128
MAGIC32 = 12582912.0
CBIG = float(np.float32(1.5 * 2**82))
QMAX = 127.0


def _host_consts():
    half = 32
    freqs = 1.0 / 10000.0 ** (np.arange(half, dtype=np.float64) / half)
    pos = np.arange(S, dtype=np.float64)
    ang = pos[None, :] * freqs[:, None]
    cos32 = np.cos(ang)
    sin32 = np.sin(ang)
    r = np.arange(128)
    i_of_r = r % 32
    sgn = np.where((r % 64) < 32, -1.0, 1.0)
    tcos = cos32[i_of_r, :]
    tsin = sin32[i_of_r, :] * sgn[:, None]
    perm = np.zeros((128, 128), dtype=np.float64)
    for blk in range(2):
        b0 = 64 * blk
        perm[b0:b0 + 32, b0 + 32:b0 + 64] = np.eye(32)
        perm[b0 + 32:b0 + 64, b0:b0 + 32] = np.eye(32)
    rr, cc = np.meshgrid(np.arange(128), np.arange(128), indexing="ij")
    tri01 = (rr <= cc).astype(np.float64)
    c = {
        "tcos": tcos, "tsin": tsin,
        "permc": perm, "tri01": tri01,
    }
    return {k: np.asarray(v, dtype=np.float32).astype(BF) for k, v in c.items()}


def build_program(dbg=(), stop_after=None):
    nc = bass.Bass("TRN2", target_bir_lowering=False, debug=False)
    es = contextlib.ExitStack()

    def dram(name, shape, dtype, kind=None):
        if name in dbg:
            kind = "ExternalOutput"
        if kind is None:
            return nc.dram_tensor(name, shape, dtype)
        return nc.dram_tensor(name, shape, dtype, kind=kind)

    x_d = dram("x", [T, D], F32, "ExternalInput")
    qkvw_d = dram("qkv_w", [3 * D, D], F32, "ExternalInput")
    qkvb_d = dram("qkv_b", [3 * D], F32, "ExternalInput")
    pjw_d = dram("proj_w", [D, D], F32, "ExternalInput")
    pjb_d = dram("proj_b", [D], F32, "ExternalInput")
    clx_d = dram("clip_x", [1], F32, "ExternalInput")
    cld_d = dram("clip_down", [1], F32, "ExternalInput")
    wg_d = dram("w_gate", [IFF, D], F32, "ExternalInput")
    wu_d = dram("w_up", [IFF, D], F32, "ExternalInput")
    wd_d = dram("w_down", [D, IFF], F32, "ExternalInput")
    tcos_d = dram("tcos", [128, S], BF16, "ExternalInput")
    tsin_d = dram("tsin", [128, S], BF16, "ExternalInput")
    perm_d = dram("permc", [128, 128], BF16, "ExternalInput")
    tri_d = dram("tri01", [128, 128], BF16, "ExternalInput")
    out_d = dram("out", [T, D], F32, "ExternalOutput")
    xn32_d = dram("xn32", [T, D], F32)
    xnbf_d = dram("xnbf", [T, D], BF16)
    xqbf_d = dram("xqbf", [T, D], BF16)
    wkvbf_d = dram("wkvbf", [3 * D, D], BF16)
    wpjbf_d = dram("wpjbf", [D, D], BF16)
    wgi_d = dram("wgi", [IFF, D], BF16)
    wui_d = dram("wui", [IFF, D], BF16)
    wdi_d = dram("wdi", [D, IFF], BF16)
    h32_d = dram("h32", [ICH, 128, T], F32)
    attnb_d = dram("attnb", [DCH, 128, T], BF16)
    cc1i_d = dram("cc1i", [1, 1], F32)
    cc1o_d = dram("cc1o", [1, 1], F32)
    cc2i_d = dram("cc2i", [1, 1], F32)
    cc2o_d = dram("cc2o", [1, 1], F32)
    red_d = dram("redsc", [128], F32)
    dxnf_d = dram("dxnf", [128, DCH, S], BF16)
    dqro_d = dram("dqro", [128, DCH, S], BF16)
    dkro_d = dram("dkro", [128, DCH, S], BF16)
    dvt_d = dram("dvt", [128, 16, 1040], BF16)
    dnegb_d = dram("dnegb", [128, 16], F32)
    dpt0_d = dram("dpt0", [128, 1024], BF16)
    dpt1_d = dram("dpt1", [128, 1024], BF16)
    dav_d = dram("dav", [128, 512], F32)
    dwdf_d = dram("dwdf", [128, 22, 1024], BF16)
    dwpj_d = dram("dwpj", [128, 8, 1024], BF16)
    dith_d = dram("dith", [128, 22, 512], BF16)
    datst_d = dram("datst", [128, 8, 512], BF16)
    dbias_d = dram("dbias", [2, 4096], BF16)
    dsc_d = dram("dsc", [128, 8], F32)
    drr_d = dram("drr", [65, 512], F32)
    drcf_d = dram("drcf", [64, 512], F32)
    datb_d = dram("datb", [128, 512], BF16)
    hlscr_d = dram("hlscr", [2, 2048], BF16)

    sems = {}
    for s in ("sy", "tr", "gp", "te", "ve", "ac", "cc", "bar"):
        sems[s] = es.enter_context(nc.semaphore("sem_" + s))
    CN = {k: 0 for k in sems}
    DMA_FAM = {"sy": ["sy"], "tr": ["tr"], "gp": ["gp"]}

    def dsem(name, fam):
        if name not in sems:
            sems[name] = es.enter_context(nc.semaphore("sem_" + name))
            CN[name] = 0
            DMA_FAM[fam].append(name)
        return name

    def sb(name, shape, dtype):
        return es.enter_context(nc.sbuf_tensor(name, shape, dtype))

    ones2 = sb("ones2", [2, 512], BF16)
    magic = sb("magict", [128, 1], F32)
    epsc = sb("epsc", [128, 1], F32)
    onetmp = sb("onetmp", [128, 1], F32)
    bhlqk = sb("bhlqk", [2, 2 * D], BF16)
    pbhl = sb("pbhl", [2, D], BF16)
    vbp = sb("vbp", [2, 1040], BF16)
    cl32 = sb("cl32", [1, 2], F32)
    wgsc = sb("wgsc", [128, ICH], F32)
    wusc = sb("wusc", [128, ICH], F32)
    wdsc = sb("wdsc", [128, DCH], F32)
    scg = sb("scg", [128, ICH], F32)
    scu = sb("scu", [128, ICH], F32)
    rsx_bc = sb("rsx_bc", [128, 1], F32)
    sx_bc = sb("sx_bc", [128, 1], F32)
    rsh_bc = sb("rsh_bc", [128, 1], F32)
    sh_bc = sb("sh_bc", [128, 1], F32)
    tiny = sb("tinyt", [1, 16], F32)
    tinyb = sb("tinyb", [1, 16], BF16)
    trow = sb("trowt", [1, 128], F32)
    xmg = sb("xmg", [128, 64], F32)
    hmg = sb("hmg", [128, 88], F32)
    onesPb = sb("onesPb", [128, 128], BF16)
    ind1e = sb("ind1e", [128, 1], BF16)
    ind1o = sb("ind1o", [128, 1], BF16)

    ps_sA = es.enter_context(nc.psum_tensor("ps_sA", [128, 1024], F32))
    ps_sB = es.enter_context(nc.psum_tensor("ps_sB", [128, 1024], F32))
    ps_avA = es.enter_context(nc.psum_tensor("ps_avA", [128, 512], F32))
    ps_avB = es.enter_context(nc.psum_tensor("ps_avB", [128, 512], F32))
    ps_xA = es.enter_context(nc.psum_tensor("ps_xA", [128, 512], F32))
    ps_xB = es.enter_context(nc.psum_tensor("ps_xB", [128, 512], F32))

    block = es.enter_context(nc.Block())
    REC = {k: [] for k in ("sy", "sc", "ve", "te", "gp")}

    class _Pend:
        __slots__ = ("incs",)

        def __init__(self):
            self.incs = []

        def then_inc(self, sem, n=1):
            self.incs.append((sem, n))
            return self

    class _Eng:
        def __init__(self, key):
            self._k = key

        def __getattr__(self, name):
            q = REC[self._k]

            def call(*a, **kw):
                p = _Pend()
                q.append((name, a, kw, p))
                return p
            return call

    SY, SC, VE, TE, GP = _Eng("sy"), _Eng("sc"), _Eng("ve"), _Eng("te"), _Eng("gp")

    def inc(instr, s, n=1):
        instr.then_inc(sems[s], n)
        CN[s] += n
        return CN[s]

    def mark(key, s):
        name = REC[key][-1][0]
        assert name not in ("wait_ge",), name

        REC[key][-1][3].incs.append((sems[s], 1))
        CN[s] += 1
        return CN[s]

    def vemark():
        return mark("ve", "ve")

    def vsync():
        VE.drain()
        return mark("ve", "ve")

    def temark():
        return mark("te", "te")

    def w(eng, s, val=None):
        eng.wait_ge(sems[s], CN[s] if val is None else val)

    def barrier():
        for s in DMA_FAM["sy"]:
            SY.wait_ge(sems[s], CN[s])
        inc(SY.nop(), "bar")
        for s in DMA_FAM["tr"]:
            SC.wait_ge(sems[s], CN[s])
        inc(SC.nop(), "bar")
        for s in DMA_FAM["gp"]:
            GP.wait_ge(sems[s], CN[s])
        inc(GP.nop(), "bar")
        inc(TE.nop(), "bar")
        inc(VE.nop(), "bar")
        tgt = CN["bar"]
        for eng in (TE, VE, SC, GP, SY):
            eng.wait_ge(sems["bar"], tgt)

    def _finish():
        def _replay(key):
            def f(eng):
                for (name, a, kw, p) in REC[key]:
                    if key in ("ve", "te", "sc") and name == "nop" and p.incs:
                        eng.drain()
                    r = getattr(eng, name)(*a, **kw)
                    for (sem, n) in p.incs:
                        r.then_inc(sem, n)
            return f
        block.sync(_replay("sy"))
        block.scalar(_replay("sc"))
        block.vector(_replay("ve"))
        block.tensor(_replay("te"))
        block.gpsimd(_replay("gp"))
        return nc, es

    dsem("gpc", "gp")
    inc(GP.dma_start(out=wkvbf_d[:, :], in_=qkvw_d[:, :]), "gpc", 16)
    inc(GP.dma_start(out=wpjbf_d[:, :], in_=pjw_d[:, :]), "gpc", 16)
    gp_casts = dict(CN)

    VE.memset(ones2[:, :], 1.0)
    VE.memset(epsc[:, :], 1e-5)
    VE.memset(onesPb[:, :], 1.0)
    VE.memset(ind1e[0:64, 0:1], 1.0)
    VE.memset(ind1e[64:128, 0:1], 0.0)
    VE.memset(ind1o[0:64, 0:1], 0.0)
    VE.memset(ind1o[64:128, 0:1], 1.0)
    VE.memset(onetmp[:, :], 1.0)
    VE.drain()
    VE.tensor_scalar(magic[:, :], onetmp[:, :], float(2.0 ** -30), 1.0,
                     OP.add, OP.subtract)
    VE.drain()
    VE.tensor_scalar(magic[:, :], magic[:, :], CBIG, MAGIC32,
                     OP.mult, OP.add)
    VE.drain()

    p0 = contextlib.ExitStack()
    qb32 = p0.enter_context(nc.sbuf_tensor("qb32", [1, 3 * D], F32))
    hl16 = p0.enter_context(nc.sbuf_tensor("hl16", [1, 3 * D], BF16))
    hl32 = p0.enter_context(nc.sbuf_tensor("hl32", [1, 3 * D], F32))
    lo32 = p0.enter_context(nc.sbuf_tensor("lo32", [1, 3 * D], F32))
    lo16 = p0.enter_context(nc.sbuf_tensor("lo16", [1, 3 * D], BF16))
    vb32 = p0.enter_context(nc.sbuf_tensor("vb32", [1, 1040], F32))

    dsem("bia", "sy")
    inc(SY.dma_start(out=qb32[:, :], in_=qkvb_d.ap().rearrange("(a d) -> a d", a=1)), "bia", 16)
    inc(SY.dma_start(out=cl32[:, 0:1], in_=clx_d.ap().rearrange("(a d) -> a d", a=1)), "bia", 16)
    inc(SY.dma_start(out=cl32[:, 1:2], in_=cld_d.ap().rearrange("(a d) -> a d", a=1)), "bia", 16)
    ld_bias = CN["bia"]

    def hilo_row(src_ap, n, dst, dst_off):
        w(VE, "gpb") if "gpb" in sems else None
        VE.tensor_copy(hl16[:, 0:n], src_ap)
        VE.drain()
        VE.tensor_copy(hl32[:, 0:n], hl16[:, 0:n])
        VE.drain()
        VE.tensor_tensor(lo32[:, 0:n], src_ap, hl32[:, 0:n], OP.subtract)
        VE.drain()
        VE.tensor_copy(lo16[:, 0:n], lo32[:, 0:n])
        v = vsync()
        dsem("gpb", "gp")
        w(GP, "ve", v)
        inc(GP.dma_start(out=hlscr_d[0:1, 0:n], in_=hl16[:, 0:n]), "gpb", 16)
        inc(GP.dma_start(out=hlscr_d[1:2, 0:n], in_=lo16[:, 0:n]), "gpb", 16)
        w(GP, "gpb")
        inc(GP.dma_start(out=dst[0:2, dst_off:dst_off + n], in_=hlscr_d[0:2, 0:n]), "gpb", 16)

    w(VE, "bia", ld_bias)
    hilo_row(qb32[:, 0:2 * D], 2 * D, bhlqk, 0)
    VE.memset(vb32[:, :], 0.0)
    VE.tensor_copy(bass.AP(vb32, 0, [[1040, 1], [65, 16], [1, 64]]),
                   bass.AP(qb32, 2 * D, [[3 * D, 1], [64, 16], [1, 64]]))
    VE.memset(bass.AP(vb32, 64, [[1040, 1], [65, 16], [1, 1]]), 1.0)
    VE.drain()
    hilo_row(vb32[:, 0:1040], 1040, vbp, 0)
    pb32 = qb32
    pv = CN["ve"]
    w(SY, "ve", pv)
    w(SY, "gp")
    inc(SY.dma_start(out=pb32[:, 0:D], in_=pjb_d.ap().rearrange("(a d) -> a d", a=1)), "bia", 16)
    w(VE, "bia")
    hilo_row(pb32[:, 0:D], D, pbhl, 0)

    bias_gp = CN["gp"]
    barrier()
    p0.close()

    with contextlib.ExitStack() as p1:
        def sb1(name, shape, dtype):
            return p1.enter_context(nc.sbuf_tensor(name, shape, dtype))
        xb = sb1("xb", [128, 3, 4096], F32)
        sqb = sb1("sqb", [128, 3, 4096], F32)
        xn32b = sb1("xn32b", [128, 3, 4096], F32)
        xn16b = sb1("xn16b", [128, 3, 4096], BF16)
        st_sum = sb1("st_sum", [128, 3, 4], F32)
        st_sq = sb1("st_sq", [128, 3, 4], F32)
        st_m = sb1("st_m", [128, 3, 4], F32)
        st_e = sb1("st_e", [128, 3, 4], F32)
        st_v = sb1("st_v", [128, 3, 4], F32)
        st_sd = sb1("st_sd", [128, 3, 4], F32)
        st_r = sb1("st_r", [128, 3, 4], F32)
        st_n = sb1("st_n", [128, 3, 4], F32)

        slot_tok = [[], [], []]
        spill_tok = [[], [], []]
        x_rr = x_d.ap().rearrange("(b m p) d -> b p m d", p=128, m=4)
        xn32_rr = xn32_d.ap().rearrange("(b m p) d -> b p m d", p=128, m=4)
        xnbf_rr = xnbf_d.ap().rearrange("(b m p) d -> b p m d", p=128, m=4)
        stv = {}

        def p1_a(b):
            sl = b % 3
            for (s, v) in slot_tok[sl]:
                SY.wait_ge(sems[s], v)
            slot_tok[sl] = []
            xs_s = dsem(f"x{sl}", "sy")
            ldv = inc(SY.dma_start(out=bass.AP(xb, sl * 4096, [[12288, 128], [1024, 4], [1, 1024]]),
                                   in_=x_rr[b]), xs_s, 16)
            w(VE, xs_s, ldv)
            VE.tensor_reduce(st_sum[:, sl, :],
                             bass.AP(xb, sl * 4096, [[12288, 128], [1024, 4], [1, 1024]]),
                             AX, OP.add)
            w(SC, xs_s, ldv)
            sqv = inc(SC.activation(sqb[:, sl, :], xb[:, sl, :], ACT.Square), "ac")
            w(VE, "ac", sqv)
            VE.tensor_reduce(st_sq[:, sl, :],
                             bass.AP(sqb, sl * 4096, [[12288, 128], [1024, 4], [1, 1024]]),
                             AX, OP.add)
            VE.drain()
            VE.tensor_scalar(st_m[:, sl, :], st_sum[:, sl, :], 1.0 / D, None, OP.mult, OP.bypass)
            VE.tensor_scalar(st_e[:, sl, :], st_sq[:, sl, :], 1.0 / D, None, OP.mult, OP.bypass)
            VE.drain()
            VE.tensor_tensor(st_v[:, sl, :], st_m[:, sl, :], st_m[:, sl, :], OP.mult)
            VE.drain()
            VE.tensor_tensor(st_v[:, sl, :], st_e[:, sl, :], st_v[:, sl, :], OP.subtract)
            vv = vsync()
            w(SC, "ve", vv)
            stv[b] = inc(SC.activation(st_sd[:, sl, :], st_v[:, sl, :], ACT.Sqrt,
                                       bias=epsc[:, 0:1]), "ac")

        def p1_b(b):
            sl = b % 3
            sdv = stv.pop(b)
            w(VE, "ac", sdv)
            VE.reciprocal(st_r[:, sl, :], st_sd[:, sl, :])
            VE.drain()
            VE.tensor_tensor(st_n[:, sl, :], st_m[:, sl, :], st_r[:, sl, :], OP.mult)
            VE.drain()
            VE.tensor_scalar(st_n[:, sl, :], st_n[:, sl, :], -1.0, None, OP.mult, OP.bypass)
            VE.drain()
            for (s, v) in spill_tok[sl]:
                VE.wait_ge(sems[s], v)
            spill_tok[sl] = []
            for mm in range(4):
                VE.tensor_scalar(xn32b[:, sl, 1024 * mm:1024 * mm + 1024],
                                 xb[:, sl, 1024 * mm:1024 * mm + 1024],
                                 st_r[:, sl, mm:mm + 1], st_n[:, sl, mm:mm + 1],
                                 OP.mult, OP.add)
            xnv = vsync()
            VE.tensor_reduce(xmg[:, 4 * b:4 * b + 4],
                             bass.AP(xn32b, sl * 4096, [[12288, 128], [1024, 4], [1, 1024]]),
                             AX, OP.max, apply_absolute_value=True)
            xmv = vemark()
            w(SC, "ve", xnv)
            cav = inc(SC.activation(xn16b[:, sl, :], xn32b[:, sl, :], ACT.Copy), "ac")
            sp32 = dsem(f"sp32_{sl}", "sy")
            sp16 = dsem(f"sp16_{sl}", "sy")
            w(SY, "ve", xnv)
            s1 = inc(SY.dma_start(out=xn32_rr[b],
                                  in_=bass.AP(xn32b, sl * 4096, [[12288, 128], [1024, 4], [1, 1024]])),
                     sp32, 16)
            w(SY, "ac", cav)
            s2 = inc(SY.dma_start(out=xnbf_rr[b],
                                  in_=bass.AP(xn16b, sl * 4096, [[12288, 128], [1024, 4], [1, 1024]])),
                     sp16, 16)
            slot_tok[sl] = [(sp16, s2), ("ve", xmv), ("ac", cav)]
            spill_tok[sl] = [(sp32, s1), (sp16, s2)]

        p1_a(0)
        p1_a(1)
        for b in range(16):
            if b + 2 < 16:
                p1_a(b + 2)
            p1_b(b)
        barrier()
        p1_done = dict(CN)
        if stop_after == "p1":
            barrier()
            return _finish()

    def partition_max_to_scalar(src_col, out11):
        dsem("gpe", "gp")
        v0 = vemark()
        w(GP, "ve", v0)
        inc(GP.dma_start(out=red_d[:], in_=src_col), "gpe", 16)
        w(GP, "gpe")
        inc(GP.dma_start(out=trow[:, :], in_=red_d.ap().rearrange("(a d) -> a d", a=1)), "gpe", 16)
        w(VE, "gpe")
        VE.tensor_reduce(out11, trow[:, :], AX, OP.max)
        VE.drain()

    def bcast128(src11_f32, dst_col, tmp_off):
        VE.tensor_copy(tinyb[:, tmp_off:tmp_off + 1], src11_f32)
        VE.drain()
        VE.tensor_copy(tiny[:, tmp_off:tmp_off + 1], tinyb[:, tmp_off:tmp_off + 1])
        VE.drain()
        VE.tensor_tensor(tiny[:, tmp_off + 1:tmp_off + 2], src11_f32,
                         tiny[:, tmp_off:tmp_off + 1], OP.subtract)
        VE.drain()
        VE.tensor_copy(tinyb[:, tmp_off + 1:tmp_off + 2], tiny[:, tmp_off + 1:tmp_off + 2])
        v1 = vsync()
        w(TE, "ve", v1)
        TE.matmul(ps_xA[0:128, 0:1], ones2[0:1, 0:128], tinyb[:, tmp_off:tmp_off + 1],
                  start=True, stop=False)
        t1 = inc(TE.matmul(ps_xA[0:128, 0:1], ones2[0:1, 0:128],
                           tinyb[:, tmp_off + 1:tmp_off + 2], start=False, stop=True), "te")
        w(VE, "te", t1)
        VE.tensor_copy(dst_col, ps_xA[0:128, 0:1])
        vd = vemark()
        return vd

    VE.tensor_reduce(onetmp[:, :], xmg[:, :], AX, OP.max)
    VE.drain()
    partition_max_to_scalar(onetmp[:, 0:1], tiny[:, 8:9])
    v0 = vemark()
    w(GP, "ve", v0)
    inc(GP.dma_start(out=cc1i_d[:, :], in_=tiny[:, 8:9]), "gpe", 16)
    w(GP, "gpe")
    ccv = inc(GP.collective_compute("AllReduce", OP.max,
                                    replica_groups=[list(range(NCORES))],
                                    ins=[cc1i_d.ap()], outs=[cc1o_d.ap()]), "cc")
    w(GP, "cc", ccv)
    g1 = inc(GP.dma_start(out=tiny[:, 9:10], in_=cc1o_d[:, :]), "gpe", 16)
    w(SC, "bia", ld_bias)
    sgv = inc(SC.activation(tiny[:, 10:11], cl32[:, 0:1], ACT.Sigmoid), "ac")
    w(VE, "gpe", g1)
    w(VE, "ac", sgv)
    VE.tensor_tensor(tiny[:, 11:12], tiny[:, 9:10], tiny[:, 10:11], OP.mult)
    VE.drain()
    VE.tensor_scalar(tiny[:, 11:12], tiny[:, 11:12], 1.0 / QMAX, None, OP.mult, OP.bypass)
    VE.drain()
    VE.reciprocal(tiny[:, 12:13], tiny[:, 11:12])
    VE.drain()
    bcast128(tiny[:, 11:12], sx_bc[:, 0:1], 0)
    bcast128(tiny[:, 12:13], rsx_bc[:, 0:1], 2)
    ex1_done = dict(CN)
    if stop_after == "ex1":
        barrier()
        return _finish()

    with contextlib.ExitStack() as p2:
        def sb2(name, shape, dtype):
            return p2.enter_context(nc.sbuf_tensor(name, shape, dtype))
        tcos = sb2("tcos_s", [128, S], BF16)
        tsin = sb2("tsin_s", [128, S], BF16)
        permt = sb2("permt", [128, 128], BF16)
        trit = sb2("trit", [128, 128], BF16)
        wqkvst = sb2("wqkvst", [128, 2, DCH, 128], BF16)
        wvp = sb2("wvp", [128, DCH, 1040], BF16)
        vstage = sb2("vstage", [128, D], BF16)
        xnf = sb2("xnf", [128, DCH, S], BF16)
        qro = sb2("qro", [128, DCH, S], BF16)
        kro = sb2("kro", [128, DCH, S], BF16)
        vt = sb2("vt", [128, 16, 1040], BF16)
        PT = sb2("PTt", [128, 4, 1024], BF16)
        qraw = sb2("qraw", [128, 2, 512], BF16)
        rsw = sb2("rsw", [128, 2, 512], BF16)
        tmpA = sb2("tmpA", [128, 512], BF16)
        tmpC = sb2("tmpC", [128, 512], BF16)
        nqrow = sb2("nqrow", [1, 16], F32)
        nkrow = sb2("nkrow", [1, 16], F32)
        nrm8 = sb2("nrm8", [1, 8], F32)
        brow = sb2("brow", [1, 16], F32)
        browb = sb2("browb", [1, 16], BF16)
        negbc = sb2("negbc", [128, 16], F32)
        recf = sb2("recf", [65, 1024], F32)
        rh16 = sb2("rh16", [65, 1024], BF16)
        atb = [sb2("atb0", [128, 512], BF16), sb2("atb1", [128, 512], BF16)]
        wqt = sb2("wqt", [128, 1, 1408], F32)
        wqo = sb2("wqo", [128, 1, 1408], BF16)
        wqr = sb2("wqr", [128, 1, 4], F32)
        nrm4 = sb2("nrm4", [2, 4], F32)

        dsem("cst", "sy")
        for t_, d_ in ((tcos, tcos_d), (tsin, tsin_d),
                       (permt, perm_d), (trit, tri_d)):
            inc(SY.dma_start(out=t_[:, :], in_=d_[:, :]), "cst", 16)
        ld_const2 = CN["cst"]

        w(SC, "gpc", gp_casts["gpc"])
        VE.memset(wvp[:, :, :], 0.0)
        wvp_ve0 = vemark()
        for c in range(DCH):
            w(SC, "ve", wvp_ve0)
            for (s, v) in [("ve", CN["ve"])] if c > 0 else []:
                SC.wait_ge(sems[s], v)
            tv = inc(SC.dma_start_transpose(vstage[:, :],
                                            wkvbf_d[2 * D:3 * D, 128 * c:128 * c + 128]),
                     "tr", 16)
            w(VE, "tr", tv)
            VE.tensor_copy(bass.AP(wvp, c * 1040, [[DCH * 1040, 128], [65, 16], [1, 64]]),
                           bass.AP(vstage, 0, [[D, 128], [64, 16], [1, 64]]))
            vsync()
        wvp_v = CN["ve"]

        dsem("wql", "sy")
        dsem("wqs", "sy")
        wq_toks = [[]]

        def quant_weights(src_d, dst_d, sc_tile, monly, width, dequant=False):
            nch = monly + 1
            toks = wq_toks
            halves = [(0, width)] if width <= 1408 else [(0, width // 2), (width // 2, width)]
            src_rr = src_d.ap().rearrange("(m p) d -> m p d", p=128)
            dst_rr = dst_d.ap().rearrange("(m p) d -> m p d", p=128)
            for m in (monly,):
                sl = 0
                lvs = []
                red_ve = 0
                for hi_, (a, b) in enumerate(halves):
                    for (s, v) in toks[sl]:
                        SY.wait_ge(sems[s], v)
                    toks[sl] = []
                    w(SY, "ve", red_ve)
                    lv = inc(SY.dma_start(out=wqt[:, sl, 0:b - a], in_=src_rr[m][:, a:b]),
                             "wql", 16)
                    lvs.append(lv)
                    w(VE, "wql", lv)
                    VE.tensor_reduce(wqr[:, sl, hi_:hi_ + 1], wqt[:, sl, 0:b - a], AX,
                                     OP.max, apply_absolute_value=True)
                    VE.drain()
                    red_ve = vemark()
                    if hi_ == len(halves) - 1:
                        VE.tensor_reduce(wqr[:, sl, 2:3], wqr[:, sl, 0:len(halves)], AX, OP.max)
                        VE.drain()
                        VE.tensor_scalar(sc_tile[:, m:m + 1], wqr[:, sl, 2:3], 1.0 / QMAX,
                                         None, OP.mult, OP.bypass)
                        VE.drain()
                        VE.reciprocal(wqr[:, sl, 3:4], sc_tile[:, m:m + 1])
                        VE.drain()
                if len(halves) == 1:
                    VE.tensor_scalar(wqt[:, sl, 0:width], wqt[:, sl, 0:width],
                                     wqr[:, sl, 3:4], magic[:, 0:1], OP.mult, OP.add)
                    VE.drain()
                    if dequant:
                        VE.tensor_scalar(wqt[:, sl, 0:width], wqt[:, sl, 0:width],
                                         magic[:, 0:1], sc_tile[:, m:m + 1],
                                         OP.subtract, OP.mult)
                        VE.drain()
                        VE.tensor_copy(wqo[:, sl, 0:width], wqt[:, sl, 0:width])
                    else:
                        VE.tensor_scalar(wqo[:, sl, 0:width], wqt[:, sl, 0:width],
                                         magic[:, 0:1], None, OP.subtract, OP.bypass)
                    qv = vsync()
                    w(SY, "ve", qv)
                    sv = inc(SY.dma_start(out=dst_rr[m], in_=wqo[:, sl, 0:width]), "wqs", 16)
                    toks[sl] = [("wqs", sv)]
                else:
                    prev_sv = None
                    for hi_ in (1, 0):
                        a, b = halves[hi_]
                        if hi_ == 0:
                            if prev_sv is not None:
                                SY.wait_ge(sems["wqs"], prev_sv)
                            lv = inc(SY.dma_start(out=wqt[:, sl, 0:b - a],
                                                  in_=src_rr[m][:, a:b]), "wql", 16)
                            w(VE, "wql", lv)
                        VE.tensor_scalar(wqt[:, sl, 0:b - a], wqt[:, sl, 0:b - a],
                                         wqr[:, sl, 3:4], magic[:, 0:1], OP.mult, OP.add)
                        VE.drain()
                        if dequant:
                            VE.tensor_scalar(wqt[:, sl, 0:b - a], wqt[:, sl, 0:b - a],
                                             magic[:, 0:1], sc_tile[:, m:m + 1],
                                             OP.subtract, OP.mult)
                            VE.drain()
                            VE.tensor_copy(wqo[:, sl, 0:b - a], wqt[:, sl, 0:b - a])
                        else:
                            VE.tensor_scalar(wqo[:, sl, 0:b - a], wqt[:, sl, 0:b - a],
                                             magic[:, 0:1], None, OP.subtract, OP.bypass)
                        qv = vsync()
                        w(SY, "ve", qv)
                        sv = inc(SY.dma_start(out=dst_rr[m][:, a:b], in_=wqo[:, sl, 0:b - a]),
                                 "wqs", 16)
                        prev_sv = sv
                        toks[sl] = [("wqs", sv)]

        wq_tasks = ([(wg_d, wgi_d, wgsc, m, D, False) for m in range(ICH)]
                    + [(wu_d, wui_d, wusc, m, D, False) for m in range(ICH)]
                    + [(wd_d, wdi_d, wdsc, m, IFF, True) for m in range(DCH)])


        def wq_emit(n):
            for _ in range(n):
                if not wq_tasks:
                    return
                sd, dd, sct, m, wdt, dq = wq_tasks.pop(0)
                quant_weights(sd, dd, sct, m, wdt, dequant=dq)

        xnbf_mat = xnbf_d
        flash_idx = 0
        xnf_tok = []
        qk_ps = [ps_avA, ps_avB]
        qk_tok = [[], []]
        rs_ps = [ps_xA, ps_xB]
        rs_tok = [[("ve", ex1_done["ve"])], []]
        qraw_tok = [[], []]
        pt_tok = [[], [], [], []]
        av_tok = [[], []]
        sx_tok = [[], []]
        at_tok = [[], []]
        prev_seq = None
        for s_i in range(NSEQ):
            t0 = s_i * S
            if prev_seq is not None:
                w(VE, "te", prev_seq["te"])
                w(VE, "ac", prev_seq["ac"])
            for (s, v) in xnf_tok:
                SC.wait_ge(sems[s], v)
            xnf_tok = []
            for s_ in ("sp16_0", "sp16_1"):
                SC.wait_ge(sems[s_], p1_done.get(s_, CN[s_]))
            for c in range(DCH):
                inc(SC.dma_start_transpose(xnf[:, c, :],
                                           xnbf_mat[t0:t0 + S, 128 * c:128 * c + 128]),
                    "tr", 16)
            xnf_ld = CN["tr"]

            wst_tok = [[], []]
            pend = None

            def rope_flush():
                nonlocal pend
                if pend is None:
                    return
                pi, dst, dch, cols, ct, st_, ev = pend
                pend = None
                w(TE, "ac", ev)
                rp = rs_ps[pi]
                for (s, v) in rs_tok[pi]:
                    TE.wait_ge(sems[s], v)
                rs_tok[pi] = []
                rv = inc(TE.matmul(rp[:, 0:512], permt[:, :], qraw[:, pi, :],
                                   start=True, stop=True), "te")
                w(VE, "te", rv)
                VE.tensor_copy(rsw[:, pi, :], rp[:, 0:512])
                rcv = vemark()
                rs_tok[pi].append(("ve", rcv))
                VE.tensor_tensor(tmpA[:, :], qraw[:, pi, :], ct[:, cols], OP.mult)
                VE.tensor_tensor(tmpC[:, :], rsw[:, pi, :], st_[:, cols], OP.mult)
                VE.tensor_tensor(dst[:, dch, cols], tmpA[:, :], tmpC[:, :], OP.add)
                dv = vsync()
                qk_tok[pi].append(("ve", dv))
                qraw_tok[pi].append(("ve", dv))

            for m in range(16):
                dst = qro if m < 8 else kro
                dch = m if m < 8 else m - 8
                ct, st_ = tcos, tsin
                msl = m % 2
                for (s, v) in wst_tok[msl]:
                    SC.wait_ge(sems[s], v)
                wst_tok[msl] = []
                wtv = inc(SC.dma_start_transpose(wqkvst[:, msl, :, :],
                                                 wkvbf_d[128 * m:128 * m + 128, :]),
                          "tr", 16)
                for tc in range(4):
                    pi = (m * 4 + tc) % 2
                    ps = qk_ps[pi]
                    w(TE, "tr", max(xnf_ld, wtv))
                    w(TE, "cst", ld_const2)
                    w(TE, "gp", bias_gp)
                    for (s, v) in qk_tok[pi]:
                        TE.wait_ge(sems[s], v)
                    qk_tok[pi] = []
                    for d in range(DCH):
                        TE.matmul(ps[:, 0:512], wqkvst[:, msl, d, :],
                                  xnf[:, d, 512 * tc:512 * tc + 512],
                                  start=(d == 0), stop=False)
                    mv = inc(TE.matmul(ps[:, 0:512], bhlqk[0:2, 128 * m:128 * m + 128],
                                       ones2[0:2, 0:512], start=False, stop=True), "te")
                    w(SC, "te", mv)
                    for (s, v) in qraw_tok[pi]:
                        SC.wait_ge(sems[s], v)
                    qraw_tok[pi] = []
                    ev = inc(SC.activation(qraw[:, pi, :], ps[:, 0:512], ACT.Copy), "ac")
                    qk_tok[pi].append(("ac", ev))
                    if tc == 3:
                        wst_tok[msl].append(("te", CN["te"]))
                    rope_flush()
                    pend = (pi, dst, dch, slice(512 * tc, 512 * tc + 512), ct, st_, ev)
            rope_flush()
            rope_done_v = CN["ve"]

            w(VE, "ve", wvp_v)
            vcnt = 0
            for tt in range(16):
                for (c0, c1) in ((0, 512), (512, 1024), (1024, 1040)):
                    pi = vcnt % 2
                    vcnt += 1
                    ps = qk_ps[pi]
                    for (s, v) in qk_tok[pi]:
                        TE.wait_ge(sems[s], v)
                    qk_tok[pi] = []
                    w(TE, "ve", wvp_v)
                    for d in range(DCH):
                        TE.matmul(ps[:, 0:c1 - c0], xnf[:, d, 128 * tt:128 * tt + 128],
                                  wvp[:, d, c0:c1], start=(d == 0), stop=False)
                    mv = inc(TE.matmul(ps[:, 0:c1 - c0], ones2[0:2, 0:128],
                                       vbp[0:2, c0:c1], start=False, stop=True), "te")
                    w(SC, "te", mv)
                    ev = inc(SC.activation(vt[:, tt, c0:c1], ps[:, 0:c1 - c0], ACT.Copy), "ac")
                    qk_tok[pi].append(("ac", ev))
            v_done = CN["ac"]

            sq_te = 0
            for (src, dstn) in ((qro, nqrow), (kro, nkrow)):
                for c in range(DCH):
                    w(VE, "ve", rope_done_v)
                    w(VE, "te", sq_te)
                    sqbf = bass.AP(PT, 0, [[4096, 128], [1, 2048]])
                    VE.tensor_tensor(sqbf, src[:, c, :], src[:, c, :], OP.mult)
                    sqv = vsync()
                    w(TE, "ve", sqv)
                    for tc in range(4):
                        for (s, v) in sx_tok[0]:
                            TE.wait_ge(sems[s], v)
                        sx_tok[0] = []
                        sq_sl = bass.AP(PT, 512 * tc, [[4096, 128], [1, 512]])
                        TE.matmul(ps_sA[0:1, 0:512], ind1e[:, 0:1],
                                  sq_sl, start=True, stop=True)
                        nv = inc(TE.matmul(ps_sA[0:1, 512:1024], ind1o[:, 0:1],
                                           sq_sl, start=True, stop=True), "te")
                        sq_te = nv
                        w(VE, "te", nv)
                        VE.tensor_reduce(nrm8[0:1, tc:tc + 1], ps_sA[0:1, 0:512], AX, OP.max)
                        VE.tensor_reduce(nrm8[0:1, 4 + tc:5 + tc], ps_sA[0:1, 512:1024], AX, OP.max)
                        rv = vemark()
                        sx_tok[0].append(("ve", rv))
                    VE.drain()
                    VE.tensor_reduce(dstn[0:1, 2 * c:2 * c + 1], nrm8[0:1, 0:4], AX, OP.max)
                    VE.tensor_reduce(dstn[0:1, 2 * c + 1:2 * c + 2], nrm8[0:1, 4:8], AX, OP.max)
                    VE.drain()
            VE.tensor_tensor(brow[:, :], nqrow[:, :], nkrow[:, :], OP.mult)
            nbv = vemark()
            w(SC, "ve", nbv)
            nsq = inc(SC.activation(brow[:, :], brow[:, :], ACT.Sqrt, bias=0.0), "ac")
            w(VE, "ac", nsq)
            VE.tensor_scalar(brow[:, :], brow[:, :], -0.125, None, OP.mult, OP.bypass)
            VE.drain()
            VE.tensor_copy(browb[:, :], brow[:, :])
            bbv = vsync()
            w(TE, "ve", bbv)
            for (s, v) in rs_tok[0]:
                TE.wait_ge(sems[s], v)
            rs_tok[0] = []
            nbm = inc(TE.matmul(ps_xA[0:128, 0:16], onesPb[0:1, 0:128], browb[0:1, 0:16],
                                start=True, stop=True), "te")
            w(VE, "te", nbm)
            VE.tensor_copy(negbc[:, :], ps_xA[0:128, 0:16])
            nbcv = vsync()
            rs_tok[0].append(("ve", nbcv))

            if stop_after == "rope0" and s_i == 0:
                w(SY, "ve", nbcv)
                for (t_, d_) in ((xnf, dxnf_d), (qro, dqro_d), (kro, dkro_d), (vt, dvt_d)):
                    inc(SY.dma_start(out=d_[:, :, :], in_=t_[:, :, :]), "sy", 16)
                inc(SY.dma_start(out=dnegb_d[:, :], in_=negbc[:, :]), "sy", 16)
                barrier()
                return _finish()
            w(SC, "ve", nbcv)
            ps_s = [ps_sA, ps_sB]
            pend_tails = []

            def tail_flush():
                while pend_tails:
                    ai, av, rxp, h_, ch_, r0_, r1_, qr_, t0_, asl, rhv = pend_tails.pop(0)
                    w(TE, "ve", rhv)
                    for (s, v) in rs_tok[ai]:
                        TE.wait_ge(sems[s], v)
                    rs_tok[ai] = []
                    rbm = inc(TE.matmul(rxp[0:64, 0:512], onesPb[64:65, 0:64],
                                        rh16[64:65, 512 * ai:512 * ai + 512],
                                        start=True, stop=True), "te")
                    w(VE, "te", rbm)
                    for (s, v) in at_tok[asl]:
                        VE.wait_ge(sems[s], v)
                    at_tok[asl] = []
                    VE.tensor_copy(recf[0:64, 512 * ai:512 * ai + 512], rxp[0:64, 0:512])
                    rcv = vemark()
                    rs_tok[ai].append(("ve", rcv))
                    VE.tensor_tensor(atb[asl][0:64, 0:512], av[0:64, 0:512],
                                     recf[0:64, 512 * ai:512 * ai + 512], OP.mult)
                    ntv = vsync()
                    av_tok[ai].append(("ve", ntv))
                    ab_s = dsem(f"atb{asl}", "sy")
                    w(SY, "ve", ntv)
                    spv = inc(SY.dma_start(out=attnb_d[ch_, r0_:r1_, t0_ + 512 * qr_:t0_ + 512 * qr_ + 512],
                                           in_=atb[asl][0:64, 0:512]), ab_s, 16)
                    at_tok[asl].append((ab_s, spv))

            class U:
                pass

            for chp in range(8):
                for qr in range(4):
                    qc = slice(512 * qr, 512 * qr + 512)
                    kts = list(range(4 * qr + 4))
                    groups = [kts[i:i + 2] for i in range(0, len(kts), 2)]
                    G = len(groups)
                    units = []
                    for par in range(2):
                        u = U()
                        u.h = 2 * chp + par
                        u.par = par
                        u.r0, u.r1 = 64 * par, 64 * par + 64
                        u.ai = par
                        u.av = qk_ps[par]
                        u.rxp = rs_ps[par]
                        u.sps = ps_s[par]
                        u.first_av = True
                        u.smv = [None] * G
                        u.epv = [None] * G
                        u.mkv = [None] * G
                        units.append(u)

                    def emit_scores(u, g):
                        grp = groups[g]
                        for (s, v) in sx_tok[u.par]:
                            TE.wait_ge(sems[s], v)
                        sx_tok[u.par] = []
                        w(TE, "ve", nbcv)
                        for j, kt in enumerate(grp):
                            mm_ = TE.matmul(u.sps[:, 512 * j:512 * j + 512],
                                            kro[u.r0:u.r1, chp, 128 * kt:128 * kt + 128],
                                            qro[u.r0:u.r1, chp, qc], start=True, stop=True)
                            if j == len(grp) - 1:
                                u.smv[g] = inc(mm_, "te")

                    def emit_exp(u, g):
                        grp = groups[g]
                        w(SC, "te", u.smv[g])
                        for (s, v) in pt_tok[2 * u.par + (g % 2)]:
                            SC.wait_ge(sems[s], v)
                        pt_tok[2 * u.par + (g % 2)] = []
                        pt = bass.AP(PT, (2 * u.par + (g % 2)) * 1024, [[4096, 128], [1, 512 * len(grp)]])
                        u.epv[g] = inc(SC.activation(pt, u.sps[:, 0:512 * len(grp)], ACT.Exp,
                                                     bias=negbc[:, u.h:u.h + 1], scale=0.125), "ac")
                        sx_tok[u.par].append(("ac", u.epv[g]))
                        need_ve = False
                        w(VE, "ac", u.epv[g])
                        base = (2 * u.par + (g % 2)) * 1024
                        for j, kt in enumerate(grp):
                            ja = kt - 4 * qr
                            if ja >= 0:
                                need_ve = True
                                if ja > 0:
                                    VE.memset(bass.AP(PT, base + 512 * j, [[4096, 128], [1, 128 * ja]]), 0.0)
                                mp = bass.AP(PT, base + 512 * j + 128 * ja, [[4096, 128], [1, 128]])
                                VE.tensor_tensor(mp, mp, trit[:, :], OP.mult)
                        u.mkv[g] = vsync() if need_ve else None

                    def emit_av(u, g):
                        grp = groups[g]
                        w(TE, "ac", u.epv[g])
                        if u.mkv[g] is not None:
                            w(TE, "ve", u.mkv[g])
                        w(TE, "ac", v_done)
                        if u.first_av:
                            for (s, v) in av_tok[u.ai]:
                                TE.wait_ge(sems[s], v)
                            av_tok[u.ai] = []
                        base = (2 * u.par + (g % 2)) * 1024
                        for j, kt in enumerate(grp):
                            last = (g == G - 1) and (j == len(grp) - 1)
                            mm_ = TE.matmul(u.av[0:65, 0:512], vt[:, kt, 65 * u.h:65 * u.h + 65],
                                            bass.AP(PT, base + 512 * j, [[4096, 128], [1, 512]]),
                                            start=u.first_av, stop=last)
                            u.first_av = False
                            if j == len(grp) - 1:
                                avv = inc(mm_, "te")
                        pt_tok[2 * u.par + (g % 2)].append(("te", avv))
                        u.avv = avv

                    uA, uB = units
                    emit_scores(uA, 0)
                    emit_scores(uB, 0)
                    tail_flush()
                    for g in range(G):
                        emit_exp(uA, g)
                        if g + 1 < G:
                            emit_scores(uA, g + 1)
                        emit_exp(uB, g)
                        emit_av(uA, g)
                        if g + 1 < G:
                            emit_scores(uB, g + 1)
                        emit_av(uB, g)
                    for u in units:
                        w(VE, "te", u.avv)
                        VE.reciprocal(recf[64:65, 512 * u.par:512 * u.par + 512],
                                      u.av[64:65, 0:512])
                        VE.drain()
                        VE.tensor_copy(rh16[64:65, 512 * u.par:512 * u.par + 512],
                                       recf[64:65, 512 * u.par:512 * u.par + 512])
                        rhv = vsync()
                        asl = u.par
                        pend_tails.append((u.ai, u.av, u.rxp, u.h, chp, u.r0, u.r1,
                                           qr, t0, asl, rhv))
                        flash_idx += 1
            tail_flush()
            qk_tok[0] += av_tok[0]
            qk_tok[1] += av_tok[1]
            av_tok = [[], []]
            xnf_tok = [("te", CN["te"])]
            prev_seq = dict(CN)
            wq_emit(14)
        wq_emit(99)
        wq_done = dict(CN)
        VE.tensor_scalar(scg[:, :], wgsc[:, :], sx_bc[:, 0:1], None, OP.mult, OP.bypass)
        VE.tensor_scalar(scu[:, :], wusc[:, :], sx_bc[:, 0:1], None, OP.mult, OP.bypass)
        scgu_v = vsync()
        p2_done = dict(CN)
        barrier()
        if stop_after == "p2":
            return _finish()

    late = contextlib.ExitStack()
    wpjf = late.enter_context(nc.sbuf_tensor("wpjf", [128, DCH, D], BF16))
    w(SC, "gpc", gp_casts["gpc"])
    for c in range(DCH):
        inc(SC.dma_start_transpose(wpjf[:, c, :], wpjbf_d[:, 128 * c:128 * c + 128]),
            "tr", 16)
    wpjf_tr = CN["tr"]
    with contextlib.ExitStack() as p25:
        def sb25(name, shape, dtype):
            return p25.enter_context(nc.sbuf_tensor(name, shape, dtype))
        xqf = sb25("xqf", [128, DCH, S], BF16)
        wgst = sb25("wgst", [128, 2, DCH, 128], BF16)
        wust = sb25("wust", [128, 2, DCH, 128], BF16)
        qx32 = sb25("qx32", [128, 2, 2048], F32)
        xq16 = sb25("xq16", [128, 2, 2048], BF16)
        gsb = sb25("gsb", [128, 2, 512], F32)
        silb = sb25("silb", [128, 2, 512], F32)
        usb = sb25("usb", [128, 2, 512], F32)
        hbuf = sb25("hbuf", [128, 2, 2048], F32)

        xn32_rr2 = xn32_d.ap().rearrange("(b m p) d -> b p m d", p=128, m=2)
        xqbf_rr = xqbf_d.ap().rearrange("(b m p) d -> b p m d", p=128, m=2)
        toks = [[], []]
        qld = {}

        def quant_load(tb):
            sl = tb % 2
            for (s, v) in toks[sl]:
                SY.wait_ge(sems[s], v)
            toks[sl] = []
            qx_s = dsem(f"qx{sl}", "sy")
            qld[tb] = inc(SY.dma_start(out=bass.AP(qx32, sl * 2048,
                                                   [[4096, 128], [1024, 2], [1, 1024]]),
                                       in_=xn32_rr2[tb]), qx_s, 16)

        def quant_ve(tb):
            sl = tb % 2
            w(VE, f"qx{sl}", qld[tb])
            VE.tensor_scalar(qx32[:, sl, :], qx32[:, sl, :], rsx_bc[:, 0:1], magic[:, 0:1],
                             OP.mult, OP.add)
            VE.tensor_scalar(qx32[:, sl, :], qx32[:, sl, :], magic[:, 0:1], QMAX,
                             OP.subtract, OP.min)
            VE.tensor_scalar(xq16[:, sl, :], qx32[:, sl, :], -QMAX - 1.0, None,
                             OP.max, OP.bypass)
            xqs_s = dsem(f"xqs{sl}", "sy")
            qv = vsync()
            w(SY, "ve", qv)
            sv = inc(SY.dma_start(out=xqbf_rr[tb],
                                  in_=bass.AP(xq16, sl * 2048, [[4096, 128], [1024, 2], [1, 1024]])),
                     xqs_s, 16)
            toks[sl] = [(xqs_s, sv), ("ve", qv)]

        for tb in range(8):
            quant_load(tb)
            quant_ve(tb)
        xq_spg = [(CN["xqs0"], CN["xqs1"])]

        gu_ps = [ps_avA, ps_xA, ps_avB, ps_xB]
        gu_tok = [[], [], [], []]
        tt_hv = [None, None]
        wgu_tok = [[], []]
        hb_tok = [[], []]
        xqf_tok = []
        hm_col = 0
        for b4 in range(4):
            tb0 = b4 * S
            for (s, v) in xqf_tok:
                SC.wait_ge(sems[s], v)
            xqf_tok = []
            SC.wait_ge(sems["xqs0"], xq_spg[b4][0])
            SC.wait_ge(sems["xqs1"], xq_spg[b4][1])
            for c in range(DCH):
                inc(SC.dma_start_transpose(xqf[:, c, :],
                                           xqbf_d[tb0:tb0 + S, 128 * c:128 * c + 128]),
                    "tr", 16)
            xqf_ld = CN["tr"]
            def gu_fetch(m):
                msl = m % 2
                for (s, v) in wgu_tok[msl]:
                    SC.wait_ge(sems[s], v)
                wgu_tok[msl] = []
                w(SC, "wqs", wq_done["wqs"])
                inc(SC.dma_start_transpose(wgst[:, msl, :, :],
                                           wgi_d[128 * m:128 * m + 128, :]), "tr", 16)
                return inc(SC.dma_start_transpose(wust[:, msl, :, :],
                                                  wui_d[128 * m:128 * m + 128, :]), "tr", 16)

            if b4 == 0:
                wtv_next = gu_fetch(0)
            for m in range(ICH):
                if b4 < 3:
                    if m < 8:
                        quant_load(8 * (b4 + 1) + m)
                    if 1 <= m <= 8:
                        quant_ve(8 * (b4 + 1) + m - 1)
                    if m == 8:
                        xq_spg.append((CN["xqs0"], CN["xqs1"]))
                hsl = (b4 * ICH + m) % 2
                msl = m % 2
                wtv = wtv_next
                if m + 1 < ICH:
                    wtv_next = gu_fetch(m + 1)
                elif b4 < 3:
                    wtv_next = gu_fetch(0)
                for (s, v) in hb_tok[hsl]:
                    VE.wait_ge(sems[s], v)
                hb_tok[hsl] = []
                for tc in range(4):
                    gi = (2 * tc) % 4
                    ui = gi + 1
                    for (s, v) in gu_tok[gi]:
                        TE.wait_ge(sems[s], v)
                    gu_tok[gi] = []
                    for (s, v) in gu_tok[ui]:
                        TE.wait_ge(sems[s], v)
                    gu_tok[ui] = []
                    w(TE, "tr", max(xqf_ld, wtv))
                    cols = slice(512 * tc, 512 * tc + 512)
                    gv = uv = None
                    for d in range(DCH):
                        mm_ = TE.matmul(gu_ps[gi][:, 0:512], wgst[:, msl, d, :],
                                        xqf[:, d, cols], start=(d == 0), stop=(d == DCH - 1))
                        if d == DCH - 1:
                            gv = inc(mm_, "te")
                    for d in range(DCH):
                        mm_ = TE.matmul(gu_ps[ui][:, 0:512], wust[:, msl, d, :],
                                        xqf[:, d, cols], start=(d == 0), stop=(d == DCH - 1))
                        if d == DCH - 1:
                            uv = inc(mm_, "te")
                    esl = tc % 2
                    w(SC, "te", gv)
                    w(SC, "ve", scgu_v)
                    if tt_hv[esl] is not None:
                        w(SC, "ve", tt_hv[esl])
                    slv = inc(SC.activation(silb[:, esl, :], gu_ps[gi][:, 0:512], ACT.Silu,
                                            scale=scg[:, m:m + 1]), "ac")
                    gu_tok[gi].append(("ac", slv))
                    w(SC, "te", uv)
                    uev = inc(SC.activation(usb[:, esl, :], gu_ps[ui][:, 0:512], ACT.Copy,
                                            scale=scu[:, m:m + 1]), "ac")
                    gu_tok[ui].append(("ac", uev))
                    w(VE, "ac", uev)
                    VE.tensor_tensor(hbuf[:, hsl, cols], silb[:, esl, :], usb[:, esl, :],
                                     OP.mult)
                    hv = vemark()
                    tt_hv[esl] = hv
                VE.tensor_reduce(hmg[:, hm_col:hm_col + 1], hbuf[:, hsl, :], AX, OP.max,
                                 apply_absolute_value=True)
                hmv = vsync()
                hm_col += 1
                hb_s = dsem(f"hb{hsl}", "sy")
                w(SY, "ve", hmv)
                hs = inc(SY.dma_start(out=h32_d[m, :, tb0:tb0 + S], in_=hbuf[:, hsl, :]),
                         hb_s, 16)
                hb_tok[hsl] = [(hb_s, hs)]
                wgu_tok[msl].append(("te", CN["te"]))
            xqf_tok = [("te", CN["te"])]
        p25_done = dict(CN)
        barrier()
        if stop_after == "p25":
            return _finish()

    VE.tensor_reduce(onetmp[:, :], hmg[:, :], AX, OP.max)
    VE.drain()
    partition_max_to_scalar(onetmp[:, 0:1], tiny[:, 8:9])
    v0 = vemark()
    w(GP, "ve", v0)
    inc(GP.dma_start(out=cc2i_d[:, :], in_=tiny[:, 8:9]), "gpe", 16)
    w(GP, "gpe")
    ccv = inc(GP.collective_compute("AllReduce", OP.max,
                                    replica_groups=[list(range(NCORES))],
                                    ins=[cc2i_d.ap()], outs=[cc2o_d.ap()]), "cc")
    w(GP, "cc", ccv)
    g2 = inc(GP.dma_start(out=tiny[:, 9:10], in_=cc2o_d[:, :]), "gpe", 16)
    sgv = inc(SC.activation(tiny[:, 10:11], cl32[:, 1:2], ACT.Sigmoid), "ac")
    w(VE, "gpe", g2)
    w(VE, "ac", sgv)
    VE.tensor_tensor(tiny[:, 11:12], tiny[:, 9:10], tiny[:, 10:11], OP.mult)
    VE.drain()
    VE.tensor_scalar(tiny[:, 11:12], tiny[:, 11:12], 1.0 / QMAX, None, OP.mult, OP.bypass)
    VE.drain()
    VE.reciprocal(tiny[:, 12:13], tiny[:, 11:12])
    VE.drain()
    bcast128(tiny[:, 11:12], sh_bc[:, 0:1], 4)
    bcast128(tiny[:, 12:13], rsh_bc[:, 0:1], 6)

    with contextlib.ExitStack() as p3:
        def sb3(name, shape, dtype):
            return p3.enter_context(nc.sbuf_tensor(name, shape, dtype))
        wdf = sb3("wdf", [128, ICH, D], BF16)
        inth = sb3("inth", [128, 2, ICH, 512], BF16)
        atst = sb3("atst", [128, 2, DCH, 512], BF16)
        hld = sb3("hld", [128, 2, 1024], F32)
        xb3 = sb3("xb3", [128, 2, 4096], F32)
        outb = sb3("outb", [128, 2, 4096], F32)

        w(SC, "wqs", wq_done["wqs"])
        for i in range(ICH):
            inc(SC.dma_start_transpose(wdf[:, i, :], wdi_d[:, 128 * i:128 * i + 128]),
                "tr", 16)
        wdf_tr = CN["tr"]

        x_rr3 = x_d.ap().rearrange("(tc m p) d -> tc p m d", p=128, m=4)
        out_rr = out_d.ap().rearrange("(tc m p) d -> tc p m d", p=128, m=4)
        tc_tok = [[], []]
        hld_tok = [[], []]
        ps3 = [(ps_sA, ps_avA, ps_xA), (ps_sB, ps_avB, ps_xB)]
        ps3_tok = [[], []]
        ob_tok = [[], []]
        stage = {}

        def p3_load(tc):
            sl = tc % 2
            for (s, v) in tc_tok[sl]:
                SY.wait_ge(sems[s], v)
            tc_tok[sl] = []
            ats_s = dsem(f"ats{sl}", "sy")
            xb_s = dsem(f"xb{sl}", "sy")
            for c in range(DCH):
                inc(SY.dma_start(out=atst[:, sl, c, :],
                                 in_=attnb_d[c, :, 512 * tc:512 * tc + 512]), ats_s, 16)
            at_ld = CN[ats_s]
            xl = inc(SY.dma_start(out=bass.AP(xb3, sl * 4096, [[8192, 128], [1024, 4], [1, 1024]]),
                                  in_=x_rr3[tc]), xb_s, 16)
            for i2 in range(0, ICH, 2):
                npair = 2 if i2 + 1 < ICH else 1
                hsl = (i2 // 2) % 2
                for (s, v) in hld_tok[hsl]:
                    SY.wait_ge(sems[s], v)
                hld_tok[hsl] = []
                hl_s = dsem(f"hl{hsl}", "sy")
                lv = inc(SY.dma_start(out=bass.AP(hld, hsl * 1024, [[2048, 128], [1, 512 * npair]]),
                                      in_=bass.AP(h32_d, i2 * 128 * T + 512 * tc,
                                                  [[T, 128], [128 * T, npair], [1, 512]])),
                         hl_s, 16)
                w(VE, hl_s, lv)
                fd = 512 * npair
                hap = bass.AP(hld, hsl * 1024, [[2048, 128], [1, fd]])
                iap = bass.AP(inth, sl * (ICH * 512) + i2 * 512,
                              [[2 * ICH * 512, 128], [1, fd]])
                VE.tensor_scalar(hap, hap, rsh_bc[:, 0:1],
                                 magic[:, 0:1], OP.mult, OP.add)
                VE.tensor_scalar(hap, hap, magic[:, 0:1],
                                 QMAX, OP.subtract, OP.min)
                VE.tensor_scalar(iap, hap, -QMAX - 1.0,
                                 None, OP.max, OP.bypass)
                qv = vsync()
                hld_tok[hsl] = [("ve", qv)]
                tc_tok[sl].append(("ve", qv))
            stage[tc] = (CN["ve"], at_ld, xl, ats_s, xb_s)

        def p3_compute(tc):
            sl = tc % 2
            hq_done, at_ld, xl, ats_s, xb_s = stage.pop(tc)
            for tt in range(4):
                pp = (tc * 4 + tt) % 2
                prj, dn0, dn1 = ps3[pp]
                for (s, v) in ps3_tok[pp]:
                    TE.wait_ge(sems[s], v)
                ps3_tok[pp] = []
                w(TE, ats_s, at_ld)
                w(TE, "ve", hq_done)
                w(TE, "tr", wdf_tr)
                tsl = slice(128 * tt, 128 * tt + 128)
                for cc_i, cols in enumerate((slice(0, 512), slice(512, 1024))):
                    for d in range(DCH):
                        TE.matmul(prj[:, cols], atst[:, sl, d, tsl], wpjf[:, d, cols],
                                  start=(d == 0), stop=False)
                    TE.matmul(prj[:, cols], ones2[0:2, 0:128], pbhl[0:2, cols],
                              start=False, stop=True)
                    dps = dn0 if cc_i == 0 else dn1
                    for i in range(ICH):
                        TE.matmul(dps[:, 0:512], inth[:, sl, i, tsl], wdf[:, i, cols],
                                  start=(i == 0), stop=(i == ICH - 1))
                mv = temark()
                w(VE, "te", mv)
                w(VE, xb_s, xl)
                for (s, v) in (ob_tok[sl] if tt == 0 else []):
                    VE.wait_ge(sems[s], v)
                if tt == 0:
                    ob_tok[sl] = []
                for cc_i in range(2):
                    dps = (dn0, dn1)[cc_i]
                    cols = slice(512 * cc_i, 512 * cc_i + 512)
                    ocol = slice(1024 * tt + 512 * cc_i, 1024 * tt + 512 * cc_i + 512)
                    VE.tensor_scalar(outb[:, sl, ocol], dps[:, 0:512], sh_bc[:, 0:1], None,
                                     OP.mult, OP.bypass)
                    VE.tensor_tensor(outb[:, sl, ocol], outb[:, sl, ocol], prj[:, cols], OP.add)
                    VE.tensor_tensor(outb[:, sl, ocol], outb[:, sl, ocol],
                                     xb3[:, sl, ocol], OP.add)
                ev = vsync()
                ps3_tok[pp].append(("ve", ev))
            ov_s = dsem(f"ov{sl}", "sy")
            w(SY, "ve", CN["ve"])
            ov = inc(SY.dma_start(out=out_rr[tc],
                                  in_=bass.AP(outb, sl * 4096, [[8192, 128], [1024, 4], [1, 1024]])),
                     ov_s, 16)
            tc_tok[sl].append((ov_s, ov))
            tc_tok[sl].append(("te", CN["te"]))
            tc_tok[sl].append(("ve", CN["ve"]))
            ob_tok[sl].append((ov_s, ov))

        p3_load(0)
        for tc in range(16):
            if tc + 1 < 16:
                p3_load(tc + 1)
            p3_compute(tc)
            if stop_after == "p3serial":
                barrier()
        SY.wait_ge(sems["ov0"], CN["ov0"])
        SY.wait_ge(sems["ov1"], CN["ov1"])
    late.close()

    return _finish()


_CACHE = {}


def kernel(L0_attn_x, L0_attn_qkv_w, L0_attn_qkv_b, L0_attn_proj_w, L0_attn_proj_b,
           L0_ffn_clip_x, L0_ffn_clip_down, L0_ffn_w_gate, L0_ffn_w_up, L0_ffn_w_down):
    consts = _host_consts()
    if "nc" not in _CACHE:
        _CACHE["nc"] = build_program()
    nc, es = _CACHE["nc"]
    x = np.ascontiguousarray(np.asarray(L0_attn_x, dtype=np.float32))
    shared = {
        "qkv_w": np.ascontiguousarray(np.asarray(L0_attn_qkv_w, np.float32)),
        "qkv_b": np.ascontiguousarray(np.asarray(L0_attn_qkv_b, np.float32)),
        "proj_w": np.ascontiguousarray(np.asarray(L0_attn_proj_w, np.float32)),
        "proj_b": np.ascontiguousarray(np.asarray(L0_attn_proj_b, np.float32)),
        "clip_x": np.ascontiguousarray(np.asarray(L0_ffn_clip_x, np.float32)),
        "clip_down": np.ascontiguousarray(np.asarray(L0_ffn_clip_down, np.float32)),
        "w_gate": np.ascontiguousarray(np.asarray(L0_ffn_w_gate, np.float32)),
        "w_up": np.ascontiguousarray(np.asarray(L0_ffn_w_up, np.float32)),
        "w_down": np.ascontiguousarray(np.asarray(L0_ffn_w_down, np.float32)),
        **consts,
    }
    in_maps = []
    for c in range(NCORES):
        im = dict(shared)
        im["x"] = np.ascontiguousarray(x[NSEQ * c:NSEQ * (c + 1)].reshape(T, D))
        in_maps.append(im)
    res = run_bass_kernel_spmd(nc, in_maps, core_ids=list(range(NCORES)))
    out = np.concatenate(
        [np.asarray(r["out"], dtype=np.float32).reshape(NSEQ, S, D) for r in res.results],
        axis=0)
    return out
